# version 03
# baseline (speedup 1.0000x reference)
import math
import numpy as np
import ml_dtypes
from contextlib import ExitStack

import concourse.bass as bass
import concourse.mybir as mybir
from concourse.bass_utils import run_bass_kernel_spmd

F32 = mybir.dt.float32
BF16 = mybir.dt.bfloat16
AF = mybir.ActivationFunctionType
ALU = mybir.AluOpType
AX = mybir.AxisListType

D = 1024
T = 2048
NCH = D // 128
NTT = T // 128
N_CORES = 8

ENGINES = ("sync", "scalar", "vector", "gpsimd", "tensor")


class Sched:
    def __init__(self, nc):
        self.nc = nc
        self.items = {e: [] for e in ENGINES}
        self.cnt = {e: 0 for e in ENGINES}

    def add(self, eng, emit, waits=(), dma=False, n_dma=1):
        inc = 16 * n_dma if dma else 1
        for (we, wc) in waits:
            assert wc <= self.cnt[we], (we, wc, self.cnt[we])
        self.items[eng].append((emit, tuple(waits), inc, dma))
        self.cnt[eng] += inc
        return (eng, self.cnt[eng])

    def add0(self, eng, emit, waits=()):
        for (we, wc) in waits:
            assert wc <= self.cnt[we], (we, wc, self.cnt[we])
        self.items[eng].append((emit, tuple(waits), 0, False))

    def barrier(self):
        return tuple((e, self.cnt[e]) for e in ENGINES if self.cnt[e] > 0)

    def full_barrier(self):
        snap = {e: self.cnt[e] for e in ENGINES}
        toks = {}
        for e in ENGINES:
            waits = [(e2, snap[e2]) for e2 in ENGINES if snap[e2] > 0]
            toks[e] = self.add(e, lambda q: q.nop(), waits=waits)
        return toks

    def realize(self, es):
        nc = self.nc
        sems = {e: es.enter_context(nc.semaphore("S_" + e)) for e in ENGINES}
        blk = es.enter_context(nc.Block())

        def mk(eng):
            def body(q):
                mysem = sems[eng]
                for emit, waits, inc, dma in self.items[eng]:
                    for (we, wc) in waits:
                        q.wait_ge(sems[we], wc)
                    if dma:
                        emit(q, mysem)
                    else:
                        last = emit(q)
                        if inc:
                            assert last is not None, f"compute step on {eng} returned None"
                            last.then_inc(mysem, inc)
            return body

        for e in ENGINES:
            if self.items[e]:
                getattr(blk, e)(mk(e))
        self.sems = sems



def ceil_div(a, b):
    return (a + b - 1) // b


class _CarveHandle:

    def __init__(self, ap):
        self._ap = ap

    def __getitem__(self, key):
        return self._ap[key]

    def bitcast(self, dt_):
        return self._ap.bitcast(dt_)

    def rearrange(self, *a, **kw):
        return self._ap.rearrange(*a, **kw)


class Builder:

    ARENA_WORDS = 53000
    GLOB_WORDS = 1024

    def __init__(self, name, fused=False):
        self.nc = bass.Bass("TRN2", target_bir_lowering=False, debug=False)
        self.s = Sched(self.nc)
        self.es = ExitStack()
        self._uniq = 0
        self.fused = fused
        if fused:
            self.arena = self.es.enter_context(self.nc.sbuf_tensor(
                "arena", [128, self.ARENA_WORDS], F32))
            self.arena_off = self.GLOB_WORDS
            self.glob_off = 0
            self.pbanks = [self.es.enter_context(self.nc.psum_tensor(
                f"pbank{i}", [128, 512], F32)) for i in range(8)]
            self.pbank_i = 0

    def reset_arena(self):
        self.arena_off = self.GLOB_WORDS
        self.pbank_i = 0

    def _carve(self, shape, dtype, glob=False):
        assert shape[0] <= 128
        per = 1
        for d in shape[1:]:
            per *= d
        words = ceil_div(per * mybir.dt.size(dtype), 4)
        if glob:
            off = self.glob_off
            self.glob_off += words
            assert self.glob_off <= self.GLOB_WORDS
        else:
            off = self.arena_off
            self.arena_off += words
            assert self.arena_off <= self.ARENA_WORDS, (
                f"arena overflow: {self.arena_off} words")
        view = self.arena[0:shape[0], off:off + words]
        if dtype != F32:
            view = view.bitcast(dtype)[0:shape[0], 0:per]
        else:
            view = view[0:shape[0], 0:per]
        if len(shape) == 2:
            return view if per == shape[1] else view[:, 0:shape[1]]
        if len(shape) == 3:
            return view.rearrange("p (a b) -> p a b", b=shape[2])
        raise AssertionError(shape)

    def uname(self, base):
        self._uniq += 1
        return f"{base}_{self._uniq}"

    def cst(self, val):
        if not hasattr(self, "_csts"):
            self._csts = {}
        if val not in self._csts:
            if self.fused:
                t = _CarveHandle(self._carve([128, 1], F32, glob=True))
            else:
                t = self.sbuf("cst", [128, 1], F32)
            tok = self.s.add("gpsimd", lambda q, t=t, val=val: q.memset(t[:], val))
            self._csts[val] = (t[:, 0:1], tok)
        return self._csts[val]

    def sbuf(self, base, shape, dtype):
        if self.fused:
            return _CarveHandle(self._carve(shape, dtype))
        return self.es.enter_context(self.nc.sbuf_tensor(self.uname(base), shape, dtype))

    pref = ""

    def dram_in(self, name, shape, dtype=F32):
        return self.nc.dram_tensor(self.pref + name, shape, dtype,
                                   kind="ExternalInput").ap()

    def dram_out(self, name, shape, dtype=F32):
        return self.nc.dram_tensor(self.pref + name, shape, dtype,
                                   kind="ExternalOutput").ap()

    def dram_tmp(self, name, shape, dtype=F32):
        return self.nc.dram_tensor(self.uname(name), shape, dtype, kind="Internal").ap()

    def alloc_psum_banks(self, n=8, dtype=F32, cols=512):
        if self.fused:
            out = []
            for i in range(n):
                bk = self.pbanks[self.pbank_i]
                self.pbank_i += 1
                v = bk[:] if dtype == F32 else bk[:].bitcast(dtype)
                out.append(_CarveHandle(v[:, 0:cols]))
            return out
        banks = []
        for i in range(n):
            banks.append(self.es.enter_context(
                self.nc.psum_tensor(self.uname("pb"), [128, cols], dtype)))
        return banks

    def finish(self):
        self.s.realize(self.es)
        self.es.close()
        return self.nc



def load_fm(b, dst, src_hbm, waits=(), queue="sync", cast=False):
    Dh, Tp = src_hbm.shape

    def emit(q, sem):
        q.dma_start(out=dst[:],
                    in_=src_hbm.rearrange("(c p) t -> p c t", p=128)).then_inc(sem, 16)
    return b.s.add(queue, emit, waits=waits, dma=True)


def store_fm(b, dst_hbm, src, waits=(), queue="sync"):
    Dh, Tp = dst_hbm.shape

    def emit(q, sem):
        q.dma_start(out=dst_hbm.rearrange("(c p) t -> p c t", p=128),
                    in_=src[:]).then_inc(sem, 16)
    return b.s.add(queue, emit, waits=waits, dma=True)


MAGIC6 = 1.5 * 2.0 ** 23 / 8.0
MAGIC4 = 1.5 * 2.0 ** 23 / 2.0


def emit_quant_tm(b, x_tm, out_bf, scr, n_cols, block, fmax, waits, engines=("vector",)):
    nb = n_cols // block
    v = engines[0]
    s = b.s
    eps = 1e-12 if fmax == 6.0 else 1e-30
    base_magic = MAGIC4 if fmax == 6.0 else MAGIC6
    t1 = s.add(v, lambda q: q.reduce_max(
        scr["bm"][:, 0:nb], x_tm.rearrange("p (n k) -> p n k", k=block),
        axis=AX.X, apply_absolute_value=True), waits=waits)
    t2 = s.add(v, lambda q: q.tensor_scalar(
        scr["bm"][:, 0:nb], scr["bm"][:, 0:nb], 1.0 / fmax, eps, ALU.mult, ALU.max),
        waits=[t1])
    t3 = s.add(v, lambda q: q.reciprocal(scr["inv"][:, 0:nb], scr["bm"][:, 0:nb]),
               waits=[t2])
    inv_bc = scr["inv"][:, 0:nb].broadcast_to([128, nb, block])
    t4 = s.add(v, lambda q: q.tensor_tensor(
        scr["a"][:, 0:n_cols].rearrange("p (n k) -> p n k", k=block),
        x_tm.rearrange("p (n k) -> p n k", k=block), inv_bc, ALU.mult),
        waits=[t3])
    t5 = s.add(v, lambda q: q.tensor_tensor(
        scr["sq"][:, 0:n_cols], scr["a"][:, 0:n_cols], scr["a"][:, 0:n_cols], ALU.mult),
        waits=[t4])
    t6 = s.add(v, lambda q: q.tensor_scalar(
        scr["g"][:, 0:n_cols], scr["sq"][:, 0:n_cols], 4.0, base_magic, ALU.is_gt, ALU.mult),
        waits=[t5])
    t7 = s.add(v, lambda q: q.tensor_scalar(
        scr["c"][:, 0:n_cols], scr["sq"][:, 0:n_cols], 16.0, 2.0 * base_magic, ALU.is_gt, ALU.mult),
        waits=[t5])
    t8 = s.add(v, lambda q: q.tensor_tensor(
        scr["c"][:, 0:n_cols], scr["c"][:, 0:n_cols], scr["g"][:, 0:n_cols], ALU.add),
        waits=[t6, t7])
    t9 = s.add(v, lambda q: q.tensor_scalar_add(
        scr["c"][:, 0:n_cols], scr["c"][:, 0:n_cols], base_magic), waits=[t8])
    tsg = s.add("scalar", lambda q: q.activation(
        scr["sg"][:, 0:n_cols], scr["a"][:, 0:n_cols], AF.Sign), waits=[t4])
    t10 = s.add(v, lambda q: q.tensor_tensor(
        scr["c"][:, 0:n_cols], scr["c"][:, 0:n_cols], scr["sg"][:, 0:n_cols], ALU.mult),
        waits=[t9, tsg])
    t11 = s.add(v, lambda q: q.tensor_tensor(
        scr["a"][:, 0:n_cols], scr["a"][:, 0:n_cols], scr["c"][:, 0:n_cols], ALU.add),
        waits=[t10])
    t12 = s.add(v, lambda q: q.tensor_tensor(
        scr["a"][:, 0:n_cols], scr["a"][:, 0:n_cols], scr["c"][:, 0:n_cols], ALU.subtract),
        waits=[t11])
    sc_bc = scr["bm"][:, 0:nb].broadcast_to([128, nb, block])
    t13 = s.add(v, lambda q: q.tensor_tensor(
        out_bf.rearrange("p (n k) -> p n k", k=block),
        scr["a"][:, 0:n_cols].rearrange("p (n k) -> p n k", k=block), sc_bc, ALU.mult),
        waits=[t12])
    return t13


def np_quant_ref(x, block, fmax):
    shape = x.shape
    xb = x.reshape(-1, block)
    scale = np.maximum(np.abs(xb).max(-1, keepdims=True) / fmax,
                       1e-12 if fmax == 6.0 else 1e-30)
    xn = xb / scale
    if fmax == 6.0:
        GRID = np.array([0.0, 0.5, 1.0, 1.5, 2.0, 3.0, 4.0, 6.0], np.float32)
        MID = np.array([0.25, 0.75, 1.25, 1.75, 2.5, 3.5, 5.0], np.float32)
        idx = np.searchsorted(MID, np.abs(xn))
        q = GRID[idx] * np.sign(xn)
    else:
        vals = set()
        for bits in range(64):
            s_ = bits >> 5 & 1
            e = bits >> 3 & 3
            m = bits & 7
            vv = m / 8.0 if e == 0 else (1.0 + m / 8.0) * 2.0 ** (e - 1)
            vals.add(-vv if s_ else vv)
        grid = np.array(sorted(vals), np.float32)
        idx = np.clip(np.searchsorted(grid, xn), 0, grid.shape[0] - 1)
        idx_lo = np.maximum(idx - 1, 0)
        v_hi = grid[idx]
        v_lo = grid[idx_lo]
        q = np.where(np.abs(xn - v_lo) <= np.abs(xn - v_hi), v_lo, v_hi)
    return (q * scale).reshape(shape)


def emit_transpose_fm_tm(b, src, dst, ident, pbanks, waits, n_pch, n_tch,
                         src_dtype_bf=False, tok=None):
    s = b.s
    toks = []
    prev_evac = {}
    for j in range(n_tch):
        for c in range(n_pch):
            bank = pbanks[(j * n_pch + c) % len(pbanks)]
            w = list(waits)
            key = (j * n_pch + c) % len(pbanks)
            if key in prev_evac:
                w.append(prev_evac[key])
            tt = s.add("tensor", lambda q, c=c, j=j, bank=bank: q.transpose(
                bank[:, 0:128], src[:, c, j * 128:(j + 1) * 128], ident[:]),
                waits=w)
            te = s.add("scalar", lambda q, c=c, j=j, bank=bank: q.copy(
                dst[:, j, c * 128:(c + 1) * 128], bank[:, 0:128]), waits=[tt])
            prev_evac[key] = te
            toks.append(te)
    return toks[-1], toks


def emit_linear_fm(b, out, x_bf, wT, pbanks, waits, n_in_ch, n_out_ch, Tn,
                   bias=None, act=None, out_dtype_bf=False, evac_eng="scalar",
                   accum_into=None, out_off=0, prior_evacs=None, extra_x_off=0):
    s = b.s
    ntile = ceil_div(Tn, 512)
    evacs = []
    prev = dict(prior_evacs or {})
    last = None
    for oc in range(n_out_ch):
        for ti in range(ntile):
            t0 = ti * 512
            tw = min(512, Tn - t0)
            bk = (oc * ntile + ti) % len(pbanks)
            bank = pbanks[bk]
            w = list(waits)
            if bk in prev:
                w.append(prev[bk])

            def mm(q, oc=oc, t0=t0, tw=tw, bank=bank):
                r = None
                for ic in range(n_in_ch):
                    r = q.matmul(bank[:, 0:tw],
                                 wT[:, ic, oc * 128:(oc + 1) * 128],
                                 x_bf[:, ic, extra_x_off + t0:extra_x_off + t0 + tw],
                                 start=(ic == 0), stop=(ic == n_in_ch - 1))
                return r
            tm = s.add("tensor", mm, waits=w)

            def evac(q, oc=oc, t0=t0, tw=tw, bank=bank):
                dst = out[:, out_off + oc, t0:t0 + tw]
                if act is not None or bias is not None:
                    kw = {}
                    if bias is not None:
                        kw["bias"] = bias[:, oc:oc + 1]
                    return q.activation(dst, bank[:, 0:tw], act or AF.Identity, **kw)
                else:
                    return q.copy(dst, bank[:, 0:tw])
            te = s.add(evac_eng, evac, waits=[tm])
            prev[bk] = te
            evacs.append(te)
            last = te
    return last, prev


def emit_ln_tm_stats(b, x_tm, stats, n_tt, waits, eng_red="vector"):
    s = b.s
    n = n_tt
    tk = []
    for j in range(n_tt):
        t1 = s.add(eng_red, lambda q, j=j: q.reduce_sum(
            stats[:, j:j + 1], x_tm[:, j, :], axis=AX.X), waits=waits)
        t2 = s.add("scalar", lambda q, j=j: q.activation(
            b.lnscr[:, j * D:(j + 1) * D], x_tm[:, j, :], AF.Square,
            accum_out=stats[:, n + j:n + j + 1]), waits=waits)
        tk += [t1, t2]
    t3 = s.add("vector", lambda q: q.tensor_scalar_mul(
        stats[:, 2 * n:3 * n], stats[:, 0:n], 1.0 / D), waits=tk)
    t4 = s.add("vector", lambda q: q.tensor_tensor(
        stats[:, 3 * n:4 * n], stats[:, 2 * n:3 * n], stats[:, 2 * n:3 * n], ALU.mult),
        waits=[t3])
    t5 = s.add("vector", lambda q: q.scalar_tensor_tensor(
        stats[:, 3 * n:4 * n], stats[:, n:2 * n], 1.0 / D, stats[:, 3 * n:4 * n],
        ALU.mult, ALU.subtract), waits=[t4] + tk)
    t6 = s.add("scalar", lambda q: q.activation(
        stats[:, 3 * n:4 * n], stats[:, 3 * n:4 * n], AF.Ln, bias=1e-5), waits=[t5])
    t7 = s.add("scalar", lambda q: q.activation(
        stats[:, 3 * n:4 * n], stats[:, 3 * n:4 * n], AF.Exp, scale=-0.5), waits=[t6])
    return t7



def emit_ln_quant_tm_pipeline(b, x_fm, xd_fm, identf, identb, pbt, pbb, scr,
                              stats, lnscr, waits, fmax, block, n_tt=NTT,
                              tile_off=0):
    s = b.s
    last_q = None
    prev = {}
    last_tok = None
    for j in range(n_tt):
        x_tm = scr["xtm"]
        xd_tm = scr["xdtm"]
        w0 = list(waits) + ([last_q] if last_q else [])
        evs = []
        for c in range(NCH):
            bk = c % len(pbt)
            ww = list(w0)
            if bk in prev:
                ww.append(prev[bk])
            tt = s.add("tensor", lambda q, c=c, j=j, bk=bk: q.transpose(
                pbt[bk][:, 0:128], x_fm[:, c, j * 128:(j + 1) * 128], identf[:]),
                waits=ww)
            te = s.add("scalar", lambda q, c=c, bk=bk: q.copy(
                x_tm[:, c * 128:(c + 1) * 128], pbt[bk][:, 0:128]), waits=[tt])
            prev[bk] = te
            evs.append(te)
        t1 = s.add("vector", lambda q: q.reduce_sum(
            stats[:, 0:1], x_tm[:], axis=AX.X), waits=evs)
        t2 = s.add("scalar", lambda q: q.activation(
            lnscr[:], x_tm[:], AF.Square, accum_out=stats[:, 1:2]), waits=evs)
        t3 = s.add("vector", lambda q: q.tensor_scalar_mul(
            stats[:, 2:3], stats[:, 0:1], 1.0 / D), waits=[t1])
        t4 = s.add("vector", lambda q: q.tensor_tensor(
            stats[:, 3:4], stats[:, 2:3], stats[:, 2:3], ALU.mult), waits=[t3])
        t5 = s.add("vector", lambda q: q.scalar_tensor_tensor(
            stats[:, 3:4], stats[:, 1:2], 1.0 / D, stats[:, 3:4],
            ALU.mult, ALU.subtract), waits=[t2, t4])
        eps_ap, eps_tok = b.cst(1e-5)
        t6 = s.add("scalar", lambda q: q.activation(
            stats[:, 3:4], stats[:, 3:4], AF.Ln, bias=eps_ap), waits=[t5, eps_tok])
        t7 = s.add("scalar", lambda q: q.activation(
            stats[:, 3:4], stats[:, 3:4], AF.Exp, scale=-0.5), waits=[t6])
        t8 = s.add("vector", lambda q: q.tensor_scalar(
            x_tm[:], x_tm[:], stats[:, 2:3], stats[:, 3:4],
            ALU.subtract, ALU.mult), waits=[t7])
        tq = emit_quant_tm(b, x_tm[:], xd_tm[:], scr, D, block, fmax, [t8])
        for c in range(NCH):
            bk = c % len(pbb)
            key = ("b", bk)
            ww = [tq]
            if key in prev:
                ww.append(prev[key])
            tt = s.add("tensor", lambda q, c=c, bk=bk: q.transpose(
                pbb[bk][:, 0:128], xd_tm[:, c * 128:(c + 1) * 128], identb[:]),
                waits=ww)
            te = s.add("scalar", lambda q, c=c, j=j, bk=bk: q.copy(
                xd_fm[:, c, (tile_off + j) * 128:(tile_off + j + 1) * 128],
                pbb[bk][:, 0:128]), waits=[tt])
            prev[key] = te
            last_tok = te
        last_q = last_tok
    return last_tok


def build_l2_ffn(b=None, xin=None, xout=None):
    sa = b is None
    if sa:
        b = Builder("l2ffn")
    x_hbm = xin if xin is not None else b.dram_in("xres", [D, T])
    w_hbm = b.dram_in("w", [D, D])
    bias_hbm = b.dram_in("bias", [128, NCH])
    ident_hbm = b.dram_in("identf", [128, 128])
    identb_hbm = b.dram_in("identb", [128, 128], BF16)
    out_hbm = xout if xout is not None else b.dram_out("xout", [D, T])

    x_fm = b.sbuf("x", [128, NCH, T], F32)
    xd_fm = b.sbuf("xdfm", [128, NCH, T], BF16)
    w_sb = b.sbuf("w", [128, NCH, D], F32)
    wq_sb = b.sbuf("wq", [128, NCH, D], BF16)
    wT = b.sbuf("wT", [128, NCH, D], BF16)
    bias_sb = b.sbuf("bias", [128, NCH], F32)
    identf = b.sbuf("identf", [128, 128], F32)
    identb = b.sbuf("identb", [128, 128], BF16)
    stats = b.sbuf("stats", [128, 4], F32)
    lnscr = b.sbuf("lnscr", [128, D], F32)

    scr = {k: b.sbuf("q" + k, [128, D], F32) for k in ("a", "sq", "g", "c", "sg")}
    scr["xtm"] = b.sbuf("qxtm", [128, D], F32)
    scr["xdtm"] = b.sbuf("qxdtm", [128, D], BF16)
    scr["bm"] = b.sbuf("qbm", [128, D // 32], F32)
    scr["inv"] = b.sbuf("qinv", [128, D // 32], F32)

    pb = b.alloc_psum_banks(4, F32, 512)
    pbt = b.alloc_psum_banks(2, F32, 512)
    pbb = b.alloc_psum_banks(2, BF16, 1024)

    s = b.s
    tload_x = load_fm(b, x_fm, x_hbm)
    tload_w = load_fm(b, w_sb, w_hbm, queue="gpsimd")

    def ld_misc(q, sem):
        q.dma_start(out=bias_sb[:], in_=bias_hbm).then_inc(sem, 16)
        q.dma_start(out=identf[:], in_=ident_hbm).then_inc(sem, 16)
        q.dma_start(out=identb[:], in_=identb_hbm).then_inc(sem, 16)
    tmisc = s.add("sync", ld_misc, dma=True, n_dma=3)

    t_xd = emit_ln_quant_tm_pipeline(b, x_fm, xd_fm, identf, identb, pbt, pbb,
                                     scr, stats, lnscr, [tload_x, tmisc], 7.5, 32)

    wq_toks = []
    for c in range(NCH):
        tq = emit_quant_tm(b, w_sb[:, c, :], wq_sb[:, c, :], scr, D, 32, 7.5,
                           [tload_w, t_xd] + ([wq_toks[-1]] if wq_toks else []))
        wq_toks.append(tq)
    t_wT, _ = emit_transpose_fm_tm(b, wq_sb, wT, identb, pbb, [wq_toks[-1]],
                                   NCH, NCH, tok=None)

    add_toks = []
    prev = {}
    for oc in range(NCH):
        for ti in range(T // 512):
            t0 = ti * 512
            bk = (oc * (T // 512) + ti) % len(pb)
            w = [t_wT, t_xd]
            if bk in prev:
                w.append(prev[bk])

            def mm(q, oc=oc, t0=t0, bk=bk):
                r = None
                for ic in range(NCH):
                    r = q.matmul(pb[bk][:, 0:512],
                                 wT[:, ic, oc * 128:(oc + 1) * 128],
                                 xd_fm[:, ic, t0:t0 + 512],
                                 start=(ic == 0), stop=(ic == NCH - 1))
                return r
            tm = s.add("tensor", mm, waits=w)

            def evac(q, oc=oc, t0=t0, bk=bk):
                return q.scalar_tensor_tensor(
                    x_fm[:, oc, t0:t0 + 512], pb[bk][:, 0:512],
                    bias_sb[:, oc:oc + 1], x_fm[:, oc, t0:t0 + 512],
                    ALU.add, ALU.add)
            te = s.add("vector", evac, waits=[tm])
            prev[bk] = te
            add_toks.append(te)

    t_store = store_fm(b, out_hbm, x_fm, waits=add_toks)
    s.add("sync", lambda q: q.nop(), waits=[t_store])
    if sa:
        return b.finish()


I_FF = 2816
NIC_FF = I_FF // 128


def emit_wquant_stream(b, w_hbm, wT_dram, n_row_ch, in_dim, scr, stage, wrow,
                       identb, pbb, waits, block, fmax):
    s = b.s
    n_ic = in_dim // 128
    tprev = None
    tok_dma = []
    for c in range(n_row_ch):
        w = list(waits) + ([tprev] if tprev else [])

        def ld(q, sem, c=c):
            q.dma_start(out=wrow[:, 0:in_dim],
                        in_=w_hbm[c * 128:(c + 1) * 128, :]).then_inc(sem, 16)
        tl = s.add("gpsimd", ld, waits=w, dma=True)
        tq = emit_quant_tm(b, wrow[:, 0:in_dim], stage[:, 0:in_dim], scr,
                           in_dim, block, fmax, [tl])
        evs = []
        for ic in range(n_ic):
            bk = ic % len(pbb)
            ww = [tq]
            if evs and len(evs) >= len(pbb):
                ww.append(evs[-len(pbb)])
            tt = s.add("tensor", lambda q, ic=ic, bk=bk: q.transpose(
                pbb[bk][:, 0:128], stage[:, ic * 128:(ic + 1) * 128], identb[:]),
                waits=ww)
            te = s.add("scalar", lambda q, ic=ic, bk=bk: q.copy(
                wrow.bitcast(BF16)[:, ic * 128:(ic + 1) * 128], pbb[bk][:, 0:128]),
                waits=[tt])
            evs.append(te)

        def st(q, sem, c=c):
            q.dma_start(out=wT_dram.rearrange("(ic p) o -> p ic o", p=128)[:, :, c * 128:(c + 1) * 128],
                        in_=wrow.bitcast(BF16)[:, 0:n_ic * 128].rearrange(
                            "p (ic o) -> p ic o", o=128)).then_inc(sem, 16)
        tprev = s.add("sync", st, waits=evs, dma=True)
        tok_dma.append(tprev)
    return tok_dma[-1], tok_dma


def build_l0_ffn(b=None, xin=None, xout=None):
    sa = b is None
    if sa:
        b = Builder("l0ffn")
    x_hbm = xin if xin is not None else b.dram_in("xres", [D, T])
    wg_hbm = b.dram_in("wgate", [I_FF, D])
    wu_hbm = b.dram_in("wup", [I_FF, D])
    wd_hbm = b.dram_in("wdown", [D, I_FF])
    ident_hbm = b.dram_in("identf", [128, 128])
    identb_hbm = b.dram_in("identb", [128, 128], BF16)
    out_hbm = xout if xout is not None else b.dram_out("xout", [D, T])

    wgT_dram = b.dram_tmp("wgT", [D, I_FF], BF16)
    wuT_dram = b.dram_tmp("wuT", [D, I_FF], BF16)
    wdT_dram = b.dram_tmp("wdT", [I_FF, D], BF16)

    xd_fm = b.sbuf("xdfm", [128, NCH, T], BF16)
    x_st = b.sbuf("xst", [128, NCH, 1024], F32)
    h_sb = b.sbuf("h", [128, NIC_FF, 1024], BF16)
    htm = b.sbuf("htm", [128, I_FF], BF16)
    gscr = b.sbuf("gscr", [128, 1024], F32)
    identf = b.sbuf("identf", [128, 128], F32)
    identb = b.sbuf("identb", [128, 128], BF16)
    stats = b.sbuf("stats", [128, 4], F32)
    lnscr = b.sbuf("lnscr", [128, D], F32)
    wrow = b.sbuf("wrow", [128, I_FF], F32)
    wstage = b.sbuf("wstage", [128, I_FF], BF16)
    wgbuf = [b.sbuf("wgb", [128, NCH, 128], BF16) for _ in range(2)]
    wubuf = [b.sbuf("wub", [128, NCH, 128], BF16) for _ in range(2)]
    wdbuf = [b.sbuf("wdb", [128, NIC_FF, 128], BF16) for _ in range(2)]

    scr = {"a": b.sbuf("qa", [128, I_FF], F32),
           "sq": b.sbuf("qsq", [128, I_FF], BF16),
           "g": b.sbuf("qg", [128, I_FF], BF16),
           "c": b.sbuf("qc", [128, I_FF], BF16),
           "sg": b.sbuf("qsg", [128, I_FF], BF16),
           "bm": b.sbuf("qbm", [128, I_FF // 16], F32),
           "inv": b.sbuf("qinv", [128, I_FF // 16], F32),
           "xtm": b.sbuf("qxtm", [128, D], F32),
           "xdtm": b.sbuf("qxdtm", [128, D], BF16)}

    pb = b.alloc_psum_banks(4, F32, 512)
    pbt = b.alloc_psum_banks(2, F32, 512)
    pbb = b.alloc_psum_banks(2, BF16, 1024)

    s = b.s

    def ld_misc(q, sem):
        q.dma_start(out=identf[:], in_=ident_hbm).then_inc(sem, 16)
        q.dma_start(out=identb[:], in_=identb_hbm).then_inc(sem, 16)
    tmisc = s.add("sync", ld_misc, dma=True, n_dma=2)

    twg, _ = emit_wquant_stream(b, wg_hbm, wgT_dram, NIC_FF, D, scr, wstage,
                                wrow, identb, pbb, [tmisc], 16, 6.0)
    twu, _ = emit_wquant_stream(b, wu_hbm, wuT_dram, NIC_FF, D, scr, wstage,
                                wrow, identb, pbb, [twg], 16, 6.0)
    twd, _ = emit_wquant_stream(b, wd_hbm, wdT_dram, NCH, I_FF, scr, wstage,
                                wrow, identb, pbb, [twu], 16, 6.0)

    t_xq = None
    xq_done = []
    for st_i in range(2):
        tld = load_fm(b, x_st, x_hbm[:, st_i * 1024:(st_i + 1) * 1024],
                      waits=([t_xq] if t_xq else [tmisc]))
        t_xq = emit_ln_quant_tm_pipeline(
            b, x_st, xd_fm, identf, identb, pbt, pbb, scr, stats, lnscr,
            [tld, twd], 6.0, 16, n_tt=8, tile_off=st_i * 8)
        xq_done.append(t_xq)

    tprev_store = None
    wstream_prev = {}
    for st_i in range(2):
        t0 = st_i * 1024
        tld = load_fm(b, x_st, x_hbm[:, t0:t0 + 1024],
                      waits=[xq_done[-1]] + ([tprev_store] if tprev_store else []))
        h_toks = []
        mm_done = {}
        for oc in range(NIC_FF):
            buf = oc % 2
            wld_w = [twd]
            key = ("g", buf)
            if key in wstream_prev:
                wld_w.append(mm_done.get(key, wstream_prev[key]))

            def ldw(q, sem, oc=oc, buf=buf):
                q.dma_start(out=wgbuf[buf][:],
                            in_=wgT_dram.rearrange("(ic p) o -> p ic o", p=128)[
                                :, :, oc * 128:(oc + 1) * 128]).then_inc(sem, 16)
                q.dma_start(out=wubuf[buf][:],
                            in_=wuT_dram.rearrange("(ic p) o -> p ic o", p=128)[
                                :, :, oc * 128:(oc + 1) * 128]).then_inc(sem, 16)
            tw = s.add("gpsimd", ldw, waits=wld_w, dma=True, n_dma=2)
            wstream_prev[key] = tw
            for half in range(2):
                hh = half * 512
                bkA = (2 * oc + half) % 2
                bkB = 2 + (2 * oc + half) % 2
                wA = [tw, xq_done[-1]]
                if ("A", bkA) in mm_done:
                    wA.append(mm_done[("A", bkA)])

                def mmg(q, oc=oc, hh=hh, bkA=bkA, t0=t0):
                    r = None
                    for ic in range(NCH):
                        r = q.matmul(pb[bkA][:, 0:512], wgbuf[oc % 2][:, ic, :],
                                     xd_fm[:, ic, t0 + hh:t0 + hh + 512],
                                     start=(ic == 0), stop=(ic == NCH - 1))
                    return r
                tg = s.add("tensor", mmg, waits=wA)
                wB = [tw]
                if ("B", bkB) in mm_done:
                    wB.append(mm_done[("B", bkB)])

                def mmu(q, oc=oc, hh=hh, bkB=bkB, t0=t0):
                    r = None
                    for ic in range(NCH):
                        r = q.matmul(pb[bkB][:, 0:512], wubuf[oc % 2][:, ic, :],
                                     xd_fm[:, ic, t0 + hh:t0 + hh + 512],
                                     start=(ic == 0), stop=(ic == NCH - 1))
                    return r
                tu = s.add("tensor", mmu, waits=wB)
                wsi = [tg]
                if ("gs", hh) in mm_done:
                    wsi.append(mm_done[("gs", hh)])
                tsi = s.add("scalar", lambda q, oc=oc, hh=hh, bkA=bkA: q.activation(
                    gscr[:, hh:hh + 512], pb[bkA][:, 0:512], AF.Silu), waits=wsi)
                mm_done[("A", bkA)] = tsi
                th = s.add("vector", lambda q, oc=oc, hh=hh, bkB=bkB: q.tensor_tensor(
                    h_sb[:, oc, hh:hh + 512], gscr[:, hh:hh + 512],
                    pb[bkB][:, 0:512], ALU.mult), waits=[tsi, tu])
                mm_done[("B", bkB)] = th
                mm_done[("gs", hh)] = th
                h_toks.append(th)
            mm_done[("g", buf)] = h_toks[-1]
        tq_prev = None
        tp_prev = {}
        for j in range(8):
            evs = []
            for c in range(NIC_FF):
                bk = c % len(pbb)
                ww = h_toks + ([tq_prev] if tq_prev else [])
                if ("i", bk) in tp_prev:
                    ww.append(tp_prev[("i", bk)])
                tt = s.add("tensor", lambda q, c=c, j=j, bk=bk: q.transpose(
                    pbb[bk][:, 0:128], h_sb[:, c, j * 128:(j + 1) * 128], identb[:]),
                    waits=ww)
                te = s.add("scalar", lambda q, c=c, bk=bk: q.copy(
                    htm[:, c * 128:(c + 1) * 128], pbb[bk][:, 0:128]), waits=[tt])
                tp_prev[("i", bk)] = te
                evs.append(te)
            tq = emit_quant_tm(b, htm[:], htm[:], scr, I_FF, 16, 6.0, evs)
            back = []
            for c in range(NIC_FF):
                bk = c % len(pbb)
                ww = [tq]
                if ("o", bk) in tp_prev:
                    ww.append(tp_prev[("o", bk)])
                tt = s.add("tensor", lambda q, c=c, bk=bk: q.transpose(
                    pbb[bk][:, 0:128],
                    htm[:, c * 128:(c + 1) * 128], identb[:]),
                    waits=ww)
                te = s.add("scalar", lambda q, c=c, j=j, bk=bk: q.copy(
                    h_sb[:, c, j * 128:(j + 1) * 128], pbb[bk][:, 0:128]), waits=[tt])
                tp_prev[("o", bk)] = te
                back.append(te)
            tq_prev = back[-1]
        add_toks = []
        dprev = {}
        for oc in range(NCH):
            buf = oc % 2
            wldw = [tq_prev]
            if ("d", buf) in dprev:
                wldw.append(dprev[("d", buf)])

            def ldwd(q, sem, oc=oc, buf=buf):
                q.dma_start(out=wdbuf[buf][:],
                            in_=wdT_dram.rearrange("(ic p) o -> p ic o", p=128)[
                                :, :, oc * 128:(oc + 1) * 128]).then_inc(sem, 16)
            twdl = s.add("gpsimd", ldwd, waits=wldw, dma=True)
            for half in range(2):
                hh = half * 512
                bk = (2 * oc + half) % len(pb)
                ww = [twdl, tld]
                if ("m", bk) in dprev:
                    ww.append(dprev[("m", bk)])

                def mmd(q, oc=oc, hh=hh, bk=bk, buf=buf):
                    r = None
                    for ic in range(NIC_FF):
                        r = q.matmul(pb[bk][:, 0:512], wdbuf[buf][:, ic, :],
                                     h_sb[:, ic, hh:hh + 512],
                                     start=(ic == 0), stop=(ic == NIC_FF - 1))
                    return r
                tm = s.add("tensor", mmd, waits=ww)
                ta = s.add("vector", lambda q, oc=oc, hh=hh, bk=bk: q.tensor_tensor(
                    x_st[:, oc, hh:hh + 512], x_st[:, oc, hh:hh + 512],
                    pb[bk][:, 0:512], ALU.add), waits=[tm])
                dprev[("m", bk)] = ta
                add_toks.append(ta)
            dprev[("d", buf)] = add_toks[-1]
        tprev_store = store_fm(b, out_hbm[:, t0:t0 + 1024], x_st, waits=add_toks)
    s.add("sync", lambda q: q.nop(), waits=[tprev_store])
    if sa:
        return b.finish()


def emit_ln_fm(b, x_fm, xn_bf, ones_bf, rowscr, sqscr, xtmp, bankA, bankB,
               bankC, bankD, waits, Tn=T, n_ch=NCH, eps=1e-5, out_also_f32=None,
               gain=None, bias2=None, out_lo=None, loscr=None, cast_buf=None,
               double_ln=False, out0_f32=None, bankE=None):
    s = b.s
    ntile = Tn // 512
    cb = cast_buf if cast_buf is not None else xn_bf
    cast_toks = []
    for c in range(n_ch):
        tc = s.add("vector", lambda q, c=c: q.tensor_copy(
            cb[:, c, :], x_fm[:, c, :]), waits=waits)
        cast_toks.append(tc)
    sum_evacs = []
    prevA = None
    prevB = None
    sq_toks = {}
    for ti in range(ntile):
        t0 = ti * 512
        for c in range(n_ch):
            half = c % 2
            ww = list(waits)
            if (ti, c - 2, "mm") in sq_toks:
                ww.append(sq_toks[(ti, c - 2, "mm")])
            tsq = s.add("scalar", lambda q, c=c, t0=t0, half=half: q.activation(
                sqscr[:, half * 512:half * 512 + 512], x_fm[:, c, t0:t0 + 512],
                AF.Square), waits=ww)
            wA = [cast_toks[c]]
            wB = [tsq]
            if c == 0:
                if prevA is not None:
                    wA.append(prevA)
                    wB.append(prevB)

            def mmsum(q, c=c, t0=t0):
                return q.matmul(bankA[0:1, 0:512], ones_bf[:, 0:1],
                                cb[:, c, t0:t0 + 512],
                                start=(c == 0), stop=(c == n_ch - 1))
            tA = s.add("tensor", mmsum, waits=wA)

            def mmsq(q, c=c, t0=t0, half=half):
                return q.matmul(bankB[0:1, 0:512], ones_bf[:, 0:1],
                                sqscr[:, half * 512:half * 512 + 512],
                                start=(c == 0), stop=(c == n_ch - 1))
            tB = s.add("tensor", mmsq, waits=wB)
            sq_toks[(ti, c, "mm")] = tB
        eA = s.add("scalar", lambda q, t0=t0: q.copy(
            rowscr[0:1, 2 * Tn + t0:2 * Tn + t0 + 512], bankA[0:1, 0:512]), waits=[tA])
        eB = s.add("scalar", lambda q, t0=t0: q.copy(
            rowscr[0:1, 3 * Tn + t0:3 * Tn + t0 + 512], bankB[0:1, 0:512]), waits=[tB])
        prevA, prevB = eA, eB
        sum_evacs += [eA, eB]
    t1 = s.add("vector", lambda q: q.tensor_scalar_mul(
        rowscr[0:1, 0:Tn], rowscr[0:1, 2 * Tn:3 * Tn], 1.0 / D), waits=sum_evacs)
    t2 = s.add("vector", lambda q: q.tensor_tensor(
        rowscr[0:1, Tn:2 * Tn], rowscr[0:1, 0:Tn],
        rowscr[0:1, 0:Tn], ALU.mult), waits=[t1])
    t3 = s.add("vector", lambda q: q.scalar_tensor_tensor(
        rowscr[0:1, Tn:2 * Tn], rowscr[0:1, 3 * Tn:4 * Tn], 1.0 / D,
        rowscr[0:1, Tn:2 * Tn], ALU.mult, ALU.subtract), waits=[t2] + sum_evacs)
    epsap, epstok = b.cst(eps)
    if double_ln:
        t4a = s.add("scalar", lambda q: q.activation(
            rowscr[0:1, 2 * Tn:3 * Tn], rowscr[0:1, Tn:2 * Tn], AF.Ln,
            bias=epsap[0:1, :]), waits=[t3, epstok])
        t4b = s.add("scalar", lambda q: q.activation(
            rowscr[0:1, 2 * Tn:3 * Tn], rowscr[0:1, 2 * Tn:3 * Tn], AF.Exp,
            scale=-0.5), waits=[t4a])
        t4c = s.add("vector", lambda q: q.tensor_tensor(
            rowscr[0:1, 3 * Tn:4 * Tn], rowscr[0:1, 2 * Tn:3 * Tn],
            rowscr[0:1, 2 * Tn:3 * Tn], ALU.mult), waits=[t4b])
        t4d = s.add("vector", lambda q: q.tensor_tensor(
            rowscr[0:1, 3 * Tn:4 * Tn], rowscr[0:1, 3 * Tn:4 * Tn],
            rowscr[0:1, Tn:2 * Tn], ALU.mult), waits=[t4c])
        t4e = s.add("scalar", lambda q: q.activation(
            rowscr[0:1, 3 * Tn:4 * Tn], rowscr[0:1, 3 * Tn:4 * Tn], AF.Ln,
            bias=epsap[0:1, :]), waits=[t4d])
        t4f = s.add("scalar", lambda q: q.activation(
            rowscr[0:1, 3 * Tn:4 * Tn], rowscr[0:1, 3 * Tn:4 * Tn], AF.Exp,
            scale=-0.5), waits=[t4e])
        if out0_f32 is not None:
            t4g = s.add("vector", lambda q: q.tensor_copy(
                rowscr[0:1, 4 * Tn:5 * Tn], rowscr[0:1, 2 * Tn:3 * Tn]),
                waits=[t4f])
        t5 = s.add("vector", lambda q: q.tensor_tensor(
            rowscr[0:1, Tn:2 * Tn], rowscr[0:1, 2 * Tn:3 * Tn],
            rowscr[0:1, 3 * Tn:4 * Tn], ALU.mult),
            waits=[t4f] + ([t4g] if out0_f32 is not None else []))
    else:
        t4 = s.add("scalar", lambda q: q.activation(
            rowscr[0:1, Tn:2 * Tn], rowscr[0:1, Tn:2 * Tn], AF.Ln,
            bias=epsap[0:1, :]), waits=[t3, epstok])
        t5 = s.add("scalar", lambda q: q.activation(
            rowscr[0:1, Tn:2 * Tn], rowscr[0:1, Tn:2 * Tn], AF.Exp,
            scale=-0.5), waits=[t4])
    bfv = rowscr.bitcast(BF16)
    mh = bfv[0:1, 4 * Tn:5 * Tn]
    ml = bfv[0:1, 5 * Tn:6 * Tn]
    rh = bfv[0:1, 6 * Tn:7 * Tn]
    rl = bfv[0:1, 7 * Tn:8 * Tn]
    t6 = s.add("vector", lambda q: q.tensor_copy(mh, rowscr[0:1, 0:Tn]),
               waits=[t5])
    t7 = s.add("vector", lambda q: q.tensor_tensor(
        ml, rowscr[0:1, 0:Tn], mh, ALU.subtract), waits=[t6])
    t8 = s.add("vector", lambda q: q.tensor_copy(rh, rowscr[0:1, Tn:2 * Tn]),
               waits=[t5])
    t9 = s.add("vector", lambda q: q.tensor_tensor(
        rl, rowscr[0:1, Tn:2 * Tn], rh, ALU.subtract), waits=[t8])
    if out0_f32 is not None:
        r1h = bfv[0:1, 10 * Tn:11 * Tn]
        r1l = bfv[0:1, 11 * Tn:12 * Tn]
        t9a = s.add("vector", lambda q: q.tensor_copy(
            r1h, rowscr[0:1, 4 * Tn:5 * Tn]), waits=[t9])
        t9b = s.add("vector", lambda q: q.tensor_tensor(
            r1l, rowscr[0:1, 4 * Tn:5 * Tn], r1h, ALU.subtract), waits=[t9a])
        t9 = t9b
    apply_toks = {}
    last = None
    prevC = None
    prevD = None
    for ti in range(ntile):
        t0 = ti * 512
        wC = [t7]
        if prevC is not None:
            wC.append(prevC)

        def bcm(q, t0=t0):
            q.matmul(bankC[:, 0:512], ones_bf[0:1, :], mh[0:1, t0:t0 + 512],
                     start=True, stop=False)
            return q.matmul(bankC[:, 0:512], ones_bf[0:1, :], ml[0:1, t0:t0 + 512],
                            start=False, stop=True)
        tC = s.add("tensor", bcm, waits=wC)
        wD = [t9]
        if prevD is not None:
            wD.append(prevD)

        def bcr(q, t0=t0):
            q.matmul(bankD[:, 0:512], ones_bf[0:1, :], rh[0:1, t0:t0 + 512],
                     start=True, stop=False)
            return q.matmul(bankD[:, 0:512], ones_bf[0:1, :], rl[0:1, t0:t0 + 512],
                            start=False, stop=True)
        tD = s.add("tensor", bcr, waits=wD)
        if out0_f32 is not None:
            wE = [t9]
            if ("lnE",) in getattr(b, "_lnE", {}):
                wE.append(b._lnE[("lnE",)])

            def bce(q, t0=t0):
                q.matmul(bankE[:, 0:512], ones_bf[0:1, :], r1h[0:1, t0:t0 + 512],
                         start=True, stop=False)
                return q.matmul(bankE[:, 0:512], ones_bf[0:1, :],
                                r1l[0:1, t0:t0 + 512], start=False, stop=True)
            tE = s.add("tensor", bce, waits=wE)
        for c in range(n_ch):
            ta = s.add("vector", lambda q, c=c, t0=t0: q.scalar_tensor_tensor(
                xtmp[:, 0:512], bankC[:, 0:512], -1.0, x_fm[:, c, t0:t0 + 512],
                ALU.mult, ALU.add), waits=[tC] + ([last] if last else []))
            if out0_f32 is not None:
                ta = s.add("vector", lambda q, c=c, t0=t0: q.tensor_tensor(
                    out0_f32[:, c, t0:t0 + 512], xtmp[:, 0:512],
                    bankE[:, 0:512], ALU.mult), waits=[ta, tE])
            if gain is not None:
                tb0 = s.add("vector", lambda q, c=c, t0=t0: q.tensor_tensor(
                    xtmp[:, 0:512], xtmp[:, 0:512], bankD[:, 0:512],
                    ALU.mult), waits=[tD, ta])
                tb = s.add("vector", lambda q, c=c, t0=t0: q.tensor_scalar(
                    xn_bf[:, c, t0:t0 + 512], xtmp[:, 0:512],
                    gain[:, c:c + 1], bias2[:, c:c + 1],
                    ALU.mult, ALU.add), waits=[tb0])
            else:
                tb = s.add("vector", lambda q, c=c, t0=t0: q.tensor_tensor(
                    xn_bf[:, c, t0:t0 + 512], xtmp[:, 0:512], bankD[:, 0:512],
                    ALU.mult), waits=[tD, ta])
            if out_also_f32 is not None:
                tb = s.add("vector", lambda q, c=c, t0=t0: q.tensor_tensor(
                    out_also_f32[:, c, t0:t0 + 512], xtmp[:, 0:512],
                    bankD[:, 0:512], ALU.mult), waits=[tb])
            if out_lo is not None:
                tb = s.add("vector", lambda q, c=c, t0=t0: q.tensor_tensor(
                    loscr[:, 0:512], xtmp[:, 0:512], bankD[:, 0:512],
                    ALU.mult), waits=[tb])
                tb = s.add("vector", lambda q, c=c, t0=t0: q.tensor_tensor(
                    out_lo[:, c, t0:t0 + 512], loscr[:, 0:512],
                    xn_bf[:, c, t0:t0 + 512], ALU.subtract), waits=[tb])
            apply_toks[(c, ti)] = tb
            last = tb
        prevC = last
        prevD = last
        if out0_f32 is not None:
            if not hasattr(b, "_lnE"):
                b._lnE = {}
            b._lnE[("lnE",)] = last
    return last, apply_toks


H_SH = 2048
NOC_SH = H_SH // 128


def build_l3_ffn(b=None, xin=None, xout=None):
    sa = b is None
    if sa:
        b = Builder("l3ffn")
    x_hbm = xin if xin is not None else b.dram_in("xres", [D, T])
    rw_hbm = b.dram_in("router_wT", [D, 8])
    biasrep_hbm = b.dram_in("rbias_rep", [128, NTT * 8])
    sw1_hbm = b.dram_in("sw1", [D, H_SH])
    sw3_hbm = b.dram_in("sw3", [D, H_SH])
    sw2_hbm = b.dram_in("sw2", [H_SH, D])
    ew1_hbm = b.dram_in("ew1", [8, D, D])
    ew2_hbm = b.dram_in("ew2", [8, D, D])
    ident_hbm = b.dram_in("identf", [128, 128])
    sel1h_hbm = b.dram_in("sel8", [8, 8 * 128], BF16)
    out_hbm = xout if xout is not None else b.dram_out("xout", [D, T])

    ST = 512
    NST = T // ST
    x_fm = b.sbuf("x", [128, NCH, T], F32)
    xn_bf = b.sbuf("xn", [128, NCH, T], BF16)
    ones_bf = b.sbuf("ones", [128, 128], BF16)
    rowscr = b.sbuf("rows", [1, 4 * T], F32)
    sqscr = b.sbuf("sqscr", [128, 1024], BF16)
    xtmp = b.sbuf("xtmp", [128, 512], F32)
    identf = b.sbuf("identf", [128, 128], F32)
    sel8 = b.sbuf("sel8", [8, 8 * 128], BF16)
    rwT_f = b.sbuf("rwT", [128, NCH, 8], F32)
    cs_sb = b.sbuf("cs", [8, 1], F32)
    onesf = b.sbuf("onesf", [128, 8], F32)
    biasrep = b.sbuf("brep", [128, NTT * 8], F32)
    srow = b.sbuf("srow", [8, T], F32)
    s_tm = b.sbuf("stm", [128, NTT, 8], F32)
    sel_tm = b.sbuf("seltm", [128, NTT, 8], F32)
    cw_tm = b.sbuf("cwtm", [128, NTT, 8], F32)
    tscr = b.sbuf("tscr", [128, NTT, 8], F32)
    tscr2 = b.sbuf("tscr2", [128, NTT, 8], F32)
    gsc = b.sbuf("gsc", [128, NTT * 8], F32)
    cwrow = b.sbuf("cwrow", [8, T], BF16)
    a_bf = b.sbuf("a", [128, NOC_SH, ST], BF16)
    he_bf = b.sbuf("he", [128, NCH, ST], BF16)
    gscr = b.sbuf("gscr", [128, 1024], F32)
    wbuf = [b.sbuf("wb", [128, NCH, 128], BF16) for _ in range(2)]
    wbuf3 = [b.sbuf("wb3", [128, NCH, 128], BF16) for _ in range(2)]
    wbuf2 = [b.sbuf("wb2", [128, NOC_SH, 128], BF16) for _ in range(2)]
    we1 = [b.sbuf("we1", [128, NCH, 128], BF16) for _ in range(2)]
    we2 = [b.sbuf("we2", [128, NCH, 128], BF16) for _ in range(2)]

    pb = b.alloc_psum_banks(4, F32, 512)
    pbA = b.alloc_psum_banks(1, F32, 512)[0]
    pbB = b.alloc_psum_banks(1, F32, 512)[0]
    pbC = b.alloc_psum_banks(1, F32, 512)[0]
    pbD = b.alloc_psum_banks(1, F32, 512)[0]

    s = b.s
    tload_x = load_fm(b, x_fm, x_hbm)

    def ld_misc(q, sem):
        q.dma_start(out=identf[:], in_=ident_hbm).then_inc(sem, 16)
        q.dma_start(out=sel8[:], in_=sel1h_hbm).then_inc(sem, 16)
        q.dma_start(out=biasrep[:], in_=biasrep_hbm).then_inc(sem, 16)
    tmisc = s.add("sync", ld_misc, dma=True, n_dma=3)
    tmisc2 = s.add("gpsimd", lambda q, sem: q.dma_start(
        out=rwT_f[:], in_=rw_hbm.rearrange("(c p) e -> p c e", p=128)).then_inc(sem, 16),
        dma=True)
    tonesf = s.add("vector", lambda q: q.memset(onesf[:], 1.0))
    tones = s.add("vector", lambda q: q.memset(ones_bf[:], 1.0))

    t_ln, _ = emit_ln_fm(b, x_fm, xn_bf, ones_bf, rowscr, sqscr, xtmp,
                         pbA, pbB, pbC, pbD, [tload_x, tones])

    tcs_mm = s.add("tensor", lambda q: [
        q.matmul(pbB[0:8, 0:1], rwT_f[:, c, :], onesf[:, 0:1],
                 start=(c == 0), stop=(c == NCH - 1)) for c in range(NCH)][-1],
        waits=[tmisc2, tonesf, t_ln])
    t_cs = s.add("vector", lambda q: q.tensor_scalar_mul(
        cs_sb[0:8, 0:1], pbB[0:8, 0:1], -1.0), waits=[tcs_mm])
    r_toks = []
    prevR = None
    muR = rowscr[0:1, 0:T]
    rstdR = rowscr[0:1, T:2 * T]
    for ti in range(4):
        t0 = ti * 512
        w = [t_ln, t_cs]
        if prevR is not None:
            w.append(prevR)

        def mmr(q, t0=t0):
            r = None
            for c in range(NCH):
                r = q.matmul(pbA[0:8, 0:512], rwT_f[:, c, :], x_fm[:, c, t0:t0 + 512],
                             start=(c == 0), stop=(c == NCH - 1))
            return r
        tr = s.add("tensor", mmr, waits=w)
        te0 = s.add("scalar", lambda q, t0=t0: q.copy(
            srow[:, t0:t0 + 512], pbA[0:8, 0:512]), waits=[tr])
        tb1 = s.add("tensor", lambda q, t0=t0: q.matmul(
            pbA[0:8, 0:512], onesf[0:1, 0:8],
            muR[0:1, t0:t0 + 512], start=True, stop=True), waits=[te0])
        tb2 = s.add("tensor", lambda q, t0=t0: q.matmul(
            pbB[0:8, 0:512], onesf[0:1, 0:8],
            rstdR[0:1, t0:t0 + 512], start=True, stop=True), waits=[te0, t_cs])
        ts1 = s.add("vector", lambda q, t0=t0: q.scalar_tensor_tensor(
            srow[:, t0:t0 + 512], pbA[0:8, 0:512], cs_sb[0:8, 0:1],
            srow[:, t0:t0 + 512], ALU.mult, ALU.add), waits=[tb1])
        ts2 = s.add("vector", lambda q, t0=t0: q.tensor_tensor(
            srow[:, t0:t0 + 512], srow[:, t0:t0 + 512], pbB[0:8, 0:512],
            ALU.mult), waits=[tb2, ts1])
        te = s.add("scalar", lambda q, t0=t0: q.activation(
            srow[:, t0:t0 + 512], srow[:, t0:t0 + 512], AF.Sigmoid), waits=[ts2])
        prevR = te
        r_toks.append(te)

    tp_toks = []
    prevT = {}
    for j in range(NTT):
        bk = j % 2
        w = list(r_toks)
        if bk in prevT:
            w.append(prevT[bk])
        tt = s.add("tensor", lambda q, j=j, bk=bk: q.transpose(
            (pbC if bk == 0 else pbD)[:, 0:8], srow[0:8, j * 128:(j + 1) * 128],
            identf[0:8, 0:8]), waits=w)
        te = s.add("scalar", lambda q, j=j, bk=bk: q.copy(
            s_tm[:, j, :], (pbC if bk == 0 else pbD)[:, 0:8]), waits=[tt])
        prevT[bk] = te
        tp_toks.append(te)
    t_sel = s.add("vector", lambda q: q.tensor_tensor(
        sel_tm[:], s_tm[:], biasrep[:].rearrange("p (j e) -> p j e", e=8),
        ALU.add), waits=tp_toks + [tmisc])

    t_m1 = s.add("vector", lambda q: q.reduce_max(
        gsc[:, 0:NTT], sel_tm[:], axis=AX.X), waits=[t_sel])
    m1bc = gsc[:, 0:NTT].broadcast_to([128, NTT, 8])
    t_eq1 = s.add("vector", lambda q: q.tensor_tensor(
        tscr[:], sel_tm[:], m1bc, ALU.is_ge), waits=[t_m1])
    t_s2 = s.add("vector", lambda q: q.scalar_tensor_tensor(
        tscr2[:], tscr[:], -1e30, sel_tm[:], ALU.mult, ALU.add), waits=[t_eq1])
    t_m2 = s.add("vector", lambda q: q.reduce_max(
        gsc[:, NTT:2 * NTT], tscr2[:], axis=AX.X), waits=[t_s2])
    m2bc = gsc[:, NTT:2 * NTT].broadcast_to([128, NTT, 8])
    t_eq2 = s.add("vector", lambda q: q.tensor_tensor(
        tscr2[:], tscr2[:], m2bc, ALU.is_ge), waits=[t_m2])
    t_p1 = s.add("vector", lambda q: q.tensor_tensor(
        cw_tm[:], s_tm[:], tscr[:], ALU.mult), waits=[t_eq1])
    t_s1s = s.add("vector", lambda q: q.reduce_sum(
        gsc[:, 2 * NTT:3 * NTT], cw_tm[:], axis=AX.X), waits=[t_p1])
    t_p2 = s.add("vector", lambda q: q.tensor_tensor(
        cw_tm[:], s_tm[:], tscr2[:], ALU.mult), waits=[t_eq2, t_s1s])
    t_s2s = s.add("vector", lambda q: q.reduce_sum(
        gsc[:, 3 * NTT:4 * NTT], cw_tm[:], axis=AX.X), waits=[t_p2])
    t_dn = s.add("vector", lambda q: q.tensor_tensor(
        gsc[:, 4 * NTT:5 * NTT], gsc[:, 2 * NTT:3 * NTT],
        gsc[:, 3 * NTT:4 * NTT], ALU.add), waits=[t_s2s])
    t_dp = s.add("vector", lambda q: q.tensor_scalar_add(
        gsc[:, 5 * NTT:6 * NTT], gsc[:, 4 * NTT:5 * NTT], 1e-9), waits=[t_dn])
    t_rc = s.add("vector", lambda q: q.reciprocal(
        gsc[:, 6 * NTT:7 * NTT], gsc[:, 5 * NTT:6 * NTT]), waits=[t_dp])
    t_cd = s.add("vector", lambda q: q.tensor_single_scalar(
        gsc[:, 7 * NTT:8 * NTT], gsc[:, 4 * NTT:5 * NTT], 1e-9, ALU.is_gt),
        waits=[t_dn])
    def gate_calc(dsti, srci):
        ta = s.add("vector", lambda q: q.tensor_tensor(
            gsc[:, dsti * NTT:(dsti + 1) * NTT], gsc[:, srci * NTT:(srci + 1) * NTT],
            gsc[:, 6 * NTT:7 * NTT], ALU.mult), waits=[t_rc, t_cd])
        tb = s.add("vector", lambda q: q.tensor_tensor(
            gsc[:, dsti * NTT:(dsti + 1) * NTT], gsc[:, dsti * NTT:(dsti + 1) * NTT],
            gsc[:, 7 * NTT:8 * NTT], ALU.mult), waits=[ta])
        tc = s.add("vector", lambda q: q.tensor_scalar(
            gsc[:, 8 * NTT:9 * NTT] if False else xtmp[:, 0:NTT],
            gsc[:, 7 * NTT:8 * NTT], -0.5, 0.5, ALU.mult, ALU.add), waits=[tb])
        td = s.add("vector", lambda q: q.tensor_tensor(
            gsc[:, dsti * NTT:(dsti + 1) * NTT], gsc[:, dsti * NTT:(dsti + 1) * NTT],
            xtmp[:, 0:NTT], ALU.add), waits=[tc])
        return td
    t_g1 = gate_calc(0, 2)
    t_g2 = gate_calc(1, 3)
    g1bc = gsc[:, 0:NTT].broadcast_to([128, NTT, 8])
    g2bc = gsc[:, NTT:2 * NTT].broadcast_to([128, NTT, 8])
    t_cw1 = s.add("vector", lambda q: q.tensor_tensor(
        cw_tm[:], tscr[:], g1bc, ALU.mult), waits=[t_g1, t_g2])
    t_cw2 = s.add("vector", lambda q: q.tensor_tensor(
        tscr2[:], tscr2[:], g2bc, ALU.mult), waits=[t_cw1])
    t_cw = s.add("vector", lambda q: q.tensor_tensor(
        cw_tm[:], cw_tm[:], tscr2[:], ALU.add), waits=[t_cw2])

    cw_toks = []
    prevT = {}
    for j in range(NTT):
        w = [t_cw]
        if 0 in prevT:
            w.append(prevT[0])
        tt = s.add("tensor", lambda q, j=j: q.transpose(
            pbC[0:8, 0:128], cw_tm[:, j, :], identf[:]), waits=w)
        te = s.add("scalar", lambda q, j=j: q.copy(
            cwrow[0:8, j * 128:(j + 1) * 128], pbC[0:8, 0:128]), waits=[tt])
        prevT[0] = te
        cw_toks.append(te)
    t_cwr = cw_toks[-1]

    tprev_store = None
    dprev = {}
    for st_i in range(NST):
        t0 = st_i * ST
        for oc in range(NOC_SH):
            buf = oc % 2
            wld = [tmisc]
            if ("s", buf) in dprev:
                wld.append(dprev[("s", buf)])

            def ldw(q, sem, oc=oc, buf=buf):
                q.dma_start(out=wbuf[buf][:],
                            in_=sw1_hbm.rearrange("(c p) h -> p c h", p=128)[
                                :, :, oc * 128:(oc + 1) * 128]).then_inc(sem, 16)
                q.dma_start(out=wbuf3[buf][:],
                            in_=sw3_hbm.rearrange("(c p) h -> p c h", p=128)[
                                :, :, oc * 128:(oc + 1) * 128]).then_inc(sem, 16)
            tw = s.add("gpsimd", ldw, waits=wld, dma=True, n_dma=2)
            bkA = oc % 2
            bkB = 2 + oc % 2
            wA = [tw, t_ln]
            if ("bank", bkA) in dprev:
                wA.append(dprev[("bank", bkA)])

            def mmg(q, oc=oc, bkA=bkA, t0=t0, buf=buf):
                r = None
                for c in range(NCH):
                    r = q.matmul(pb[bkA][:, 0:ST], wbuf[buf][:, c, :],
                                 xn_bf[:, c, t0:t0 + ST],
                                 start=(c == 0), stop=(c == NCH - 1))
                return r
            tg = s.add("tensor", mmg, waits=wA)
            wB = [tw]
            if ("bank", bkB) in dprev:
                wB.append(dprev[("bank", bkB)])

            def mmu(q, oc=oc, bkB=bkB, t0=t0, buf=buf):
                r = None
                for c in range(NCH):
                    r = q.matmul(pb[bkB][:, 0:ST], wbuf3[buf][:, c, :],
                                 xn_bf[:, c, t0:t0 + ST],
                                 start=(c == 0), stop=(c == NCH - 1))
                return r
            tu = s.add("tensor", mmu, waits=wB)
            wsi = [tg]
            if "gscr" in dprev:
                wsi.append(dprev["gscr"])
            tsi = s.add("scalar", lambda q, bkA=bkA: q.activation(
                gscr[:, 0:ST], pb[bkA][:, 0:ST], AF.Silu), waits=wsi)
            dprev[("bank", bkA)] = tsi
            th = s.add("vector", lambda q, oc=oc, bkB=bkB: q.tensor_tensor(
                a_bf[:, oc, 0:ST], gscr[:, 0:ST],
                pb[bkB][:, 0:ST], ALU.mult), waits=[tsi, tu])
            dprev[("bank", bkB)] = th
            dprev["gscr"] = th
            dprev[("s", buf)] = th
        t_a = th
        for oc in range(NCH):
            buf = oc % 2
            wld = [t_a]
            if ("s2", buf) in dprev:
                wld.append(dprev[("s2", buf)])

            def ldw2(q, sem, oc=oc, buf=buf):
                q.dma_start(out=wbuf2[buf][:],
                            in_=sw2_hbm.rearrange("(c p) h -> p c h", p=128)[
                                :, :, oc * 128:(oc + 1) * 128]).then_inc(sem, 16)
            tw2 = s.add("gpsimd", ldw2, waits=wld, dma=True)
            bk = oc % 2
            ww = [tw2, t_a]
            if ("bank", bk) in dprev:
                ww.append(dprev[("bank", bk)])

            def mmd(q, oc=oc, bk=bk, buf=buf):
                r = None
                for ic in range(NOC_SH):
                    r = q.matmul(pb[bk][:, 0:ST], wbuf2[buf][:, ic, :],
                                 a_bf[:, ic, 0:ST],
                                 start=(ic == 0), stop=(ic == NOC_SH - 1))
                return r
            tm = s.add("tensor", mmd, waits=ww)
            ta = s.add("vector", lambda q, oc=oc, bk=bk, t0=t0: q.tensor_tensor(
                x_fm[:, oc, t0:t0 + ST], x_fm[:, oc, t0:t0 + ST],
                pb[bk][:, 0:ST], ALU.add), waits=[tm])
            dprev[("bank", bk)] = ta
            dprev[("s2", buf)] = ta
        t_shared = ta

        for e in range(8):
            wwc = [t_cwr, t_shared]
            if ("bank", 3) in dprev:
                wwc.append(dprev[("bank", 3)])

            def bccw(q, e=e, t0=t0):
                return q.matmul(pb[3][:, 0:ST], sel8[:, e * 128:(e + 1) * 128],
                                cwrow[0:8, t0:t0 + ST], start=True, stop=True)
            tcb = s.add("tensor", bccw, waits=wwc)
            he_toks = []
            for oc in range(NCH):
                buf = oc % 2
                wld = [t_shared]
                if ("ew", buf) in dprev:
                    wld.append(dprev[("ew", buf)])

                def lde1(q, sem, e=e, oc=oc, buf=buf):
                    q.dma_start(out=we1[buf][:],
                                in_=ew1_hbm[e].rearrange("(c p) h -> p c h", p=128)[
                                    :, :, oc * 128:(oc + 1) * 128]).then_inc(sem, 16)
                twe = s.add("gpsimd", lde1, waits=wld, dma=True)
                bkA = oc % 2
                ww = [twe]
                if ("bank", bkA) in dprev:
                    ww.append(dprev[("bank", bkA)])

                def mme1(q, oc=oc, bkA=bkA, t0=t0, buf=buf):
                    r = None
                    for c in range(NCH):
                        r = q.matmul(pb[bkA][:, 0:ST], we1[buf][:, c, :],
                                     xn_bf[:, c, t0:t0 + ST],
                                     start=(c == 0), stop=(c == NCH - 1))
                    return r
                tm1 = s.add("tensor", mme1, waits=ww)
                wge = [tm1]
                if "gscr" in dprev:
                    wge.append(dprev["gscr"])
                tge = s.add("scalar", lambda q, bkA=bkA: q.activation(
                    gscr[:, 0:ST], pb[bkA][:, 0:ST], AF.Gelu), waits=wge)
                dprev[("bank", bkA)] = tge
                the = s.add("vector", lambda q, oc=oc: q.tensor_tensor(
                    he_bf[:, oc, 0:ST], gscr[:, 0:ST],
                    pb[3][:, 0:ST], ALU.mult), waits=[tge, tcb])
                dprev["gscr"] = the
                dprev[("ew", buf)] = the
                he_toks.append(the)
            dprev[("bank", 3)] = he_toks[-1]
            t_he = he_toks[-1]
            for oc in range(NCH):
                buf = oc % 2
                wld = [t_he]
                if ("ew2", buf) in dprev:
                    wld.append(dprev[("ew2", buf)])

                def lde2(q, sem, e=e, oc=oc, buf=buf):
                    q.dma_start(out=we2[buf][:],
                                in_=ew2_hbm[e].rearrange("(c p) h -> p c h", p=128)[
                                    :, :, oc * 128:(oc + 1) * 128]).then_inc(sem, 16)
                twe2 = s.add("gpsimd", lde2, waits=wld, dma=True)
                bk = oc % 2
                ww = [twe2, t_he]
                if ("bank", bk) in dprev:
                    ww.append(dprev[("bank", bk)])

                def mme2(q, oc=oc, bk=bk, buf=buf):
                    r = None
                    for c in range(NCH):
                        r = q.matmul(pb[bk][:, 0:ST], we2[buf][:, c, :],
                                     he_bf[:, c, 0:ST],
                                     start=(c == 0), stop=(c == NCH - 1))
                    return r
                tm2 = s.add("tensor", mme2, waits=ww)
                ta2 = s.add("vector", lambda q, oc=oc, bk=bk, t0=t0: q.tensor_tensor(
                    x_fm[:, oc, t0:t0 + ST],
                    x_fm[:, oc, t0:t0 + ST],
                    pb[bk][:, 0:ST], ALU.add), waits=[tm2])
                dprev[("bank", bk)] = ta2
                dprev[("ew2", buf)] = ta2
        tprev_store = ta2

    t_store = store_fm(b, out_hbm, x_fm, waits=[tprev_store])
    s.add("sync", lambda q: q.nop(), waits=[t_store])
    if sa:
        return b.finish()


def build_l1_ffn(b=None, xin=None, xout=None):
    sa = b is None
    if sa:
        b = Builder("l1ffn")
    x_hbm = xin if xin is not None else b.dram_in("xres", [D, T])
    gw_hbm = b.dram_in("gate_w", [D, 8])
    bwgu_hbm = b.dram_in("base_wgu", [D, 2 * I_FF])
    bwd_hbm = b.dram_in("base_wd", [I_FF, D])
    ewgu_hbm = b.dram_in("exp_wgu", [8, D, 2 * I_FF])
    ewd_hbm = b.dram_in("exp_wd", [8, I_FF, D])
    ident_hbm = b.dram_in("identf", [128, 128])
    sel1h_hbm = b.dram_in("sel8", [8, 8 * 128], BF16)
    out_hbm = xout if xout is not None else b.dram_out("xout", [D, T])

    ST = 1024
    NST = T // ST
    NT8 = ST // 128
    x_st = b.sbuf("x", [128, NCH, ST], F32)
    xn_bf = b.sbuf("xn", [128, NCH, ST], BF16)
    a_bf = b.sbuf("a", [128, NIC_FF, ST], BF16)
    ones_bf = b.sbuf("ones", [128, 128], BF16)
    onesf = b.sbuf("onesf", [128, 8], F32)
    rowscr = b.sbuf("rows", [1, 4 * ST], F32)
    sqscr = b.sbuf("sqscr", [128, 1024], BF16)
    xtmp = b.sbuf("xtmp", [128, 512], F32)
    identf = b.sbuf("identf", [128, 128], F32)
    sel8 = b.sbuf("sel8", [8, 8 * 128], BF16)
    gwT_f = b.sbuf("gwT", [128, NCH, 8], F32)
    cs_sb = b.sbuf("cs", [8, 1], F32)
    lrow = b.sbuf("lrow", [8, ST], F32)
    l_tm = b.sbuf("ltm", [128, NT8, 8], F32)
    p_tm = b.sbuf("ptm", [128, NT8, 8], F32)
    cw_tm = b.sbuf("cwtm", [128, NT8, 8], F32)
    tscr = b.sbuf("tscr", [128, NT8, 8], F32)
    tscr2 = b.sbuf("tscr2", [128, NT8, 8], F32)
    gsc = b.sbuf("gsc", [128, NT8 * 8], F32)
    cwrow = b.sbuf("cwrow", [8, ST], BF16)
    cwbc = b.sbuf("cwbc", [128, ST], F32)
    gscr = b.sbuf("gscr", [128, 1024], F32)
    ytmp = b.sbuf("ytmp", [128, 1024], F32)
    wg = [b.sbuf("wg", [128, NCH, 128], BF16) for _ in range(2)]
    wu = [b.sbuf("wu", [128, NCH, 128], BF16) for _ in range(2)]
    wd = [b.sbuf("wd", [128, NIC_FF, 128], BF16) for _ in range(2)]

    pb = b.alloc_psum_banks(4, F32, 512)
    pbA = b.alloc_psum_banks(1, F32, 512)[0]
    pbB = b.alloc_psum_banks(1, F32, 512)[0]
    pbC = b.alloc_psum_banks(1, F32, 512)[0]
    pbD = b.alloc_psum_banks(1, F32, 512)[0]

    s = b.s

    def ld_misc(q, sem):
        q.dma_start(out=identf[:], in_=ident_hbm).then_inc(sem, 16)
        q.dma_start(out=sel8[:], in_=sel1h_hbm).then_inc(sem, 16)
    tmisc = s.add("sync", ld_misc, dma=True, n_dma=2)
    tgw = s.add("gpsimd", lambda q, sem: q.dma_start(
        out=gwT_f[:], in_=gw_hbm.rearrange("(c p) e -> p c e", p=128)).then_inc(sem, 16),
        dma=True)
    tones = s.add("vector", lambda q: q.memset(ones_bf[:], 1.0))
    tonesf = s.add("vector", lambda q: q.memset(onesf[:], 1.0))

    tcs_mm = s.add("tensor", lambda q: [
        q.matmul(pbB[0:8, 0:1], gwT_f[:, c, :], onesf[:, 0:1],
                 start=(c == 0), stop=(c == NCH - 1)) for c in range(NCH)][-1],
        waits=[tgw, tonesf])
    t_cs = s.add("vector", lambda q: q.tensor_scalar_mul(
        cs_sb[0:8, 0:1], pbB[0:8, 0:1], -1.0), waits=[tcs_mm])

    dprev = {}
    tprev_store = None
    tprev_ln = None
    for st_i in range(NST):
        t0g = st_i * ST
        tld = load_fm(b, x_st, x_hbm[:, t0g:t0g + ST],
                      waits=[t for t in (tprev_store, tprev_ln) if t] or [tmisc])
        t_ln, _ = emit_ln_fm(b, x_st, xn_bf, ones_bf, rowscr, sqscr, xtmp,
                             pbA, pbB, pbC, pbD, [tld, tones], Tn=ST)
        tprev_ln = t_ln
        muR = rowscr[0:1, 0:ST]
        rstdR = rowscr[0:1, ST:2 * ST]
        r_toks = []
        prevR = None
        for ti in range(ST // 512):
            t0 = ti * 512
            w = [t_ln, t_cs]
            if prevR is not None:
                w.append(prevR)

            def mmr(q, t0=t0, t0g=t0g):
                r = None
                for c in range(NCH):
                    r = q.matmul(pbA[0:8, 0:512], gwT_f[:, c, :],
                                 x_st[:, c, t0:t0 + 512],
                                 start=(c == 0), stop=(c == NCH - 1))
                return r
            tr = s.add("tensor", mmr, waits=w)
            te0 = s.add("scalar", lambda q, t0=t0: q.copy(
                lrow[:, t0:t0 + 512], pbA[0:8, 0:512]), waits=[tr])
            tb1 = s.add("tensor", lambda q, t0=t0: q.matmul(
                pbA[0:8, 0:512], onesf[0:1, 0:8],
                muR[0:1, t0:t0 + 512], start=True, stop=True), waits=[te0])
            tb2 = s.add("tensor", lambda q, t0=t0: q.matmul(
                pbB[0:8, 0:512], onesf[0:1, 0:8],
                rstdR[0:1, t0:t0 + 512], start=True, stop=True), waits=[te0, t_cs])
            ts1 = s.add("vector", lambda q, t0=t0: q.scalar_tensor_tensor(
                lrow[:, t0:t0 + 512], pbA[0:8, 0:512], cs_sb[0:8, 0:1],
                lrow[:, t0:t0 + 512], ALU.mult, ALU.add), waits=[tb1])
            ts2 = s.add("vector", lambda q, t0=t0: q.tensor_tensor(
                lrow[:, t0:t0 + 512], lrow[:, t0:t0 + 512], pbB[0:8, 0:512],
                ALU.mult), waits=[tb2, ts1])
            prevR = ts2
            r_toks.append(ts2)
        tp_toks = []
        for j in range(NT8):
            bk = j % 2
            w = list(r_toks)
            if ("tp", bk) in dprev:
                w.append(dprev[("tp", bk)])
            tt = s.add("tensor", lambda q, j=j, bk=bk: q.transpose(
                (pbC if bk == 0 else pbD)[:, 0:8], lrow[0:8, j * 128:(j + 1) * 128],
                identf[0:8, 0:8]), waits=w)
            te = s.add("scalar", lambda q, j=j, bk=bk: q.copy(
                l_tm[:, j, :], (pbC if bk == 0 else pbD)[:, 0:8]), waits=[tt])
            dprev[("tp", bk)] = te
            tp_toks.append(te)
        t_m = s.add("vector", lambda q: q.reduce_max(
            gsc[:, 0:NT8], l_tm[:], axis=AX.X), waits=tp_toks)
        t_e1 = s.add("vector", lambda q: q.tensor_tensor(
            p_tm[:], l_tm[:], gsc[:, 0:NT8].broadcast_to([128, NT8, 8]),
            ALU.subtract), waits=[t_m])
        t_e2 = s.add("scalar", lambda q: q.activation(
            p_tm[:], p_tm[:], AF.Exp), waits=[t_e1])
        t_sm = s.add("vector", lambda q: q.reduce_sum(
            gsc[:, NT8:2 * NT8], p_tm[:], axis=AX.X), waits=[t_e2])
        t_rc = s.add("vector", lambda q: q.reciprocal(
            gsc[:, NT8:2 * NT8], gsc[:, NT8:2 * NT8]), waits=[t_sm])
        t_pn = s.add("vector", lambda q: q.tensor_tensor(
            p_tm[:], p_tm[:], gsc[:, NT8:2 * NT8].broadcast_to([128, NT8, 8]),
            ALU.mult), waits=[t_rc])
        t_m1 = s.add("vector", lambda q: q.reduce_max(
            gsc[:, 2 * NT8:3 * NT8], l_tm[:], axis=AX.X), waits=[t_pn, t_e1])
        t_eq1 = s.add("vector", lambda q: q.tensor_tensor(
            tscr[:], l_tm[:], gsc[:, 2 * NT8:3 * NT8].broadcast_to([128, NT8, 8]),
            ALU.is_ge), waits=[t_m1])
        t_s2 = s.add("vector", lambda q: q.scalar_tensor_tensor(
            tscr2[:], tscr[:], -1e30, l_tm[:], ALU.mult, ALU.add), waits=[t_eq1])
        t_m2 = s.add("vector", lambda q: q.reduce_max(
            gsc[:, 3 * NT8:4 * NT8], tscr2[:], axis=AX.X), waits=[t_s2])
        t_eq2 = s.add("vector", lambda q: q.tensor_tensor(
            tscr2[:], tscr2[:], gsc[:, 3 * NT8:4 * NT8].broadcast_to([128, NT8, 8]),
            ALU.is_ge), waits=[t_m2])
        t_p1 = s.add("vector", lambda q: q.tensor_tensor(
            cw_tm[:], p_tm[:], tscr[:], ALU.mult), waits=[t_eq1, t_pn])
        t_s1s = s.add("vector", lambda q: q.reduce_sum(
            gsc[:, 4 * NT8:5 * NT8], cw_tm[:], axis=AX.X), waits=[t_p1])
        t_p2 = s.add("vector", lambda q: q.tensor_tensor(
            cw_tm[:], p_tm[:], tscr2[:], ALU.mult), waits=[t_eq2, t_s1s])
        t_s2s = s.add("vector", lambda q: q.reduce_sum(
            gsc[:, 5 * NT8:6 * NT8], cw_tm[:], axis=AX.X), waits=[t_p2])
        t_dn = s.add("vector", lambda q: q.tensor_tensor(
            gsc[:, 6 * NT8:7 * NT8], gsc[:, 4 * NT8:5 * NT8],
            gsc[:, 5 * NT8:6 * NT8], ALU.add), waits=[t_s2s])
        t_rc2 = s.add("vector", lambda q: q.reciprocal(
            gsc[:, 6 * NT8:7 * NT8], gsc[:, 6 * NT8:7 * NT8]), waits=[t_dn])
        t_g1 = s.add("vector", lambda q: q.tensor_tensor(
            gsc[:, 4 * NT8:5 * NT8], gsc[:, 4 * NT8:5 * NT8],
            gsc[:, 6 * NT8:7 * NT8], ALU.mult), waits=[t_rc2])
        t_g2 = s.add("vector", lambda q: q.tensor_tensor(
            gsc[:, 5 * NT8:6 * NT8], gsc[:, 5 * NT8:6 * NT8],
            gsc[:, 6 * NT8:7 * NT8], ALU.mult), waits=[t_g1])
        t_cw1 = s.add("vector", lambda q: q.tensor_tensor(
            cw_tm[:], tscr[:], gsc[:, 4 * NT8:5 * NT8].broadcast_to([128, NT8, 8]),
            ALU.mult), waits=[t_g2])
        t_cw2 = s.add("vector", lambda q: q.tensor_tensor(
            tscr2[:], tscr2[:], gsc[:, 5 * NT8:6 * NT8].broadcast_to([128, NT8, 8]),
            ALU.mult), waits=[t_cw1])
        t_cw = s.add("vector", lambda q: q.tensor_tensor(
            cw_tm[:], cw_tm[:], tscr2[:], ALU.add), waits=[t_cw2])
        cw_toks = []
        for j in range(NT8):
            w = [t_cw]
            if ("cw", 0) in dprev:
                w.append(dprev[("cw", 0)])
            tt = s.add("tensor", lambda q, j=j: q.transpose(
                pbC[0:8, 0:128], cw_tm[:, j, :], identf[:]), waits=w)
            te = s.add("scalar", lambda q, j=j: q.copy(
                cwrow[0:8, j * 128:(j + 1) * 128], pbC[0:8, 0:128]), waits=[tt])
            dprev[("cw", 0)] = te
            cw_toks.append(te)
        t_cwr = cw_toks[-1]

        for mi in range(9):
            is_base = (mi == 0)
            e = mi - 1
            wgu_src = bwgu_hbm if is_base else ewgu_hbm[e]
            wd_src = bwd_hbm if is_base else ewd_hbm[e]
            if not is_base:
                wwc = [t_cwr]
                if ("cwb", 0) in dprev:
                    wwc.append(dprev[("cwb", 0)])
                if ("bank", 3) in dprev:
                    wwc.append(dprev[("bank", 3)])
                tcb_toks = []
                for ti in range(2):
                    ttc = s.add("tensor", lambda q, e=e, ti=ti: q.matmul(
                        pb[3][:, 0:512], sel8[:, e * 128:(e + 1) * 128],
                        cwrow[0:8, ti * 512:ti * 512 + 512], start=True, stop=True),
                        waits=wwc if ti == 0 else [tcb_toks[-1]])
                    tte = s.add("vector", lambda q, ti=ti: q.tensor_copy(
                        cwbc[:, ti * 512:ti * 512 + 512], pb[3][:, 0:512]),
                        waits=[ttc])
                    tcb_toks.append(tte)
                t_cwb = tcb_toks[-1]
                dprev[("cwb", 0)] = t_cwb
                dprev[("bank", 3)] = t_cwb
            for oc in range(NIC_FF):
                buf = oc % 2
                wld = [t_ln]
                if ("w", buf) in dprev:
                    wld.append(dprev[("w", buf)])

                def ldw(q, sem, oc=oc, buf=buf, wgu_src=wgu_src):
                    wv = wgu_src.rearrange("(c p) h -> p c h", p=128)
                    q.dma_start(out=wg[buf][:],
                                in_=wv[:, :, oc * 128:(oc + 1) * 128]).then_inc(sem, 16)
                    q.dma_start(out=wu[buf][:],
                                in_=wv[:, :, I_FF + oc * 128:I_FF + (oc + 1) * 128]
                                ).then_inc(sem, 16)
                tw = s.add("gpsimd", ldw, waits=wld, dma=True, n_dma=2)
                for half in range(2):
                    hh = half * 512
                    bkA = half
                    bkB = 2 if half == 0 else 3
                    wA = [tw]
                    if ("bank", bkA) in dprev:
                        wA.append(dprev[("bank", bkA)])

                    def mmg(q, oc=oc, hh=hh, bkA=bkA, buf=buf):
                        r = None
                        for c in range(NCH):
                            r = q.matmul(pb[bkA][:, 0:512], wg[buf][:, c, :],
                                         xn_bf[:, c, hh:hh + 512],
                                         start=(c == 0), stop=(c == NCH - 1))
                        return r
                    tg = s.add("tensor", mmg, waits=wA)
                    wB = [tw]
                    if ("bank", bkB) in dprev:
                        wB.append(dprev[("bank", bkB)])

                    def mmu(q, oc=oc, hh=hh, bkB=bkB, buf=buf):
                        r = None
                        for c in range(NCH):
                            r = q.matmul(pb[bkB][:, 0:512], wu[buf][:, c, :],
                                         xn_bf[:, c, hh:hh + 512],
                                         start=(c == 0), stop=(c == NCH - 1))
                        return r
                    tu = s.add("tensor", mmu, waits=wB)
                    wsi = [tg]
                    if ("gscr", hh) in dprev:
                        wsi.append(dprev[("gscr", hh)])
                    tsi = s.add("scalar", lambda q, hh=hh, bkA=bkA: q.activation(
                        gscr[:, hh:hh + 512], pb[bkA][:, 0:512], AF.Silu), waits=wsi)
                    dprev[("bank", bkA)] = tsi
                    th = s.add("vector", lambda q, oc=oc, hh=hh, bkB=bkB: q.tensor_tensor(
                        a_bf[:, oc, hh:hh + 512], gscr[:, hh:hh + 512],
                        pb[bkB][:, 0:512], ALU.mult), waits=[tsi, tu])
                    dprev[("bank", bkB)] = th
                    dprev[("gscr", hh)] = th
                dprev[("w", buf)] = th
            t_a = th
            for oc in range(NCH):
                buf = oc % 2
                wld = [t_a]
                if ("wd", buf) in dprev:
                    wld.append(dprev[("wd", buf)])

                def ldwd(q, sem, oc=oc, buf=buf, wd_src=wd_src):
                    q.dma_start(out=wd[buf][:],
                                in_=wd_src.rearrange("(c p) h -> p c h", p=128)[
                                    :, :, oc * 128:(oc + 1) * 128]).then_inc(sem, 16)
                twd = s.add("gpsimd", ldwd, waits=wld, dma=True)
                for half in range(2):
                    hh = half * 512
                    bk = half
                    ww = [twd]
                    if ("bank", bk) in dprev:
                        ww.append(dprev[("bank", bk)])

                    def mmd(q, oc=oc, hh=hh, bk=bk, buf=buf):
                        r = None
                        for ic in range(NIC_FF):
                            r = q.matmul(pb[bk][:, 0:512], wd[buf][:, ic, :],
                                         a_bf[:, ic, hh:hh + 512],
                                         start=(ic == 0), stop=(ic == NIC_FF - 1))
                        return r
                    tm = s.add("tensor", mmd, waits=ww)
                    if is_base:
                        ta = s.add("vector", lambda q, oc=oc, hh=hh, bk=bk: q.tensor_tensor(
                            x_st[:, oc, hh:hh + 512], x_st[:, oc, hh:hh + 512],
                            pb[bk][:, 0:512], ALU.add), waits=[tm])
                    else:
                        ty = s.add("vector", lambda q, oc=oc, hh=hh, bk=bk: q.tensor_tensor(
                            ytmp[:, hh:hh + 512], pb[bk][:, 0:512],
                            cwbc[:, hh:hh + 512], ALU.mult), waits=[tm, t_cwb])
                        ta = s.add("vector", lambda q, oc=oc, hh=hh: q.tensor_tensor(
                            x_st[:, oc, hh:hh + 512], x_st[:, oc, hh:hh + 512],
                            ytmp[:, hh:hh + 512], ALU.add), waits=[ty])
                    dprev[("bank", bk)] = ta
                dprev[("wd", buf)] = ta
        tprev_store = store_fm(b, out_hbm[:, t0g:t0g + ST], x_st,
                               waits=[ta])
    s.add("sync", lambda q: q.nop(), waits=[tprev_store])
    if sa:
        return b.finish()


def build_l3_ffn(b=None, xin=None, xout=None):
    sa = b is None
    if sa:
        b = Builder("l3ffn")
    x_hbm = xin if xin is not None else b.dram_in("xres", [D, T])
    rw_hbm = b.dram_in("router_wT", [D, 8])
    biasrep_hbm = b.dram_in("rbias_rep", [128, NTT * 8])
    sw1_hbm = b.dram_in("sw1", [D, H_SH])
    sw3_hbm = b.dram_in("sw3", [D, H_SH])
    sw2_hbm = b.dram_in("sw2", [H_SH, D])
    ew1_hbm = b.dram_in("ew1", [8, D, D])
    ew2_hbm = b.dram_in("ew2", [8, D, D])
    ident_hbm = b.dram_in("identf", [128, 128])
    sel1h_hbm = b.dram_in("sel8", [8, 8 * 128], BF16)
    out_hbm = xout if xout is not None else b.dram_out("xout", [D, T])

    ST = 512
    NST = T // ST
    x_fm = b.sbuf("x", [128, NCH, T], F32)
    xn_bf = b.sbuf("xn", [128, NCH, T], BF16)
    ones_bf = b.sbuf("ones", [128, 128], BF16)
    rowscr = b.sbuf("rows", [1, 4 * T], F32)
    sqscr = b.sbuf("sqscr", [128, 1024], BF16)
    xtmp = b.sbuf("xtmp", [128, 512], F32)
    identf = b.sbuf("identf", [128, 128], F32)
    sel8 = b.sbuf("sel8", [8, 8 * 128], BF16)
    rwT_f = b.sbuf("rwT", [128, NCH, 8], F32)
    cs_sb = b.sbuf("cs", [8, 1], F32)
    onesf = b.sbuf("onesf", [128, 8], F32)
    biasrep = b.sbuf("brep", [128, NTT * 8], F32)
    srow = b.sbuf("srow", [8, T], F32)
    s_tm = b.sbuf("stm", [128, NTT, 8], F32)
    sel_tm = b.sbuf("seltm", [128, NTT, 8], F32)
    cw_tm = b.sbuf("cwtm", [128, NTT, 8], F32)
    tscr = b.sbuf("tscr", [128, NTT, 8], F32)
    tscr2 = b.sbuf("tscr2", [128, NTT, 8], F32)
    gsc = b.sbuf("gsc", [128, NTT * 8], F32)
    cwrow = b.sbuf("cwrow", [8, T], BF16)
    a_bf = b.sbuf("a", [128, NOC_SH, ST], BF16)
    he_bf = b.sbuf("he", [128, NCH, ST], BF16)
    gscr = b.sbuf("gscr", [128, 1024], F32)
    wbuf = [b.sbuf("wb", [128, NCH, 128], BF16) for _ in range(2)]
    wbuf3 = [b.sbuf("wb3", [128, NCH, 128], BF16) for _ in range(2)]
    wbuf2 = [b.sbuf("wb2", [128, NOC_SH, 128], BF16) for _ in range(2)]
    we1 = [b.sbuf("we1", [128, NCH, 128], BF16) for _ in range(2)]
    we2 = [b.sbuf("we2", [128, NCH, 128], BF16) for _ in range(2)]

    pb = b.alloc_psum_banks(4, F32, 512)
    pbA = b.alloc_psum_banks(1, F32, 512)[0]
    pbB = b.alloc_psum_banks(1, F32, 512)[0]
    pbC = b.alloc_psum_banks(1, F32, 512)[0]
    pbD = b.alloc_psum_banks(1, F32, 512)[0]

    s = b.s
    tload_x = load_fm(b, x_fm, x_hbm)

    def ld_misc(q, sem):
        q.dma_start(out=identf[:], in_=ident_hbm).then_inc(sem, 16)
        q.dma_start(out=sel8[:], in_=sel1h_hbm).then_inc(sem, 16)
        q.dma_start(out=biasrep[:], in_=biasrep_hbm).then_inc(sem, 16)
    tmisc = s.add("sync", ld_misc, dma=True, n_dma=3)
    tmisc2 = s.add("gpsimd", lambda q, sem: q.dma_start(
        out=rwT_f[:], in_=rw_hbm.rearrange("(c p) e -> p c e", p=128)).then_inc(sem, 16),
        dma=True)
    tonesf = s.add("vector", lambda q: q.memset(onesf[:], 1.0))
    tones = s.add("vector", lambda q: q.memset(ones_bf[:], 1.0))

    t_ln, _ = emit_ln_fm(b, x_fm, xn_bf, ones_bf, rowscr, sqscr, xtmp,
                         pbA, pbB, pbC, pbD, [tload_x, tones])

    tcs_mm = s.add("tensor", lambda q: [
        q.matmul(pbB[0:8, 0:1], rwT_f[:, c, :], onesf[:, 0:1],
                 start=(c == 0), stop=(c == NCH - 1)) for c in range(NCH)][-1],
        waits=[tmisc2, tonesf, t_ln])
    t_cs = s.add("vector", lambda q: q.tensor_scalar_mul(
        cs_sb[0:8, 0:1], pbB[0:8, 0:1], -1.0), waits=[tcs_mm])
    r_toks = []
    prevR = None
    muR = rowscr[0:1, 0:T]
    rstdR = rowscr[0:1, T:2 * T]
    for ti in range(4):
        t0 = ti * 512
        w = [t_ln, t_cs]
        if prevR is not None:
            w.append(prevR)

        def mmr(q, t0=t0):
            r = None
            for c in range(NCH):
                r = q.matmul(pbA[0:8, 0:512], rwT_f[:, c, :], x_fm[:, c, t0:t0 + 512],
                             start=(c == 0), stop=(c == NCH - 1))
            return r
        tr = s.add("tensor", mmr, waits=w)
        te0 = s.add("scalar", lambda q, t0=t0: q.copy(
            srow[:, t0:t0 + 512], pbA[0:8, 0:512]), waits=[tr])
        tb1 = s.add("tensor", lambda q, t0=t0: q.matmul(
            pbA[0:8, 0:512], onesf[0:1, 0:8],
            muR[0:1, t0:t0 + 512], start=True, stop=True), waits=[te0])
        tb2 = s.add("tensor", lambda q, t0=t0: q.matmul(
            pbB[0:8, 0:512], onesf[0:1, 0:8],
            rstdR[0:1, t0:t0 + 512], start=True, stop=True), waits=[te0, t_cs])
        ts1 = s.add("vector", lambda q, t0=t0: q.scalar_tensor_tensor(
            srow[:, t0:t0 + 512], pbA[0:8, 0:512], cs_sb[0:8, 0:1],
            srow[:, t0:t0 + 512], ALU.mult, ALU.add), waits=[tb1])
        ts2 = s.add("vector", lambda q, t0=t0: q.tensor_tensor(
            srow[:, t0:t0 + 512], srow[:, t0:t0 + 512], pbB[0:8, 0:512],
            ALU.mult), waits=[tb2, ts1])
        te = s.add("scalar", lambda q, t0=t0: q.activation(
            srow[:, t0:t0 + 512], srow[:, t0:t0 + 512], AF.Sigmoid), waits=[ts2])
        prevR = te
        r_toks.append(te)

    tp_toks = []
    prevT = {}
    for j in range(NTT):
        bk = j % 2
        w = list(r_toks)
        if bk in prevT:
            w.append(prevT[bk])
        tt = s.add("tensor", lambda q, j=j, bk=bk: q.transpose(
            (pbC if bk == 0 else pbD)[:, 0:8], srow[0:8, j * 128:(j + 1) * 128],
            identf[0:8, 0:8]), waits=w)
        te = s.add("scalar", lambda q, j=j, bk=bk: q.copy(
            s_tm[:, j, :], (pbC if bk == 0 else pbD)[:, 0:8]), waits=[tt])
        prevT[bk] = te
        tp_toks.append(te)
    t_sel = s.add("vector", lambda q: q.tensor_tensor(
        sel_tm[:], s_tm[:], biasrep[:].rearrange("p (j e) -> p j e", e=8),
        ALU.add), waits=tp_toks + [tmisc])

    t_m1 = s.add("vector", lambda q: q.reduce_max(
        gsc[:, 0:NTT], sel_tm[:], axis=AX.X), waits=[t_sel])
    m1bc = gsc[:, 0:NTT].broadcast_to([128, NTT, 8])
    t_eq1 = s.add("vector", lambda q: q.tensor_tensor(
        tscr[:], sel_tm[:], m1bc, ALU.is_ge), waits=[t_m1])
    t_s2 = s.add("vector", lambda q: q.scalar_tensor_tensor(
        tscr2[:], tscr[:], -1e30, sel_tm[:], ALU.mult, ALU.add), waits=[t_eq1])
    t_m2 = s.add("vector", lambda q: q.reduce_max(
        gsc[:, NTT:2 * NTT], tscr2[:], axis=AX.X), waits=[t_s2])
    m2bc = gsc[:, NTT:2 * NTT].broadcast_to([128, NTT, 8])
    t_eq2 = s.add("vector", lambda q: q.tensor_tensor(
        tscr2[:], tscr2[:], m2bc, ALU.is_ge), waits=[t_m2])
    t_p1 = s.add("vector", lambda q: q.tensor_tensor(
        cw_tm[:], s_tm[:], tscr[:], ALU.mult), waits=[t_eq1])
    t_s1s = s.add("vector", lambda q: q.reduce_sum(
        gsc[:, 2 * NTT:3 * NTT], cw_tm[:], axis=AX.X), waits=[t_p1])
    t_p2 = s.add("vector", lambda q: q.tensor_tensor(
        cw_tm[:], s_tm[:], tscr2[:], ALU.mult), waits=[t_eq2, t_s1s])
    t_s2s = s.add("vector", lambda q: q.reduce_sum(
        gsc[:, 3 * NTT:4 * NTT], cw_tm[:], axis=AX.X), waits=[t_p2])
    t_dn = s.add("vector", lambda q: q.tensor_tensor(
        gsc[:, 4 * NTT:5 * NTT], gsc[:, 2 * NTT:3 * NTT],
        gsc[:, 3 * NTT:4 * NTT], ALU.add), waits=[t_s2s])
    t_dp = s.add("vector", lambda q: q.tensor_scalar_add(
        gsc[:, 5 * NTT:6 * NTT], gsc[:, 4 * NTT:5 * NTT], 1e-9), waits=[t_dn])
    t_rc = s.add("vector", lambda q: q.reciprocal(
        gsc[:, 6 * NTT:7 * NTT], gsc[:, 5 * NTT:6 * NTT]), waits=[t_dp])
    t_cd = s.add("vector", lambda q: q.tensor_single_scalar(
        gsc[:, 7 * NTT:8 * NTT], gsc[:, 4 * NTT:5 * NTT], 1e-9, ALU.is_gt),
        waits=[t_dn])
    def gate_calc(dsti, srci):
        ta = s.add("vector", lambda q: q.tensor_tensor(
            gsc[:, dsti * NTT:(dsti + 1) * NTT], gsc[:, srci * NTT:(srci + 1) * NTT],
            gsc[:, 6 * NTT:7 * NTT], ALU.mult), waits=[t_rc, t_cd])
        tb = s.add("vector", lambda q: q.tensor_tensor(
            gsc[:, dsti * NTT:(dsti + 1) * NTT], gsc[:, dsti * NTT:(dsti + 1) * NTT],
            gsc[:, 7 * NTT:8 * NTT], ALU.mult), waits=[ta])
        tc = s.add("vector", lambda q: q.tensor_scalar(
            gsc[:, 8 * NTT:9 * NTT] if False else xtmp[:, 0:NTT],
            gsc[:, 7 * NTT:8 * NTT], -0.5, 0.5, ALU.mult, ALU.add), waits=[tb])
        td = s.add("vector", lambda q: q.tensor_tensor(
            gsc[:, dsti * NTT:(dsti + 1) * NTT], gsc[:, dsti * NTT:(dsti + 1) * NTT],
            xtmp[:, 0:NTT], ALU.add), waits=[tc])
        return td
    t_g1 = gate_calc(0, 2)
    t_g2 = gate_calc(1, 3)
    g1bc = gsc[:, 0:NTT].broadcast_to([128, NTT, 8])
    g2bc = gsc[:, NTT:2 * NTT].broadcast_to([128, NTT, 8])
    t_cw1 = s.add("vector", lambda q: q.tensor_tensor(
        cw_tm[:], tscr[:], g1bc, ALU.mult), waits=[t_g1, t_g2])
    t_cw2 = s.add("vector", lambda q: q.tensor_tensor(
        tscr2[:], tscr2[:], g2bc, ALU.mult), waits=[t_cw1])
    t_cw = s.add("vector", lambda q: q.tensor_tensor(
        cw_tm[:], cw_tm[:], tscr2[:], ALU.add), waits=[t_cw2])

    cw_toks = []
    prevT = {}
    for j in range(NTT):
        w = [t_cw]
        if 0 in prevT:
            w.append(prevT[0])
        tt = s.add("tensor", lambda q, j=j: q.transpose(
            pbC[0:8, 0:128], cw_tm[:, j, :], identf[:]), waits=w)
        te = s.add("scalar", lambda q, j=j: q.copy(
            cwrow[0:8, j * 128:(j + 1) * 128], pbC[0:8, 0:128]), waits=[tt])
        prevT[0] = te
        cw_toks.append(te)
    t_cwr = cw_toks[-1]

    tprev_store = None
    dprev = {}
    for st_i in range(NST):
        t0 = st_i * ST
        for oc in range(NOC_SH):
            buf = oc % 2
            wld = [tmisc]
            if ("s", buf) in dprev:
                wld.append(dprev[("s", buf)])

            def ldw(q, sem, oc=oc, buf=buf):
                q.dma_start(out=wbuf[buf][:],
                            in_=sw1_hbm.rearrange("(c p) h -> p c h", p=128)[
                                :, :, oc * 128:(oc + 1) * 128]).then_inc(sem, 16)
                q.dma_start(out=wbuf3[buf][:],
                            in_=sw3_hbm.rearrange("(c p) h -> p c h", p=128)[
                                :, :, oc * 128:(oc + 1) * 128]).then_inc(sem, 16)
            tw = s.add("gpsimd", ldw, waits=wld, dma=True, n_dma=2)
            bkA = oc % 2
            bkB = 2 + oc % 2
            wA = [tw, t_ln]
            if ("bank", bkA) in dprev:
                wA.append(dprev[("bank", bkA)])

            def mmg(q, oc=oc, bkA=bkA, t0=t0, buf=buf):
                r = None
                for c in range(NCH):
                    r = q.matmul(pb[bkA][:, 0:ST], wbuf[buf][:, c, :],
                                 xn_bf[:, c, t0:t0 + ST],
                                 start=(c == 0), stop=(c == NCH - 1))
                return r
            tg = s.add("tensor", mmg, waits=wA)
            wB = [tw]
            if ("bank", bkB) in dprev:
                wB.append(dprev[("bank", bkB)])

            def mmu(q, oc=oc, bkB=bkB, t0=t0, buf=buf):
                r = None
                for c in range(NCH):
                    r = q.matmul(pb[bkB][:, 0:ST], wbuf3[buf][:, c, :],
                                 xn_bf[:, c, t0:t0 + ST],
                                 start=(c == 0), stop=(c == NCH - 1))
                return r
            tu = s.add("tensor", mmu, waits=wB)
            wsi = [tg]
            if "gscr" in dprev:
                wsi.append(dprev["gscr"])
            tsi = s.add("scalar", lambda q, bkA=bkA: q.activation(
                gscr[:, 0:ST], pb[bkA][:, 0:ST], AF.Silu), waits=wsi)
            dprev[("bank", bkA)] = tsi
            th = s.add("vector", lambda q, oc=oc, bkB=bkB: q.tensor_tensor(
                a_bf[:, oc, 0:ST], gscr[:, 0:ST],
                pb[bkB][:, 0:ST], ALU.mult), waits=[tsi, tu])
            dprev[("bank", bkB)] = th
            dprev["gscr"] = th
            dprev[("s", buf)] = th
        t_a = th
        for oc in range(NCH):
            buf = oc % 2
            wld = [t_a]
            if ("s2", buf) in dprev:
                wld.append(dprev[("s2", buf)])

            def ldw2(q, sem, oc=oc, buf=buf):
                q.dma_start(out=wbuf2[buf][:],
                            in_=sw2_hbm.rearrange("(c p) h -> p c h", p=128)[
                                :, :, oc * 128:(oc + 1) * 128]).then_inc(sem, 16)
            tw2 = s.add("gpsimd", ldw2, waits=wld, dma=True)
            bk = oc % 2
            ww = [tw2, t_a]
            if ("bank", bk) in dprev:
                ww.append(dprev[("bank", bk)])

            def mmd(q, oc=oc, bk=bk, buf=buf):
                r = None
                for ic in range(NOC_SH):
                    r = q.matmul(pb[bk][:, 0:ST], wbuf2[buf][:, ic, :],
                                 a_bf[:, ic, 0:ST],
                                 start=(ic == 0), stop=(ic == NOC_SH - 1))
                return r
            tm = s.add("tensor", mmd, waits=ww)
            ta = s.add("vector", lambda q, oc=oc, bk=bk, t0=t0: q.tensor_tensor(
                x_fm[:, oc, t0:t0 + ST], x_fm[:, oc, t0:t0 + ST],
                pb[bk][:, 0:ST], ALU.add), waits=[tm])
            dprev[("bank", bk)] = ta
            dprev[("s2", buf)] = ta
        t_shared = ta

        for e in range(8):
            wwc = [t_cwr, t_shared]
            if ("bank", 3) in dprev:
                wwc.append(dprev[("bank", 3)])

            def bccw(q, e=e, t0=t0):
                return q.matmul(pb[3][:, 0:ST], sel8[:, e * 128:(e + 1) * 128],
                                cwrow[0:8, t0:t0 + ST], start=True, stop=True)
            tcb = s.add("tensor", bccw, waits=wwc)
            he_toks = []
            for oc in range(NCH):
                buf = oc % 2
                wld = [t_shared]
                if ("ew", buf) in dprev:
                    wld.append(dprev[("ew", buf)])

                def lde1(q, sem, e=e, oc=oc, buf=buf):
                    q.dma_start(out=we1[buf][:],
                                in_=ew1_hbm[e].rearrange("(c p) h -> p c h", p=128)[
                                    :, :, oc * 128:(oc + 1) * 128]).then_inc(sem, 16)
                twe = s.add("gpsimd", lde1, waits=wld, dma=True)
                bkA = oc % 2
                ww = [twe]
                if ("bank", bkA) in dprev:
                    ww.append(dprev[("bank", bkA)])

                def mme1(q, oc=oc, bkA=bkA, t0=t0, buf=buf):
                    r = None
                    for c in range(NCH):
                        r = q.matmul(pb[bkA][:, 0:ST], we1[buf][:, c, :],
                                     xn_bf[:, c, t0:t0 + ST],
                                     start=(c == 0), stop=(c == NCH - 1))
                    return r
                tm1 = s.add("tensor", mme1, waits=ww)
                wge = [tm1]
                if "gscr" in dprev:
                    wge.append(dprev["gscr"])
                tge = s.add("scalar", lambda q, bkA=bkA: q.activation(
                    gscr[:, 0:ST], pb[bkA][:, 0:ST], AF.Gelu), waits=wge)
                dprev[("bank", bkA)] = tge
                the = s.add("vector", lambda q, oc=oc: q.tensor_tensor(
                    he_bf[:, oc, 0:ST], gscr[:, 0:ST],
                    pb[3][:, 0:ST], ALU.mult), waits=[tge, tcb])
                dprev["gscr"] = the
                dprev[("ew", buf)] = the
                he_toks.append(the)
            dprev[("bank", 3)] = he_toks[-1]
            t_he = he_toks[-1]
            for oc in range(NCH):
                buf = oc % 2
                wld = [t_he]
                if ("ew2", buf) in dprev:
                    wld.append(dprev[("ew2", buf)])

                def lde2(q, sem, e=e, oc=oc, buf=buf):
                    q.dma_start(out=we2[buf][:],
                                in_=ew2_hbm[e].rearrange("(c p) h -> p c h", p=128)[
                                    :, :, oc * 128:(oc + 1) * 128]).then_inc(sem, 16)
                twe2 = s.add("gpsimd", lde2, waits=wld, dma=True)
                bk = oc % 2
                ww = [twe2, t_he]
                if ("bank", bk) in dprev:
                    ww.append(dprev[("bank", bk)])

                def mme2(q, oc=oc, bk=bk, buf=buf):
                    r = None
                    for c in range(NCH):
                        r = q.matmul(pb[bk][:, 0:ST], we2[buf][:, c, :],
                                     he_bf[:, c, 0:ST],
                                     start=(c == 0), stop=(c == NCH - 1))
                    return r
                tm2 = s.add("tensor", mme2, waits=ww)
                ta2 = s.add("vector", lambda q, oc=oc, bk=bk, t0=t0: q.tensor_tensor(
                    x_fm[:, oc, t0:t0 + ST],
                    x_fm[:, oc, t0:t0 + ST],
                    pb[bk][:, 0:ST], ALU.add), waits=[tm2])
                dprev[("bank", bk)] = ta2
                dprev[("ew2", buf)] = ta2
        tprev_store = ta2

    t_store = store_fm(b, out_hbm, x_fm, waits=[tprev_store])
    s.add("sync", lambda q: q.nop(), waits=[t_store])
    if sa:
        return b.finish()


def build_l1_ffn(b=None, xin=None, xout=None):
    sa = b is None
    if sa:
        b = Builder("l1ffn")
    x_hbm = xin if xin is not None else b.dram_in("xres", [D, T])
    gw_hbm = b.dram_in("gate_w", [D, 8])
    bwgu_hbm = b.dram_in("base_wgu", [D, 2 * I_FF])
    bwd_hbm = b.dram_in("base_wd", [I_FF, D])
    ewgu_hbm = b.dram_in("exp_wgu", [8, D, 2 * I_FF])
    ewd_hbm = b.dram_in("exp_wd", [8, I_FF, D])
    ident_hbm = b.dram_in("identf", [128, 128])
    sel1h_hbm = b.dram_in("sel8", [8, 8 * 128], BF16)
    out_hbm = xout if xout is not None else b.dram_out("xout", [D, T])

    ST = 1024
    NST = T // ST
    NT8 = ST // 128
    x_st = b.sbuf("x", [128, NCH, ST], F32)
    xn_bf = b.sbuf("xn", [128, NCH, ST], BF16)
    a_bf = b.sbuf("a", [128, NIC_FF, ST], BF16)
    ones_bf = b.sbuf("ones", [128, 128], BF16)
    onesf = b.sbuf("onesf", [128, 8], F32)
    rowscr = b.sbuf("rows", [1, 4 * ST], F32)
    sqscr = b.sbuf("sqscr", [128, 1024], BF16)
    xtmp = b.sbuf("xtmp", [128, 512], F32)
    identf = b.sbuf("identf", [128, 128], F32)
    sel8 = b.sbuf("sel8", [8, 8 * 128], BF16)
    gwT_f = b.sbuf("gwT", [128, NCH, 8], F32)
    cs_sb = b.sbuf("cs", [8, 1], F32)
    lrow = b.sbuf("lrow", [8, ST], F32)
    l_tm = b.sbuf("ltm", [128, NT8, 8], F32)
    p_tm = b.sbuf("ptm", [128, NT8, 8], F32)
    cw_tm = b.sbuf("cwtm", [128, NT8, 8], F32)
    tscr = b.sbuf("tscr", [128, NT8, 8], F32)
    tscr2 = b.sbuf("tscr2", [128, NT8, 8], F32)
    gsc = b.sbuf("gsc", [128, NT8 * 8], F32)
    cwrow = b.sbuf("cwrow", [8, ST], BF16)
    cwbc = b.sbuf("cwbc", [128, ST], F32)
    gscr = b.sbuf("gscr", [128, 1024], F32)
    ytmp = b.sbuf("ytmp", [128, 1024], F32)
    wg = [b.sbuf("wg", [128, NCH, 128], BF16) for _ in range(2)]
    wu = [b.sbuf("wu", [128, NCH, 128], BF16) for _ in range(2)]
    wd = [b.sbuf("wd", [128, NIC_FF, 128], BF16) for _ in range(2)]

    pb = b.alloc_psum_banks(4, F32, 512)
    pbA = b.alloc_psum_banks(1, F32, 512)[0]
    pbB = b.alloc_psum_banks(1, F32, 512)[0]
    pbC = b.alloc_psum_banks(1, F32, 512)[0]
    pbD = b.alloc_psum_banks(1, F32, 512)[0]

    s = b.s

    def ld_misc(q, sem):
        q.dma_start(out=identf[:], in_=ident_hbm).then_inc(sem, 16)
        q.dma_start(out=sel8[:], in_=sel1h_hbm).then_inc(sem, 16)
    tmisc = s.add("sync", ld_misc, dma=True, n_dma=2)
    tgw = s.add("gpsimd", lambda q, sem: q.dma_start(
        out=gwT_f[:], in_=gw_hbm.rearrange("(c p) e -> p c e", p=128)).then_inc(sem, 16),
        dma=True)
    tones = s.add("vector", lambda q: q.memset(ones_bf[:], 1.0))
    tonesf = s.add("vector", lambda q: q.memset(onesf[:], 1.0))

    tcs_mm = s.add("tensor", lambda q: [
        q.matmul(pbB[0:8, 0:1], gwT_f[:, c, :], onesf[:, 0:1],
                 start=(c == 0), stop=(c == NCH - 1)) for c in range(NCH)][-1],
        waits=[tgw, tonesf])
    t_cs = s.add("vector", lambda q: q.tensor_scalar_mul(
        cs_sb[0:8, 0:1], pbB[0:8, 0:1], -1.0), waits=[tcs_mm])

    dprev = {}
    tprev_store = None
    tprev_ln = None
    for st_i in range(NST):
        t0g = st_i * ST
        tld = load_fm(b, x_st, x_hbm[:, t0g:t0g + ST],
                      waits=[t for t in (tprev_store, tprev_ln) if t] or [tmisc])
        t_ln, _ = emit_ln_fm(b, x_st, xn_bf, ones_bf, rowscr, sqscr, xtmp,
                             pbA, pbB, pbC, pbD, [tld, tones], Tn=ST)
        tprev_ln = t_ln
        muR = rowscr[0:1, 0:ST]
        rstdR = rowscr[0:1, ST:2 * ST]
        r_toks = []
        prevR = None
        for ti in range(ST // 512):
            t0 = ti * 512
            w = [t_ln, t_cs]
            if prevR is not None:
                w.append(prevR)

            def mmr(q, t0=t0, t0g=t0g):
                r = None
                for c in range(NCH):
                    r = q.matmul(pbA[0:8, 0:512], gwT_f[:, c, :],
                                 x_st[:, c, t0:t0 + 512],
                                 start=(c == 0), stop=(c == NCH - 1))
                return r
            tr = s.add("tensor", mmr, waits=w)
            te0 = s.add("scalar", lambda q, t0=t0: q.copy(
                lrow[:, t0:t0 + 512], pbA[0:8, 0:512]), waits=[tr])
            tb1 = s.add("tensor", lambda q, t0=t0: q.matmul(
                pbA[0:8, 0:512], onesf[0:1, 0:8],
                muR[0:1, t0:t0 + 512], start=True, stop=True), waits=[te0])
            tb2 = s.add("tensor", lambda q, t0=t0: q.matmul(
                pbB[0:8, 0:512], onesf[0:1, 0:8],
                rstdR[0:1, t0:t0 + 512], start=True, stop=True), waits=[te0, t_cs])
            ts1 = s.add("vector", lambda q, t0=t0: q.scalar_tensor_tensor(
                lrow[:, t0:t0 + 512], pbA[0:8, 0:512], cs_sb[0:8, 0:1],
                lrow[:, t0:t0 + 512], ALU.mult, ALU.add), waits=[tb1])
            ts2 = s.add("vector", lambda q, t0=t0: q.tensor_tensor(
                lrow[:, t0:t0 + 512], lrow[:, t0:t0 + 512], pbB[0:8, 0:512],
                ALU.mult), waits=[tb2, ts1])
            prevR = ts2
            r_toks.append(ts2)
        tp_toks = []
        for j in range(NT8):
            bk = j % 2
            w = list(r_toks)
            if ("tp", bk) in dprev:
                w.append(dprev[("tp", bk)])
            tt = s.add("tensor", lambda q, j=j, bk=bk: q.transpose(
                (pbC if bk == 0 else pbD)[:, 0:8], lrow[0:8, j * 128:(j + 1) * 128],
                identf[0:8, 0:8]), waits=w)
            te = s.add("scalar", lambda q, j=j, bk=bk: q.copy(
                l_tm[:, j, :], (pbC if bk == 0 else pbD)[:, 0:8]), waits=[tt])
            dprev[("tp", bk)] = te
            tp_toks.append(te)
        t_m = s.add("vector", lambda q: q.reduce_max(
            gsc[:, 0:NT8], l_tm[:], axis=AX.X), waits=tp_toks)
        t_e1 = s.add("vector", lambda q: q.tensor_tensor(
            p_tm[:], l_tm[:], gsc[:, 0:NT8].broadcast_to([128, NT8, 8]),
            ALU.subtract), waits=[t_m])
        t_e2 = s.add("scalar", lambda q: q.activation(
            p_tm[:], p_tm[:], AF.Exp), waits=[t_e1])
        t_sm = s.add("vector", lambda q: q.reduce_sum(
            gsc[:, NT8:2 * NT8], p_tm[:], axis=AX.X), waits=[t_e2])
        t_rc = s.add("vector", lambda q: q.reciprocal(
            gsc[:, NT8:2 * NT8], gsc[:, NT8:2 * NT8]), waits=[t_sm])
        t_pn = s.add("vector", lambda q: q.tensor_tensor(
            p_tm[:], p_tm[:], gsc[:, NT8:2 * NT8].broadcast_to([128, NT8, 8]),
            ALU.mult), waits=[t_rc])
        t_m1 = s.add("vector", lambda q: q.reduce_max(
            gsc[:, 2 * NT8:3 * NT8], l_tm[:], axis=AX.X), waits=[t_pn, t_e1])
        t_eq1 = s.add("vector", lambda q: q.tensor_tensor(
            tscr[:], l_tm[:], gsc[:, 2 * NT8:3 * NT8].broadcast_to([128, NT8, 8]),
            ALU.is_ge), waits=[t_m1])
        t_s2 = s.add("vector", lambda q: q.scalar_tensor_tensor(
            tscr2[:], tscr[:], -1e30, l_tm[:], ALU.mult, ALU.add), waits=[t_eq1])
        t_m2 = s.add("vector", lambda q: q.reduce_max(
            gsc[:, 3 * NT8:4 * NT8], tscr2[:], axis=AX.X), waits=[t_s2])
        t_eq2 = s.add("vector", lambda q: q.tensor_tensor(
            tscr2[:], tscr2[:], gsc[:, 3 * NT8:4 * NT8].broadcast_to([128, NT8, 8]),
            ALU.is_ge), waits=[t_m2])
        t_p1 = s.add("vector", lambda q: q.tensor_tensor(
            cw_tm[:], p_tm[:], tscr[:], ALU.mult), waits=[t_eq1, t_pn])
        t_s1s = s.add("vector", lambda q: q.reduce_sum(
            gsc[:, 4 * NT8:5 * NT8], cw_tm[:], axis=AX.X), waits=[t_p1])
        t_p2 = s.add("vector", lambda q: q.tensor_tensor(
            cw_tm[:], p_tm[:], tscr2[:], ALU.mult), waits=[t_eq2, t_s1s])
        t_s2s = s.add("vector", lambda q: q.reduce_sum(
            gsc[:, 5 * NT8:6 * NT8], cw_tm[:], axis=AX.X), waits=[t_p2])
        t_dn = s.add("vector", lambda q: q.tensor_tensor(
            gsc[:, 6 * NT8:7 * NT8], gsc[:, 4 * NT8:5 * NT8],
            gsc[:, 5 * NT8:6 * NT8], ALU.add), waits=[t_s2s])
        t_rc2 = s.add("vector", lambda q: q.reciprocal(
            gsc[:, 6 * NT8:7 * NT8], gsc[:, 6 * NT8:7 * NT8]), waits=[t_dn])
        t_g1 = s.add("vector", lambda q: q.tensor_tensor(
            gsc[:, 4 * NT8:5 * NT8], gsc[:, 4 * NT8:5 * NT8],
            gsc[:, 6 * NT8:7 * NT8], ALU.mult), waits=[t_rc2])
        t_g2 = s.add("vector", lambda q: q.tensor_tensor(
            gsc[:, 5 * NT8:6 * NT8], gsc[:, 5 * NT8:6 * NT8],
            gsc[:, 6 * NT8:7 * NT8], ALU.mult), waits=[t_g1])
        t_cw1 = s.add("vector", lambda q: q.tensor_tensor(
            cw_tm[:], tscr[:], gsc[:, 4 * NT8:5 * NT8].broadcast_to([128, NT8, 8]),
            ALU.mult), waits=[t_g2])
        t_cw2 = s.add("vector", lambda q: q.tensor_tensor(
            tscr2[:], tscr2[:], gsc[:, 5 * NT8:6 * NT8].broadcast_to([128, NT8, 8]),
            ALU.mult), waits=[t_cw1])
        t_cw = s.add("vector", lambda q: q.tensor_tensor(
            cw_tm[:], cw_tm[:], tscr2[:], ALU.add), waits=[t_cw2])
        cw_toks = []
        for j in range(NT8):
            w = [t_cw]
            if ("cw", 0) in dprev:
                w.append(dprev[("cw", 0)])
            tt = s.add("tensor", lambda q, j=j: q.transpose(
                pbC[0:8, 0:128], cw_tm[:, j, :], identf[:]), waits=w)
            te = s.add("scalar", lambda q, j=j: q.copy(
                cwrow[0:8, j * 128:(j + 1) * 128], pbC[0:8, 0:128]), waits=[tt])
            dprev[("cw", 0)] = te
            cw_toks.append(te)
        t_cwr = cw_toks[-1]

        for mi in range(9):
            is_base = (mi == 0)
            e = mi - 1
            wgu_src = bwgu_hbm if is_base else ewgu_hbm[e]
            wd_src = bwd_hbm if is_base else ewd_hbm[e]
            if not is_base:
                wwc = [t_cwr]
                if ("cwb", 0) in dprev:
                    wwc.append(dprev[("cwb", 0)])
                if ("bank", 3) in dprev:
                    wwc.append(dprev[("bank", 3)])
                tcb_toks = []
                for ti in range(2):
                    ttc = s.add("tensor", lambda q, e=e, ti=ti: q.matmul(
                        pb[3][:, 0:512], sel8[:, e * 128:(e + 1) * 128],
                        cwrow[0:8, ti * 512:ti * 512 + 512], start=True, stop=True),
                        waits=wwc if ti == 0 else [tcb_toks[-1]])
                    tte = s.add("vector", lambda q, ti=ti: q.tensor_copy(
                        cwbc[:, ti * 512:ti * 512 + 512], pb[3][:, 0:512]),
                        waits=[ttc])
                    tcb_toks.append(tte)
                t_cwb = tcb_toks[-1]
                dprev[("cwb", 0)] = t_cwb
                dprev[("bank", 3)] = t_cwb
            for oc in range(NIC_FF):
                buf = oc % 2
                wld = [t_ln]
                if ("w", buf) in dprev:
                    wld.append(dprev[("w", buf)])

                def ldw(q, sem, oc=oc, buf=buf, wgu_src=wgu_src):
                    wv = wgu_src.rearrange("(c p) h -> p c h", p=128)
                    q.dma_start(out=wg[buf][:],
                                in_=wv[:, :, oc * 128:(oc + 1) * 128]).then_inc(sem, 16)
                    q.dma_start(out=wu[buf][:],
                                in_=wv[:, :, I_FF + oc * 128:I_FF + (oc + 1) * 128]
                                ).then_inc(sem, 16)
                tw = s.add("gpsimd", ldw, waits=wld, dma=True, n_dma=2)
                for half in range(2):
                    hh = half * 512
                    bkA = half
                    bkB = 2 if half == 0 else 3
                    wA = [tw]
                    if ("bank", bkA) in dprev:
                        wA.append(dprev[("bank", bkA)])

                    def mmg(q, oc=oc, hh=hh, bkA=bkA, buf=buf):
                        r = None
                        for c in range(NCH):
                            r = q.matmul(pb[bkA][:, 0:512], wg[buf][:, c, :],
                                         xn_bf[:, c, hh:hh + 512],
                                         start=(c == 0), stop=(c == NCH - 1))
                        return r
                    tg = s.add("tensor", mmg, waits=wA)
                    wB = [tw]
                    if ("bank", bkB) in dprev:
                        wB.append(dprev[("bank", bkB)])

                    def mmu(q, oc=oc, hh=hh, bkB=bkB, buf=buf):
                        r = None
                        for c in range(NCH):
                            r = q.matmul(pb[bkB][:, 0:512], wu[buf][:, c, :],
                                         xn_bf[:, c, hh:hh + 512],
                                         start=(c == 0), stop=(c == NCH - 1))
                        return r
                    tu = s.add("tensor", mmu, waits=wB)
                    wsi = [tg]
                    if ("gscr", hh) in dprev:
                        wsi.append(dprev[("gscr", hh)])
                    tsi = s.add("scalar", lambda q, hh=hh, bkA=bkA: q.activation(
                        gscr[:, hh:hh + 512], pb[bkA][:, 0:512], AF.Silu), waits=wsi)
                    dprev[("bank", bkA)] = tsi
                    th = s.add("vector", lambda q, oc=oc, hh=hh, bkB=bkB: q.tensor_tensor(
                        a_bf[:, oc, hh:hh + 512], gscr[:, hh:hh + 512],
                        pb[bkB][:, 0:512], ALU.mult), waits=[tsi, tu])
                    dprev[("bank", bkB)] = th
                    dprev[("gscr", hh)] = th
                dprev[("w", buf)] = th
            t_a = th
            for oc in range(NCH):
                buf = oc % 2
                wld = [t_a]
                if ("wd", buf) in dprev:
                    wld.append(dprev[("wd", buf)])

                def ldwd(q, sem, oc=oc, buf=buf, wd_src=wd_src):
                    q.dma_start(out=wd[buf][:],
                                in_=wd_src.rearrange("(c p) h -> p c h", p=128)[
                                    :, :, oc * 128:(oc + 1) * 128]).then_inc(sem, 16)
                twd = s.add("gpsimd", ldwd, waits=wld, dma=True)
                for half in range(2):
                    hh = half * 512
                    bk = half
                    ww = [twd]
                    if ("bank", bk) in dprev:
                        ww.append(dprev[("bank", bk)])

                    def mmd(q, oc=oc, hh=hh, bk=bk, buf=buf):
                        r = None
                        for ic in range(NIC_FF):
                            r = q.matmul(pb[bk][:, 0:512], wd[buf][:, ic, :],
                                         a_bf[:, ic, hh:hh + 512],
                                         start=(ic == 0), stop=(ic == NIC_FF - 1))
                        return r
                    tm = s.add("tensor", mmd, waits=ww)
                    if is_base:
                        ta = s.add("vector", lambda q, oc=oc, hh=hh, bk=bk: q.tensor_tensor(
                            x_st[:, oc, hh:hh + 512], x_st[:, oc, hh:hh + 512],
                            pb[bk][:, 0:512], ALU.add), waits=[tm])
                    else:
                        ty = s.add("vector", lambda q, oc=oc, hh=hh, bk=bk: q.tensor_tensor(
                            ytmp[:, hh:hh + 512], pb[bk][:, 0:512],
                            cwbc[:, hh:hh + 512], ALU.mult), waits=[tm, t_cwb])
                        ta = s.add("vector", lambda q, oc=oc, hh=hh: q.tensor_tensor(
                            x_st[:, oc, hh:hh + 512], x_st[:, oc, hh:hh + 512],
                            ytmp[:, hh:hh + 512], ALU.add), waits=[ty])
                    dprev[("bank", bk)] = ta
                dprev[("wd", buf)] = ta
        tprev_store = store_fm(b, out_hbm[:, t0g:t0g + ST], x_st,
                               waits=[ta])
    s.add("sync", lambda q: q.nop(), waits=[tprev_store])
    if sa:
        return b.finish()


def build_l2_attn(b=None, xin=None, xout=None):
    sa = b is None
    if sa:
        b = Builder("l2attn")
    NSS = 16
    TST = 512
    NSTT = T // TST
    x_hbm = xin if xin is not None else b.dram_in("xres", [D, T])
    lng_hbm = b.dram_in("ln_g", [128, NCH])
    lnb_hbm = b.dram_in("ln_b", [128, NCH])
    win_hbm = b.dram_in("w_in", [D, D])
    bin_hbm = b.dram_in("b_in", [128, NCH])
    convw_hbm = b.dram_in("conv_w", [128, NCH * 4])
    convb_hbm = b.dram_in("conv_b", [128, NCH])
    alog_hbm = b.dram_in("A_log", [D, NSS])
    dssm_hbm = b.dram_in("D_ssm", [128, NCH])
    wdt_hbm = b.dram_in("w_dt", [D, D])
    bdt_hbm = b.dram_in("b_dt", [128, NCH])
    wbc_hbm = b.dram_in("w_bc", [D, 2 * NSS])
    wout_hbm = b.dram_in("w_out", [D, D])
    bout_hbm = b.dram_in("b_out", [128, NCH])
    sel16_hbm = b.dram_in("sel16", [2 * NSS, 2 * NSS * 128], BF16)
    out_hbm = xout if xout is not None else b.dram_out("xout", [D, T])

    x_st = b.sbuf("x", [128, NCH, TST], F32)
    xn_bf = b.sbuf("xn", [128, NCH, TST], BF16)
    left_f = b.sbuf("left", [128, NCH, TST + 3], F32)
    xc_bf = b.sbuf("xc", [128, NCH, TST], BF16)
    dt_bf = b.sbuf("dt", [128, NCH, TST], BF16)
    w2_bf = b.sbuf("w2", [128, NCH, TST], BF16)
    y_acc = b.sbuf("y", [128, NCH, TST], F32)
    carry = b.sbuf("carry", [128, NSS * NCH], F32)
    bcrow = b.sbuf("bcrow", [2 * NSS, TST], BF16)
    btbc = b.sbuf("btbc", [128, TST], BF16)
    ctbc = b.sbuf("ctbc", [128, TST], BF16)
    dA_s = b.sbuf("dAs", [128, TST], BF16)
    u_s = b.sbuf("us", [128, TST], BF16)
    s_s = b.sbuf("ss", [128, TST], BF16)
    gscr = b.sbuf("gscr", [128, TST], F32)
    A_sb = b.sbuf("A", [128, NCH, NSS], F32)
    lng = b.sbuf("lng", [128, NCH], F32)
    lnb = b.sbuf("lnb", [128, NCH], F32)
    bin_ = b.sbuf("bin", [128, NCH], F32)
    convw = b.sbuf("convw", [128, NCH * 4], F32)
    convb = b.sbuf("convb", [128, NCH], F32)
    dssm = b.sbuf("dssm", [128, NCH], F32)
    bdt = b.sbuf("bdt", [128, NCH], F32)
    bout = b.sbuf("bout", [128, NCH], F32)
    sel16 = b.sbuf("sel16", [2 * NSS, 2 * NSS * 128], BF16)
    winT = b.sbuf("winT", [128, NCH, D], BF16)
    wdtT = b.sbuf("wdtT", [128, NCH, D], BF16)
    wbcT = b.sbuf("wbcT", [128, NCH, 2 * NSS], BF16)
    woutT = b.sbuf("woutT", [128, NCH, D], BF16)
    ones_bf = b.sbuf("ones", [128, 128], BF16)
    rowscr = b.sbuf("rows", [1, 6 * TST], F32)
    xn0_f = b.sbuf("xn0", [128, NCH, TST], F32)
    sqscr = b.sbuf("sqscr", [128, 1024], BF16)
    xtmp = b.sbuf("xtmp", [128, 512], F32)

    pb = b.alloc_psum_banks(4, F32, 512)
    pbA = b.alloc_psum_banks(1, F32, 512)[0]
    pbB = b.alloc_psum_banks(1, F32, 512)[0]
    pbC = b.alloc_psum_banks(1, F32, 512)[0]
    pbD = b.alloc_psum_banks(1, F32, 512)[0]

    s = b.s

    def ld_misc(q, sem):
        for dst, src in ((lng, lng_hbm), (lnb, lnb_hbm), (bin_, bin_hbm),
                         (convw, convw_hbm), (convb, convb_hbm),
                         (dssm, dssm_hbm), (bdt, bdt_hbm), (bout, bout_hbm)):
            q.dma_start(out=dst[:], in_=src).then_inc(sem, 16)
        q.dma_start(out=sel16[:], in_=sel16_hbm).then_inc(sem, 16)
    tmisc = s.add("sync", ld_misc, dma=True, n_dma=9)

    def ld_w(q, sem):
        q.dma_start(out=winT[:], in_=win_hbm.rearrange("(c p) o -> p c o", p=128)).then_inc(sem, 16)
        q.dma_start(out=wdtT[:], in_=wdt_hbm.rearrange("(c p) o -> p c o", p=128)).then_inc(sem, 16)
        q.dma_start(out=wbcT[:], in_=wbc_hbm.rearrange("(c p) o -> p c o", p=128)).then_inc(sem, 16)
        q.dma_start(out=woutT[:], in_=wout_hbm.rearrange("(c p) o -> p c o", p=128)).then_inc(sem, 16)
        q.dma_start(out=A_sb[:], in_=alog_hbm.rearrange("(c p) n -> p c n", p=128)).then_inc(sem, 16)
    tw = s.add("gpsimd", ld_w, dma=True, n_dma=5)

    tones = s.add("vector", lambda q: q.memset(ones_bf[:], 1.0))
    tcz = s.add("vector", lambda q: q.memset(carry[:], 0.0))
    tez = s.add("vector", lambda q: q.memset(left_f[:, :, 0:3], 0.0))
    one_ap, one_tok = b.cst(1.0)
    ta1 = s.add("scalar", lambda q: q.activation(A_sb[:], A_sb[:], AF.Exp),
                waits=[tw])
    ta2 = s.add("vector", lambda q: q.tensor_scalar_mul(A_sb[:], A_sb[:], -1.0),
                waits=[ta1])

    tprev_store = None
    dprev = {}
    t_lnprev = None
    t_edge = {}
    t_scan_done = tcz
    for st_i in range(NSTT):
        t0g = st_i * TST
        tld = load_fm(b, x_st, x_hbm[:, t0g:t0g + TST],
                      waits=[t for t in (tprev_store, t_lnprev) if t] or [tmisc])
        t_ln, _ = emit_ln_fm(b, x_st, xn_bf, ones_bf, rowscr, sqscr, xtmp,
                             pbA, pbB, pbC, pbD, [tld, tones, tmisc], Tn=TST,
                             gain=lng, bias2=lnb, double_ln=True,
                             out0_f32=xn0_f, bankE=pb[3])
        t_lnprev = t_ln
        lf_toks = []
        for oc in range(NCH):
            bk = oc % 2
            ww = [t_ln, tw]
            if ("bank", bk) in dprev:
                ww.append(dprev[("bank", bk)])
            if oc in t_edge:
                ww.append(t_edge[oc])

            def mmi(q, oc=oc, bk=bk):
                r = None
                for c in range(NCH):
                    r = q.matmul(pb[bk][:, 0:TST], winT[:, c, oc * 128:(oc + 1) * 128],
                                 xn_bf[:, c, :], start=(c == 0), stop=(c == NCH - 1))
                return r
            tm = s.add("tensor", mmi, waits=ww)
            te = s.add("scalar", lambda q, oc=oc, bk=bk: q.activation(
                left_f[:, oc, 3:TST + 3], pb[bk][:, 0:TST], AF.Identity,
                bias=bin_[:, oc:oc + 1]), waits=[tm])
            dprev[("bank", bk)] = te
            lf_toks.append(te)
        xc_toks = []
        for c in range(NCH):
            wv = lf_toks + [tez] + ([xc_toks[-1]] if xc_toks else [])
            t1 = s.add("vector", lambda q, c=c: q.tensor_scalar_mul(
                gscr[:, 0:TST], left_f[:, c, 0:TST], convw[:, 4 * c:4 * c + 1]),
                waits=wv)
            for k in range(1, 4):
                t1 = s.add("vector", lambda q, c=c, k=k: q.scalar_tensor_tensor(
                    gscr[:, 0:TST], left_f[:, c, k:TST + k],
                    convw[:, 4 * c + k:4 * c + k + 1], gscr[:, 0:TST],
                    ALU.mult, ALU.add), waits=[t1])
            t2 = s.add("scalar", lambda q, c=c: q.activation(
                xc_bf[:, c, :], gscr[:, 0:TST], AF.Silu, bias=convb[:, c:c + 1]),
                waits=[t1])
            t3 = s.add("vector", lambda q, c=c: q.tensor_copy(
                left_f[:, c, 0:3], left_f[:, c, TST:TST + 3]), waits=[t1])
            t_edge[c] = t3
            xc_toks.append(t2)
        dt_toks = []
        for oc in range(NCH):
            bk = oc % 2
            ww = list(xc_toks) + [tw]
            if ("bank", bk) in dprev:
                ww.append(dprev[("bank", bk)])

            def mmdt(q, oc=oc, bk=bk):
                r = None
                for c in range(NCH):
                    r = q.matmul(pb[bk][:, 0:TST], wdtT[:, c, oc * 128:(oc + 1) * 128],
                                 xc_bf[:, c, :], start=(c == 0), stop=(c == NCH - 1))
                return r
            tm = s.add("tensor", mmdt, waits=ww)
            tso = s.add("scalar", lambda q, oc=oc, bk=bk: q.activation(
                gscr[:, 0:TST], pb[bk][:, 0:TST], AF.Exp, bias=bdt[:, oc:oc + 1]),
                waits=[tm] + ([dt_toks[-1]] if dt_toks else []))
            tso2 = s.add("scalar", lambda q, oc=oc: q.activation(
                dt_bf[:, oc, :], gscr[:, 0:TST], AF.Ln, bias=one_ap), waits=[tso, one_tok])
            dprev[("bank", bk)] = tso
            dt_toks.append(tso2)
        ww = [tw] + list(xc_toks)
        if ("bank", 2) in dprev:
            ww.append(dprev[("bank", 2)])

        def mmbc(q):
            r = None
            for c in range(NCH):
                r = q.matmul(pb[2][0:2 * NSS, 0:TST], wbcT[:, c, :],
                             xc_bf[:, c, :], start=(c == 0), stop=(c == NCH - 1))
            return r
        tmbc = s.add("tensor", mmbc, waits=ww)
        tebc = s.add("scalar", lambda q: q.copy(
            bcrow[:, 0:TST], pb[2][0:2 * NSS, 0:TST]), waits=[tmbc])
        dprev[("bank", 2)] = tebc
        w2_toks = []
        for c in range(NCH):
            twv = s.add("vector", lambda q, c=c: q.tensor_tensor(
                w2_bf[:, c, :], dt_bf[:, c, :], xc_bf[:, c, :], ALU.mult),
                waits=[dt_toks[c]] + ([w2_toks[-1]] if w2_toks else []))
            w2_toks.append(twv)
        tyz = s.add("vector", lambda q: q.memset(y_acc[:], 0.0), waits=w2_toks)
        for n in range(NSS):
            wb1 = [tebc, t_scan_done]
            if ("bank", 3) in dprev:
                wb1.append(dprev[("bank", 3)])
            tb1 = s.add("tensor", lambda q, n=n: q.matmul(
                pb[3][:, 0:TST], sel16[:, n * 128:(n + 1) * 128], bcrow[:, 0:TST],
                start=True, stop=True), waits=wb1)
            te1 = s.add("scalar", lambda q: q.copy(btbc[:, 0:TST], pb[3][:, 0:TST]),
                        waits=[tb1])
            tb2 = s.add("tensor", lambda q, n=n: q.matmul(
                pb[3][:, 0:TST], sel16[:, (NSS + n) * 128:(NSS + n + 1) * 128],
                bcrow[:, 0:TST], start=True, stop=True), waits=[te1])
            te2 = s.add("scalar", lambda q: q.copy(ctbc[:, 0:TST], pb[3][:, 0:TST]),
                        waits=[tb2])
            dprev[("bank", 3)] = te2
            for c in range(NCH):
                col = n * NCH + c
                tda = s.add("scalar", lambda q, n=n, c=c: q.activation(
                    dA_s[:], dt_bf[:, c, :], AF.Exp, scale=A_sb[:, c, n:n + 1]),
                    waits=[t_scan_done, ta2] + dt_toks)
                tu = s.add("vector", lambda q, n=n, c=c: q.tensor_tensor(
                    u_s[:], w2_bf[:, c, :], btbc[:, 0:TST], ALU.mult),
                    waits=[te1, t_scan_done] + w2_toks)
                tsc = s.add("vector", lambda q, col=col: q.tensor_tensor_scan(
                    s_s[:], dA_s[:], u_s[:], carry[:, col:col + 1],
                    ALU.mult, ALU.add), waits=[tda, tu, tcz])
                tcar = s.add("vector", lambda q, col=col: q.tensor_copy(
                    carry[:, col:col + 1], s_s[:, TST - 1:TST]), waits=[tsc])
                twm = s.add("vector", lambda q: q.tensor_tensor(
                    u_s[:], s_s[:], ctbc[:, 0:TST], ALU.mult), waits=[tsc, te2])
                tya = s.add("vector", lambda q, c=c: q.tensor_tensor(
                    y_acc[:, c, :], y_acc[:, c, :], u_s[:], ALU.add),
                    waits=[twm, tyz])
                t_scan_done = tya
        yg_toks = []
        for c in range(NCH):
            t1 = s.add("vector", lambda q, c=c: q.scalar_tensor_tensor(
                y_acc[:, c, :], xc_bf[:, c, :], dssm[:, c:c + 1], y_acc[:, c, :],
                ALU.mult, ALU.add), waits=[t_scan_done])
            t2 = s.add("scalar", lambda q, c=c: q.activation(
                gscr[:, 0:TST], xn_bf[:, c, :], AF.Silu),
                waits=[t1] + ([yg_toks[-1]] if yg_toks else []))
            t3 = s.add("vector", lambda q, c=c: q.tensor_tensor(
                xn_bf[:, c, :], y_acc[:, c, :], gscr[:, 0:TST], ALU.mult),
                waits=[t2])
            yg_toks.append(t3)
        add_toks = []
        for oc in range(NCH):
            bk = oc % 2
            ww = list(yg_toks) + [tw]
            if ("bank", bk) in dprev:
                ww.append(dprev[("bank", bk)])

            def mmo(q, oc=oc, bk=bk):
                r = None
                for c in range(NCH):
                    r = q.matmul(pb[bk][:, 0:TST], woutT[:, c, oc * 128:(oc + 1) * 128],
                                 xn_bf[:, c, :], start=(c == 0), stop=(c == NCH - 1))
                return r
            tm = s.add("tensor", mmo, waits=ww)
            ta0 = s.add("vector", lambda q, oc=oc: q.tensor_tensor(
                x_st[:, oc, :], x_st[:, oc, :], xn0_f[:, oc, :], ALU.add),
                waits=[t_scan_done])
            ta = s.add("vector", lambda q, oc=oc, bk=bk: q.scalar_tensor_tensor(
                x_st[:, oc, :], pb[bk][:, 0:TST], bout[:, oc:oc + 1],
                x_st[:, oc, :], ALU.add, ALU.add), waits=[tm, ta0])
            dprev[("bank", bk)] = ta
            add_toks.append(ta)
        tprev_store = store_fm(b, out_hbm[:, t0g:t0g + TST], x_st, waits=add_toks)
    s.add("sync", lambda q: q.nop(), waits=[tprev_store])
    if sa:
        return b.finish()


def build_l0_attn(b=None, xin=None, xout=None):
    sa = b is None
    if sa:
        b = Builder("l0attn")
    NS0 = 32
    TST = 512
    NSTT = T // TST
    x_hbm = xin if xin is not None else b.dram_in("xres", [D, T])
    inwT_hbm = b.dram_in("in_wT", [D, 2 * D])
    inb_hbm = b.dram_in("in_b", [128, 16])
    convw_hbm = b.dram_in("conv_w", [128, NCH * 4])
    convb_hbm = b.dram_in("conv_b", [128, NCH])
    alog_hbm = b.dram_in("A_log", [D, NS0])
    bwT_hbm = b.dram_in("B_wT", [D, NS0])
    bb_hbm = b.dram_in("B_b", [NS0, 1])
    cwT_hbm = b.dram_in("C_wT", [D, NS0])
    cb_hbm = b.dram_in("C_b", [NS0, 1])
    outwT_hbm = b.dram_in("out_wT", [D, D])
    outb_hbm = b.dram_in("out_b", [128, NCH])
    lnw_hbm = b.dram_in("ln_w", [128, NCH])
    lnb_hbm = b.dram_in("ln_b", [128, NCH])
    ident_hbm = b.dram_in("identf", [128, 128])
    sel32_hbm = b.dram_in("sel32", [NS0, NS0 * 128], BF16)
    out_hbm = xout if xout is not None else b.dram_out("xout", [D, T])

    x_sl = [b.sbuf("x", [128, NCH, TST], F32) for _ in range(2)]
    xnh_sl = [b.sbuf("xnh", [128, NCH, TST], BF16) for _ in range(2)]
    xnl_sl = [b.sbuf("xnl", [128, NCH, TST], BF16) for _ in range(2)]
    xa_sl = [b.sbuf("xa", [128, NCH, TST + 3], BF16) for _ in range(2)]
    z_sl = [b.sbuf("z", [128, NCH, TST], BF16) for _ in range(2)]
    xc_bf = b.sbuf("xc", [128, NCH, TST], BF16)
    y_acc = b.sbuf("y", [128, NCH, TST], F32)
    h_st = b.sbuf("h", [128, NCH, TST], F32)
    u_out = b.sbuf("uout", [128, NCH, TST], F32)
    carry = b.sbuf("carry", [128, NS0 * NCH], F32)
    bm_row = b.sbuf("bmrow", [NS0, TST], BF16)
    cm_row = b.sbuf("cmrow", [NS0, TST], BF16)
    btbc = b.sbuf("btbc", [128, TST], BF16)
    ctbc = b.sbuf("ctbc", [128, TST], BF16)
    u_s = b.sbuf("us", [128, TST], BF16)
    s_s = b.sbuf("ss", [128, TST], BF16)
    gscr = b.sbuf("gscr", [128, TST], F32)
    loscr = b.sbuf("loscr", [128, 512], F32)
    A_sb = b.sbuf("A", [128, NCH, NS0], F32)
    inb = b.sbuf("inb", [128, 16], F32)
    convw = b.sbuf("convw", [128, NCH * 4], F32)
    convb = b.sbuf("convb", [128, NCH], F32)
    bb = b.sbuf("bb", [NS0, 1], F32)
    cb = b.sbuf("cb", [NS0, 1], F32)
    outb = b.sbuf("outb", [128, NCH], F32)
    lnw = b.sbuf("lnw", [128, NCH], F32)
    lnb2 = b.sbuf("lnb", [128, NCH], F32)
    sel32 = b.sbuf("sel32", [NS0, NS0 * 128], BF16)
    identf = b.sbuf("identf", [128, 128], F32)
    ones_bf = b.sbuf("ones", [128, 128], BF16)
    rowscr = b.sbuf("rows", [1, 4 * TST], F32)
    sqscr = b.sbuf("sqscr", [128, 1024], BF16)
    xtmp = b.sbuf("xtmp", [128, 512], F32)
    wbuf = [b.sbuf("wb", [128, NCH, 128], BF16) for _ in range(2)]
    wobuf = [b.sbuf("wo", [128, NCH, 128], BF16) for _ in range(2)]
    bcw = b.sbuf("bcw", [128, NCH, 2 * NS0], BF16)

    pb = b.alloc_psum_banks(4, F32, 512)
    pbA = b.alloc_psum_banks(1, F32, 512)[0]
    pbB = b.alloc_psum_banks(1, F32, 512)[0]
    pbC = b.alloc_psum_banks(1, F32, 512)[0]
    pbD = b.alloc_psum_banks(1, F32, 512)[0]

    s = b.s

    def ld_misc(q, sem):
        for dst, src in ((inb, inb_hbm), (convw, convw_hbm), (convb, convb_hbm),
                         (bb, bb_hbm), (cb, cb_hbm), (outb, outb_hbm),
                         (lnw, lnw_hbm), (lnb2, lnb_hbm), (identf, ident_hbm)):
            q.dma_start(out=dst[:], in_=src).then_inc(sem, 16)
        q.dma_start(out=sel32[:], in_=sel32_hbm).then_inc(sem, 16)
    tmisc = s.add("sync", ld_misc, dma=True, n_dma=10)

    def ld_w(q, sem):
        q.dma_start(out=A_sb[:], in_=alog_hbm.rearrange("(c p) n -> p c n", p=128)).then_inc(sem, 16)
        q.dma_start(out=bcw[:, :, 0:NS0],
                    in_=bwT_hbm.rearrange("(c p) n -> p c n", p=128)).then_inc(sem, 16)
        q.dma_start(out=bcw[:, :, NS0:2 * NS0],
                    in_=cwT_hbm.rearrange("(c p) n -> p c n", p=128)).then_inc(sem, 16)
    tw = s.add("gpsimd", ld_w, dma=True, n_dma=3)

    tones = s.add("vector", lambda q: q.memset(ones_bf[:], 1.0))
    tcz = s.add("vector", lambda q: q.memset(carry[:], 0.0))
    tez = [s.add("vector", lambda q, sl=sl: q.memset(xa_sl[sl][:, :, 0:3], 0.0))
           for sl in range(2)]
    tez2 = [s.add("vector", lambda q, sl=sl: q.memset(
        xa_sl[sl][:, :, TST + 2:TST + 3], 0.0)) for sl in range(2)]
    ta1 = s.add("vector", lambda q: q.tensor_scalar(
        A_sb[:], A_sb[:], -5.0, 2.0, ALU.max, ALU.min), waits=[tw])
    ta2 = s.add("scalar", lambda q: q.activation(A_sb[:], A_sb[:], AF.Exp),
                waits=[ta1])
    ta3 = s.add("vector", lambda q: q.tensor_scalar_mul(A_sb[:], A_sb[:], -1.0),
                waits=[ta2])

    dprev = {}
    t_inproj = {}
    t_scan_done = tcz
    tprev_store = None
    t_front_prev = None

    def front(st_i):
        nonlocal t_front_prev
        sl = st_i % 2
        t0g = st_i * TST
        wv = []
        if st_i >= 2:
            wv.append(t_inproj[st_i - 2][1])
        tld = load_fm(b, x_sl[sl], x_hbm[:, t0g:t0g + TST],
                      waits=wv + ([t_front_prev] if t_front_prev else [tmisc]))
        t_ln, _ = emit_ln_fm(b, x_sl[sl], xnh_sl[sl], ones_bf, rowscr, sqscr,
                             xtmp, pbA, pbB, pbC, pbD,
                             [tld, tones] + tez + tez2, Tn=TST,
                             out_lo=xnl_sl[sl], loscr=loscr)
        toks = []
        for oc in range(16):
            buf = oc % 2
            wld = [t_ln]
            if ("wi", buf) in dprev:
                wld.append(dprev[("wi", buf)])

            def ldw(q, sem, oc=oc, buf=buf):
                q.dma_start(out=wbuf[buf][:],
                            in_=inwT_hbm.rearrange("(c p) o -> p c o", p=128)[
                                :, :, oc * 128:(oc + 1) * 128]).then_inc(sem, 16)
            twl = s.add("gpsimd", ldw, waits=wld, dma=True)
            bk = oc % 2
            ww = [twl, t_ln]
            if ("bank", bk) in dprev:
                ww.append(dprev[("bank", bk)])

            def mmi(q, oc=oc, bk=bk, buf=buf, sl=sl):
                r = None
                for c in range(NCH):
                    r = q.matmul(pb[bk][:, 0:TST], wbuf[buf][:, c, :],
                                 xnh_sl[sl][:, c, :], start=(c == 0), stop=(c == NCH - 1))
                return r
            tm = s.add("tensor", mmi, waits=ww)
            if oc < NCH:
                dst = xa_sl[sl][:, oc, 2:TST + 2]
            else:
                dst = z_sl[sl][:, oc - NCH, :]
            te = s.add("scalar", lambda q, oc=oc, bk=bk, dst=dst: q.activation(
                dst, pb[bk][:, 0:TST], AF.Identity, bias=inb[:, oc:oc + 1]),
                waits=[tm])
            dprev[("bank", bk)] = te
            dprev[("wi", buf)] = te
            toks.append(te)
        tE = toks[-1]
        if st_i >= 1:
            psl = (st_i - 1) % 2
            tE1 = s.add("vector", lambda q, sl=sl, psl=psl: q.tensor_copy(
                xa_sl[psl][:, :, TST + 2:TST + 3], xa_sl[sl][:, :, 2:3]),
                waits=toks + [t_inproj[st_i - 1][0]])
            tE2 = s.add("vector", lambda q, sl=sl, psl=psl: q.tensor_copy(
                xa_sl[sl][:, :, 0:2], xa_sl[psl][:, :, TST:TST + 2]),
                waits=toks + [t_inproj[st_i - 1][0]])
            tE = tE2
            t_inproj[st_i - 1] = (tE1, t_inproj[st_i - 1][1])
        if st_i == NSTT - 1:
            tE = s.add("vector", lambda q, sl=sl: q.memset(
                xa_sl[sl][:, :, TST + 2:TST + 3], 0.0), waits=toks + [tE])
        t_inproj[st_i] = (tE, None)
        t_front_prev = tE
        return tE

    def back(st_i):
        nonlocal t_scan_done, tprev_store
        sl = st_i % 2
        t0g = st_i * TST
        t_edge_ready = t_inproj[st_i][0]
        xc_toks = []
        for c in range(NCH):
            wv = [t_edge_ready] + ([xc_toks[-1]] if xc_toks else [])
            t1 = s.add("vector", lambda q, c=c, sl=sl: q.tensor_scalar_mul(
                gscr[:, 0:TST], xa_sl[sl][:, c, 0:TST], convw[:, 4 * c:4 * c + 1]),
                waits=wv)
            for k in range(1, 4):
                t1 = s.add("vector", lambda q, c=c, k=k, sl=sl: q.scalar_tensor_tensor(
                    gscr[:, 0:TST], xa_sl[sl][:, c, k:TST + k],
                    convw[:, 4 * c + k:4 * c + k + 1], gscr[:, 0:TST],
                    ALU.mult, ALU.add), waits=[t1])
            t2 = s.add("scalar", lambda q, c=c: q.activation(
                xc_bf[:, c, :], gscr[:, 0:TST], AF.Silu, bias=convb[:, c:c + 1]),
                waits=[t1])
            xc_toks.append(t2)
        ww = [tw] + xc_toks
        if ("bank", 2) in dprev:
            ww.append(dprev[("bank", 2)])

        def mmbc(q):
            r = None
            for c in range(NCH):
                r = q.matmul(pb[2][0:NS0, 0:TST], bcw[:, c, 0:NS0],
                             xc_bf[:, c, :], start=(c == 0), stop=(c == NCH - 1))
            return r
        tb1 = s.add("tensor", mmbc, waits=ww)
        te1 = s.add("scalar", lambda q: q.activation(
            bm_row[:, 0:TST], pb[2][0:NS0, 0:TST], AF.Identity, bias=bb[0:NS0, 0:1]),
            waits=[tb1])

        def mmcc(q):
            r = None
            for c in range(NCH):
                r = q.matmul(pb[2][0:NS0, 0:TST], bcw[:, c, NS0:2 * NS0],
                             xc_bf[:, c, :], start=(c == 0), stop=(c == NCH - 1))
            return r
        tb2 = s.add("tensor", mmcc, waits=[te1])
        te2 = s.add("scalar", lambda q: q.activation(
            cm_row[:, 0:TST], pb[2][0:NS0, 0:TST], AF.Identity, bias=cb[0:NS0, 0:1]),
            waits=[tb2])
        dprev[("bank", 2)] = te2
        tyz = s.add("vector", lambda q: q.memset(y_acc[:], 0.0),
                    waits=[t_scan_done])
        for n in range(NS0):
            wb1 = [te1, t_scan_done]
            if ("bank", 3) in dprev:
                wb1.append(dprev[("bank", 3)])
            tbb = s.add("tensor", lambda q, n=n: q.matmul(
                pb[3][:, 0:TST], sel32[:, n * 128:(n + 1) * 128], bm_row[:, 0:TST],
                start=True, stop=True), waits=wb1)
            teb = s.add("scalar", lambda q: q.copy(btbc[:, 0:TST], pb[3][:, 0:TST]),
                        waits=[tbb])
            tbc_ = s.add("tensor", lambda q, n=n: q.matmul(
                pb[3][:, 0:TST], sel32[:, n * 128:(n + 1) * 128], cm_row[:, 0:TST],
                start=True, stop=True), waits=[teb, te2])
            tec = s.add("scalar", lambda q: q.copy(ctbc[:, 0:TST], pb[3][:, 0:TST]),
                        waits=[tbc_])
            dprev[("bank", 3)] = tec
            for c in range(NCH):
                col = n * NCH + c
                tu = s.add("vector", lambda q, c=c: q.tensor_tensor(
                    u_s[:], xc_bf[:, c, :], btbc[:, 0:TST], ALU.mult),
                    waits=[teb, t_scan_done] + xc_toks)
                tsc = s.add("vector", lambda q, col=col, c=c, n=n: q.tensor_tensor_scan(
                    s_s[:], A_sb[:, c, n:n + 1].broadcast_to([128, TST]), u_s[:],
                    carry[:, col:col + 1], ALU.mult, ALU.add),
                    waits=[tu, tcz, ta3])
                tcl = s.add("vector", lambda q: q.tensor_scalar(
                    s_s[:], s_s[:], -10.0, 10.0, ALU.max, ALU.min), waits=[tsc])
                tcar = s.add("vector", lambda q, col=col: q.tensor_copy(
                    carry[:, col:col + 1], s_s[:, TST - 1:TST]), waits=[tcl])
                twm = s.add("vector", lambda q: q.tensor_tensor(
                    u_s[:], s_s[:], ctbc[:, 0:TST], ALU.mult), waits=[tcl, tec])
                tya = s.add("vector", lambda q, c=c: q.tensor_tensor(
                    y_acc[:, c, :], y_acc[:, c, :], u_s[:], ALU.add),
                    waits=[twm, tyz])
                t_scan_done = tya
        yg_toks = []
        for c in range(NCH):
            t1 = s.add("scalar", lambda q, c=c, sl=sl: q.activation(
                gscr[:, 0:TST], z_sl[sl][:, c, :], AF.Silu),
                waits=[t_scan_done] + ([yg_toks[-1]] if yg_toks else []))
            t2 = s.add("vector", lambda q, c=c: q.tensor_tensor(
                xc_bf[:, c, :], y_acc[:, c, :], gscr[:, 0:TST], ALU.mult),
                waits=[t1])
            yg_toks.append(t2)
        h_toks = []
        for c in range(NCH):
            th1 = s.add("vector", lambda q, c=c, sl=sl: q.tensor_tensor(
                h_st[:, c, :], xnh_sl[sl][:, c, :], xnl_sl[sl][:, c, :], ALU.add),
                waits=[t_scan_done] + ([h_toks[-1]] if h_toks else []))
            h_toks.append(th1)
        for oc in range(NCH):
            buf = oc % 2
            wld = list(yg_toks)
            if ("wo", buf) in dprev:
                wld.append(dprev[("wo", buf)])

            def ldw(q, sem, oc=oc, buf=buf):
                q.dma_start(out=wobuf[buf][:],
                            in_=outwT_hbm.rearrange("(c p) o -> p c o", p=128)[
                                :, :, oc * 128:(oc + 1) * 128]).then_inc(sem, 16)
            twl = s.add("gpsimd", ldw, waits=wld, dma=True)
            bk = oc % 2
            ww = [twl]
            if ("bank", bk) in dprev:
                ww.append(dprev[("bank", bk)])

            def mmo(q, oc=oc, bk=bk, buf=buf):
                r = None
                for c in range(NCH):
                    r = q.matmul(pb[bk][:, 0:TST], wobuf[buf][:, c, :],
                                 xc_bf[:, c, :], start=(c == 0), stop=(c == NCH - 1))
                return r
            tm = s.add("tensor", mmo, waits=ww)
            ta = s.add("vector", lambda q, oc=oc, bk=bk: q.scalar_tensor_tensor(
                h_st[:, oc, :], pb[bk][:, 0:TST], outb[:, oc:oc + 1],
                h_st[:, oc, :], ALU.add, ALU.add), waits=[tm, h_toks[oc]])
            dprev[("bank", bk)] = ta
            dprev[("wo", buf)] = ta
        t_h = ta
        t_ln2, _ = emit_ln_fm(b, h_st, u_out, ones_bf, rowscr, sqscr, xtmp,
                              pbA, pbB, pbC, pbD, [t_h], Tn=TST,
                              gain=lnw, bias2=lnb2, cast_buf=xc_bf)
        add_toks = []
        for c in range(NCH):
            ta2_ = s.add("vector", lambda q, c=c, sl=sl: q.tensor_tensor(
                x_sl[sl][:, c, :], x_sl[sl][:, c, :], u_out[:, c, :], ALU.add),
                waits=[t_ln2])
            add_toks.append(ta2_)
        tprev_store = store_fm(b, out_hbm[:, t0g:t0g + TST], x_sl[sl],
                               waits=add_toks)
        t_inproj[st_i] = (t_inproj[st_i][0], tprev_store)

    front(0)
    for st_i in range(NSTT):
        if st_i + 1 < NSTT:
            front(st_i + 1)
        back(st_i)
    s.add("sync", lambda q: q.nop(), waits=[tprev_store])
    if sa:
        return b.finish()


def build_l1_attn(b=None, xin=None, xout=None):
    sa = b is None
    if sa:
        b = Builder("l1attn")
    NH = 16
    DH = 64
    TST = 512
    NSTT = T // TST
    x_hbm = xin if xin is not None else b.dram_in("xres", [D, T])
    ln1g_hbm = b.dram_in("ln1_g", [128, NCH])
    ln1b_hbm = b.dram_in("ln1_b", [128, NCH])
    inwT_hbm = b.dram_in("in_wT", [D, 3 * D])
    outwT_hbm = b.dram_in("out_wT", [D, D])
    ln2g_hbm = b.dram_in("ln2_g", [128, NCH])
    ln2b_hbm = b.dram_in("ln2_b", [128, NCH])
    slopes_hbm = b.dram_in("slopes", [1, NH])
    smear_hbm = b.dram_in("smear", [1, NH])
    jramp_hbm = b.dram_in("jramp", [128, NH])
    subend_hbm = b.dram_in("subend", [128, 64])
    maskq_hbm = b.dram_in("maskq4", [128, 4 * 512])
    maskt_hbm = b.dram_in("maskt4", [128, 4 * 512])
    ident_hbm = b.dram_in("identf", [128, 128])
    identb_hbm = b.dram_in("identb", [128, 128], BF16)
    out_hbm = xout if xout is not None else b.dram_out("xout", [D, T])

    qkv_dram = b.dram_tmp("qkv", [3 * D, T], BF16)
    ao_dram = b.dram_tmp("ao", [D, T], BF16)

    x_st = b.sbuf("x", [128, NCH, TST], F32)
    hin_st = b.sbuf("hin", [128, NCH, TST], BF16)
    stg = b.sbuf("stg", [128, TST], BF16)
    qh = b.sbuf("qh", [64, T], BF16)
    kh = b.sbuf("kh", [64, T], BF16)
    ksm = b.sbuf("ksm", [64, T], BF16)
    vstg = b.sbuf("vstg", [64, T], BF16)
    vtm = b.sbuf("vtm", [128, NTT, DH], BF16)
    PT = b.sbuf("PT", [128, NTT, TST], BF16)
    shf = b.sbuf("shf", [128, TST], F32)
    m8bc = b.sbuf("m8bc", [128, TST], F32)
    submax = b.sbuf("submax", [128, 64], F32)
    cand = b.sbuf("cand", [128, 64], F32)
    m8col = b.sbuf("m8col", [128, 1], F32)
    m8row = b.sbuf("m8row", [1, TST], F32)
    rrow = b.sbuf("rrow", [1, TST], F32)
    rrowh = b.sbuf("rrowh", [1, TST], BF16)
    rrowl = b.sbuf("rrowl", [1, TST], BF16)
    osb = b.sbuf("osb", [64, TST], F32)
    ao_sb = b.sbuf("aosb", [64, T], BF16)
    y2_st = b.sbuf("y2", [128, NCH, TST], F32)
    u_out = b.sbuf("uout", [128, NCH, TST], F32)
    cast_b = b.sbuf("castb", [128, NCH, TST], BF16)
    srow = b.sbuf("srowx", [1, 2 * NH], F32)
    sg_bc = b.sbuf("sgbc", [64, NH], F32)
    om_bc = b.sbuf("ombc", [64, NH], F32)
    slope_bc = b.sbuf("slopebc", [128, NH], F32)
    jramp = b.sbuf("jramp", [128, NH], F32)
    subend = b.sbuf("subend", [128, 64], F32)
    aramp = b.sbuf("aramp", [128, NH], F32)
    aramp32 = b.sbuf("aramp32", [128, 64], F32)
    maskq = b.sbuf("maskq", [128, 4 * 512], F32)
    maskt = b.sbuf("maskt", [128, 4 * 512], F32)
    ln1g = b.sbuf("ln1g", [128, NCH], F32)
    ln1b = b.sbuf("ln1b", [128, NCH], F32)
    ln2g = b.sbuf("ln2g", [128, NCH], F32)
    ln2b = b.sbuf("ln2b", [128, NCH], F32)
    identf = b.sbuf("identf", [128, 128], F32)
    identb_l1 = b.sbuf("identb", [128, 128], BF16)
    ones_bf = b.sbuf("ones", [128, 128], BF16)
    onesf = b.sbuf("onesf", [128, 128], F32)
    rowscr = b.sbuf("rows", [1, 4 * TST], F32)
    sqscr = b.sbuf("sqscr", [128, 1024], BF16)
    xtmp = b.sbuf("xtmp", [128, 512], F32)
    wbuf = [b.sbuf("wb", [128, NCH, 128], BF16) for _ in range(2)]

    pb = b.alloc_psum_banks(3, F32, 512)
    pbT = b.alloc_psum_banks(1, BF16, 1024)[0]
    pbA = b.alloc_psum_banks(1, F32, 512)[0]
    pbB = b.alloc_psum_banks(1, F32, 512)[0]
    pbC = b.alloc_psum_banks(1, F32, 512)[0]
    pbD = b.alloc_psum_banks(1, F32, 512)[0]

    s = b.s

    def ld_misc(q, sem):
        for dst, src in ((ln1g, ln1g_hbm), (ln1b, ln1b_hbm), (ln2g, ln2g_hbm),
                         (ln2b, ln2b_hbm), (jramp, jramp_hbm),
                         (subend, subend_hbm), (maskq, maskq_hbm),
                         (maskt, maskt_hbm), (identf, ident_hbm),
                         (identb_l1, identb_hbm)):
            q.dma_start(out=dst[:], in_=src).then_inc(sem, 16)
        q.dma_start(out=srow[0:1, 0:NH], in_=smear_hbm).then_inc(sem, 16)
        q.dma_start(out=srow[0:1, NH:2 * NH], in_=slopes_hbm).then_inc(sem, 16)
    tmisc = s.add("sync", ld_misc, dma=True, n_dma=12)
    tones = s.add("vector", lambda q: q.memset(ones_bf[:], 1.0))
    tonesf = s.add("vector", lambda q: q.memset(onesf[:], 1.0))
    tsg = s.add("scalar", lambda q: q.activation(
        srow[0:1, 0:NH], srow[0:1, 0:NH], AF.Sigmoid), waits=[tmisc])
    tb1 = s.add("tensor", lambda q: q.matmul(
        pbA[0:64, 0:NH], onesf[0:1, 0:64],
        srow[0:1, 0:NH], start=True, stop=True), waits=[tsg, tonesf])
    te1 = s.add("scalar", lambda q: q.copy(sg_bc[:], pbA[0:64, 0:NH]), waits=[tb1])
    tom = s.add("vector", lambda q: q.tensor_scalar(
        om_bc[:], sg_bc[:], -1.0, 1.0, ALU.mult, ALU.add), waits=[te1])
    tb2 = s.add("tensor", lambda q: q.matmul(
        pbA[0:128, 0:NH], onesf[0:1, 0:128],
        srow[0:1, NH:2 * NH], start=True, stop=True), waits=[te1, tmisc])
    te2 = s.add("scalar", lambda q: q.copy(slope_bc[:], pbA[0:128, 0:NH]),
                waits=[tb2])
    dprev = {}

    tq_done = []
    t_lnp = None
    for st_i in range(NSTT):
        t0g = st_i * TST
        tld = load_fm(b, x_st, x_hbm[:, t0g:t0g + TST],
                      waits=[t_lnp] if t_lnp else [tmisc])
        t_ln, _ = emit_ln_fm(b, x_st, hin_st, ones_bf, rowscr, sqscr, xtmp,
                             pbA, pbB, pbC, pbD, [tld, tones, te2], Tn=TST,
                             gain=ln1g, bias2=ln1b, double_ln=True)
        t_lnp = t_ln
        for oc in range(3 * NCH):
            buf = oc % 2
            wld = [t_ln]
            if ("wi", buf) in dprev:
                wld.append(dprev[("wi", buf)])

            def ldw(q, sem, oc=oc, buf=buf):
                q.dma_start(out=wbuf[buf][:],
                            in_=inwT_hbm.rearrange("(c p) o -> p c o", p=128)[
                                :, :, oc * 128:(oc + 1) * 128]).then_inc(sem, 16)
            twl = s.add("gpsimd", ldw, waits=wld, dma=True)
            bk = oc % 2
            ww = [twl]
            if ("bank", bk) in dprev:
                ww.append(dprev[("bank", bk)])

            def mmi(q, oc=oc, bk=bk, buf=buf):
                r = None
                for c in range(NCH):
                    r = q.matmul(pb[bk][:, 0:TST], wbuf[buf][:, c, :],
                                 hin_st[:, c, :], start=(c == 0), stop=(c == NCH - 1))
                return r
            tm = s.add("tensor", mmi, waits=ww)
            tev = s.add("scalar", lambda q, bk=bk: q.copy(stg[:], pb[bk][:, 0:TST]),
                        waits=[tm] + ([dprev["stg"]] if "stg" in dprev else []))
            dprev[("bank", bk)] = tev

            def st_dma(q, sem, oc=oc, t0g=t0g):
                q.dma_start(out=qkv_dram[oc * 128:(oc + 1) * 128, t0g:t0g + TST],
                            in_=stg[:]).then_inc(sem, 16)
            tsd = s.add("sync", st_dma, waits=[tev], dma=True)
            dprev["stg"] = tsd
            dprev[("wi", buf)] = tev
            tq_done.append(tsd)

    ao_done = []
    t_prev_head = None
    for h in range(NH):
        hb = (h % 2) * 64
        wv = [tq_done[-1]] + ([t_prev_head] if t_prev_head else [])

        def ldqkv(q, sem, h=h):
            q.dma_start(out=qh[:], in_=qkv_dram[h * 64:(h + 1) * 64, :]).then_inc(sem, 16)
            q.dma_start(out=kh[:], in_=qkv_dram[D + h * 64:D + (h + 1) * 64, :]).then_inc(sem, 16)
            q.dma_start(out=vstg[:], in_=qkv_dram[2 * D + h * 64:2 * D + (h + 1) * 64, :]).then_inc(sem, 16)
        tqkv = s.add("sync", ldqkv, waits=wv, dma=True, n_dma=3)
        tar = s.add("vector", lambda q, h=h: q.tensor_scalar_mul(
            aramp[:], jramp[:], slope_bc[:, h:h + 1]), waits=[te2, tmisc] + ([t_prev_head] if t_prev_head else []))
        tar32 = s.add("vector", lambda q, h=h: q.tensor_scalar_mul(
            aramp32[:], subend[:], slope_bc[:, h:h + 1]), waits=[tar])
        tk1 = s.add("vector", lambda q, h=h: q.tensor_scalar_mul(
            ksm[:], kh[:], om_bc[:, h:h + 1]), waits=[tqkv, tom, tar32])
        tk2 = s.add("vector", lambda q, h=h: q.scalar_tensor_tensor(
            ksm[:, 1:T], kh[:, 0:T - 1], sg_bc[:, h:h + 1], ksm[:, 1:T],
            ALU.mult, ALU.add), waits=[tk1])
        vt_toks = []
        for jt in range(NTT):
            ww = [tqkv]
            if ("pbT",) in dprev:
                ww.append(dprev[("pbT",)])
            tt = s.add("tensor", lambda q, jt=jt: q.transpose(
                pbT[:, 0:64], vstg[:, jt * 128:(jt + 1) * 128],
                identb_l1[0:64, 0:64]), waits=ww)
            tev = s.add("scalar", lambda q, jt=jt: q.copy(
                vtm[:, jt, :], pbT[:, 0:64]), waits=[tt])
            dprev[("pbT",)] = tev
            vt_toks.append(tev)
        for qg in range(4):
            q0 = qg * TST
            for qs in range(4):
                qt = qg * 4 + qs
                nsub = (qt + 1) * 4
                njg = ceil_div(qt + 1, 4)
                treds = []
                for jg in range(njg):
                    jw = min(512, (qt + 1) * 128 - jg * 512)
                    bk = jg % 2
                    ww = [tk2]
                    if ("bank", bk) in dprev:
                        ww.append(dprev[("bank", bk)])

                    def mmS(q, h=h, qt=qt, jg=jg, jw=jw, bk=bk):
                        return q.matmul(pb[bk][:, 0:jw],
                                        qh[:, qt * 128:(qt + 1) * 128],
                                        ksm[:, jg * 512:jg * 512 + jw],
                                        start=True, stop=True)
                    tmS = s.add("tensor", mmS, waits=ww)
                    if jg == njg - 1:
                        r = qt - jg * 4
                        tmS = s.add("vector", lambda q, r=r, jw=jw, bk=bk: q.tensor_tensor(
                            pb[bk][:, 0:jw], pb[bk][:, 0:jw],
                            maskq[:, r * 512:r * 512 + jw], ALU.add), waits=[tmS])
                    tred = s.add("vector", lambda q, jg=jg, jw=jw, bk=bk: q.reduce_max(
                        submax[:, jg * 16:jg * 16 + jw // 32],
                        pb[bk][:, 0:jw].rearrange("p (a k) -> p a k", k=32),
                        axis=AX.X), waits=[tmS])
                    dprev[("bank", bk)] = tred
                    treds.append(tred)
                tc1 = s.add("vector", lambda q, nsub=nsub: q.tensor_scalar_mul(
                    cand[:, 0:nsub], submax[:, 0:nsub], 0.125), waits=treds)
                tc2 = s.add("vector", lambda q, nsub=nsub: q.tensor_tensor(
                    cand[:, 0:nsub], cand[:, 0:nsub], aramp32[:, 0:nsub], ALU.add),
                    waits=[tc1, tar32])
                tc3 = s.add("vector", lambda q, nsub=nsub: q.reduce_max(
                    m8col[:], cand[:, 0:nsub], axis=AX.X), waits=[tc2])
                tc4 = s.add("vector", lambda q: q.tensor_scalar_mul(
                    m8col[:], m8col[:], 8.0), waits=[tc3])
                ww = [tc4]
                if ("bank", 2) in dprev:
                    ww.append(dprev[("bank", 2)])
                tt = s.add("tensor", lambda q: q.transpose(
                    pb[2][0:1, 0:128], m8col[:], identf[:]), waits=ww)
                tev = s.add("scalar", lambda q, qs=qs: q.copy(
                    m8row[0:1, qs * 128:(qs + 1) * 128], pb[2][0:1, 0:128]),
                    waits=[tt])
                dprev[("bank", 2)] = tev
            t_m8 = tev
            ww = [t_m8, tonesf]
            if ("bank", 2) in dprev:
                ww.append(dprev[("bank", 2)])
            tbm = s.add("tensor", lambda q: q.matmul(
                pb[2][:, 0:TST], onesf[0:1, 0:128],
                m8row[0:1, 0:TST], start=True, stop=True), waits=ww)
            tem = s.add("scalar", lambda q: q.copy(m8bc[:], pb[2][:, 0:TST]),
                        waits=[tbm])
            dprev[("bank", 2)] = tem
            njt = 4 * qg + 4
            for jt in range(njt):
                bk = jt % 2
                ww = [tem, tk2]
                if ("bank", bk) in dprev:
                    ww.append(dprev[("bank", bk)])

                def mmT(q, h=h, jt=jt, q0=q0, bk=bk):
                    return q.matmul(pb[bk][:, 0:TST],
                                    ksm[:, jt * 128:(jt + 1) * 128],
                                    qh[:, q0:q0 + TST], start=True, stop=True)
                tmT = s.add("tensor", mmT, waits=ww)
                tsh = s.add("vector", lambda q, bk=bk: q.scalar_tensor_tensor(
                    shf[:], m8bc[:], -1.0, pb[bk][:, 0:TST], ALU.mult, ALU.add),
                    waits=[tmT, dprev["shf"]] if "shf" in dprev else [tmT])
                if jt >= 4 * qg:
                    r = jt - 4 * qg
                    tsh = s.add("vector", lambda q, r=r: q.tensor_tensor(
                        shf[:], shf[:], maskt[:, r * 512:(r + 1) * 512], ALU.add),
                        waits=[tsh])
                dprev[("bank", bk)] = tsh
                tex = s.add("scalar", lambda q, jt=jt: q.activation(
                    PT[:, jt, :], shf[:], AF.Exp, scale=0.125,
                    bias=aramp[:, jt:jt + 1]), waits=[tsh, tar])
                dprev["shf"] = tex

                def mmR(q, jt=jt, njt=njt):
                    return q.matmul(pbA[0:1, 0:TST], ones_bf[:, 0:1],
                                    PT[:, jt, :], start=(jt == 0),
                                    stop=(jt == njt - 1), skip_group_check=True)
                wR = [tex]
                if jt == 0 and ("bankA",) in dprev:
                    wR.append(dprev[("bankA",)])
                tmR = s.add("tensor", mmR, waits=wR)
            ter = s.add("scalar", lambda q: q.copy(rrow[0:1, :], pbA[0:1, 0:TST]),
                        waits=[tmR])
            dprev[("bankA",)] = ter
            tr1 = s.add("vector", lambda q: q.reciprocal(rrow[0:1, :], rrow[0:1, :]),
                        waits=[ter])
            tr2 = s.add("vector", lambda q: q.tensor_copy(rrowh[0:1, :], rrow[0:1, :]),
                        waits=[tr1])
            tr3 = s.add("vector", lambda q: q.tensor_tensor(
                rrowl[0:1, :], rrow[0:1, :], rrowh[0:1, :], ALU.subtract),
                waits=[tr2])
            tr4 = tr3
            for jt in range(njt):
                def mmV(q, jt=jt, njt=njt):
                    return q.matmul(pbB[0:64, 0:TST], vtm[:, jt, :],
                                    PT[:, jt, :], start=(jt == 0),
                                    stop=(jt == njt - 1))
                wV = [dprev["shf"]] + vt_toks
                if jt == 0 and ("bankB",) in dprev:
                    wV.append(dprev[("bankB",)])
                tmV = s.add("tensor", mmV, waits=wV)
            teo = s.add("scalar", lambda q: q.copy(osb[:], pbB[0:64, 0:TST]),
                        waits=[tmV])
            dprev[("bankB",)] = teo
            ww = [tr4, teo]
            if ("bankC",) in dprev:
                ww.append(dprev[("bankC",)])

            def mmrb(q):
                q.matmul(pbC[0:64, 0:TST], ones_bf[0:1, 0:64], rrowh[0:1, :],
                         start=True, stop=False)
                return q.matmul(pbC[0:64, 0:TST], ones_bf[0:1, 0:64], rrowl[0:1, :],
                                start=False, stop=True)
            tmrb = s.add("tensor", mmrb, waits=ww)
            tno = s.add("vector", lambda q, q0=q0: q.tensor_tensor(
                ao_sb[:, q0:q0 + TST], osb[:], pbC[0:64, 0:TST], ALU.mult),
                waits=[tmrb, teo])
            dprev[("bankC",)] = tno
        def st_ao(q, sem, h=h):
            q.dma_start(out=ao_dram[h * 64:(h + 1) * 64, :], in_=ao_sb[:]).then_inc(sem, 16)
        t_prev_head = s.add("sync", st_ao, waits=[tno], dma=True)
        ao_done.append(t_prev_head)

    tprev_store = None
    for st_i in range(NSTT):
        t0g = st_i * TST
        tld = load_fm(b, x_st, x_hbm[:, t0g:t0g + TST],
                      waits=[tprev_store] if tprev_store else [ao_done[-1]])
        tla = s.add("gpsimd", lambda q, sem, t0g=t0g: q.dma_start(
            out=cast_b[:], in_=ao_dram.rearrange("(c p) t -> p c t", p=128)[
                :, :, t0g:t0g + TST]).then_inc(sem, 16),
            waits=[ao_done[-1]] + ([tprev_store] if tprev_store else []), dma=True)
        mm_toks = []
        for oc in range(NCH):
            buf = oc % 2
            wld = [tla]
            if ("wo", buf) in dprev:
                wld.append(dprev[("wo", buf)])

            def ldw(q, sem, oc=oc, buf=buf):
                q.dma_start(out=wbuf[buf][:],
                            in_=outwT_hbm.rearrange("(c p) o -> p c o", p=128)[
                                :, :, oc * 128:(oc + 1) * 128]).then_inc(sem, 16)
            twl = s.add("gpsimd", ldw, waits=wld, dma=True)
            bk = oc % 2
            ww = [twl, tla]
            if ("bank", bk) in dprev:
                ww.append(dprev[("bank", bk)])

            def mmo(q, oc=oc, bk=bk, buf=buf):
                r = None
                for c in range(NCH):
                    r = q.matmul(pb[bk][:, 0:TST], wbuf[buf][:, c, :],
                                 cast_b[:, c, :], start=(c == 0), stop=(c == NCH - 1))
                return r
            tm = s.add("tensor", mmo, waits=ww)
            tev = s.add("vector", lambda q, oc=oc, bk=bk: q.tensor_copy(
                y2_st[:, oc, :], pb[bk][:, 0:TST]), waits=[tm])
            dprev[("bank", bk)] = tev
            dprev[("wo", buf)] = tev
            mm_toks.append(tev)
        t_ln2, _ = emit_ln_fm(b, y2_st, u_out, ones_bf, rowscr, sqscr, xtmp,
                              pbA, pbB, pbC, pbD, mm_toks, Tn=TST,
                              gain=ln2g, bias2=ln2b, cast_buf=hin_st)
        add_toks = []
        for c in range(NCH):
            ta = s.add("vector", lambda q, c=c: q.tensor_tensor(
                x_st[:, c, :], x_st[:, c, :], u_out[:, c, :], ALU.add),
                waits=[t_ln2, tld])
            add_toks.append(ta)
        tprev_store = store_fm(b, out_hbm[:, t0g:t0g + TST], x_st, waits=add_toks)
    s.add("sync", lambda q: q.nop(), waits=[tprev_store])
    if sa:
        return b.finish()


def build_l3_attn(b=None, xin=None, xout=None):
    sa = b is None
    if sa:
        b = Builder("l3attn")
    NH = 16
    TST = 512
    NSTT = T // TST
    NG = TST // 8
    x_hbm = xin if xin is not None else b.dram_in("xres", [D, T])
    inwT_hbm = b.dram_in("in_wT", [D, 3 * D])
    inb_hbm = b.dram_in("in_b", [128, 3 * NCH])
    outwT_hbm = b.dram_in("out_wT", [D, D])
    outb_hbm = b.dram_in("out_b", [128, NCH])
    hm_hbm = b.dram_in("hm", [128, NCH * NH], BF16)
    hmT_hbm = b.dram_in("hmT", [NH, NCH * 128], BF16)
    out_hbm = xout if xout is not None else b.dram_out("xout", [D, T])

    x_st = b.sbuf("x", [128, NCH, TST], F32)
    hin = b.sbuf("hin", [128, NCH, TST], BF16)
    qkv = b.sbuf("qkv", [128, 3 * NCH, TST], BF16)
    qkscr = b.sbuf("qkscr", [128, TST], BF16)
    ltile = b.sbuf("ltile", [NH, 8 * TST], F32)
    lmax = b.sbuf("lmax", [NH, TST], F32)
    abf = b.sbuf("abf", [NH, 8 * TST], BF16)
    vascr = b.sbuf("vascr", [128, TST], F32)
    o_acc = b.sbuf("oacc", [128, NCH, TST], F32)
    inb = b.sbuf("inb", [128, 3 * NCH], F32)
    outb = b.sbuf("outb", [128, NCH], F32)
    hm = b.sbuf("hm", [128, NCH * NH], BF16)
    hmT = b.sbuf("hmT", [NH, NCH * 128], BF16)
    ones_bf = b.sbuf("ones", [128, 128], BF16)
    rowscr = b.sbuf("rows", [1, 4 * TST], F32)
    sqscr = b.sbuf("sqscr", [128, 1024], BF16)
    xtmp = b.sbuf("xtmp", [128, 512], F32)
    wbuf = [b.sbuf("wb", [128, NCH, 128], BF16) for _ in range(2)]

    pb = b.alloc_psum_banks(4, F32, 512)
    pbA = b.alloc_psum_banks(1, F32, 512)[0]
    pbB = b.alloc_psum_banks(1, F32, 512)[0]
    pbC = b.alloc_psum_banks(1, F32, 512)[0]
    pbD = b.alloc_psum_banks(1, F32, 512)[0]

    s = b.s

    def ld_misc(q, sem):
        for dst, src in ((inb, inb_hbm), (outb, outb_hbm), (hm, hm_hbm),
                         (hmT, hmT_hbm)):
            q.dma_start(out=dst[:], in_=src).then_inc(sem, 16)
    tmisc = s.add("sync", ld_misc, dma=True, n_dma=4)
    tones = s.add("vector", lambda q: q.memset(ones_bf[:], 1.0))

    dprev = {}
    tprev_store = None
    t_lnp = None
    for st_i in range(NSTT):
        t0g = st_i * TST
        tld = load_fm(b, x_st, x_hbm[:, t0g:t0g + TST],
                      waits=[t for t in (tprev_store, t_lnp) if t] or [tmisc])
        t_ln, _ = emit_ln_fm(b, x_st, hin, ones_bf, rowscr, sqscr, xtmp,
                             pbA, pbB, pbC, pbD, [tld, tones], Tn=TST)
        t_lnp = t_ln
        qk_toks = []
        for oc in range(3 * NCH):
            buf = oc % 2
            wld = [t_ln]
            if ("wi", buf) in dprev:
                wld.append(dprev[("wi", buf)])

            def ldw(q, sem, oc=oc, buf=buf):
                q.dma_start(out=wbuf[buf][:],
                            in_=inwT_hbm.rearrange("(c p) o -> p c o", p=128)[
                                :, :, oc * 128:(oc + 1) * 128]).then_inc(sem, 16)
            twl = s.add("gpsimd", ldw, waits=wld, dma=True)
            bk = oc % 2
            ww = [twl]
            if ("bank", bk) in dprev:
                ww.append(dprev[("bank", bk)])

            def mmi(q, oc=oc, bk=bk, buf=buf):
                r = None
                for c in range(NCH):
                    r = q.matmul(pb[bk][:, 0:TST], wbuf[buf][:, c, :],
                                 hin[:, c, :], start=(c == 0), stop=(c == NCH - 1))
                return r
            tm = s.add("tensor", mmi, waits=ww)
            te = s.add("scalar", lambda q, oc=oc, bk=bk: q.activation(
                qkv[:, oc, :], pb[bk][:, 0:TST], AF.Identity,
                bias=inb[:, oc:oc + 1]), waits=[tm])
            dprev[("bank", bk)] = te
            dprev[("wi", buf)] = te
            qk_toks.append(te)
        for dd in range(8):
            mm_toks = []
            for c in range(NCH):
                wq = list(qk_toks)
                if ("qk",) in dprev:
                    wq.append(dprev[("qk",)])
                kpart = qkv[:, NCH + c, :].rearrange("p (g e) -> p g e", e=8)[
                    :, :, dd:dd + 1].squeeze(2).broadcast_to([128, NG, 8])
                tqk = s.add("vector", lambda q, c=c, kpart=kpart: q.tensor_tensor(
                    qkscr[:].rearrange("p (g e) -> p g e", e=8),
                    qkv[:, c, :].rearrange("p (g e) -> p g e", e=8),
                    kpart, ALU.mult), waits=wq)

                def mml(q, c=c, dd=dd):
                    return q.matmul(pbA[0:NH, 0:TST], hm[:, c * NH:(c + 1) * NH],
                                    qkscr[:], start=(c == 0), stop=(c == NCH - 1),
                                    skip_group_check=True)
                tl = s.add("tensor", mml, waits=[tqk] + (
                    [dprev[("lA",)]] if c == 0 and ("lA",) in dprev else []))
                dprev[("qk",)] = tl
                mm_toks.append(tl)
            tev = s.add("scalar", lambda q, dd=dd: q.copy(
                ltile[:, dd * TST:(dd + 1) * TST], pbA[0:NH, 0:TST]),
                waits=[mm_toks[-1]])
            dprev[("lA",)] = tev
            dprev[("lt", dd)] = tev
        l3d = ltile[:].rearrange("p (e t) -> p t e", e=8)
        lt_toks = [dprev[("lt", dd)] for dd in range(8)]
        t_m = s.add("vector", lambda q: q.reduce_max(
            lmax[:], l3d, axis=AX.X), waits=lt_toks)
        t_sub = s.add("vector", lambda q: q.tensor_tensor(
            l3d, l3d, lmax[:].broadcast_to([NH, TST, 8]), ALU.subtract),
            waits=[t_m])
        t_exp = s.add("scalar", lambda q: q.activation(
            ltile[:], ltile[:], AF.Exp, scale=0.125), waits=[t_sub])
        t_sm = s.add("vector", lambda q: q.reduce_sum(
            lmax[:], l3d, axis=AX.X), waits=[t_exp])
        t_rc = s.add("vector", lambda q: q.reciprocal(lmax[:], lmax[:]),
                     waits=[t_sm])
        t_nm = s.add("vector", lambda q: q.tensor_tensor(
            abf[:].rearrange("p (e t) -> p t e", e=8), l3d,
            lmax[:].broadcast_to([NH, TST, 8]), ALU.mult), waits=[t_rc])
        tz = s.add("vector", lambda q: q.memset(o_acc[:], 0.0),
                   waits=[t_ln] + ([dprev[("oa",)]] if ("oa",) in dprev else []))
        t_oa = tz
        for dd in range(8):
            for c in range(NCH):
                ww = [t_nm, tmisc]
                if ("bank", 2 + c % 2) in dprev:
                    ww.append(dprev[("bank", 2 + c % 2)])
                bk = 2 + c % 2

                def mmb(q, c=c, dd=dd, bk=bk):
                    return q.matmul(pb[bk][:, 0:TST], hmT[:, c * 128:(c + 1) * 128],
                                    abf[:, dd * TST:(dd + 1) * TST],
                                    start=True, stop=True)
                tb_ = s.add("tensor", mmb, waits=ww)
                vpart = qkv[:, 2 * NCH + c, :].rearrange("p (g e) -> p g e", e=8)[
                    :, :, dd:dd + 1].squeeze(2).broadcast_to([128, NG, 8])
                tva = s.add("vector", lambda q, c=c, bk=bk, vpart=vpart: q.tensor_tensor(
                    vascr[:].rearrange("p (g e) -> p g e", e=8), vpart,
                    pb[bk][:, 0:TST].rearrange("p (g e) -> p g e", e=8), ALU.mult),
                    waits=[tb_, t_oa])
                t_oa = s.add("vector", lambda q, c=c: q.tensor_tensor(
                    o_acc[:, c, :], o_acc[:, c, :], vascr[:], ALU.add),
                    waits=[tva])
                dprev[("bank", bk)] = t_oa
        dprev[("oa",)] = t_oa
        oc_toks = []
        for c in range(NCH):
            tc = s.add("vector", lambda q, c=c: q.tensor_copy(
                hin[:, c, :], o_acc[:, c, :]), waits=[t_oa])
            oc_toks.append(tc)
        add_toks = []
        for oc in range(NCH):
            buf = oc % 2
            wld = list(oc_toks)
            if ("wo", buf) in dprev:
                wld.append(dprev[("wo", buf)])

            def ldwo(q, sem, oc=oc, buf=buf):
                q.dma_start(out=wbuf[buf][:],
                            in_=outwT_hbm.rearrange("(c p) o -> p c o", p=128)[
                                :, :, oc * 128:(oc + 1) * 128]).then_inc(sem, 16)
            twl = s.add("gpsimd", ldwo, waits=wld, dma=True)
            bk = oc % 2
            ww = [twl]
            if ("bank", bk) in dprev:
                ww.append(dprev[("bank", bk)])

            def mmo(q, oc=oc, bk=bk, buf=buf):
                r = None
                for c in range(NCH):
                    r = q.matmul(pb[bk][:, 0:TST], wbuf[buf][:, c, :],
                                 hin[:, c, :], start=(c == 0), stop=(c == NCH - 1))
                return r
            tm = s.add("tensor", mmo, waits=ww)
            ta = s.add("vector", lambda q, oc=oc, bk=bk: q.scalar_tensor_tensor(
                x_st[:, oc, :], pb[bk][:, 0:TST], outb[:, oc:oc + 1],
                x_st[:, oc, :], ALU.add, ALU.add), waits=[tm, tld])
            dprev[("bank", bk)] = ta
            dprev[("wo", buf)] = ta
            add_toks.append(ta)
        tprev_store = store_fm(b, out_hbm[:, t0g:t0g + TST], x_st, waits=add_toks)
    s.add("sync", lambda q: q.nop(), waits=[tprev_store])
    if sa:
        return b.finish()


def np_ident(dtype):
    return np.eye(128, dtype=dtype)


def pack_pc(v):
    return np.ascontiguousarray(v.reshape(NCH, 128).T)


def l2_ffn_inmaps(x_cores, weight, bias):
    maps = []
    identf = np_ident(np.float32)
    identb = np_ident(ml_dtypes.bfloat16)
    for i in range(N_CORES):
        maps.append({
            "xres": np.ascontiguousarray(x_cores[i]),
            "w": np.ascontiguousarray(weight),
            "bias": pack_pc(bias),
            "identf": identf,
            "identb": identb,
        })
    return maps


def build_fused_a():
    b = Builder("fuseda", fused=True)
    xin = b.dram_in("xres", [D, T])
    xbuf = [b.dram_tmp("xping", [D, T]) for _ in range(2)]
    xout = b.dram_out("xout", [D, T])
    seq = [(build_l0_attn, "l0a_"), (build_l0_ffn, "l0f_"),
           (build_l1_attn, "l1a_"), (build_l1_ffn, "l1f_"),
           (build_l2_attn, "l2a_"), (build_l2_ffn, "l2f_")]
    cur = xin
    for i, (fn, pref) in enumerate(seq):
        b.pref = pref
        nxt = xout if i == len(seq) - 1 else xbuf[i % 2]
        fn(b, xin=cur, xout=nxt)
        b.s.full_barrier()
        b.reset_arena()
        cur = nxt
    return b.finish()


def build_fused_b():
    b = Builder("fusedb", fused=True)
    xin = b.dram_in("xres", [D, T])
    xmid = b.dram_tmp("xmid", [D, T])
    xout = b.dram_out("xout", [D, T])
    b.pref = "l3a_"
    build_l3_attn(b, xin=xin, xout=xmid)
    b.s.full_barrier()
    b.reset_arena()
    b.pref = "l3f_"
    build_l3_ffn(b, xin=xmid, xout=xout)
    return b.finish()


_BUILD_CACHE = {}


def _built(name, fn):
    if name not in _BUILD_CACHE:
        _BUILD_CACHE[name] = fn()
    return _BUILD_CACHE[name]


def _mk_consts():
    c = {}
    c["identf"] = np_ident(np.float32)
    c["identb"] = np_ident(ml_dtypes.bfloat16)
    for nm, n in (("sel8", 8), ("sel16", 32), ("sel32", 32)):
        m = np.zeros((n, n * 128), dtype=ml_dtypes.bfloat16)
        for e in range(n):
            m[e, e * 128:(e + 1) * 128] = 1
        c[nm] = m
    c["jramp"] = (np.arange(16)[None, :] * 128
                  + np.arange(128)[:, None]).astype(np.float32)
    c["subend"] = ((np.arange(64)[None, :] * 32 + 31)
                   * np.ones((128, 1))).astype(np.float32)
    maskq = np.zeros((128, 4, 4, 128), np.float32)
    maskt = np.zeros((128, 4, 4, 128), np.float32)
    for r in range(4):
        for a_ in range(4):
            for pp in range(128):
                if a_ > r:
                    maskq[pp, r, a_, :] = -1e9
                elif a_ == r:
                    maskq[pp, r, a_, pp + 1:] = -1e9
                if a_ < r:
                    maskt[pp, r, a_, :] = -1e9
                elif a_ == r:
                    maskt[pp, r, a_, :pp] = -1e9
    c["maskq4"] = maskq.reshape(128, 2048)
    c["maskt4"] = maskt.reshape(128, 2048)
    hm = np.zeros((128, 8 * 16), dtype=ml_dtypes.bfloat16)
    hmT = np.zeros((16, 8 * 128), dtype=ml_dtypes.bfloat16)
    for cc in range(8):
        for p in range(128):
            h = (2 * cc) + (p // 64)
            hm[p, cc * 16 + h] = 1
            hmT[h, cc * 128 + p] = 1
    c["hm"] = hm
    c["hmT"] = hmT
    return c


def _pack_conv(conv_w):
    return np.ascontiguousarray(
        conv_w.reshape(8, 128, 4).transpose(1, 0, 2).reshape(128, 32))


def _run(nc, shared, per_core):
    maps = []
    for i in range(N_CORES):
        m = dict(shared)
        m.update(per_core[i])
        maps.append(m)
    res = run_bass_kernel_spmd(nc, maps, core_ids=list(range(N_CORES)))
    return [np.asarray(res.results[i]["xout"]) for i in range(N_CORES)]


def _shared_dicts(a, C):
    sh = {}
    sh["l0a"] = {
        "in_wT": np.ascontiguousarray(a["L0_attn_in_w"].T),
        "in_b": np.ascontiguousarray(a["L0_attn_in_b"].reshape(16, 128).T),
        "conv_w": _pack_conv(a["L0_attn_conv_w"]),
        "conv_b": pack_pc(a["L0_attn_conv_b"]),
        "A_log": a["L0_attn_A_log"],
        "B_wT": np.ascontiguousarray(a["L0_attn_B_w"].T),
        "B_b": a["L0_attn_B_b"].reshape(32, 1),
        "C_wT": np.ascontiguousarray(a["L0_attn_C_w"].T),
        "C_b": a["L0_attn_C_b"].reshape(32, 1),
        "out_wT": np.ascontiguousarray(a["L0_attn_out_w"].T),
        "out_b": pack_pc(a["L0_attn_out_b"]),
        "ln_w": pack_pc(a["L0_attn_ln_w"]),
        "ln_b": pack_pc(a["L0_attn_ln_b"]),
        "identf": C["identf"], "sel32": C["sel32"],
    }
    sh["l0f"] = {
        "wgate": a["L0_ffn_w_gate"], "wup": a["L0_ffn_w_up"],
        "wdown": a["L0_ffn_w_down"],
        "identf": C["identf"], "identb": C["identb"],
    }
    sh["l1a"] = {
        "ln1_g": pack_pc(a["L1_attn_ln1_g"]), "ln1_b": pack_pc(a["L1_attn_ln1_b"]),
        "in_wT": np.ascontiguousarray(a["L1_attn_in_w"].T),
        "out_wT": np.ascontiguousarray(a["L1_attn_out_w"].T),
        "ln2_g": pack_pc(a["L1_attn_ln2_g"]), "ln2_b": pack_pc(a["L1_attn_ln2_b"]),
        "slopes": a["L1_attn_slopes"].reshape(1, 16),
        "smear": a["L1_attn_smear"].reshape(1, 16),
        "jramp": C["jramp"], "subend": C["subend"],
        "maskq4": C["maskq4"], "maskt4": C["maskt4"],
        "identf": C["identf"], "identb": C["identb"],
    }
    sh["l1f"] = {
        "gate_w": a["L1_ffn_gate_w"], "base_wgu": a["L1_ffn_base_wgu"],
        "base_wd": a["L1_ffn_base_wd"], "exp_wgu": a["L1_ffn_exp_wgu"],
        "exp_wd": a["L1_ffn_exp_wd"],
        "identf": C["identf"], "sel8": C["sel8"],
    }
    sh["l2a"] = {
        "ln_g": pack_pc(a["L2_attn_ln_g"]), "ln_b": pack_pc(a["L2_attn_ln_b"]),
        "w_in": a["L2_attn_w_in"], "b_in": pack_pc(a["L2_attn_b_in"]),
        "conv_w": _pack_conv(a["L2_attn_conv_w"]),
        "conv_b": pack_pc(a["L2_attn_conv_b"]),
        "A_log": a["L2_attn_A_log"], "D_ssm": pack_pc(a["L2_attn_D_ssm"]),
        "w_dt": a["L2_attn_w_dt"], "b_dt": pack_pc(a["L2_attn_b_dt"]),
        "w_bc": a["L2_attn_w_bc"],
        "w_out": a["L2_attn_w_out"], "b_out": pack_pc(a["L2_attn_b_out"]),
        "sel16": C["sel16"],
    }
    sh["l2f"] = {
        "w": a["L2_ffn_weight"], "bias": pack_pc(a["L2_ffn_bias"]),
        "identf": C["identf"], "identb": C["identb"],
    }
    sh["l3a"] = {
        "in_wT": np.ascontiguousarray(a["L3_attn_in_proj_weight"].T),
        "in_b": np.ascontiguousarray(a["L3_attn_in_proj_bias"].reshape(24, 128).T),
        "out_wT": np.ascontiguousarray(a["L3_attn_out_proj_weight"].T),
        "out_b": pack_pc(a["L3_attn_out_proj_bias"]),
        "hm": C["hm"], "hmT": C["hmT"],
    }
    sh["l3f"] = {
        "router_wT": np.ascontiguousarray(a["L3_ffn_router_w"].T),
        "rbias_rep": np.ascontiguousarray(np.tile(a["L3_ffn_router_bias"], (128, 16))),
        "sw1": a["L3_ffn_sw1"], "sw3": a["L3_ffn_sw3"], "sw2": a["L3_ffn_sw2"],
        "ew1": a["L3_ffn_ew1"], "ew2": a["L3_ffn_ew2"],
        "identf": C["identf"], "sel8": C["sel8"],
    }
    return sh


def kernel(**inp):
    C = _mk_consts()
    a = {k: np.asarray(v, np.float32) for k, v in inp.items()}
    sh = _shared_dicts(a, C)
    x = a["L0_attn_x"]
    xs = [np.ascontiguousarray(x[i].T) for i in range(N_CORES)]

    nc = _built("fa", build_fused_a)
    shared_a = {}
    for blk in ("l0a", "l0f", "l1a", "l1f", "l2a", "l2f"):
        for k, v in sh[blk].items():
            shared_a[blk + "_" + k] = v
    xs = _run(nc, shared_a, [{"xres": v} for v in xs])

    X = np.stack(xs)
    xs = [np.ascontiguousarray(
        X[:, :, m * 256:(m + 1) * 256].transpose(1, 2, 0).reshape(D, T))
        for m in range(N_CORES)]

    nc = _built("fb", build_fused_b)
    shared_b = {}
    for blk in ("l3a", "l3f"):
        for k, v in sh[blk].items():
            shared_b[blk + "_" + k] = v
    xs = _run(nc, shared_b, [{"xres": v} for v in xs])

    Y = np.stack(xs).reshape(N_CORES, D, 256, 8)
    out = Y.transpose(3, 0, 2, 1).reshape(N_CORES, T, D)
    return np.ascontiguousarray(out)


# version 06
# speedup vs baseline: 1.0021x; 1.0021x over previous
import math
import numpy as np
import ml_dtypes
from contextlib import ExitStack

import concourse.bass as bass
import concourse.mybir as mybir
from concourse.bass_utils import run_bass_kernel_spmd

F32 = mybir.dt.float32
BF16 = mybir.dt.bfloat16
AF = mybir.ActivationFunctionType
ALU = mybir.AluOpType
AX = mybir.AxisListType

D = 1024
T = 2048
NCH = D // 128
NTT = T // 128
N_CORES = 8

ENGINES = ("sync", "scalar", "vector", "gpsimd", "tensor")


class Sched:

    def __init__(self, nc):
        self.nc = nc
        self.items = {e: [] for e in ENGINES}
        self.cnt = {}
        for e in ENGINES:
            self.cnt[e] = 0
            self.cnt[e + "_dma"] = 0

    def add(self, eng, emit, waits=(), dma=False, n_dma=1):
        virt = eng + "_dma" if dma else eng
        inc = 16 * n_dma if dma else 1
        for (we, wc) in waits:
            assert wc <= self.cnt[we], (we, wc, self.cnt[we])
        self.items[eng].append((emit, tuple(waits), inc, dma, virt))
        self.cnt[virt] += inc
        return (virt, self.cnt[virt])

    def add0(self, eng, emit, waits=()):
        for (we, wc) in waits:
            assert wc <= self.cnt[we], (we, wc, self.cnt[we])
        self.items[eng].append((emit, tuple(waits), 0, False, eng))

    def barrier(self):
        return tuple((v, c) for v, c in self.cnt.items() if c > 0)

    def full_barrier(self):
        snap = dict(self.cnt)
        toks = {}
        for e in ENGINES:
            waits = [(v, c) for v, c in snap.items() if c > 0]
            toks[e] = self.add(e, lambda q: q.nop(), waits=waits)
        return toks

    def realize(self, es):
        nc = self.nc
        used = set()
        for e in ENGINES:
            for emit, waits, inc, dma, virt in self.items[e]:
                used.add(virt)
                for (we, wc) in waits:
                    used.add(we)
        sems = {v: es.enter_context(nc.semaphore("S_" + v)) for v in used}
        blk = es.enter_context(nc.Block())

        def mk(eng):
            def body(q):
                for emit, waits, inc, dma, virt in self.items[eng]:
                    for (we, wc) in waits:
                        q.wait_ge(sems[we], wc)
                    if dma:
                        emit(q, sems[virt])
                    else:
                        last = emit(q)
                        if inc:
                            assert last is not None, f"compute step on {eng} returned None"
                            last.then_inc(sems[virt], inc)
            return body

        for e in ENGINES:
            if self.items[e]:
                getattr(blk, e)(mk(e))
        self.sems = sems



def ceil_div(a, b):
    return (a + b - 1) // b


class _CarveHandle:

    def __init__(self, ap):
        self._ap = ap

    def __getitem__(self, key):
        return self._ap[key]

    def bitcast(self, dt_):
        return self._ap.bitcast(dt_)

    def rearrange(self, *a, **kw):
        return self._ap.rearrange(*a, **kw)


class Builder:

    ARENA_WORDS = 53000
    GLOB_WORDS = 1024

    def __init__(self, name, fused=False):
        self.nc = bass.Bass("TRN2", target_bir_lowering=False, debug=False)
        self.s = Sched(self.nc)
        self.es = ExitStack()
        self._uniq = 0
        self.fused = fused
        if fused:
            self.arena = self.es.enter_context(self.nc.sbuf_tensor(
                "arena", [128, self.ARENA_WORDS], F32))
            self.arena_off = self.GLOB_WORDS
            self.glob_off = 0
            self.pbanks = [self.es.enter_context(self.nc.psum_tensor(
                f"pbank{i}", [128, 512], F32)) for i in range(8)]
            self.pbank_i = 0

    def reset_arena(self):
        self.arena_off = self.GLOB_WORDS
        self.pbank_i = 0

    def _carve(self, shape, dtype, glob=False):
        assert shape[0] <= 128
        per = 1
        for d in shape[1:]:
            per *= d
        words = ceil_div(per * mybir.dt.size(dtype), 4)
        if glob:
            off = self.glob_off
            self.glob_off += words
            assert self.glob_off <= self.GLOB_WORDS
        else:
            off = self.arena_off
            self.arena_off += words
            assert self.arena_off <= self.ARENA_WORDS, (
                f"arena overflow: {self.arena_off} words")
        view = self.arena[0:shape[0], off:off + words]
        if dtype != F32:
            view = view.bitcast(dtype)[0:shape[0], 0:per]
        else:
            view = view[0:shape[0], 0:per]
        if len(shape) == 2:
            return view if per == shape[1] else view[:, 0:shape[1]]
        if len(shape) == 3:
            return view.rearrange("p (a b) -> p a b", b=shape[2])
        raise AssertionError(shape)

    def uname(self, base):
        self._uniq += 1
        return f"{base}_{self._uniq}"

    def cst(self, val):
        if not hasattr(self, "_csts"):
            self._csts = {}
        if val not in self._csts:
            if self.fused:
                t = _CarveHandle(self._carve([128, 1], F32, glob=True))
            else:
                t = self.sbuf("cst", [128, 1], F32)
            tok = self.s.add("gpsimd", lambda q, t=t, val=val: q.memset(t[:], val))
            self._csts[val] = (t[:, 0:1], tok)
        return self._csts[val]

    def sbuf(self, base, shape, dtype):
        if self.fused:
            return _CarveHandle(self._carve(shape, dtype))
        return self.es.enter_context(self.nc.sbuf_tensor(self.uname(base), shape, dtype))

    pref = ""

    def dram_in(self, name, shape, dtype=F32):
        return self.nc.dram_tensor(self.pref + name, shape, dtype,
                                   kind="ExternalInput").ap()

    def dram_out(self, name, shape, dtype=F32):
        return self.nc.dram_tensor(self.pref + name, shape, dtype,
                                   kind="ExternalOutput").ap()

    def dram_tmp(self, name, shape, dtype=F32):
        return self.nc.dram_tensor(self.uname(name), shape, dtype, kind="Internal").ap()

    def alloc_psum_banks(self, n=8, dtype=F32, cols=512):
        if self.fused:
            out = []
            for i in range(n):
                bk = self.pbanks[self.pbank_i]
                self.pbank_i += 1
                v = bk[:] if dtype == F32 else bk[:].bitcast(dtype)
                out.append(_CarveHandle(v[:, 0:cols]))
            return out
        banks = []
        for i in range(n):
            banks.append(self.es.enter_context(
                self.nc.psum_tensor(self.uname("pb"), [128, cols], dtype)))
        return banks

    def finish(self):
        self.s.realize(self.es)
        self.es.close()
        return self.nc



def load_fm(b, dst, src_hbm, waits=(), queue="sync", cast=False):
    Dh, Tp = src_hbm.shape

    def emit(q, sem):
        q.dma_start(out=dst[:],
                    in_=src_hbm.rearrange("(c p) t -> p c t", p=128)).then_inc(sem, 16)
    return b.s.add(queue, emit, waits=waits, dma=True)


def store_fm(b, dst_hbm, src, waits=(), queue="sync"):
    Dh, Tp = dst_hbm.shape

    def emit(q, sem):
        q.dma_start(out=dst_hbm.rearrange("(c p) t -> p c t", p=128),
                    in_=src[:]).then_inc(sem, 16)
    return b.s.add(queue, emit, waits=waits, dma=True)


MAGIC6 = 1.5 * 2.0 ** 23 / 8.0
MAGIC4 = 1.5 * 2.0 ** 23 / 2.0


def emit_quant_tm(b, x_tm, out_bf, scr, n_cols, block, fmax, waits, engines=("vector",)):
    nb = n_cols // block
    v = engines[0]
    s = b.s
    eps = 1e-12 if fmax == 6.0 else 1e-30
    base_magic = MAGIC4 if fmax == 6.0 else MAGIC6
    if getattr(b, "_last_quant_tsg", None) is not None:
        waits = list(waits) + [b._last_quant_tsg]
    t1 = s.add(v, lambda q: q.reduce_max(
        scr["bm"][:, 0:nb], x_tm.rearrange("p (n k) -> p n k", k=block),
        axis=AX.X, apply_absolute_value=True), waits=waits)
    t2 = s.add(v, lambda q: q.tensor_scalar(
        scr["bm"][:, 0:nb], scr["bm"][:, 0:nb], 1.0 / fmax, eps, ALU.mult, ALU.max),
        waits=[t1])
    t3 = s.add(v, lambda q: q.reciprocal(scr["inv"][:, 0:nb], scr["bm"][:, 0:nb]),
               waits=[t2])
    inv_bc = scr["inv"][:, 0:nb].broadcast_to([128, nb, block])
    t4 = s.add(v, lambda q: q.tensor_tensor(
        scr["a"][:, 0:n_cols].rearrange("p (n k) -> p n k", k=block),
        x_tm.rearrange("p (n k) -> p n k", k=block), inv_bc, ALU.mult),
        waits=[t3])
    t5 = s.add(v, lambda q: q.tensor_tensor(
        scr["sq"][:, 0:n_cols], scr["a"][:, 0:n_cols], scr["a"][:, 0:n_cols], ALU.mult),
        waits=[t4])
    t6 = s.add(v, lambda q: q.tensor_scalar(
        scr["g"][:, 0:n_cols], scr["sq"][:, 0:n_cols], 4.0, base_magic, ALU.is_gt, ALU.mult),
        waits=[t5])
    t7 = s.add(v, lambda q: q.tensor_scalar(
        scr["c"][:, 0:n_cols], scr["sq"][:, 0:n_cols], 16.0, 2.0 * base_magic, ALU.is_gt, ALU.mult),
        waits=[t5])
    t8 = s.add(v, lambda q: q.tensor_tensor(
        scr["c"][:, 0:n_cols], scr["c"][:, 0:n_cols], scr["g"][:, 0:n_cols], ALU.add),
        waits=[t6, t7])
    t9 = s.add(v, lambda q: q.tensor_scalar_add(
        scr["c"][:, 0:n_cols], scr["c"][:, 0:n_cols], base_magic), waits=[t8])
    tsg = s.add("scalar", lambda q: q.activation(
        scr["sg"][:, 0:n_cols], scr["a"][:, 0:n_cols], AF.Sign), waits=[t4])
    b._last_quant_tsg = tsg
    t10 = s.add(v, lambda q: q.tensor_tensor(
        scr["c"][:, 0:n_cols], scr["c"][:, 0:n_cols], scr["sg"][:, 0:n_cols], ALU.mult),
        waits=[t9, tsg])
    t11 = s.add(v, lambda q: q.tensor_tensor(
        scr["a"][:, 0:n_cols], scr["a"][:, 0:n_cols], scr["c"][:, 0:n_cols], ALU.add),
        waits=[t10])
    t12 = s.add(v, lambda q: q.tensor_tensor(
        scr["a"][:, 0:n_cols], scr["a"][:, 0:n_cols], scr["c"][:, 0:n_cols], ALU.subtract),
        waits=[t11])
    sc_bc = scr["bm"][:, 0:nb].broadcast_to([128, nb, block])
    t13 = s.add(v, lambda q: q.tensor_tensor(
        out_bf.rearrange("p (n k) -> p n k", k=block),
        scr["a"][:, 0:n_cols].rearrange("p (n k) -> p n k", k=block), sc_bc, ALU.mult),
        waits=[t12])
    return t13


def np_quant_ref(x, block, fmax):
    shape = x.shape
    xb = x.reshape(-1, block)
    scale = np.maximum(np.abs(xb).max(-1, keepdims=True) / fmax,
                       1e-12 if fmax == 6.0 else 1e-30)
    xn = xb / scale
    if fmax == 6.0:
        GRID = np.array([0.0, 0.5, 1.0, 1.5, 2.0, 3.0, 4.0, 6.0], np.float32)
        MID = np.array([0.25, 0.75, 1.25, 1.75, 2.5, 3.5, 5.0], np.float32)
        idx = np.searchsorted(MID, np.abs(xn))
        q = GRID[idx] * np.sign(xn)
    else:
        vals = set()
        for bits in range(64):
            s_ = bits >> 5 & 1
            e = bits >> 3 & 3
            m = bits & 7
            vv = m / 8.0 if e == 0 else (1.0 + m / 8.0) * 2.0 ** (e - 1)
            vals.add(-vv if s_ else vv)
        grid = np.array(sorted(vals), np.float32)
        idx = np.clip(np.searchsorted(grid, xn), 0, grid.shape[0] - 1)
        idx_lo = np.maximum(idx - 1, 0)
        v_hi = grid[idx]
        v_lo = grid[idx_lo]
        q = np.where(np.abs(xn - v_lo) <= np.abs(xn - v_hi), v_lo, v_hi)
    return (q * scale).reshape(shape)


def emit_transpose_fm_tm(b, src, dst, ident, pbanks, waits, n_pch, n_tch,
                         src_dtype_bf=False, tok=None):
    s = b.s
    toks = []
    prev_evac = {}
    for j in range(n_tch):
        for c in range(n_pch):
            bank = pbanks[(j * n_pch + c) % len(pbanks)]
            w = list(waits)
            key = (j * n_pch + c) % len(pbanks)
            if key in prev_evac:
                w.append(prev_evac[key])
            tt = s.add("tensor", lambda q, c=c, j=j, bank=bank: q.transpose(
                bank[:, 0:128], src[:, c, j * 128:(j + 1) * 128], ident[:]),
                waits=w)
            te = s.add("scalar", lambda q, c=c, j=j, bank=bank: q.copy(
                dst[:, j, c * 128:(c + 1) * 128], bank[:, 0:128]), waits=[tt])
            prev_evac[key] = te
            toks.append(te)
    return toks[-1], toks


def emit_linear_fm(b, out, x_bf, wT, pbanks, waits, n_in_ch, n_out_ch, Tn,
                   bias=None, act=None, out_dtype_bf=False, evac_eng="scalar",
                   accum_into=None, out_off=0, prior_evacs=None, extra_x_off=0):
    s = b.s
    ntile = ceil_div(Tn, 512)
    evacs = []
    prev = dict(prior_evacs or {})
    last = None
    for oc in range(n_out_ch):
        for ti in range(ntile):
            t0 = ti * 512
            tw = min(512, Tn - t0)
            bk = (oc * ntile + ti) % len(pbanks)
            bank = pbanks[bk]
            w = list(waits)
            if bk in prev:
                w.append(prev[bk])

            def mm(q, oc=oc, t0=t0, tw=tw, bank=bank):
                r = None
                for ic in range(n_in_ch):
                    r = q.matmul(bank[:, 0:tw],
                                 wT[:, ic, oc * 128:(oc + 1) * 128],
                                 x_bf[:, ic, extra_x_off + t0:extra_x_off + t0 + tw],
                                 start=(ic == 0), stop=(ic == n_in_ch - 1))
                return r
            tm = s.add("tensor", mm, waits=w)

            def evac(q, oc=oc, t0=t0, tw=tw, bank=bank):
                dst = out[:, out_off + oc, t0:t0 + tw]
                if act is not None or bias is not None:
                    kw = {}
                    if bias is not None:
                        kw["bias"] = bias[:, oc:oc + 1]
                    return q.activation(dst, bank[:, 0:tw], act or AF.Identity, **kw)
                else:
                    return q.copy(dst, bank[:, 0:tw])
            te = s.add(evac_eng, evac, waits=[tm])
            prev[bk] = te
            evacs.append(te)
            last = te
    return last, prev


def emit_ln_tm_stats(b, x_tm, stats, n_tt, waits, eng_red="vector"):
    s = b.s
    n = n_tt
    tk = []
    for j in range(n_tt):
        t1 = s.add(eng_red, lambda q, j=j: q.reduce_sum(
            stats[:, j:j + 1], x_tm[:, j, :], axis=AX.X), waits=waits)
        t2 = s.add("scalar", lambda q, j=j: q.activation(
            b.lnscr[:, j * D:(j + 1) * D], x_tm[:, j, :], AF.Square,
            accum_out=stats[:, n + j:n + j + 1]), waits=waits)
        tk += [t1, t2]
    t3 = s.add("vector", lambda q: q.tensor_scalar_mul(
        stats[:, 2 * n:3 * n], stats[:, 0:n], 1.0 / D), waits=tk)
    t4 = s.add("vector", lambda q: q.tensor_tensor(
        stats[:, 3 * n:4 * n], stats[:, 2 * n:3 * n], stats[:, 2 * n:3 * n], ALU.mult),
        waits=[t3])
    t5 = s.add("vector", lambda q: q.scalar_tensor_tensor(
        stats[:, 3 * n:4 * n], stats[:, n:2 * n], 1.0 / D, stats[:, 3 * n:4 * n],
        ALU.mult, ALU.subtract), waits=[t4] + tk)
    t6 = s.add("scalar", lambda q: q.activation(
        stats[:, 3 * n:4 * n], stats[:, 3 * n:4 * n], AF.Ln, bias=1e-5), waits=[t5])
    t7 = s.add("scalar", lambda q: q.activation(
        stats[:, 3 * n:4 * n], stats[:, 3 * n:4 * n], AF.Exp, scale=-0.5), waits=[t6])
    return t7



def emit_ln_quant_tm_pipeline(b, x_fm, xd_fm, identf, identb, pbt, pbb, scr,
                              stats, lnscr, waits, fmax, block, n_tt=NTT,
                              tile_off=0):
    s = b.s
    last_q = None
    prev = {}
    last_tok = None
    for j in range(n_tt):
        x_tm = scr["xtm"]
        xd_tm = scr["xdtm"]
        w0 = list(waits) + ([last_q] if last_q else [])
        evs = []
        for c in range(NCH):
            bk = c % len(pbt)
            ww = list(w0)
            if bk in prev:
                ww.append(prev[bk])
            tt = s.add("tensor", lambda q, c=c, j=j, bk=bk: q.transpose(
                pbt[bk][:, 0:128], x_fm[:, c, j * 128:(j + 1) * 128], identf[:]),
                waits=ww)
            te = s.add("scalar", lambda q, c=c, bk=bk: q.copy(
                x_tm[:, c * 128:(c + 1) * 128], pbt[bk][:, 0:128]), waits=[tt])
            prev[bk] = te
            evs.append(te)
        t1 = s.add("vector", lambda q: q.reduce_sum(
            stats[:, 0:1], x_tm[:], axis=AX.X), waits=evs)
        t2 = s.add("scalar", lambda q: q.activation(
            lnscr[:], x_tm[:], AF.Square, accum_out=stats[:, 1:2]), waits=evs)
        t3 = s.add("vector", lambda q: q.tensor_scalar_mul(
            stats[:, 2:3], stats[:, 0:1], 1.0 / D), waits=[t1])
        t4 = s.add("vector", lambda q: q.tensor_tensor(
            stats[:, 3:4], stats[:, 2:3], stats[:, 2:3], ALU.mult), waits=[t3])
        t5 = s.add("vector", lambda q: q.scalar_tensor_tensor(
            stats[:, 3:4], stats[:, 1:2], 1.0 / D, stats[:, 3:4],
            ALU.mult, ALU.subtract), waits=[t2, t4])
        eps_ap, eps_tok = b.cst(1e-5)
        t6 = s.add("scalar", lambda q: q.activation(
            stats[:, 3:4], stats[:, 3:4], AF.Ln, bias=eps_ap), waits=[t5, eps_tok])
        t7 = s.add("scalar", lambda q: q.activation(
            stats[:, 3:4], stats[:, 3:4], AF.Exp, scale=-0.5), waits=[t6])
        t8 = s.add("vector", lambda q: q.tensor_scalar(
            x_tm[:], x_tm[:], stats[:, 2:3], stats[:, 3:4],
            ALU.subtract, ALU.mult), waits=[t7])
        tq = emit_quant_tm(b, x_tm[:], xd_tm[:], scr, D, block, fmax, [t8])
        for c in range(NCH):
            bk = c % len(pbb)
            key = ("b", bk)
            ww = [tq]
            if key in prev:
                ww.append(prev[key])
            tt = s.add("tensor", lambda q, c=c, bk=bk: q.transpose(
                pbb[bk][:, 0:128], xd_tm[:, c * 128:(c + 1) * 128], identb[:]),
                waits=ww)
            te = s.add("scalar", lambda q, c=c, j=j, bk=bk: q.copy(
                xd_fm[:, c, (tile_off + j) * 128:(tile_off + j + 1) * 128],
                pbb[bk][:, 0:128]), waits=[tt])
            prev[key] = te
            last_tok = te
        last_q = last_tok
    return last_tok


def build_l2_ffn(b=None, xin=None, xout=None):
    sa = b is None
    if sa:
        b = Builder("l2ffn")
    x_hbm = xin if xin is not None else b.dram_in("xres", [D, T])
    w_hbm = b.dram_in("w", [D, D])
    bias_hbm = b.dram_in("bias", [128, NCH])
    ident_hbm = b.dram_in("identf", [128, 128])
    identb_hbm = b.dram_in("identb", [128, 128], BF16)
    out_hbm = xout if xout is not None else b.dram_out("xout", [D, T])

    x_fm = b.sbuf("x", [128, NCH, T], F32)
    xd_fm = b.sbuf("xdfm", [128, NCH, T], BF16)
    w_sb = b.sbuf("w", [128, NCH, D], F32)
    wq_sb = b.sbuf("wq", [128, NCH, D], BF16)
    wT = b.sbuf("wT", [128, NCH, D], BF16)
    bias_sb = b.sbuf("bias", [128, NCH], F32)
    identf = b.sbuf("identf", [128, 128], F32)
    identb = b.sbuf("identb", [128, 128], BF16)
    stats = b.sbuf("stats", [128, 4], F32)
    lnscr = b.sbuf("lnscr", [128, D], F32)

    scr = {k: b.sbuf("q" + k, [128, D], F32) for k in ("a", "sq", "g", "c", "sg")}
    scr["xtm"] = b.sbuf("qxtm", [128, D], F32)
    scr["xdtm"] = b.sbuf("qxdtm", [128, D], BF16)
    scr["bm"] = b.sbuf("qbm", [128, D // 32], F32)
    scr["inv"] = b.sbuf("qinv", [128, D // 32], F32)

    pb = b.alloc_psum_banks(4, F32, 512)
    pbt = b.alloc_psum_banks(2, F32, 512)
    pbb = b.alloc_psum_banks(2, BF16, 1024)

    s = b.s
    tload_x = load_fm(b, x_fm, x_hbm)
    tload_w = load_fm(b, w_sb, w_hbm, queue="gpsimd")

    def ld_misc(q, sem):
        q.dma_start(out=bias_sb[:], in_=bias_hbm).then_inc(sem, 16)
        q.dma_start(out=identf[:], in_=ident_hbm).then_inc(sem, 16)
        q.dma_start(out=identb[:], in_=identb_hbm).then_inc(sem, 16)
    tmisc = s.add("sync", ld_misc, dma=True, n_dma=3)

    t_xd = emit_ln_quant_tm_pipeline(b, x_fm, xd_fm, identf, identb, pbt, pbb,
                                     scr, stats, lnscr, [tload_x, tmisc], 7.5, 32)

    wq_toks = []
    for c in range(NCH):
        tq = emit_quant_tm(b, w_sb[:, c, :], wq_sb[:, c, :], scr, D, 32, 7.5,
                           [tload_w, t_xd] + ([wq_toks[-1]] if wq_toks else []))
        wq_toks.append(tq)
    t_wT, _ = emit_transpose_fm_tm(b, wq_sb, wT, identb, pbb, [wq_toks[-1]],
                                   NCH, NCH, tok=None)

    add_toks = []
    prev = {}
    for oc in range(NCH):
        for ti in range(T // 512):
            t0 = ti * 512
            bk = (oc * (T // 512) + ti) % len(pb)
            w = [t_wT, t_xd]
            if bk in prev:
                w.append(prev[bk])

            def mm(q, oc=oc, t0=t0, bk=bk):
                r = None
                for ic in range(NCH):
                    r = q.matmul(pb[bk][:, 0:512],
                                 wT[:, ic, oc * 128:(oc + 1) * 128],
                                 xd_fm[:, ic, t0:t0 + 512],
                                 start=(ic == 0), stop=(ic == NCH - 1))
                return r
            tm = s.add("tensor", mm, waits=w)

            def evac(q, oc=oc, t0=t0, bk=bk):
                return q.scalar_tensor_tensor(
                    x_fm[:, oc, t0:t0 + 512], pb[bk][:, 0:512],
                    bias_sb[:, oc:oc + 1], x_fm[:, oc, t0:t0 + 512],
                    ALU.add, ALU.add)
            te = s.add("vector", evac, waits=[tm])
            prev[bk] = te
            add_toks.append(te)

    t_store = store_fm(b, out_hbm, x_fm, waits=add_toks)
    s.add("sync", lambda q: q.nop(), waits=[t_store])
    if sa:
        return b.finish()


I_FF = 2816
NIC_FF = I_FF // 128


def emit_wquant_stream(b, w_hbm, wT_dram, n_row_ch, in_dim, scr, stage2, wrow2,
                       identb, pbb, waits, block, fmax):
    s = b.s
    n_ic = in_dim // 128
    wq = b.__dict__.setdefault("_wq", {})
    tok_dma = []
    for c in range(n_row_ch):
        i = wq.get("i", 0)
        wq["i"] = i + 1
        p2 = i % 2
        wrow = wrow2[p2]
        stage = stage2[p2]
        w = list(waits)
        if ("st", p2) in wq:
            w.append(wq[("st", p2)])

        def ld(q, sem, c=c, wrow=wrow):
            q.dma_start(out=wrow[:, 0:in_dim],
                        in_=w_hbm[c * 128:(c + 1) * 128, :]).then_inc(sem, 16)
        tl = s.add("gpsimd", ld, waits=w, dma=True)
        qw = [tl]
        if ("tt", p2) in wq:
            qw.append(wq[("tt", p2)])
        tq = emit_quant_tm(b, wrow[:, 0:in_dim], stage[:, 0:in_dim], scr,
                           in_dim, block, fmax, qw)
        evs = []
        for ic in range(n_ic):
            bk = ic % len(pbb)
            ww = [tq]
            if ("pbb", bk) in wq:
                ww.append(wq[("pbb", bk)])
            tt = s.add("tensor", lambda q, ic=ic, bk=bk, stage=stage: q.transpose(
                pbb[bk][:, 0:128], stage[:, ic * 128:(ic + 1) * 128], identb[:]),
                waits=ww)
            te = s.add("scalar", lambda q, ic=ic, bk=bk, wrow=wrow: q.copy(
                wrow.bitcast(BF16)[:, ic * 128:(ic + 1) * 128], pbb[bk][:, 0:128]),
                waits=[tt])
            wq[("pbb", bk)] = te
            evs.append(te)
        wq[("tt", p2)] = tt

        def st(q, sem, c=c, wrow=wrow):
            q.dma_start(out=wT_dram.rearrange("(ic p) o -> p ic o", p=128)[:, :, c * 128:(c + 1) * 128],
                        in_=wrow.bitcast(BF16)[:, 0:n_ic * 128].rearrange(
                            "p (ic o) -> p ic o", o=128)).then_inc(sem, 16)
        tprev = s.add("sync", st, waits=evs, dma=True)
        wq[("st", p2)] = tprev
        tok_dma.append(tprev)
    return tok_dma[-1], tok_dma


def build_l0_ffn(b=None, xin=None, xout=None):
    sa = b is None
    if sa:
        b = Builder("l0ffn")
    x_hbm = xin if xin is not None else b.dram_in("xres", [D, T])
    wg_hbm = b.dram_in("wgate", [I_FF, D])
    wu_hbm = b.dram_in("wup", [I_FF, D])
    wd_hbm = b.dram_in("wdown", [D, I_FF])
    ident_hbm = b.dram_in("identf", [128, 128])
    identb_hbm = b.dram_in("identb", [128, 128], BF16)
    out_hbm = xout if xout is not None else b.dram_out("xout", [D, T])

    wgT_dram = b.dram_tmp("wgT", [D, I_FF], BF16)
    wuT_dram = b.dram_tmp("wuT", [D, I_FF], BF16)
    wdT_dram = b.dram_tmp("wdT", [I_FF, D], BF16)

    xd_fm = b.sbuf("xdfm", [128, NCH, T], BF16)
    x_st = b.sbuf("xst", [128, NCH, 1024], F32)
    h_sb = b.sbuf("h", [128, NIC_FF, 1024], BF16)
    htm0 = b.sbuf("htm", [128, I_FF], BF16)
    gscr = b.sbuf("gscr", [128, 1024], F32)
    identf = b.sbuf("identf", [128, 128], F32)
    identb = b.sbuf("identb", [128, 128], BF16)
    stats = b.sbuf("stats", [128, 4], F32)
    lnscr = b.sbuf("lnscr", [128, D], F32)
    wrow0 = b.sbuf("wrow", [128, I_FF], F32)
    wstage0 = b.sbuf("wstage", [128, I_FF], BF16)
    h_flat = h_sb[:].rearrange("p a t -> p (a t)")
    wrow = [wrow0, h_flat[:, 0:2 * I_FF].bitcast(F32)]
    wstage = [wstage0, h_flat[:, 2 * I_FF:3 * I_FF]]
    htm = [htm0, wrow0.bitcast(BF16)[:, 0:I_FF]]
    wgbuf = [b.sbuf("wgb", [128, NCH, 128], BF16) for _ in range(2)]
    wubuf = [b.sbuf("wub", [128, NCH, 128], BF16) for _ in range(2)]
    wdbuf = [b.sbuf("wdb", [128, NIC_FF, 128], BF16) for _ in range(2)]

    scr = {"a": b.sbuf("qa", [128, I_FF], F32),
           "sq": b.sbuf("qsq", [128, I_FF], BF16),
           "g": b.sbuf("qg", [128, I_FF], BF16),
           "c": b.sbuf("qc", [128, I_FF], BF16),
           "sg": b.sbuf("qsg", [128, I_FF], BF16),
           "bm": b.sbuf("qbm", [128, I_FF // 16], F32),
           "inv": b.sbuf("qinv", [128, I_FF // 16], F32),
           "xtm": b.sbuf("qxtm", [128, D], F32),
           "xdtm": b.sbuf("qxdtm", [128, D], BF16)}

    pb = b.alloc_psum_banks(4, F32, 512)
    pbt = b.alloc_psum_banks(2, F32, 512)
    pbb = b.alloc_psum_banks(2, BF16, 1024)

    s = b.s

    def ld_misc(q, sem):
        q.dma_start(out=identf[:], in_=ident_hbm).then_inc(sem, 16)
        q.dma_start(out=identb[:], in_=identb_hbm).then_inc(sem, 16)
    tmisc = s.add("sync", ld_misc, dma=True, n_dma=2)

    twg, _ = emit_wquant_stream(b, wg_hbm, wgT_dram, NIC_FF, D, scr, wstage,
                                wrow, identb, pbb, [tmisc], 16, 6.0)
    twu, _ = emit_wquant_stream(b, wu_hbm, wuT_dram, NIC_FF, D, scr, wstage,
                                wrow, identb, pbb, [twg], 16, 6.0)
    twd, _ = emit_wquant_stream(b, wd_hbm, wdT_dram, NCH, I_FF, scr, wstage,
                                wrow, identb, pbb, [twu], 16, 6.0)

    t_xq = None
    xq_done = []
    for st_i in range(2):
        tld = load_fm(b, x_st, x_hbm[:, st_i * 1024:(st_i + 1) * 1024],
                      waits=([t_xq] if t_xq else [tmisc]))
        t_xq = emit_ln_quant_tm_pipeline(
            b, x_st, xd_fm, identf, identb, pbt, pbb, scr, stats, lnscr,
            [tld, twd], 6.0, 16, n_tt=8, tile_off=st_i * 8)
        xq_done.append(t_xq)

    tprev_store = None
    wstream_prev = {}
    for st_i in range(2):
        t0 = st_i * 1024
        tld = load_fm(b, x_st, x_hbm[:, t0:t0 + 1024],
                      waits=[xq_done[-1]] + ([tprev_store] if tprev_store else []))
        h_toks = []
        mm_done = {}
        for oc in range(NIC_FF):
            buf = oc % 2
            wld_w = [twd]
            key = ("g", buf)
            if key in wstream_prev:
                wld_w.append(mm_done.get(key, wstream_prev[key]))

            def ldw(q, sem, oc=oc, buf=buf):
                q.dma_start(out=wgbuf[buf][:],
                            in_=wgT_dram.rearrange("(ic p) o -> p ic o", p=128)[
                                :, :, oc * 128:(oc + 1) * 128]).then_inc(sem, 16)
                q.dma_start(out=wubuf[buf][:],
                            in_=wuT_dram.rearrange("(ic p) o -> p ic o", p=128)[
                                :, :, oc * 128:(oc + 1) * 128]).then_inc(sem, 16)
            tw = s.add("gpsimd", ldw, waits=wld_w, dma=True, n_dma=2)
            wstream_prev[key] = tw
            for half in range(2):
                hh = half * 512
                bkA = (2 * oc + half) % 2
                bkB = 2 + (2 * oc + half) % 2
                wA = [tw, xq_done[-1]]
                if ("A", bkA) in mm_done:
                    wA.append(mm_done[("A", bkA)])

                def mmg(q, oc=oc, hh=hh, bkA=bkA, t0=t0):
                    r = None
                    for ic in range(NCH):
                        r = q.matmul(pb[bkA][:, 0:512], wgbuf[oc % 2][:, ic, :],
                                     xd_fm[:, ic, t0 + hh:t0 + hh + 512],
                                     start=(ic == 0), stop=(ic == NCH - 1))
                    return r
                tg = s.add("tensor", mmg, waits=wA)
                wB = [tw]
                if ("B", bkB) in mm_done:
                    wB.append(mm_done[("B", bkB)])

                def mmu(q, oc=oc, hh=hh, bkB=bkB, t0=t0):
                    r = None
                    for ic in range(NCH):
                        r = q.matmul(pb[bkB][:, 0:512], wubuf[oc % 2][:, ic, :],
                                     xd_fm[:, ic, t0 + hh:t0 + hh + 512],
                                     start=(ic == 0), stop=(ic == NCH - 1))
                    return r
                tu = s.add("tensor", mmu, waits=wB)
                wsi = [tg]
                if ("gs", hh) in mm_done:
                    wsi.append(mm_done[("gs", hh)])
                tsi = s.add("scalar", lambda q, oc=oc, hh=hh, bkA=bkA: q.activation(
                    gscr[:, hh:hh + 512], pb[bkA][:, 0:512], AF.Silu), waits=wsi)
                mm_done[("A", bkA)] = tsi
                th = s.add("vector", lambda q, oc=oc, hh=hh, bkB=bkB: q.tensor_tensor(
                    h_sb[:, oc, hh:hh + 512], gscr[:, hh:hh + 512],
                    pb[bkB][:, 0:512], ALU.mult), waits=[tsi, tu])
                mm_done[("B", bkB)] = th
                mm_done[("gs", hh)] = th
                h_toks.append(th)
            mm_done[("g", buf)] = h_toks[-1]
        tq_prev = None
        tp_prev = {}
        for j in range(8):
            j2 = j % 2
            evs = []
            for c in range(NIC_FF):
                bk = c % len(pbb)
                ww = list(h_toks)
                if ("pbb2", bk) in tp_prev:
                    ww.append(tp_prev[("pbb2", bk)])
                tt = s.add("tensor", lambda q, c=c, j=j, bk=bk: q.transpose(
                    pbb[bk][:, 0:128], h_sb[:, c, j * 128:(j + 1) * 128], identb[:]),
                    waits=ww)
                te = s.add("scalar", lambda q, c=c, bk=bk, j2=j2: q.copy(
                    htm[j2][:, c * 128:(c + 1) * 128], pbb[bk][:, 0:128]), waits=[tt])
                tp_prev[("pbb2", bk)] = te
                evs.append(te)
            qw = list(evs)
            if ("btt", j2) in tp_prev:
                qw.append(tp_prev[("btt", j2)])
            tq = emit_quant_tm(b, htm[j2][:], htm[j2][:], scr, I_FF, 16, 6.0, qw)
            back = []
            for c in range(NIC_FF):
                bk = c % len(pbb)
                ww = [tq]
                if ("pbb2", bk) in tp_prev:
                    ww.append(tp_prev[("pbb2", bk)])
                tt = s.add("tensor", lambda q, c=c, bk=bk, j2=j2: q.transpose(
                    pbb[bk][:, 0:128],
                    htm[j2][:, c * 128:(c + 1) * 128], identb[:]),
                    waits=ww)
                te = s.add("scalar", lambda q, c=c, j=j, bk=bk: q.copy(
                    h_sb[:, c, j * 128:(j + 1) * 128], pbb[bk][:, 0:128]), waits=[tt])
                tp_prev[("pbb2", bk)] = te
                back.append(te)
            tp_prev[("btt", j2)] = tt
            tq_prev = back[-1]
        add_toks = []
        dprev = {}
        for oc in range(NCH):
            buf = oc % 2
            wldw = [tq_prev]
            if ("d", buf) in dprev:
                wldw.append(dprev[("d", buf)])

            def ldwd(q, sem, oc=oc, buf=buf):
                q.dma_start(out=wdbuf[buf][:],
                            in_=wdT_dram.rearrange("(ic p) o -> p ic o", p=128)[
                                :, :, oc * 128:(oc + 1) * 128]).then_inc(sem, 16)
            twdl = s.add("gpsimd", ldwd, waits=wldw, dma=True)
            for half in range(2):
                hh = half * 512
                bk = (2 * oc + half) % len(pb)
                ww = [twdl, tld]
                if ("m", bk) in dprev:
                    ww.append(dprev[("m", bk)])

                def mmd(q, oc=oc, hh=hh, bk=bk, buf=buf):
                    r = None
                    for ic in range(NIC_FF):
                        r = q.matmul(pb[bk][:, 0:512], wdbuf[buf][:, ic, :],
                                     h_sb[:, ic, hh:hh + 512],
                                     start=(ic == 0), stop=(ic == NIC_FF - 1))
                    return r
                tm = s.add("tensor", mmd, waits=ww)
                ta = s.add("vector", lambda q, oc=oc, hh=hh, bk=bk: q.tensor_tensor(
                    x_st[:, oc, hh:hh + 512], x_st[:, oc, hh:hh + 512],
                    pb[bk][:, 0:512], ALU.add), waits=[tm])
                dprev[("m", bk)] = ta
                add_toks.append(ta)
            dprev[("d", buf)] = add_toks[-1]
        tprev_store = store_fm(b, out_hbm[:, t0:t0 + 1024], x_st, waits=add_toks)
    s.add("sync", lambda q: q.nop(), waits=[tprev_store])
    if sa:
        return b.finish()


def emit_ln_fm(b, x_fm, xn_bf, ones_bf, rowscr, sqscr, xtmp, bankA, bankB,
               bankC, bankD, waits, Tn=T, n_ch=NCH, eps=1e-5, out_also_f32=None,
               gain=None, bias2=None, out_lo=None, loscr=None, cast_buf=None,
               double_ln=False, out0_f32=None, bankE=None):
    s = b.s
    ntile = Tn // 512
    cb = cast_buf if cast_buf is not None else xn_bf
    cast_toks = []
    for c in range(n_ch):
        tc = s.add("vector", lambda q, c=c: q.tensor_copy(
            cb[:, c, :], x_fm[:, c, :]), waits=waits)
        cast_toks.append(tc)
    sum_evacs = []
    prevA = None
    prevB = None
    sq_toks = {}
    for ti in range(ntile):
        t0 = ti * 512
        for c in range(n_ch):
            half = c % 2
            ww = list(waits)
            if (ti, c - 2, "mm") in sq_toks:
                ww.append(sq_toks[(ti, c - 2, "mm")])
            tsq = s.add("scalar", lambda q, c=c, t0=t0, half=half: q.activation(
                sqscr[:, half * 512:half * 512 + 512], x_fm[:, c, t0:t0 + 512],
                AF.Square), waits=ww)
            wA = [cast_toks[c]]
            wB = [tsq]
            if c == 0:
                if prevA is not None:
                    wA.append(prevA)
                    wB.append(prevB)

            def mmsum(q, c=c, t0=t0):
                return q.matmul(bankA[0:1, 0:512], ones_bf[:, 0:1],
                                cb[:, c, t0:t0 + 512],
                                start=(c == 0), stop=(c == n_ch - 1))
            tA = s.add("tensor", mmsum, waits=wA)

            def mmsq(q, c=c, t0=t0, half=half):
                return q.matmul(bankB[0:1, 0:512], ones_bf[:, 0:1],
                                sqscr[:, half * 512:half * 512 + 512],
                                start=(c == 0), stop=(c == n_ch - 1))
            tB = s.add("tensor", mmsq, waits=wB)
            sq_toks[(ti, c, "mm")] = tB
        eA = s.add("scalar", lambda q, t0=t0: q.copy(
            rowscr[0:1, 2 * Tn + t0:2 * Tn + t0 + 512], bankA[0:1, 0:512]), waits=[tA])
        eB = s.add("scalar", lambda q, t0=t0: q.copy(
            rowscr[0:1, 3 * Tn + t0:3 * Tn + t0 + 512], bankB[0:1, 0:512]), waits=[tB])
        prevA, prevB = eA, eB
        sum_evacs += [eA, eB]
    t1 = s.add("vector", lambda q: q.tensor_scalar_mul(
        rowscr[0:1, 0:Tn], rowscr[0:1, 2 * Tn:3 * Tn], 1.0 / D), waits=sum_evacs)
    t2 = s.add("vector", lambda q: q.tensor_tensor(
        rowscr[0:1, Tn:2 * Tn], rowscr[0:1, 0:Tn],
        rowscr[0:1, 0:Tn], ALU.mult), waits=[t1])
    t3 = s.add("vector", lambda q: q.scalar_tensor_tensor(
        rowscr[0:1, Tn:2 * Tn], rowscr[0:1, 3 * Tn:4 * Tn], 1.0 / D,
        rowscr[0:1, Tn:2 * Tn], ALU.mult, ALU.subtract), waits=[t2] + sum_evacs)
    epsap, epstok = b.cst(eps)
    if double_ln:
        t4a = s.add("scalar", lambda q: q.activation(
            rowscr[0:1, 2 * Tn:3 * Tn], rowscr[0:1, Tn:2 * Tn], AF.Ln,
            bias=epsap[0:1, :]), waits=[t3, epstok])
        t4b = s.add("scalar", lambda q: q.activation(
            rowscr[0:1, 2 * Tn:3 * Tn], rowscr[0:1, 2 * Tn:3 * Tn], AF.Exp,
            scale=-0.5), waits=[t4a])
        t4c = s.add("vector", lambda q: q.tensor_tensor(
            rowscr[0:1, 3 * Tn:4 * Tn], rowscr[0:1, 2 * Tn:3 * Tn],
            rowscr[0:1, 2 * Tn:3 * Tn], ALU.mult), waits=[t4b])
        t4d = s.add("vector", lambda q: q.tensor_tensor(
            rowscr[0:1, 3 * Tn:4 * Tn], rowscr[0:1, 3 * Tn:4 * Tn],
            rowscr[0:1, Tn:2 * Tn], ALU.mult), waits=[t4c])
        t4e = s.add("scalar", lambda q: q.activation(
            rowscr[0:1, 3 * Tn:4 * Tn], rowscr[0:1, 3 * Tn:4 * Tn], AF.Ln,
            bias=epsap[0:1, :]), waits=[t4d])
        t4f = s.add("scalar", lambda q: q.activation(
            rowscr[0:1, 3 * Tn:4 * Tn], rowscr[0:1, 3 * Tn:4 * Tn], AF.Exp,
            scale=-0.5), waits=[t4e])
        if out0_f32 is not None:
            t4g = s.add("vector", lambda q: q.tensor_copy(
                rowscr[0:1, 4 * Tn:5 * Tn], rowscr[0:1, 2 * Tn:3 * Tn]),
                waits=[t4f])
        t5 = s.add("vector", lambda q: q.tensor_tensor(
            rowscr[0:1, Tn:2 * Tn], rowscr[0:1, 2 * Tn:3 * Tn],
            rowscr[0:1, 3 * Tn:4 * Tn], ALU.mult),
            waits=[t4f] + ([t4g] if out0_f32 is not None else []))
    else:
        t4 = s.add("scalar", lambda q: q.activation(
            rowscr[0:1, Tn:2 * Tn], rowscr[0:1, Tn:2 * Tn], AF.Ln,
            bias=epsap[0:1, :]), waits=[t3, epstok])
        t5 = s.add("scalar", lambda q: q.activation(
            rowscr[0:1, Tn:2 * Tn], rowscr[0:1, Tn:2 * Tn], AF.Exp,
            scale=-0.5), waits=[t4])
    bfv = rowscr.bitcast(BF16)
    mh = bfv[0:1, 4 * Tn:5 * Tn]
    ml = bfv[0:1, 5 * Tn:6 * Tn]
    rh = bfv[0:1, 6 * Tn:7 * Tn]
    rl = bfv[0:1, 7 * Tn:8 * Tn]
    t6 = s.add("vector", lambda q: q.tensor_copy(mh, rowscr[0:1, 0:Tn]),
               waits=[t5])
    t7 = s.add("vector", lambda q: q.tensor_tensor(
        ml, rowscr[0:1, 0:Tn], mh, ALU.subtract), waits=[t6])
    t8 = s.add("vector", lambda q: q.tensor_copy(rh, rowscr[0:1, Tn:2 * Tn]),
               waits=[t5])
    t9 = s.add("vector", lambda q: q.tensor_tensor(
        rl, rowscr[0:1, Tn:2 * Tn], rh, ALU.subtract), waits=[t8])
    if out0_f32 is not None:
        r1h = bfv[0:1, 10 * Tn:11 * Tn]
        r1l = bfv[0:1, 11 * Tn:12 * Tn]
        t9a = s.add("vector", lambda q: q.tensor_copy(
            r1h, rowscr[0:1, 4 * Tn:5 * Tn]), waits=[t9])
        t9b = s.add("vector", lambda q: q.tensor_tensor(
            r1l, rowscr[0:1, 4 * Tn:5 * Tn], r1h, ALU.subtract), waits=[t9a])
        t9 = t9b
    apply_toks = {}
    last = None
    prevC = None
    prevD = None
    for ti in range(ntile):
        t0 = ti * 512
        wC = [t7]
        if prevC is not None:
            wC.append(prevC)

        def bcm(q, t0=t0):
            q.matmul(bankC[:, 0:512], ones_bf[0:1, :], mh[0:1, t0:t0 + 512],
                     start=True, stop=False)
            return q.matmul(bankC[:, 0:512], ones_bf[0:1, :], ml[0:1, t0:t0 + 512],
                            start=False, stop=True)
        tC = s.add("tensor", bcm, waits=wC)
        wD = [t9]
        if prevD is not None:
            wD.append(prevD)

        def bcr(q, t0=t0):
            q.matmul(bankD[:, 0:512], ones_bf[0:1, :], rh[0:1, t0:t0 + 512],
                     start=True, stop=False)
            return q.matmul(bankD[:, 0:512], ones_bf[0:1, :], rl[0:1, t0:t0 + 512],
                            start=False, stop=True)
        tD = s.add("tensor", bcr, waits=wD)
        if out0_f32 is not None:
            wE = [t9]
            if ("lnE",) in getattr(b, "_lnE", {}):
                wE.append(b._lnE[("lnE",)])

            def bce(q, t0=t0):
                q.matmul(bankE[:, 0:512], ones_bf[0:1, :], r1h[0:1, t0:t0 + 512],
                         start=True, stop=False)
                return q.matmul(bankE[:, 0:512], ones_bf[0:1, :],
                                r1l[0:1, t0:t0 + 512], start=False, stop=True)
            tE = s.add("tensor", bce, waits=wE)
        for c in range(n_ch):
            ta = s.add("vector", lambda q, c=c, t0=t0: q.scalar_tensor_tensor(
                xtmp[:, 0:512], bankC[:, 0:512], -1.0, x_fm[:, c, t0:t0 + 512],
                ALU.mult, ALU.add), waits=[tC] + ([last] if last else []))
            if out0_f32 is not None:
                ta = s.add("vector", lambda q, c=c, t0=t0: q.tensor_tensor(
                    out0_f32[:, c, t0:t0 + 512], xtmp[:, 0:512],
                    bankE[:, 0:512], ALU.mult), waits=[ta, tE])
            if gain is not None:
                tb0 = s.add("vector", lambda q, c=c, t0=t0: q.tensor_tensor(
                    xtmp[:, 0:512], xtmp[:, 0:512], bankD[:, 0:512],
                    ALU.mult), waits=[tD, ta])
                tb = s.add("vector", lambda q, c=c, t0=t0: q.tensor_scalar(
                    xn_bf[:, c, t0:t0 + 512], xtmp[:, 0:512],
                    gain[:, c:c + 1], bias2[:, c:c + 1],
                    ALU.mult, ALU.add), waits=[tb0])
            else:
                tb = s.add("vector", lambda q, c=c, t0=t0: q.tensor_tensor(
                    xn_bf[:, c, t0:t0 + 512], xtmp[:, 0:512], bankD[:, 0:512],
                    ALU.mult), waits=[tD, ta])
            if out_also_f32 is not None:
                tb = s.add("vector", lambda q, c=c, t0=t0: q.tensor_tensor(
                    out_also_f32[:, c, t0:t0 + 512], xtmp[:, 0:512],
                    bankD[:, 0:512], ALU.mult), waits=[tb])
            if out_lo is not None:
                tb = s.add("vector", lambda q, c=c, t0=t0: q.tensor_tensor(
                    loscr[:, 0:512], xtmp[:, 0:512], bankD[:, 0:512],
                    ALU.mult), waits=[tb])
                tb = s.add("vector", lambda q, c=c, t0=t0: q.tensor_tensor(
                    out_lo[:, c, t0:t0 + 512], loscr[:, 0:512],
                    xn_bf[:, c, t0:t0 + 512], ALU.subtract), waits=[tb])
            apply_toks[(c, ti)] = tb
            last = tb
        prevC = last
        prevD = last
        if out0_f32 is not None:
            if not hasattr(b, "_lnE"):
                b._lnE = {}
            b._lnE[("lnE",)] = last
    return last, apply_toks


H_SH = 2048
NOC_SH = H_SH // 128


def build_l3_ffn(b=None, xin=None, xout=None):
    sa = b is None
    if sa:
        b = Builder("l3ffn")
    x_hbm = xin if xin is not None else b.dram_in("xres", [D, T])
    rw_hbm = b.dram_in("router_wT", [D, 8])
    biasrep_hbm = b.dram_in("rbias_rep", [128, NTT * 8])
    sw1_hbm = b.dram_in("sw1", [D, H_SH])
    sw3_hbm = b.dram_in("sw3", [D, H_SH])
    sw2_hbm = b.dram_in("sw2", [H_SH, D])
    ew1_hbm = b.dram_in("ew1", [8, D, D])
    ew2_hbm = b.dram_in("ew2", [8, D, D])
    ident_hbm = b.dram_in("identf", [128, 128])
    sel1h_hbm = b.dram_in("sel8", [8, 8 * 128], BF16)
    out_hbm = xout if xout is not None else b.dram_out("xout", [D, T])

    ST = 512
    NST = T // ST
    x_fm = b.sbuf("x", [128, NCH, T], F32)
    xn_bf = b.sbuf("xn", [128, NCH, T], BF16)
    ones_bf = b.sbuf("ones", [128, 128], BF16)
    rowscr = b.sbuf("rows", [1, 4 * T], F32)
    sqscr = b.sbuf("sqscr", [128, 1024], BF16)
    xtmp = b.sbuf("xtmp", [128, 512], F32)
    identf = b.sbuf("identf", [128, 128], F32)
    sel8 = b.sbuf("sel8", [8, 8 * 128], BF16)
    rwT_f = b.sbuf("rwT", [128, NCH, 8], F32)
    cs_sb = b.sbuf("cs", [8, 1], F32)
    onesf = b.sbuf("onesf", [128, 8], F32)
    biasrep = b.sbuf("brep", [128, NTT * 8], F32)
    srow = b.sbuf("srow", [8, T], F32)
    s_tm = b.sbuf("stm", [128, NTT, 8], F32)
    sel_tm = b.sbuf("seltm", [128, NTT, 8], F32)
    cw_tm = b.sbuf("cwtm", [128, NTT, 8], F32)
    tscr = b.sbuf("tscr", [128, NTT, 8], F32)
    tscr2 = b.sbuf("tscr2", [128, NTT, 8], F32)
    gsc = b.sbuf("gsc", [128, NTT * 8], F32)
    cwrow = b.sbuf("cwrow", [8, T], BF16)
    a_bf = b.sbuf("a", [128, NOC_SH, ST], BF16)
    he_bf = b.sbuf("he", [128, NCH, ST], BF16)
    gscr = b.sbuf("gscr", [128, 1024], F32)
    wbuf = [b.sbuf("wb", [128, NCH, 128], BF16) for _ in range(2)]
    wbuf3 = [b.sbuf("wb3", [128, NCH, 128], BF16) for _ in range(2)]
    wbuf2 = [b.sbuf("wb2", [128, NOC_SH, 128], BF16) for _ in range(2)]
    we1 = [b.sbuf("we1", [128, NCH, 128], BF16) for _ in range(2)]
    we2 = [b.sbuf("we2", [128, NCH, 128], BF16) for _ in range(2)]

    pb = b.alloc_psum_banks(4, F32, 512)
    pbA = b.alloc_psum_banks(1, F32, 512)[0]
    pbB = b.alloc_psum_banks(1, F32, 512)[0]
    pbC = b.alloc_psum_banks(1, F32, 512)[0]
    pbD = b.alloc_psum_banks(1, F32, 512)[0]

    s = b.s
    tload_x = load_fm(b, x_fm, x_hbm)

    def ld_misc(q, sem):
        q.dma_start(out=identf[:], in_=ident_hbm).then_inc(sem, 16)
        q.dma_start(out=sel8[:], in_=sel1h_hbm).then_inc(sem, 16)
        q.dma_start(out=biasrep[:], in_=biasrep_hbm).then_inc(sem, 16)
    tmisc = s.add("sync", ld_misc, dma=True, n_dma=3)
    tmisc2 = s.add("gpsimd", lambda q, sem: q.dma_start(
        out=rwT_f[:], in_=rw_hbm.rearrange("(c p) e -> p c e", p=128)).then_inc(sem, 16),
        dma=True)
    tonesf = s.add("vector", lambda q: q.memset(onesf[:], 1.0))
    tones = s.add("vector", lambda q: q.memset(ones_bf[:], 1.0))

    t_ln, _ = emit_ln_fm(b, x_fm, xn_bf, ones_bf, rowscr, sqscr, xtmp,
                         pbA, pbB, pbC, pbD, [tload_x, tones])

    tcs_mm = s.add("tensor", lambda q: [
        q.matmul(pbB[0:8, 0:1], rwT_f[:, c, :], onesf[:, 0:1],
                 start=(c == 0), stop=(c == NCH - 1)) for c in range(NCH)][-1],
        waits=[tmisc2, tonesf, t_ln])
    t_cs = s.add("vector", lambda q: q.tensor_scalar_mul(
        cs_sb[0:8, 0:1], pbB[0:8, 0:1], -1.0), waits=[tcs_mm])
    r_toks = []
    prevR = None
    muR = rowscr[0:1, 0:T]
    rstdR = rowscr[0:1, T:2 * T]
    for ti in range(4):
        t0 = ti * 512
        w = [t_ln, t_cs]
        if prevR is not None:
            w.append(prevR)

        def mmr(q, t0=t0):
            r = None
            for c in range(NCH):
                r = q.matmul(pbA[0:8, 0:512], rwT_f[:, c, :], x_fm[:, c, t0:t0 + 512],
                             start=(c == 0), stop=(c == NCH - 1))
            return r
        tr = s.add("tensor", mmr, waits=w)
        te0 = s.add("scalar", lambda q, t0=t0: q.copy(
            srow[:, t0:t0 + 512], pbA[0:8, 0:512]), waits=[tr])
        tb1 = s.add("tensor", lambda q, t0=t0: q.matmul(
            pbA[0:8, 0:512], onesf[0:1, 0:8],
            muR[0:1, t0:t0 + 512], start=True, stop=True), waits=[te0])
        tb2 = s.add("tensor", lambda q, t0=t0: q.matmul(
            pbB[0:8, 0:512], onesf[0:1, 0:8],
            rstdR[0:1, t0:t0 + 512], start=True, stop=True), waits=[te0, t_cs])
        ts1 = s.add("vector", lambda q, t0=t0: q.scalar_tensor_tensor(
            srow[:, t0:t0 + 512], pbA[0:8, 0:512], cs_sb[0:8, 0:1],
            srow[:, t0:t0 + 512], ALU.mult, ALU.add), waits=[tb1])
        ts2 = s.add("vector", lambda q, t0=t0: q.tensor_tensor(
            srow[:, t0:t0 + 512], srow[:, t0:t0 + 512], pbB[0:8, 0:512],
            ALU.mult), waits=[tb2, ts1])
        te = s.add("scalar", lambda q, t0=t0: q.activation(
            srow[:, t0:t0 + 512], srow[:, t0:t0 + 512], AF.Sigmoid), waits=[ts2])
        prevR = te
        r_toks.append(te)

    tp_toks = []
    prevT = {}
    for j in range(NTT):
        bk = j % 2
        w = list(r_toks)
        if bk in prevT:
            w.append(prevT[bk])
        tt = s.add("tensor", lambda q, j=j, bk=bk: q.transpose(
            (pbC if bk == 0 else pbD)[:, 0:8], srow[0:8, j * 128:(j + 1) * 128],
            identf[0:8, 0:8]), waits=w)
        te = s.add("scalar", lambda q, j=j, bk=bk: q.copy(
            s_tm[:, j, :], (pbC if bk == 0 else pbD)[:, 0:8]), waits=[tt])
        prevT[bk] = te
        tp_toks.append(te)
    t_sel = s.add("vector", lambda q: q.tensor_tensor(
        sel_tm[:], s_tm[:], biasrep[:].rearrange("p (j e) -> p j e", e=8),
        ALU.add), waits=tp_toks + [tmisc])

    t_m1 = s.add("vector", lambda q: q.reduce_max(
        gsc[:, 0:NTT], sel_tm[:], axis=AX.X), waits=[t_sel])
    m1bc = gsc[:, 0:NTT].broadcast_to([128, NTT, 8])
    t_eq1 = s.add("vector", lambda q: q.tensor_tensor(
        tscr[:], sel_tm[:], m1bc, ALU.is_ge), waits=[t_m1])
    t_s2 = s.add("vector", lambda q: q.scalar_tensor_tensor(
        tscr2[:], tscr[:], -1e30, sel_tm[:], ALU.mult, ALU.add), waits=[t_eq1])
    t_m2 = s.add("vector", lambda q: q.reduce_max(
        gsc[:, NTT:2 * NTT], tscr2[:], axis=AX.X), waits=[t_s2])
    m2bc = gsc[:, NTT:2 * NTT].broadcast_to([128, NTT, 8])
    t_eq2 = s.add("vector", lambda q: q.tensor_tensor(
        tscr2[:], tscr2[:], m2bc, ALU.is_ge), waits=[t_m2])
    t_p1 = s.add("vector", lambda q: q.tensor_tensor(
        cw_tm[:], s_tm[:], tscr[:], ALU.mult), waits=[t_eq1])
    t_s1s = s.add("vector", lambda q: q.reduce_sum(
        gsc[:, 2 * NTT:3 * NTT], cw_tm[:], axis=AX.X), waits=[t_p1])
    t_p2 = s.add("vector", lambda q: q.tensor_tensor(
        cw_tm[:], s_tm[:], tscr2[:], ALU.mult), waits=[t_eq2, t_s1s])
    t_s2s = s.add("vector", lambda q: q.reduce_sum(
        gsc[:, 3 * NTT:4 * NTT], cw_tm[:], axis=AX.X), waits=[t_p2])
    t_dn = s.add("vector", lambda q: q.tensor_tensor(
        gsc[:, 4 * NTT:5 * NTT], gsc[:, 2 * NTT:3 * NTT],
        gsc[:, 3 * NTT:4 * NTT], ALU.add), waits=[t_s2s])
    t_dp = s.add("vector", lambda q: q.tensor_scalar_add(
        gsc[:, 5 * NTT:6 * NTT], gsc[:, 4 * NTT:5 * NTT], 1e-9), waits=[t_dn])
    t_rc = s.add("vector", lambda q: q.reciprocal(
        gsc[:, 6 * NTT:7 * NTT], gsc[:, 5 * NTT:6 * NTT]), waits=[t_dp])
    t_cd = s.add("vector", lambda q: q.tensor_single_scalar(
        gsc[:, 7 * NTT:8 * NTT], gsc[:, 4 * NTT:5 * NTT], 1e-9, ALU.is_gt),
        waits=[t_dn])
    def gate_calc(dsti, srci):
        ta = s.add("vector", lambda q: q.tensor_tensor(
            gsc[:, dsti * NTT:(dsti + 1) * NTT], gsc[:, srci * NTT:(srci + 1) * NTT],
            gsc[:, 6 * NTT:7 * NTT], ALU.mult), waits=[t_rc, t_cd])
        tb = s.add("vector", lambda q: q.tensor_tensor(
            gsc[:, dsti * NTT:(dsti + 1) * NTT], gsc[:, dsti * NTT:(dsti + 1) * NTT],
            gsc[:, 7 * NTT:8 * NTT], ALU.mult), waits=[ta])
        tc = s.add("vector", lambda q: q.tensor_scalar(
            gsc[:, 8 * NTT:9 * NTT] if False else xtmp[:, 0:NTT],
            gsc[:, 7 * NTT:8 * NTT], -0.5, 0.5, ALU.mult, ALU.add), waits=[tb])
        td = s.add("vector", lambda q: q.tensor_tensor(
            gsc[:, dsti * NTT:(dsti + 1) * NTT], gsc[:, dsti * NTT:(dsti + 1) * NTT],
            xtmp[:, 0:NTT], ALU.add), waits=[tc])
        return td
    t_g1 = gate_calc(0, 2)
    t_g2 = gate_calc(1, 3)
    g1bc = gsc[:, 0:NTT].broadcast_to([128, NTT, 8])
    g2bc = gsc[:, NTT:2 * NTT].broadcast_to([128, NTT, 8])
    t_cw1 = s.add("vector", lambda q: q.tensor_tensor(
        cw_tm[:], tscr[:], g1bc, ALU.mult), waits=[t_g1, t_g2])
    t_cw2 = s.add("vector", lambda q: q.tensor_tensor(
        tscr2[:], tscr2[:], g2bc, ALU.mult), waits=[t_cw1])
    t_cw = s.add("vector", lambda q: q.tensor_tensor(
        cw_tm[:], cw_tm[:], tscr2[:], ALU.add), waits=[t_cw2])

    cw_toks = []
    prevT = {}
    for j in range(NTT):
        w = [t_cw]
        if 0 in prevT:
            w.append(prevT[0])
        tt = s.add("tensor", lambda q, j=j: q.transpose(
            pbC[0:8, 0:128], cw_tm[:, j, :], identf[:]), waits=w)
        te = s.add("scalar", lambda q, j=j: q.copy(
            cwrow[0:8, j * 128:(j + 1) * 128], pbC[0:8, 0:128]), waits=[tt])
        prevT[0] = te
        cw_toks.append(te)
    t_cwr = cw_toks[-1]

    tprev_store = None
    dprev = {}
    for st_i in range(NST):
        t0 = st_i * ST
        for oc in range(NOC_SH):
            buf = oc % 2
            wld = [tmisc]
            if ("s", buf) in dprev:
                wld.append(dprev[("s", buf)])

            def ldw(q, sem, oc=oc, buf=buf):
                q.dma_start(out=wbuf[buf][:],
                            in_=sw1_hbm.rearrange("(c p) h -> p c h", p=128)[
                                :, :, oc * 128:(oc + 1) * 128]).then_inc(sem, 16)
                q.dma_start(out=wbuf3[buf][:],
                            in_=sw3_hbm.rearrange("(c p) h -> p c h", p=128)[
                                :, :, oc * 128:(oc + 1) * 128]).then_inc(sem, 16)
            tw = s.add("gpsimd", ldw, waits=wld, dma=True, n_dma=2)
            bkA = oc % 2
            bkB = 2 + oc % 2
            wA = [tw, t_ln]
            if ("bank", bkA) in dprev:
                wA.append(dprev[("bank", bkA)])

            def mmg(q, oc=oc, bkA=bkA, t0=t0, buf=buf):
                r = None
                for c in range(NCH):
                    r = q.matmul(pb[bkA][:, 0:ST], wbuf[buf][:, c, :],
                                 xn_bf[:, c, t0:t0 + ST],
                                 start=(c == 0), stop=(c == NCH - 1))
                return r
            tg = s.add("tensor", mmg, waits=wA)
            wB = [tw]
            if ("bank", bkB) in dprev:
                wB.append(dprev[("bank", bkB)])

            def mmu(q, oc=oc, bkB=bkB, t0=t0, buf=buf):
                r = None
                for c in range(NCH):
                    r = q.matmul(pb[bkB][:, 0:ST], wbuf3[buf][:, c, :],
                                 xn_bf[:, c, t0:t0 + ST],
                                 start=(c == 0), stop=(c == NCH - 1))
                return r
            tu = s.add("tensor", mmu, waits=wB)
            wsi = [tg]
            if "gscr" in dprev:
                wsi.append(dprev["gscr"])
            tsi = s.add("scalar", lambda q, bkA=bkA: q.activation(
                gscr[:, 0:ST], pb[bkA][:, 0:ST], AF.Silu), waits=wsi)
            dprev[("bank", bkA)] = tsi
            th = s.add("vector", lambda q, oc=oc, bkB=bkB: q.tensor_tensor(
                a_bf[:, oc, 0:ST], gscr[:, 0:ST],
                pb[bkB][:, 0:ST], ALU.mult), waits=[tsi, tu])
            dprev[("bank", bkB)] = th
            dprev["gscr"] = th
            dprev[("s", buf)] = th
        t_a = th
        for oc in range(NCH):
            buf = oc % 2
            wld = [t_a]
            if ("s2", buf) in dprev:
                wld.append(dprev[("s2", buf)])

            def ldw2(q, sem, oc=oc, buf=buf):
                q.dma_start(out=wbuf2[buf][:],
                            in_=sw2_hbm.rearrange("(c p) h -> p c h", p=128)[
                                :, :, oc * 128:(oc + 1) * 128]).then_inc(sem, 16)
            tw2 = s.add("gpsimd", ldw2, waits=wld, dma=True)
            bk = oc % 2
            ww = [tw2, t_a]
            if ("bank", bk) in dprev:
                ww.append(dprev[("bank", bk)])

            def mmd(q, oc=oc, bk=bk, buf=buf):
                r = None
                for ic in range(NOC_SH):
                    r = q.matmul(pb[bk][:, 0:ST], wbuf2[buf][:, ic, :],
                                 a_bf[:, ic, 0:ST],
                                 start=(ic == 0), stop=(ic == NOC_SH - 1))
                return r
            tm = s.add("tensor", mmd, waits=ww)
            ta = s.add("vector", lambda q, oc=oc, bk=bk, t0=t0: q.tensor_tensor(
                x_fm[:, oc, t0:t0 + ST], x_fm[:, oc, t0:t0 + ST],
                pb[bk][:, 0:ST], ALU.add), waits=[tm])
            dprev[("bank", bk)] = ta
            dprev[("s2", buf)] = ta
        t_shared = ta

        for e in range(8):
            wwc = [t_cwr, t_shared]
            if ("bank", 3) in dprev:
                wwc.append(dprev[("bank", 3)])

            def bccw(q, e=e, t0=t0):
                return q.matmul(pb[3][:, 0:ST], sel8[:, e * 128:(e + 1) * 128],
                                cwrow[0:8, t0:t0 + ST], start=True, stop=True)
            tcb = s.add("tensor", bccw, waits=wwc)
            he_toks = []
            for oc in range(NCH):
                buf = oc % 2
                wld = [t_shared]
                if ("ew", buf) in dprev:
                    wld.append(dprev[("ew", buf)])

                def lde1(q, sem, e=e, oc=oc, buf=buf):
                    q.dma_start(out=we1[buf][:],
                                in_=ew1_hbm[e].rearrange("(c p) h -> p c h", p=128)[
                                    :, :, oc * 128:(oc + 1) * 128]).then_inc(sem, 16)
                twe = s.add("gpsimd", lde1, waits=wld, dma=True)
                bkA = oc % 2
                ww = [twe]
                if ("bank", bkA) in dprev:
                    ww.append(dprev[("bank", bkA)])

                def mme1(q, oc=oc, bkA=bkA, t0=t0, buf=buf):
                    r = None
                    for c in range(NCH):
                        r = q.matmul(pb[bkA][:, 0:ST], we1[buf][:, c, :],
                                     xn_bf[:, c, t0:t0 + ST],
                                     start=(c == 0), stop=(c == NCH - 1))
                    return r
                tm1 = s.add("tensor", mme1, waits=ww)
                wge = [tm1]
                if "gscr" in dprev:
                    wge.append(dprev["gscr"])
                tge = s.add("scalar", lambda q, bkA=bkA: q.activation(
                    gscr[:, 0:ST], pb[bkA][:, 0:ST], AF.Gelu), waits=wge)
                dprev[("bank", bkA)] = tge
                the = s.add("vector", lambda q, oc=oc: q.tensor_tensor(
                    he_bf[:, oc, 0:ST], gscr[:, 0:ST],
                    pb[3][:, 0:ST], ALU.mult), waits=[tge, tcb])
                dprev["gscr"] = the
                dprev[("ew", buf)] = the
                he_toks.append(the)
            dprev[("bank", 3)] = he_toks[-1]
            t_he = he_toks[-1]
            for oc in range(NCH):
                buf = oc % 2
                wld = [t_he]
                if ("ew2", buf) in dprev:
                    wld.append(dprev[("ew2", buf)])

                def lde2(q, sem, e=e, oc=oc, buf=buf):
                    q.dma_start(out=we2[buf][:],
                                in_=ew2_hbm[e].rearrange("(c p) h -> p c h", p=128)[
                                    :, :, oc * 128:(oc + 1) * 128]).then_inc(sem, 16)
                twe2 = s.add("gpsimd", lde2, waits=wld, dma=True)
                bk = oc % 2
                ww = [twe2, t_he]
                if ("bank", bk) in dprev:
                    ww.append(dprev[("bank", bk)])

                def mme2(q, oc=oc, bk=bk, buf=buf):
                    r = None
                    for c in range(NCH):
                        r = q.matmul(pb[bk][:, 0:ST], we2[buf][:, c, :],
                                     he_bf[:, c, 0:ST],
                                     start=(c == 0), stop=(c == NCH - 1))
                    return r
                tm2 = s.add("tensor", mme2, waits=ww)
                ta2 = s.add("vector", lambda q, oc=oc, bk=bk, t0=t0: q.tensor_tensor(
                    x_fm[:, oc, t0:t0 + ST],
                    x_fm[:, oc, t0:t0 + ST],
                    pb[bk][:, 0:ST], ALU.add), waits=[tm2])
                dprev[("bank", bk)] = ta2
                dprev[("ew2", buf)] = ta2
        tprev_store = ta2

    t_store = store_fm(b, out_hbm, x_fm, waits=[tprev_store])
    s.add("sync", lambda q: q.nop(), waits=[t_store])
    if sa:
        return b.finish()


def build_l1_ffn(b=None, xin=None, xout=None):
    sa = b is None
    if sa:
        b = Builder("l1ffn")
    x_hbm = xin if xin is not None else b.dram_in("xres", [D, T])
    gw_hbm = b.dram_in("gate_w", [D, 8])
    bwgu_hbm = b.dram_in("base_wgu", [D, 2 * I_FF])
    bwd_hbm = b.dram_in("base_wd", [I_FF, D])
    ewgu_hbm = b.dram_in("exp_wgu", [8, D, 2 * I_FF])
    ewd_hbm = b.dram_in("exp_wd", [8, I_FF, D])
    ident_hbm = b.dram_in("identf", [128, 128])
    sel1h_hbm = b.dram_in("sel8", [8, 8 * 128], BF16)
    out_hbm = xout if xout is not None else b.dram_out("xout", [D, T])

    ST = 1024
    NST = T // ST
    NT8 = ST // 128
    x_st = b.sbuf("x", [128, NCH, ST], F32)
    xn_bf = b.sbuf("xn", [128, NCH, ST], BF16)
    a_bf = b.sbuf("a", [128, NIC_FF, ST], BF16)
    ones_bf = b.sbuf("ones", [128, 128], BF16)
    onesf = b.sbuf("onesf", [128, 8], F32)
    rowscr = b.sbuf("rows", [1, 4 * ST], F32)
    sqscr = b.sbuf("sqscr", [128, 1024], BF16)
    xtmp = b.sbuf("xtmp", [128, 512], F32)
    identf = b.sbuf("identf", [128, 128], F32)
    sel8 = b.sbuf("sel8", [8, 8 * 128], BF16)
    gwT_f = b.sbuf("gwT", [128, NCH, 8], F32)
    cs_sb = b.sbuf("cs", [8, 1], F32)
    lrow = b.sbuf("lrow", [8, ST], F32)
    l_tm = b.sbuf("ltm", [128, NT8, 8], F32)
    p_tm = b.sbuf("ptm", [128, NT8, 8], F32)
    cw_tm = b.sbuf("cwtm", [128, NT8, 8], F32)
    tscr = b.sbuf("tscr", [128, NT8, 8], F32)
    tscr2 = b.sbuf("tscr2", [128, NT8, 8], F32)
    gsc = b.sbuf("gsc", [128, NT8 * 8], F32)
    cwrow = b.sbuf("cwrow", [8, ST], BF16)
    cwbc = b.sbuf("cwbc", [128, ST], F32)
    gscr = b.sbuf("gscr", [128, 1024], F32)
    ytmp = b.sbuf("ytmp", [128, 1024], F32)
    wg = [b.sbuf("wg", [128, NCH, 128], BF16) for _ in range(2)]
    wu = [b.sbuf("wu", [128, NCH, 128], BF16) for _ in range(2)]
    wd = [b.sbuf("wd", [128, NIC_FF, 128], BF16) for _ in range(2)]

    pb = b.alloc_psum_banks(4, F32, 512)
    pbA = b.alloc_psum_banks(1, F32, 512)[0]
    pbB = b.alloc_psum_banks(1, F32, 512)[0]
    pbC = b.alloc_psum_banks(1, F32, 512)[0]
    pbD = b.alloc_psum_banks(1, F32, 512)[0]

    s = b.s

    def ld_misc(q, sem):
        q.dma_start(out=identf[:], in_=ident_hbm).then_inc(sem, 16)
        q.dma_start(out=sel8[:], in_=sel1h_hbm).then_inc(sem, 16)
    tmisc = s.add("sync", ld_misc, dma=True, n_dma=2)
    tgw = s.add("gpsimd", lambda q, sem: q.dma_start(
        out=gwT_f[:], in_=gw_hbm.rearrange("(c p) e -> p c e", p=128)).then_inc(sem, 16),
        dma=True)
    tones = s.add("vector", lambda q: q.memset(ones_bf[:], 1.0))
    tonesf = s.add("vector", lambda q: q.memset(onesf[:], 1.0))

    tcs_mm = s.add("tensor", lambda q: [
        q.matmul(pbB[0:8, 0:1], gwT_f[:, c, :], onesf[:, 0:1],
                 start=(c == 0), stop=(c == NCH - 1)) for c in range(NCH)][-1],
        waits=[tgw, tonesf])
    t_cs = s.add("vector", lambda q: q.tensor_scalar_mul(
        cs_sb[0:8, 0:1], pbB[0:8, 0:1], -1.0), waits=[tcs_mm])

    dprev = {}
    tprev_store = None
    tprev_ln = None
    for st_i in range(NST):
        t0g = st_i * ST
        tld = load_fm(b, x_st, x_hbm[:, t0g:t0g + ST],
                      waits=[t for t in (tprev_store, tprev_ln) if t] or [tmisc])
        t_ln, _ = emit_ln_fm(b, x_st, xn_bf, ones_bf, rowscr, sqscr, xtmp,
                             pbA, pbB, pbC, pbD, [tld, tones], Tn=ST)
        tprev_ln = t_ln
        muR = rowscr[0:1, 0:ST]
        rstdR = rowscr[0:1, ST:2 * ST]
        r_toks = []
        prevR = None
        for ti in range(ST // 512):
            t0 = ti * 512
            w = [t_ln, t_cs]
            if prevR is not None:
                w.append(prevR)

            def mmr(q, t0=t0, t0g=t0g):
                r = None
                for c in range(NCH):
                    r = q.matmul(pbA[0:8, 0:512], gwT_f[:, c, :],
                                 x_st[:, c, t0:t0 + 512],
                                 start=(c == 0), stop=(c == NCH - 1))
                return r
            tr = s.add("tensor", mmr, waits=w)
            te0 = s.add("scalar", lambda q, t0=t0: q.copy(
                lrow[:, t0:t0 + 512], pbA[0:8, 0:512]), waits=[tr])
            tb1 = s.add("tensor", lambda q, t0=t0: q.matmul(
                pbA[0:8, 0:512], onesf[0:1, 0:8],
                muR[0:1, t0:t0 + 512], start=True, stop=True), waits=[te0])
            tb2 = s.add("tensor", lambda q, t0=t0: q.matmul(
                pbB[0:8, 0:512], onesf[0:1, 0:8],
                rstdR[0:1, t0:t0 + 512], start=True, stop=True), waits=[te0, t_cs])
            ts1 = s.add("vector", lambda q, t0=t0: q.scalar_tensor_tensor(
                lrow[:, t0:t0 + 512], pbA[0:8, 0:512], cs_sb[0:8, 0:1],
                lrow[:, t0:t0 + 512], ALU.mult, ALU.add), waits=[tb1])
            ts2 = s.add("vector", lambda q, t0=t0: q.tensor_tensor(
                lrow[:, t0:t0 + 512], lrow[:, t0:t0 + 512], pbB[0:8, 0:512],
                ALU.mult), waits=[tb2, ts1])
            prevR = ts2
            r_toks.append(ts2)
        tp_toks = []
        for j in range(NT8):
            bk = j % 2
            w = list(r_toks)
            if ("tp", bk) in dprev:
                w.append(dprev[("tp", bk)])
            tt = s.add("tensor", lambda q, j=j, bk=bk: q.transpose(
                (pbC if bk == 0 else pbD)[:, 0:8], lrow[0:8, j * 128:(j + 1) * 128],
                identf[0:8, 0:8]), waits=w)
            te = s.add("scalar", lambda q, j=j, bk=bk: q.copy(
                l_tm[:, j, :], (pbC if bk == 0 else pbD)[:, 0:8]), waits=[tt])
            dprev[("tp", bk)] = te
            tp_toks.append(te)
        t_m = s.add("vector", lambda q: q.reduce_max(
            gsc[:, 0:NT8], l_tm[:], axis=AX.X), waits=tp_toks)
        t_e1 = s.add("vector", lambda q: q.tensor_tensor(
            p_tm[:], l_tm[:], gsc[:, 0:NT8].broadcast_to([128, NT8, 8]),
            ALU.subtract), waits=[t_m])
        t_e2 = s.add("scalar", lambda q: q.activation(
            p_tm[:], p_tm[:], AF.Exp), waits=[t_e1])
        t_sm = s.add("vector", lambda q: q.reduce_sum(
            gsc[:, NT8:2 * NT8], p_tm[:], axis=AX.X), waits=[t_e2])
        t_rc = s.add("vector", lambda q: q.reciprocal(
            gsc[:, NT8:2 * NT8], gsc[:, NT8:2 * NT8]), waits=[t_sm])
        t_pn = s.add("vector", lambda q: q.tensor_tensor(
            p_tm[:], p_tm[:], gsc[:, NT8:2 * NT8].broadcast_to([128, NT8, 8]),
            ALU.mult), waits=[t_rc])
        t_m1 = s.add("vector", lambda q: q.reduce_max(
            gsc[:, 2 * NT8:3 * NT8], l_tm[:], axis=AX.X), waits=[t_pn, t_e1])
        t_eq1 = s.add("vector", lambda q: q.tensor_tensor(
            tscr[:], l_tm[:], gsc[:, 2 * NT8:3 * NT8].broadcast_to([128, NT8, 8]),
            ALU.is_ge), waits=[t_m1])
        t_s2 = s.add("vector", lambda q: q.scalar_tensor_tensor(
            tscr2[:], tscr[:], -1e30, l_tm[:], ALU.mult, ALU.add), waits=[t_eq1])
        t_m2 = s.add("vector", lambda q: q.reduce_max(
            gsc[:, 3 * NT8:4 * NT8], tscr2[:], axis=AX.X), waits=[t_s2])
        t_eq2 = s.add("vector", lambda q: q.tensor_tensor(
            tscr2[:], tscr2[:], gsc[:, 3 * NT8:4 * NT8].broadcast_to([128, NT8, 8]),
            ALU.is_ge), waits=[t_m2])
        t_p1 = s.add("vector", lambda q: q.tensor_tensor(
            cw_tm[:], p_tm[:], tscr[:], ALU.mult), waits=[t_eq1, t_pn])
        t_s1s = s.add("vector", lambda q: q.reduce_sum(
            gsc[:, 4 * NT8:5 * NT8], cw_tm[:], axis=AX.X), waits=[t_p1])
        t_p2 = s.add("vector", lambda q: q.tensor_tensor(
            cw_tm[:], p_tm[:], tscr2[:], ALU.mult), waits=[t_eq2, t_s1s])
        t_s2s = s.add("vector", lambda q: q.reduce_sum(
            gsc[:, 5 * NT8:6 * NT8], cw_tm[:], axis=AX.X), waits=[t_p2])
        t_dn = s.add("vector", lambda q: q.tensor_tensor(
            gsc[:, 6 * NT8:7 * NT8], gsc[:, 4 * NT8:5 * NT8],
            gsc[:, 5 * NT8:6 * NT8], ALU.add), waits=[t_s2s])
        t_rc2 = s.add("vector", lambda q: q.reciprocal(
            gsc[:, 6 * NT8:7 * NT8], gsc[:, 6 * NT8:7 * NT8]), waits=[t_dn])
        t_g1 = s.add("vector", lambda q: q.tensor_tensor(
            gsc[:, 4 * NT8:5 * NT8], gsc[:, 4 * NT8:5 * NT8],
            gsc[:, 6 * NT8:7 * NT8], ALU.mult), waits=[t_rc2])
        t_g2 = s.add("vector", lambda q: q.tensor_tensor(
            gsc[:, 5 * NT8:6 * NT8], gsc[:, 5 * NT8:6 * NT8],
            gsc[:, 6 * NT8:7 * NT8], ALU.mult), waits=[t_g1])
        t_cw1 = s.add("vector", lambda q: q.tensor_tensor(
            cw_tm[:], tscr[:], gsc[:, 4 * NT8:5 * NT8].broadcast_to([128, NT8, 8]),
            ALU.mult), waits=[t_g2])
        t_cw2 = s.add("vector", lambda q: q.tensor_tensor(
            tscr2[:], tscr2[:], gsc[:, 5 * NT8:6 * NT8].broadcast_to([128, NT8, 8]),
            ALU.mult), waits=[t_cw1])
        t_cw = s.add("vector", lambda q: q.tensor_tensor(
            cw_tm[:], cw_tm[:], tscr2[:], ALU.add), waits=[t_cw2])
        cw_toks = []
        for j in range(NT8):
            w = [t_cw]
            if ("cw", 0) in dprev:
                w.append(dprev[("cw", 0)])
            tt = s.add("tensor", lambda q, j=j: q.transpose(
                pbC[0:8, 0:128], cw_tm[:, j, :], identf[:]), waits=w)
            te = s.add("scalar", lambda q, j=j: q.copy(
                cwrow[0:8, j * 128:(j + 1) * 128], pbC[0:8, 0:128]), waits=[tt])
            dprev[("cw", 0)] = te
            cw_toks.append(te)
        t_cwr = cw_toks[-1]

        for mi in range(9):
            is_base = (mi == 0)
            e = mi - 1
            wgu_src = bwgu_hbm if is_base else ewgu_hbm[e]
            wd_src = bwd_hbm if is_base else ewd_hbm[e]
            if not is_base:
                wwc = [t_cwr]
                if ("cwb", 0) in dprev:
                    wwc.append(dprev[("cwb", 0)])
                if ("bank", 3) in dprev:
                    wwc.append(dprev[("bank", 3)])
                tcb_toks = []
                for ti in range(2):
                    ttc = s.add("tensor", lambda q, e=e, ti=ti: q.matmul(
                        pb[3][:, 0:512], sel8[:, e * 128:(e + 1) * 128],
                        cwrow[0:8, ti * 512:ti * 512 + 512], start=True, stop=True),
                        waits=wwc if ti == 0 else [tcb_toks[-1]])
                    tte = s.add("vector", lambda q, ti=ti: q.tensor_copy(
                        cwbc[:, ti * 512:ti * 512 + 512], pb[3][:, 0:512]),
                        waits=[ttc])
                    tcb_toks.append(tte)
                t_cwb = tcb_toks[-1]
                dprev[("cwb", 0)] = t_cwb
                dprev[("bank", 3)] = t_cwb
            for oc in range(NIC_FF):
                buf = oc % 2
                wld = [t_ln]
                if ("w", buf) in dprev:
                    wld.append(dprev[("w", buf)])

                def ldw(q, sem, oc=oc, buf=buf, wgu_src=wgu_src):
                    wv = wgu_src.rearrange("(c p) h -> p c h", p=128)
                    q.dma_start(out=wg[buf][:],
                                in_=wv[:, :, oc * 128:(oc + 1) * 128]).then_inc(sem, 16)
                    q.dma_start(out=wu[buf][:],
                                in_=wv[:, :, I_FF + oc * 128:I_FF + (oc + 1) * 128]
                                ).then_inc(sem, 16)
                tw = s.add("gpsimd", ldw, waits=wld, dma=True, n_dma=2)
                for half in range(2):
                    hh = half * 512
                    bkA = half
                    bkB = 2 if half == 0 else 3
                    wA = [tw]
                    if ("bank", bkA) in dprev:
                        wA.append(dprev[("bank", bkA)])

                    def mmg(q, oc=oc, hh=hh, bkA=bkA, buf=buf):
                        r = None
                        for c in range(NCH):
                            r = q.matmul(pb[bkA][:, 0:512], wg[buf][:, c, :],
                                         xn_bf[:, c, hh:hh + 512],
                                         start=(c == 0), stop=(c == NCH - 1))
                        return r
                    tg = s.add("tensor", mmg, waits=wA)
                    wB = [tw]
                    if ("bank", bkB) in dprev:
                        wB.append(dprev[("bank", bkB)])

                    def mmu(q, oc=oc, hh=hh, bkB=bkB, buf=buf):
                        r = None
                        for c in range(NCH):
                            r = q.matmul(pb[bkB][:, 0:512], wu[buf][:, c, :],
                                         xn_bf[:, c, hh:hh + 512],
                                         start=(c == 0), stop=(c == NCH - 1))
                        return r
                    tu = s.add("tensor", mmu, waits=wB)
                    wsi = [tg]
                    if ("gscr", hh) in dprev:
                        wsi.append(dprev[("gscr", hh)])
                    tsi = s.add("scalar", lambda q, hh=hh, bkA=bkA: q.activation(
                        gscr[:, hh:hh + 512], pb[bkA][:, 0:512], AF.Silu), waits=wsi)
                    dprev[("bank", bkA)] = tsi
                    th = s.add("vector", lambda q, oc=oc, hh=hh, bkB=bkB: q.tensor_tensor(
                        a_bf[:, oc, hh:hh + 512], gscr[:, hh:hh + 512],
                        pb[bkB][:, 0:512], ALU.mult), waits=[tsi, tu])
                    dprev[("bank", bkB)] = th
                    dprev[("gscr", hh)] = th
                dprev[("w", buf)] = th
            t_a = th
            for oc in range(NCH):
                buf = oc % 2
                wld = [t_a]
                if ("wd", buf) in dprev:
                    wld.append(dprev[("wd", buf)])

                def ldwd(q, sem, oc=oc, buf=buf, wd_src=wd_src):
                    q.dma_start(out=wd[buf][:],
                                in_=wd_src.rearrange("(c p) h -> p c h", p=128)[
                                    :, :, oc * 128:(oc + 1) * 128]).then_inc(sem, 16)
                twd = s.add("gpsimd", ldwd, waits=wld, dma=True)
                for half in range(2):
                    hh = half * 512
                    bk = half
                    ww = [twd]
                    if ("bank", bk) in dprev:
                        ww.append(dprev[("bank", bk)])

                    def mmd(q, oc=oc, hh=hh, bk=bk, buf=buf):
                        r = None
                        for ic in range(NIC_FF):
                            r = q.matmul(pb[bk][:, 0:512], wd[buf][:, ic, :],
                                         a_bf[:, ic, hh:hh + 512],
                                         start=(ic == 0), stop=(ic == NIC_FF - 1))
                        return r
                    tm = s.add("tensor", mmd, waits=ww)
                    if is_base:
                        ta = s.add("vector", lambda q, oc=oc, hh=hh, bk=bk: q.tensor_tensor(
                            x_st[:, oc, hh:hh + 512], x_st[:, oc, hh:hh + 512],
                            pb[bk][:, 0:512], ALU.add), waits=[tm])
                    else:
                        ty = s.add("vector", lambda q, oc=oc, hh=hh, bk=bk: q.tensor_tensor(
                            ytmp[:, hh:hh + 512], pb[bk][:, 0:512],
                            cwbc[:, hh:hh + 512], ALU.mult), waits=[tm, t_cwb])
                        ta = s.add("vector", lambda q, oc=oc, hh=hh: q.tensor_tensor(
                            x_st[:, oc, hh:hh + 512], x_st[:, oc, hh:hh + 512],
                            ytmp[:, hh:hh + 512], ALU.add), waits=[ty])
                    dprev[("bank", bk)] = ta
                dprev[("wd", buf)] = ta
        tprev_store = store_fm(b, out_hbm[:, t0g:t0g + ST], x_st,
                               waits=[ta])
    s.add("sync", lambda q: q.nop(), waits=[tprev_store])
    if sa:
        return b.finish()


def build_l3_ffn(b=None, xin=None, xout=None):
    sa = b is None
    if sa:
        b = Builder("l3ffn")
    x_hbm = xin if xin is not None else b.dram_in("xres", [D, T])
    rw_hbm = b.dram_in("router_wT", [D, 8])
    biasrep_hbm = b.dram_in("rbias_rep", [128, NTT * 8])
    sw1_hbm = b.dram_in("sw1", [D, H_SH])
    sw3_hbm = b.dram_in("sw3", [D, H_SH])
    sw2_hbm = b.dram_in("sw2", [H_SH, D])
    ew1_hbm = b.dram_in("ew1", [8, D, D])
    ew2_hbm = b.dram_in("ew2", [8, D, D])
    ident_hbm = b.dram_in("identf", [128, 128])
    sel1h_hbm = b.dram_in("sel8", [8, 8 * 128], BF16)
    out_hbm = xout if xout is not None else b.dram_out("xout", [D, T])

    ST = 512
    NST = T // ST
    x_fm = b.sbuf("x", [128, NCH, T], F32)
    xn_bf = b.sbuf("xn", [128, NCH, T], BF16)
    ones_bf = b.sbuf("ones", [128, 128], BF16)
    rowscr = b.sbuf("rows", [1, 4 * T], F32)
    sqscr = b.sbuf("sqscr", [128, 1024], BF16)
    xtmp = b.sbuf("xtmp", [128, 512], F32)
    identf = b.sbuf("identf", [128, 128], F32)
    sel8 = b.sbuf("sel8", [8, 8 * 128], BF16)
    rwT_f = b.sbuf("rwT", [128, NCH, 8], F32)
    cs_sb = b.sbuf("cs", [8, 1], F32)
    onesf = b.sbuf("onesf", [128, 8], F32)
    biasrep = b.sbuf("brep", [128, NTT * 8], F32)
    srow = b.sbuf("srow", [8, T], F32)
    s_tm = b.sbuf("stm", [128, NTT, 8], F32)
    sel_tm = b.sbuf("seltm", [128, NTT, 8], F32)
    cw_tm = b.sbuf("cwtm", [128, NTT, 8], F32)
    tscr = b.sbuf("tscr", [128, NTT, 8], F32)
    tscr2 = b.sbuf("tscr2", [128, NTT, 8], F32)
    gsc = b.sbuf("gsc", [128, NTT * 8], F32)
    cwrow = b.sbuf("cwrow", [8, T], BF16)
    a_bf = b.sbuf("a", [128, NOC_SH, ST], BF16)
    he_bf = b.sbuf("he", [128, NCH, ST], BF16)
    gscr = b.sbuf("gscr", [128, 1024], F32)
    wbuf = [b.sbuf("wb", [128, NCH, 128], BF16) for _ in range(2)]
    wbuf3 = [b.sbuf("wb3", [128, NCH, 128], BF16) for _ in range(2)]
    wbuf2 = [b.sbuf("wb2", [128, NOC_SH, 128], BF16) for _ in range(2)]
    we1 = [b.sbuf("we1", [128, NCH, 128], BF16) for _ in range(2)]
    we2 = [b.sbuf("we2", [128, NCH, 128], BF16) for _ in range(2)]

    pb = b.alloc_psum_banks(4, F32, 512)
    pbA = b.alloc_psum_banks(1, F32, 512)[0]
    pbB = b.alloc_psum_banks(1, F32, 512)[0]
    pbC = b.alloc_psum_banks(1, F32, 512)[0]
    pbD = b.alloc_psum_banks(1, F32, 512)[0]

    s = b.s
    tload_x = load_fm(b, x_fm, x_hbm)

    def ld_misc(q, sem):
        q.dma_start(out=identf[:], in_=ident_hbm).then_inc(sem, 16)
        q.dma_start(out=sel8[:], in_=sel1h_hbm).then_inc(sem, 16)
        q.dma_start(out=biasrep[:], in_=biasrep_hbm).then_inc(sem, 16)
    tmisc = s.add("sync", ld_misc, dma=True, n_dma=3)
    tmisc2 = s.add("gpsimd", lambda q, sem: q.dma_start(
        out=rwT_f[:], in_=rw_hbm.rearrange("(c p) e -> p c e", p=128)).then_inc(sem, 16),
        dma=True)
    tonesf = s.add("vector", lambda q: q.memset(onesf[:], 1.0))
    tones = s.add("vector", lambda q: q.memset(ones_bf[:], 1.0))

    t_ln, _ = emit_ln_fm(b, x_fm, xn_bf, ones_bf, rowscr, sqscr, xtmp,
                         pbA, pbB, pbC, pbD, [tload_x, tones])

    tcs_mm = s.add("tensor", lambda q: [
        q.matmul(pbB[0:8, 0:1], rwT_f[:, c, :], onesf[:, 0:1],
                 start=(c == 0), stop=(c == NCH - 1)) for c in range(NCH)][-1],
        waits=[tmisc2, tonesf, t_ln])
    t_cs = s.add("vector", lambda q: q.tensor_scalar_mul(
        cs_sb[0:8, 0:1], pbB[0:8, 0:1], -1.0), waits=[tcs_mm])
    r_toks = []
    prevR = None
    muR = rowscr[0:1, 0:T]
    rstdR = rowscr[0:1, T:2 * T]
    for ti in range(4):
        t0 = ti * 512
        w = [t_ln, t_cs]
        if prevR is not None:
            w.append(prevR)

        def mmr(q, t0=t0):
            r = None
            for c in range(NCH):
                r = q.matmul(pbA[0:8, 0:512], rwT_f[:, c, :], x_fm[:, c, t0:t0 + 512],
                             start=(c == 0), stop=(c == NCH - 1))
            return r
        tr = s.add("tensor", mmr, waits=w)
        te0 = s.add("scalar", lambda q, t0=t0: q.copy(
            srow[:, t0:t0 + 512], pbA[0:8, 0:512]), waits=[tr])
        tb1 = s.add("tensor", lambda q, t0=t0: q.matmul(
            pbA[0:8, 0:512], onesf[0:1, 0:8],
            muR[0:1, t0:t0 + 512], start=True, stop=True), waits=[te0])
        tb2 = s.add("tensor", lambda q, t0=t0: q.matmul(
            pbB[0:8, 0:512], onesf[0:1, 0:8],
            rstdR[0:1, t0:t0 + 512], start=True, stop=True), waits=[te0, t_cs])
        ts1 = s.add("vector", lambda q, t0=t0: q.scalar_tensor_tensor(
            srow[:, t0:t0 + 512], pbA[0:8, 0:512], cs_sb[0:8, 0:1],
            srow[:, t0:t0 + 512], ALU.mult, ALU.add), waits=[tb1])
        ts2 = s.add("vector", lambda q, t0=t0: q.tensor_tensor(
            srow[:, t0:t0 + 512], srow[:, t0:t0 + 512], pbB[0:8, 0:512],
            ALU.mult), waits=[tb2, ts1])
        te = s.add("scalar", lambda q, t0=t0: q.activation(
            srow[:, t0:t0 + 512], srow[:, t0:t0 + 512], AF.Sigmoid), waits=[ts2])
        prevR = te
        r_toks.append(te)

    tp_toks = []
    prevT = {}
    for j in range(NTT):
        bk = j % 2
        w = list(r_toks)
        if bk in prevT:
            w.append(prevT[bk])
        tt = s.add("tensor", lambda q, j=j, bk=bk: q.transpose(
            (pbC if bk == 0 else pbD)[:, 0:8], srow[0:8, j * 128:(j + 1) * 128],
            identf[0:8, 0:8]), waits=w)
        te = s.add("scalar", lambda q, j=j, bk=bk: q.copy(
            s_tm[:, j, :], (pbC if bk == 0 else pbD)[:, 0:8]), waits=[tt])
        prevT[bk] = te
        tp_toks.append(te)
    t_sel = s.add("vector", lambda q: q.tensor_tensor(
        sel_tm[:], s_tm[:], biasrep[:].rearrange("p (j e) -> p j e", e=8),
        ALU.add), waits=tp_toks + [tmisc])

    t_m1 = s.add("vector", lambda q: q.reduce_max(
        gsc[:, 0:NTT], sel_tm[:], axis=AX.X), waits=[t_sel])
    m1bc = gsc[:, 0:NTT].broadcast_to([128, NTT, 8])
    t_eq1 = s.add("vector", lambda q: q.tensor_tensor(
        tscr[:], sel_tm[:], m1bc, ALU.is_ge), waits=[t_m1])
    t_s2 = s.add("vector", lambda q: q.scalar_tensor_tensor(
        tscr2[:], tscr[:], -1e30, sel_tm[:], ALU.mult, ALU.add), waits=[t_eq1])
    t_m2 = s.add("vector", lambda q: q.reduce_max(
        gsc[:, NTT:2 * NTT], tscr2[:], axis=AX.X), waits=[t_s2])
    m2bc = gsc[:, NTT:2 * NTT].broadcast_to([128, NTT, 8])
    t_eq2 = s.add("vector", lambda q: q.tensor_tensor(
        tscr2[:], tscr2[:], m2bc, ALU.is_ge), waits=[t_m2])
    t_p1 = s.add("vector", lambda q: q.tensor_tensor(
        cw_tm[:], s_tm[:], tscr[:], ALU.mult), waits=[t_eq1])
    t_s1s = s.add("vector", lambda q: q.reduce_sum(
        gsc[:, 2 * NTT:3 * NTT], cw_tm[:], axis=AX.X), waits=[t_p1])
    t_p2 = s.add("vector", lambda q: q.tensor_tensor(
        cw_tm[:], s_tm[:], tscr2[:], ALU.mult), waits=[t_eq2, t_s1s])
    t_s2s = s.add("vector", lambda q: q.reduce_sum(
        gsc[:, 3 * NTT:4 * NTT], cw_tm[:], axis=AX.X), waits=[t_p2])
    t_dn = s.add("vector", lambda q: q.tensor_tensor(
        gsc[:, 4 * NTT:5 * NTT], gsc[:, 2 * NTT:3 * NTT],
        gsc[:, 3 * NTT:4 * NTT], ALU.add), waits=[t_s2s])
    t_dp = s.add("vector", lambda q: q.tensor_scalar_add(
        gsc[:, 5 * NTT:6 * NTT], gsc[:, 4 * NTT:5 * NTT], 1e-9), waits=[t_dn])
    t_rc = s.add("vector", lambda q: q.reciprocal(
        gsc[:, 6 * NTT:7 * NTT], gsc[:, 5 * NTT:6 * NTT]), waits=[t_dp])
    t_cd = s.add("vector", lambda q: q.tensor_single_scalar(
        gsc[:, 7 * NTT:8 * NTT], gsc[:, 4 * NTT:5 * NTT], 1e-9, ALU.is_gt),
        waits=[t_dn])
    def gate_calc(dsti, srci):
        ta = s.add("vector", lambda q: q.tensor_tensor(
            gsc[:, dsti * NTT:(dsti + 1) * NTT], gsc[:, srci * NTT:(srci + 1) * NTT],
            gsc[:, 6 * NTT:7 * NTT], ALU.mult), waits=[t_rc, t_cd])
        tb = s.add("vector", lambda q: q.tensor_tensor(
            gsc[:, dsti * NTT:(dsti + 1) * NTT], gsc[:, dsti * NTT:(dsti + 1) * NTT],
            gsc[:, 7 * NTT:8 * NTT], ALU.mult), waits=[ta])
        tc = s.add("vector", lambda q: q.tensor_scalar(
            gsc[:, 8 * NTT:9 * NTT] if False else xtmp[:, 0:NTT],
            gsc[:, 7 * NTT:8 * NTT], -0.5, 0.5, ALU.mult, ALU.add), waits=[tb])
        td = s.add("vector", lambda q: q.tensor_tensor(
            gsc[:, dsti * NTT:(dsti + 1) * NTT], gsc[:, dsti * NTT:(dsti + 1) * NTT],
            xtmp[:, 0:NTT], ALU.add), waits=[tc])
        return td
    t_g1 = gate_calc(0, 2)
    t_g2 = gate_calc(1, 3)
    g1bc = gsc[:, 0:NTT].broadcast_to([128, NTT, 8])
    g2bc = gsc[:, NTT:2 * NTT].broadcast_to([128, NTT, 8])
    t_cw1 = s.add("vector", lambda q: q.tensor_tensor(
        cw_tm[:], tscr[:], g1bc, ALU.mult), waits=[t_g1, t_g2])
    t_cw2 = s.add("vector", lambda q: q.tensor_tensor(
        tscr2[:], tscr2[:], g2bc, ALU.mult), waits=[t_cw1])
    t_cw = s.add("vector", lambda q: q.tensor_tensor(
        cw_tm[:], cw_tm[:], tscr2[:], ALU.add), waits=[t_cw2])

    cw_toks = []
    prevT = {}
    for j in range(NTT):
        w = [t_cw]
        if 0 in prevT:
            w.append(prevT[0])
        tt = s.add("tensor", lambda q, j=j: q.transpose(
            pbC[0:8, 0:128], cw_tm[:, j, :], identf[:]), waits=w)
        te = s.add("scalar", lambda q, j=j: q.copy(
            cwrow[0:8, j * 128:(j + 1) * 128], pbC[0:8, 0:128]), waits=[tt])
        prevT[0] = te
        cw_toks.append(te)
    t_cwr = cw_toks[-1]

    tprev_store = None
    dprev = {}
    for st_i in range(NST):
        t0 = st_i * ST
        for oc in range(NOC_SH):
            buf = oc % 2
            wld = [tmisc]
            if ("s", buf) in dprev:
                wld.append(dprev[("s", buf)])

            def ldw(q, sem, oc=oc, buf=buf):
                q.dma_start(out=wbuf[buf][:],
                            in_=sw1_hbm.rearrange("(c p) h -> p c h", p=128)[
                                :, :, oc * 128:(oc + 1) * 128]).then_inc(sem, 16)
                q.dma_start(out=wbuf3[buf][:],
                            in_=sw3_hbm.rearrange("(c p) h -> p c h", p=128)[
                                :, :, oc * 128:(oc + 1) * 128]).then_inc(sem, 16)
            tw = s.add("gpsimd", ldw, waits=wld, dma=True, n_dma=2)
            bkA = oc % 2
            bkB = 2 + oc % 2
            wA = [tw, t_ln]
            if ("bank", bkA) in dprev:
                wA.append(dprev[("bank", bkA)])

            def mmg(q, oc=oc, bkA=bkA, t0=t0, buf=buf):
                r = None
                for c in range(NCH):
                    r = q.matmul(pb[bkA][:, 0:ST], wbuf[buf][:, c, :],
                                 xn_bf[:, c, t0:t0 + ST],
                                 start=(c == 0), stop=(c == NCH - 1))
                return r
            tg = s.add("tensor", mmg, waits=wA)
            wB = [tw]
            if ("bank", bkB) in dprev:
                wB.append(dprev[("bank", bkB)])

            def mmu(q, oc=oc, bkB=bkB, t0=t0, buf=buf):
                r = None
                for c in range(NCH):
                    r = q.matmul(pb[bkB][:, 0:ST], wbuf3[buf][:, c, :],
                                 xn_bf[:, c, t0:t0 + ST],
                                 start=(c == 0), stop=(c == NCH - 1))
                return r
            tu = s.add("tensor", mmu, waits=wB)
            wsi = [tg]
            if "gscr" in dprev:
                wsi.append(dprev["gscr"])
            tsi = s.add("scalar", lambda q, bkA=bkA: q.activation(
                gscr[:, 0:ST], pb[bkA][:, 0:ST], AF.Silu), waits=wsi)
            dprev[("bank", bkA)] = tsi
            th = s.add("vector", lambda q, oc=oc, bkB=bkB: q.tensor_tensor(
                a_bf[:, oc, 0:ST], gscr[:, 0:ST],
                pb[bkB][:, 0:ST], ALU.mult), waits=[tsi, tu])
            dprev[("bank", bkB)] = th
            dprev["gscr"] = th
            dprev[("s", buf)] = th
        t_a = th
        for oc in range(NCH):
            buf = oc % 2
            wld = [t_a]
            if ("s2", buf) in dprev:
                wld.append(dprev[("s2", buf)])

            def ldw2(q, sem, oc=oc, buf=buf):
                q.dma_start(out=wbuf2[buf][:],
                            in_=sw2_hbm.rearrange("(c p) h -> p c h", p=128)[
                                :, :, oc * 128:(oc + 1) * 128]).then_inc(sem, 16)
            tw2 = s.add("gpsimd", ldw2, waits=wld, dma=True)
            bk = oc % 2
            ww = [tw2, t_a]
            if ("bank", bk) in dprev:
                ww.append(dprev[("bank", bk)])

            def mmd(q, oc=oc, bk=bk, buf=buf):
                r = None
                for ic in range(NOC_SH):
                    r = q.matmul(pb[bk][:, 0:ST], wbuf2[buf][:, ic, :],
                                 a_bf[:, ic, 0:ST],
                                 start=(ic == 0), stop=(ic == NOC_SH - 1))
                return r
            tm = s.add("tensor", mmd, waits=ww)
            ta = s.add("vector", lambda q, oc=oc, bk=bk, t0=t0: q.tensor_tensor(
                x_fm[:, oc, t0:t0 + ST], x_fm[:, oc, t0:t0 + ST],
                pb[bk][:, 0:ST], ALU.add), waits=[tm])
            dprev[("bank", bk)] = ta
            dprev[("s2", buf)] = ta
        t_shared = ta

        for e in range(8):
            wwc = [t_cwr, t_shared]
            if ("bank", 3) in dprev:
                wwc.append(dprev[("bank", 3)])

            def bccw(q, e=e, t0=t0):
                return q.matmul(pb[3][:, 0:ST], sel8[:, e * 128:(e + 1) * 128],
                                cwrow[0:8, t0:t0 + ST], start=True, stop=True)
            tcb = s.add("tensor", bccw, waits=wwc)
            he_toks = []
            for oc in range(NCH):
                buf = oc % 2
                wld = [t_shared]
                if ("ew", buf) in dprev:
                    wld.append(dprev[("ew", buf)])

                def lde1(q, sem, e=e, oc=oc, buf=buf):
                    q.dma_start(out=we1[buf][:],
                                in_=ew1_hbm[e].rearrange("(c p) h -> p c h", p=128)[
                                    :, :, oc * 128:(oc + 1) * 128]).then_inc(sem, 16)
                twe = s.add("gpsimd", lde1, waits=wld, dma=True)
                bkA = oc % 2
                ww = [twe]
                if ("bank", bkA) in dprev:
                    ww.append(dprev[("bank", bkA)])

                def mme1(q, oc=oc, bkA=bkA, t0=t0, buf=buf):
                    r = None
                    for c in range(NCH):
                        r = q.matmul(pb[bkA][:, 0:ST], we1[buf][:, c, :],
                                     xn_bf[:, c, t0:t0 + ST],
                                     start=(c == 0), stop=(c == NCH - 1))
                    return r
                tm1 = s.add("tensor", mme1, waits=ww)
                wge = [tm1]
                if "gscr" in dprev:
                    wge.append(dprev["gscr"])
                tge = s.add("scalar", lambda q, bkA=bkA: q.activation(
                    gscr[:, 0:ST], pb[bkA][:, 0:ST], AF.Gelu), waits=wge)
                dprev[("bank", bkA)] = tge
                the = s.add("vector", lambda q, oc=oc: q.tensor_tensor(
                    he_bf[:, oc, 0:ST], gscr[:, 0:ST],
                    pb[3][:, 0:ST], ALU.mult), waits=[tge, tcb])
                dprev["gscr"] = the
                dprev[("ew", buf)] = the
                he_toks.append(the)
            dprev[("bank", 3)] = he_toks[-1]
            t_he = he_toks[-1]
            for oc in range(NCH):
                buf = oc % 2
                wld = [t_he]
                if ("ew2", buf) in dprev:
                    wld.append(dprev[("ew2", buf)])

                def lde2(q, sem, e=e, oc=oc, buf=buf):
                    q.dma_start(out=we2[buf][:],
                                in_=ew2_hbm[e].rearrange("(c p) h -> p c h", p=128)[
                                    :, :, oc * 128:(oc + 1) * 128]).then_inc(sem, 16)
                twe2 = s.add("gpsimd", lde2, waits=wld, dma=True)
                bk = oc % 2
                ww = [twe2, t_he]
                if ("bank", bk) in dprev:
                    ww.append(dprev[("bank", bk)])

                def mme2(q, oc=oc, bk=bk, buf=buf):
                    r = None
                    for c in range(NCH):
                        r = q.matmul(pb[bk][:, 0:ST], we2[buf][:, c, :],
                                     he_bf[:, c, 0:ST],
                                     start=(c == 0), stop=(c == NCH - 1))
                    return r
                tm2 = s.add("tensor", mme2, waits=ww)
                ta2 = s.add("vector", lambda q, oc=oc, bk=bk, t0=t0: q.tensor_tensor(
                    x_fm[:, oc, t0:t0 + ST],
                    x_fm[:, oc, t0:t0 + ST],
                    pb[bk][:, 0:ST], ALU.add), waits=[tm2])
                dprev[("bank", bk)] = ta2
                dprev[("ew2", buf)] = ta2
        tprev_store = ta2

    t_store = store_fm(b, out_hbm, x_fm, waits=[tprev_store])
    s.add("sync", lambda q: q.nop(), waits=[t_store])
    if sa:
        return b.finish()


def build_l1_ffn(b=None, xin=None, xout=None):
    sa = b is None
    if sa:
        b = Builder("l1ffn")
    x_hbm = xin if xin is not None else b.dram_in("xres", [D, T])
    gw_hbm = b.dram_in("gate_w", [D, 8])
    bwgu_hbm = b.dram_in("base_wgu", [D, 2 * I_FF])
    bwd_hbm = b.dram_in("base_wd", [I_FF, D])
    ewgu_hbm = b.dram_in("exp_wgu", [8, D, 2 * I_FF])
    ewd_hbm = b.dram_in("exp_wd", [8, I_FF, D])
    ident_hbm = b.dram_in("identf", [128, 128])
    sel1h_hbm = b.dram_in("sel8", [8, 8 * 128], BF16)
    out_hbm = xout if xout is not None else b.dram_out("xout", [D, T])

    ST = 1024
    NST = T // ST
    NT8 = ST // 128
    x_st = b.sbuf("x", [128, NCH, ST], F32)
    xn_bf = b.sbuf("xn", [128, NCH, ST], BF16)
    a_bf = b.sbuf("a", [128, NIC_FF, ST], BF16)
    ones_bf = b.sbuf("ones", [128, 128], BF16)
    onesf = b.sbuf("onesf", [128, 8], F32)
    rowscr = b.sbuf("rows", [1, 4 * ST], F32)
    sqscr = b.sbuf("sqscr", [128, 1024], BF16)
    xtmp = b.sbuf("xtmp", [128, 512], F32)
    identf = b.sbuf("identf", [128, 128], F32)
    sel8 = b.sbuf("sel8", [8, 8 * 128], BF16)
    gwT_f = b.sbuf("gwT", [128, NCH, 8], F32)
    cs_sb = b.sbuf("cs", [8, 1], F32)
    lrow = b.sbuf("lrow", [8, ST], F32)
    l_tm = b.sbuf("ltm", [128, NT8, 8], F32)
    p_tm = b.sbuf("ptm", [128, NT8, 8], F32)
    cw_tm = b.sbuf("cwtm", [128, NT8, 8], F32)
    tscr = b.sbuf("tscr", [128, NT8, 8], F32)
    tscr2 = b.sbuf("tscr2", [128, NT8, 8], F32)
    gsc = b.sbuf("gsc", [128, NT8 * 8], F32)
    cwrow = b.sbuf("cwrow", [8, ST], BF16)
    cwbc = b.sbuf("cwbc", [128, ST], F32)
    gscr = b.sbuf("gscr", [128, 1024], F32)
    ytmp = b.sbuf("ytmp", [128, 1024], F32)
    wg = [b.sbuf("wg", [128, NCH, 128], BF16) for _ in range(2)]
    wu = [b.sbuf("wu", [128, NCH, 128], BF16) for _ in range(2)]
    wd = [b.sbuf("wd", [128, NIC_FF, 128], BF16) for _ in range(2)]

    pb = b.alloc_psum_banks(4, F32, 512)
    pbA = b.alloc_psum_banks(1, F32, 512)[0]
    pbB = b.alloc_psum_banks(1, F32, 512)[0]
    pbC = b.alloc_psum_banks(1, F32, 512)[0]
    pbD = b.alloc_psum_banks(1, F32, 512)[0]

    s = b.s

    def ld_misc(q, sem):
        q.dma_start(out=identf[:], in_=ident_hbm).then_inc(sem, 16)
        q.dma_start(out=sel8[:], in_=sel1h_hbm).then_inc(sem, 16)
    tmisc = s.add("sync", ld_misc, dma=True, n_dma=2)
    tgw = s.add("gpsimd", lambda q, sem: q.dma_start(
        out=gwT_f[:], in_=gw_hbm.rearrange("(c p) e -> p c e", p=128)).then_inc(sem, 16),
        dma=True)
    tones = s.add("vector", lambda q: q.memset(ones_bf[:], 1.0))
    tonesf = s.add("vector", lambda q: q.memset(onesf[:], 1.0))

    tcs_mm = s.add("tensor", lambda q: [
        q.matmul(pbB[0:8, 0:1], gwT_f[:, c, :], onesf[:, 0:1],
                 start=(c == 0), stop=(c == NCH - 1)) for c in range(NCH)][-1],
        waits=[tgw, tonesf])
    t_cs = s.add("vector", lambda q: q.tensor_scalar_mul(
        cs_sb[0:8, 0:1], pbB[0:8, 0:1], -1.0), waits=[tcs_mm])

    dprev = {}
    tprev_store = None
    tprev_ln = None
    for st_i in range(NST):
        t0g = st_i * ST
        tld = load_fm(b, x_st, x_hbm[:, t0g:t0g + ST],
                      waits=[t for t in (tprev_store, tprev_ln) if t] or [tmisc])
        t_ln, _ = emit_ln_fm(b, x_st, xn_bf, ones_bf, rowscr, sqscr, xtmp,
                             pbA, pbB, pbC, pbD, [tld, tones], Tn=ST)
        tprev_ln = t_ln
        muR = rowscr[0:1, 0:ST]
        rstdR = rowscr[0:1, ST:2 * ST]
        r_toks = []
        prevR = None
        for ti in range(ST // 512):
            t0 = ti * 512
            w = [t_ln, t_cs]
            if prevR is not None:
                w.append(prevR)

            def mmr(q, t0=t0, t0g=t0g):
                r = None
                for c in range(NCH):
                    r = q.matmul(pbA[0:8, 0:512], gwT_f[:, c, :],
                                 x_st[:, c, t0:t0 + 512],
                                 start=(c == 0), stop=(c == NCH - 1))
                return r
            tr = s.add("tensor", mmr, waits=w)
            te0 = s.add("scalar", lambda q, t0=t0: q.copy(
                lrow[:, t0:t0 + 512], pbA[0:8, 0:512]), waits=[tr])
            tb1 = s.add("tensor", lambda q, t0=t0: q.matmul(
                pbA[0:8, 0:512], onesf[0:1, 0:8],
                muR[0:1, t0:t0 + 512], start=True, stop=True), waits=[te0])
            tb2 = s.add("tensor", lambda q, t0=t0: q.matmul(
                pbB[0:8, 0:512], onesf[0:1, 0:8],
                rstdR[0:1, t0:t0 + 512], start=True, stop=True), waits=[te0, t_cs])
            ts1 = s.add("vector", lambda q, t0=t0: q.scalar_tensor_tensor(
                lrow[:, t0:t0 + 512], pbA[0:8, 0:512], cs_sb[0:8, 0:1],
                lrow[:, t0:t0 + 512], ALU.mult, ALU.add), waits=[tb1])
            ts2 = s.add("vector", lambda q, t0=t0: q.tensor_tensor(
                lrow[:, t0:t0 + 512], lrow[:, t0:t0 + 512], pbB[0:8, 0:512],
                ALU.mult), waits=[tb2, ts1])
            prevR = ts2
            r_toks.append(ts2)
        tp_toks = []
        for j in range(NT8):
            bk = j % 2
            w = list(r_toks)
            if ("tp", bk) in dprev:
                w.append(dprev[("tp", bk)])
            tt = s.add("tensor", lambda q, j=j, bk=bk: q.transpose(
                (pbC if bk == 0 else pbD)[:, 0:8], lrow[0:8, j * 128:(j + 1) * 128],
                identf[0:8, 0:8]), waits=w)
            te = s.add("scalar", lambda q, j=j, bk=bk: q.copy(
                l_tm[:, j, :], (pbC if bk == 0 else pbD)[:, 0:8]), waits=[tt])
            dprev[("tp", bk)] = te
            tp_toks.append(te)
        t_m = s.add("vector", lambda q: q.reduce_max(
            gsc[:, 0:NT8], l_tm[:], axis=AX.X), waits=tp_toks)
        t_e1 = s.add("vector", lambda q: q.tensor_tensor(
            p_tm[:], l_tm[:], gsc[:, 0:NT8].broadcast_to([128, NT8, 8]),
            ALU.subtract), waits=[t_m])
        t_e2 = s.add("scalar", lambda q: q.activation(
            p_tm[:], p_tm[:], AF.Exp), waits=[t_e1])
        t_sm = s.add("vector", lambda q: q.reduce_sum(
            gsc[:, NT8:2 * NT8], p_tm[:], axis=AX.X), waits=[t_e2])
        t_rc = s.add("vector", lambda q: q.reciprocal(
            gsc[:, NT8:2 * NT8], gsc[:, NT8:2 * NT8]), waits=[t_sm])
        t_pn = s.add("vector", lambda q: q.tensor_tensor(
            p_tm[:], p_tm[:], gsc[:, NT8:2 * NT8].broadcast_to([128, NT8, 8]),
            ALU.mult), waits=[t_rc])
        t_m1 = s.add("vector", lambda q: q.reduce_max(
            gsc[:, 2 * NT8:3 * NT8], l_tm[:], axis=AX.X), waits=[t_pn, t_e1])
        t_eq1 = s.add("vector", lambda q: q.tensor_tensor(
            tscr[:], l_tm[:], gsc[:, 2 * NT8:3 * NT8].broadcast_to([128, NT8, 8]),
            ALU.is_ge), waits=[t_m1])
        t_s2 = s.add("vector", lambda q: q.scalar_tensor_tensor(
            tscr2[:], tscr[:], -1e30, l_tm[:], ALU.mult, ALU.add), waits=[t_eq1])
        t_m2 = s.add("vector", lambda q: q.reduce_max(
            gsc[:, 3 * NT8:4 * NT8], tscr2[:], axis=AX.X), waits=[t_s2])
        t_eq2 = s.add("vector", lambda q: q.tensor_tensor(
            tscr2[:], tscr2[:], gsc[:, 3 * NT8:4 * NT8].broadcast_to([128, NT8, 8]),
            ALU.is_ge), waits=[t_m2])
        t_p1 = s.add("vector", lambda q: q.tensor_tensor(
            cw_tm[:], p_tm[:], tscr[:], ALU.mult), waits=[t_eq1, t_pn])
        t_s1s = s.add("vector", lambda q: q.reduce_sum(
            gsc[:, 4 * NT8:5 * NT8], cw_tm[:], axis=AX.X), waits=[t_p1])
        t_p2 = s.add("vector", lambda q: q.tensor_tensor(
            cw_tm[:], p_tm[:], tscr2[:], ALU.mult), waits=[t_eq2, t_s1s])
        t_s2s = s.add("vector", lambda q: q.reduce_sum(
            gsc[:, 5 * NT8:6 * NT8], cw_tm[:], axis=AX.X), waits=[t_p2])
        t_dn = s.add("vector", lambda q: q.tensor_tensor(
            gsc[:, 6 * NT8:7 * NT8], gsc[:, 4 * NT8:5 * NT8],
            gsc[:, 5 * NT8:6 * NT8], ALU.add), waits=[t_s2s])
        t_rc2 = s.add("vector", lambda q: q.reciprocal(
            gsc[:, 6 * NT8:7 * NT8], gsc[:, 6 * NT8:7 * NT8]), waits=[t_dn])
        t_g1 = s.add("vector", lambda q: q.tensor_tensor(
            gsc[:, 4 * NT8:5 * NT8], gsc[:, 4 * NT8:5 * NT8],
            gsc[:, 6 * NT8:7 * NT8], ALU.mult), waits=[t_rc2])
        t_g2 = s.add("vector", lambda q: q.tensor_tensor(
            gsc[:, 5 * NT8:6 * NT8], gsc[:, 5 * NT8:6 * NT8],
            gsc[:, 6 * NT8:7 * NT8], ALU.mult), waits=[t_g1])
        t_cw1 = s.add("vector", lambda q: q.tensor_tensor(
            cw_tm[:], tscr[:], gsc[:, 4 * NT8:5 * NT8].broadcast_to([128, NT8, 8]),
            ALU.mult), waits=[t_g2])
        t_cw2 = s.add("vector", lambda q: q.tensor_tensor(
            tscr2[:], tscr2[:], gsc[:, 5 * NT8:6 * NT8].broadcast_to([128, NT8, 8]),
            ALU.mult), waits=[t_cw1])
        t_cw = s.add("vector", lambda q: q.tensor_tensor(
            cw_tm[:], cw_tm[:], tscr2[:], ALU.add), waits=[t_cw2])
        cw_toks = []
        for j in range(NT8):
            w = [t_cw]
            if ("cw", 0) in dprev:
                w.append(dprev[("cw", 0)])
            tt = s.add("tensor", lambda q, j=j: q.transpose(
                pbC[0:8, 0:128], cw_tm[:, j, :], identf[:]), waits=w)
            te = s.add("scalar", lambda q, j=j: q.copy(
                cwrow[0:8, j * 128:(j + 1) * 128], pbC[0:8, 0:128]), waits=[tt])
            dprev[("cw", 0)] = te
            cw_toks.append(te)
        t_cwr = cw_toks[-1]

        for mi in range(9):
            is_base = (mi == 0)
            e = mi - 1
            wgu_src = bwgu_hbm if is_base else ewgu_hbm[e]
            wd_src = bwd_hbm if is_base else ewd_hbm[e]
            if not is_base:
                wwc = [t_cwr]
                if ("cwb", 0) in dprev:
                    wwc.append(dprev[("cwb", 0)])
                if ("bank", 3) in dprev:
                    wwc.append(dprev[("bank", 3)])
                tcb_toks = []
                for ti in range(2):
                    ttc = s.add("tensor", lambda q, e=e, ti=ti: q.matmul(
                        pb[3][:, 0:512], sel8[:, e * 128:(e + 1) * 128],
                        cwrow[0:8, ti * 512:ti * 512 + 512], start=True, stop=True),
                        waits=wwc if ti == 0 else [tcb_toks[-1]])
                    tte = s.add("vector", lambda q, ti=ti: q.tensor_copy(
                        cwbc[:, ti * 512:ti * 512 + 512], pb[3][:, 0:512]),
                        waits=[ttc])
                    tcb_toks.append(tte)
                t_cwb = tcb_toks[-1]
                dprev[("cwb", 0)] = t_cwb
                dprev[("bank", 3)] = t_cwb
            for oc in range(NIC_FF):
                buf = oc % 2
                wld = [t_ln]
                if ("w", buf) in dprev:
                    wld.append(dprev[("w", buf)])

                def ldw(q, sem, oc=oc, buf=buf, wgu_src=wgu_src):
                    wv = wgu_src.rearrange("(c p) h -> p c h", p=128)
                    q.dma_start(out=wg[buf][:],
                                in_=wv[:, :, oc * 128:(oc + 1) * 128]).then_inc(sem, 16)
                    q.dma_start(out=wu[buf][:],
                                in_=wv[:, :, I_FF + oc * 128:I_FF + (oc + 1) * 128]
                                ).then_inc(sem, 16)
                tw = s.add("gpsimd", ldw, waits=wld, dma=True, n_dma=2)
                for half in range(2):
                    hh = half * 512
                    bkA = half
                    bkB = 2 if half == 0 else 3
                    wA = [tw]
                    if ("bank", bkA) in dprev:
                        wA.append(dprev[("bank", bkA)])

                    def mmg(q, oc=oc, hh=hh, bkA=bkA, buf=buf):
                        r = None
                        for c in range(NCH):
                            r = q.matmul(pb[bkA][:, 0:512], wg[buf][:, c, :],
                                         xn_bf[:, c, hh:hh + 512],
                                         start=(c == 0), stop=(c == NCH - 1))
                        return r
                    tg = s.add("tensor", mmg, waits=wA)
                    wB = [tw]
                    if ("bank", bkB) in dprev:
                        wB.append(dprev[("bank", bkB)])

                    def mmu(q, oc=oc, hh=hh, bkB=bkB, buf=buf):
                        r = None
                        for c in range(NCH):
                            r = q.matmul(pb[bkB][:, 0:512], wu[buf][:, c, :],
                                         xn_bf[:, c, hh:hh + 512],
                                         start=(c == 0), stop=(c == NCH - 1))
                        return r
                    tu = s.add("tensor", mmu, waits=wB)
                    wsi = [tg]
                    if ("gscr", hh) in dprev:
                        wsi.append(dprev[("gscr", hh)])
                    tsi = s.add("scalar", lambda q, hh=hh, bkA=bkA: q.activation(
                        gscr[:, hh:hh + 512], pb[bkA][:, 0:512], AF.Silu), waits=wsi)
                    dprev[("bank", bkA)] = tsi
                    th = s.add("vector", lambda q, oc=oc, hh=hh, bkB=bkB: q.tensor_tensor(
                        a_bf[:, oc, hh:hh + 512], gscr[:, hh:hh + 512],
                        pb[bkB][:, 0:512], ALU.mult), waits=[tsi, tu])
                    dprev[("bank", bkB)] = th
                    dprev[("gscr", hh)] = th
                dprev[("w", buf)] = th
            t_a = th
            for oc in range(NCH):
                buf = oc % 2
                wld = [t_a]
                if ("wd", buf) in dprev:
                    wld.append(dprev[("wd", buf)])

                def ldwd(q, sem, oc=oc, buf=buf, wd_src=wd_src):
                    q.dma_start(out=wd[buf][:],
                                in_=wd_src.rearrange("(c p) h -> p c h", p=128)[
                                    :, :, oc * 128:(oc + 1) * 128]).then_inc(sem, 16)
                twd = s.add("gpsimd", ldwd, waits=wld, dma=True)
                for half in range(2):
                    hh = half * 512
                    bk = half
                    ww = [twd]
                    if ("bank", bk) in dprev:
                        ww.append(dprev[("bank", bk)])

                    def mmd(q, oc=oc, hh=hh, bk=bk, buf=buf):
                        r = None
                        for ic in range(NIC_FF):
                            r = q.matmul(pb[bk][:, 0:512], wd[buf][:, ic, :],
                                         a_bf[:, ic, hh:hh + 512],
                                         start=(ic == 0), stop=(ic == NIC_FF - 1))
                        return r
                    tm = s.add("tensor", mmd, waits=ww)
                    if is_base:
                        ta = s.add("vector", lambda q, oc=oc, hh=hh, bk=bk: q.tensor_tensor(
                            x_st[:, oc, hh:hh + 512], x_st[:, oc, hh:hh + 512],
                            pb[bk][:, 0:512], ALU.add), waits=[tm])
                    else:
                        ty = s.add("vector", lambda q, oc=oc, hh=hh, bk=bk: q.tensor_tensor(
                            ytmp[:, hh:hh + 512], pb[bk][:, 0:512],
                            cwbc[:, hh:hh + 512], ALU.mult), waits=[tm, t_cwb])
                        ta = s.add("vector", lambda q, oc=oc, hh=hh: q.tensor_tensor(
                            x_st[:, oc, hh:hh + 512], x_st[:, oc, hh:hh + 512],
                            ytmp[:, hh:hh + 512], ALU.add), waits=[ty])
                    dprev[("bank", bk)] = ta
                dprev[("wd", buf)] = ta
        tprev_store = store_fm(b, out_hbm[:, t0g:t0g + ST], x_st,
                               waits=[ta])
    s.add("sync", lambda q: q.nop(), waits=[tprev_store])
    if sa:
        return b.finish()


def build_l2_attn(b=None, xin=None, xout=None):
    sa = b is None
    if sa:
        b = Builder("l2attn")
    NSS = 16
    TST = 512
    NSTT = T // TST
    x_hbm = xin if xin is not None else b.dram_in("xres", [D, T])
    lng_hbm = b.dram_in("ln_g", [128, NCH])
    lnb_hbm = b.dram_in("ln_b", [128, NCH])
    win_hbm = b.dram_in("w_in", [D, D])
    bin_hbm = b.dram_in("b_in", [128, NCH])
    convw_hbm = b.dram_in("conv_w", [128, NCH * 4])
    convb_hbm = b.dram_in("conv_b", [128, NCH])
    alog_hbm = b.dram_in("A_log", [D, NSS])
    dssm_hbm = b.dram_in("D_ssm", [128, NCH])
    wdt_hbm = b.dram_in("w_dt", [D, D])
    bdt_hbm = b.dram_in("b_dt", [128, NCH])
    wbc_hbm = b.dram_in("w_bc", [D, 2 * NSS])
    wout_hbm = b.dram_in("w_out", [D, D])
    bout_hbm = b.dram_in("b_out", [128, NCH])
    sel16_hbm = b.dram_in("sel16", [2 * NSS, 2 * NSS * 128], BF16)
    out_hbm = xout if xout is not None else b.dram_out("xout", [D, T])

    x_st = b.sbuf("x", [128, NCH, TST], F32)
    xn_bf = b.sbuf("xn", [128, NCH, TST], BF16)
    left_f = b.sbuf("left", [128, NCH, TST + 3], F32)
    xc_bf = b.sbuf("xc", [128, NCH, TST], BF16)
    dt_bf = b.sbuf("dt", [128, NCH, TST], BF16)
    w2_bf = b.sbuf("w2", [128, NCH, TST], BF16)
    y_acc = b.sbuf("y", [128, NCH, TST], F32)
    carry = b.sbuf("carry", [128, NSS * NCH], F32)
    bcrow = b.sbuf("bcrow", [2 * NSS, TST], BF16)
    btbc = [b.sbuf("btbc", [128, TST], BF16) for _ in range(2)]
    ctbc = [b.sbuf("ctbc", [128, TST], BF16) for _ in range(2)]
    dA_s = [b.sbuf("dAs", [128, TST], BF16) for _ in range(2)]
    u_s = [b.sbuf("us", [128, TST], BF16) for _ in range(2)]
    s_s = [b.sbuf("ss", [128, TST], BF16) for _ in range(2)]
    w_s = [b.sbuf("ws", [128, TST], BF16) for _ in range(2)]
    gscr = b.sbuf("gscr", [128, TST], F32)
    A_sb = b.sbuf("A", [128, NCH, NSS], F32)
    lng = b.sbuf("lng", [128, NCH], F32)
    lnb = b.sbuf("lnb", [128, NCH], F32)
    bin_ = b.sbuf("bin", [128, NCH], F32)
    convw = b.sbuf("convw", [128, NCH * 4], F32)
    convb = b.sbuf("convb", [128, NCH], F32)
    dssm = b.sbuf("dssm", [128, NCH], F32)
    bdt = b.sbuf("bdt", [128, NCH], F32)
    bout = b.sbuf("bout", [128, NCH], F32)
    sel16 = b.sbuf("sel16", [2 * NSS, 2 * NSS * 128], BF16)
    winT = b.sbuf("winT", [128, NCH, D], BF16)
    wdtT = b.sbuf("wdtT", [128, NCH, D], BF16)
    wbcT = b.sbuf("wbcT", [128, NCH, 2 * NSS], BF16)
    woutT = b.sbuf("woutT", [128, NCH, D], BF16)
    ones_bf = b.sbuf("ones", [128, 128], BF16)
    rowscr = b.sbuf("rows", [1, 6 * TST], F32)
    xn0_f = b.sbuf("xn0", [128, NCH, TST], F32)
    sqscr = b.sbuf("sqscr", [128, 1024], BF16)
    xtmp = b.sbuf("xtmp", [128, 512], F32)

    pb = b.alloc_psum_banks(4, F32, 512)
    pbA = b.alloc_psum_banks(1, F32, 512)[0]
    pbB = b.alloc_psum_banks(1, F32, 512)[0]
    pbC = b.alloc_psum_banks(1, F32, 512)[0]
    pbD = b.alloc_psum_banks(1, F32, 512)[0]

    s = b.s

    def ld_misc(q, sem):
        for dst, src in ((lng, lng_hbm), (lnb, lnb_hbm), (bin_, bin_hbm),
                         (convw, convw_hbm), (convb, convb_hbm),
                         (dssm, dssm_hbm), (bdt, bdt_hbm), (bout, bout_hbm)):
            q.dma_start(out=dst[:], in_=src).then_inc(sem, 16)
        q.dma_start(out=sel16[:], in_=sel16_hbm).then_inc(sem, 16)
    tmisc = s.add("sync", ld_misc, dma=True, n_dma=9)

    def ld_w(q, sem):
        q.dma_start(out=winT[:], in_=win_hbm.rearrange("(c p) o -> p c o", p=128)).then_inc(sem, 16)
        q.dma_start(out=wdtT[:], in_=wdt_hbm.rearrange("(c p) o -> p c o", p=128)).then_inc(sem, 16)
        q.dma_start(out=wbcT[:], in_=wbc_hbm.rearrange("(c p) o -> p c o", p=128)).then_inc(sem, 16)
        q.dma_start(out=woutT[:], in_=wout_hbm.rearrange("(c p) o -> p c o", p=128)).then_inc(sem, 16)
        q.dma_start(out=A_sb[:], in_=alog_hbm.rearrange("(c p) n -> p c n", p=128)).then_inc(sem, 16)
    tw = s.add("gpsimd", ld_w, dma=True, n_dma=5)

    tones = s.add("vector", lambda q: q.memset(ones_bf[:], 1.0))
    tcz = s.add("vector", lambda q: q.memset(carry[:], 0.0))
    tez = s.add("vector", lambda q: q.memset(left_f[:, :, 0:3], 0.0))
    one_ap, one_tok = b.cst(1.0)
    ta1 = s.add("scalar", lambda q: q.activation(A_sb[:], A_sb[:], AF.Exp),
                waits=[tw])
    ta2 = s.add("vector", lambda q: q.tensor_scalar_mul(A_sb[:], A_sb[:], -1.0),
                waits=[ta1])

    tprev_store = None
    dprev = {}
    t_lnprev = None
    t_edge = {}
    t_scan_done = tcz
    for st_i in range(NSTT):
        t0g = st_i * TST
        tld = load_fm(b, x_st, x_hbm[:, t0g:t0g + TST],
                      waits=[t for t in (tprev_store, t_lnprev) if t] or [tmisc])
        t_ln, _ = emit_ln_fm(b, x_st, xn_bf, ones_bf, rowscr, sqscr, xtmp,
                             pbA, pbB, pbC, pbD, [tld, tones, tmisc], Tn=TST,
                             gain=lng, bias2=lnb, double_ln=True,
                             out0_f32=xn0_f, bankE=pb[3])
        t_lnprev = t_ln
        lf_toks = []
        for oc in range(NCH):
            bk = oc % 2
            ww = [t_ln, tw]
            if ("bank", bk) in dprev:
                ww.append(dprev[("bank", bk)])
            if oc in t_edge:
                ww.append(t_edge[oc])

            def mmi(q, oc=oc, bk=bk):
                r = None
                for c in range(NCH):
                    r = q.matmul(pb[bk][:, 0:TST], winT[:, c, oc * 128:(oc + 1) * 128],
                                 xn_bf[:, c, :], start=(c == 0), stop=(c == NCH - 1))
                return r
            tm = s.add("tensor", mmi, waits=ww)
            te = s.add("scalar", lambda q, oc=oc, bk=bk: q.activation(
                left_f[:, oc, 3:TST + 3], pb[bk][:, 0:TST], AF.Identity,
                bias=bin_[:, oc:oc + 1]), waits=[tm])
            dprev[("bank", bk)] = te
            lf_toks.append(te)
        xc_toks = []
        for c in range(NCH):
            wv = lf_toks + [tez] + ([xc_toks[-1]] if xc_toks else [])
            t1 = s.add("vector", lambda q, c=c: q.tensor_scalar_mul(
                gscr[:, 0:TST], left_f[:, c, 0:TST], convw[:, 4 * c:4 * c + 1]),
                waits=wv)
            for k in range(1, 4):
                t1 = s.add("vector", lambda q, c=c, k=k: q.scalar_tensor_tensor(
                    gscr[:, 0:TST], left_f[:, c, k:TST + k],
                    convw[:, 4 * c + k:4 * c + k + 1], gscr[:, 0:TST],
                    ALU.mult, ALU.add), waits=[t1])
            t2 = s.add("scalar", lambda q, c=c: q.activation(
                xc_bf[:, c, :], gscr[:, 0:TST], AF.Silu, bias=convb[:, c:c + 1]),
                waits=[t1])
            t3 = s.add("vector", lambda q, c=c: q.tensor_copy(
                left_f[:, c, 0:3], left_f[:, c, TST:TST + 3]), waits=[t1])
            t_edge[c] = t3
            xc_toks.append(t2)
        dt_toks = []
        for oc in range(NCH):
            bk = oc % 2
            ww = list(xc_toks) + [tw]
            if ("bank", bk) in dprev:
                ww.append(dprev[("bank", bk)])

            def mmdt(q, oc=oc, bk=bk):
                r = None
                for c in range(NCH):
                    r = q.matmul(pb[bk][:, 0:TST], wdtT[:, c, oc * 128:(oc + 1) * 128],
                                 xc_bf[:, c, :], start=(c == 0), stop=(c == NCH - 1))
                return r
            tm = s.add("tensor", mmdt, waits=ww)
            tso = s.add("scalar", lambda q, oc=oc, bk=bk: q.activation(
                gscr[:, 0:TST], pb[bk][:, 0:TST], AF.Exp, bias=bdt[:, oc:oc + 1]),
                waits=[tm] + ([dt_toks[-1]] if dt_toks else []))
            tso2 = s.add("scalar", lambda q, oc=oc: q.activation(
                dt_bf[:, oc, :], gscr[:, 0:TST], AF.Ln, bias=one_ap), waits=[tso, one_tok])
            dprev[("bank", bk)] = tso
            dt_toks.append(tso2)
        ww = [tw] + list(xc_toks)
        if ("bank", 2) in dprev:
            ww.append(dprev[("bank", 2)])

        def mmbc(q):
            r = None
            for c in range(NCH):
                r = q.matmul(pb[2][0:2 * NSS, 0:TST], wbcT[:, c, :],
                             xc_bf[:, c, :], start=(c == 0), stop=(c == NCH - 1))
            return r
        tmbc = s.add("tensor", mmbc, waits=ww)
        tebc = s.add("scalar", lambda q: q.copy(
            bcrow[:, 0:TST], pb[2][0:2 * NSS, 0:TST]), waits=[tmbc])
        dprev[("bank", 2)] = tebc
        w2_toks = []
        for c in range(NCH):
            twv = s.add("vector", lambda q, c=c: q.tensor_tensor(
                w2_bf[:, c, :], dt_bf[:, c, :], xc_bf[:, c, :], ALU.mult),
                waits=[dt_toks[c]] + ([w2_toks[-1]] if w2_toks else []))
            w2_toks.append(twv)
        tyz = s.add("vector", lambda q: q.memset(y_acc[:], 0.0), waits=w2_toks)
        guard = dprev.setdefault("l2sg", {})
        for n in range(NSS):
            np_ = n % 2
            bsel = pb[2 + np_]
            wb1 = [tebc] + list(guard.get(("pb", np_), ()))
            tb1 = s.add("tensor", lambda q, n=n, bsel=bsel: q.matmul(
                bsel[:, 0:TST], sel16[:, n * 128:(n + 1) * 128], bcrow[:, 0:TST],
                start=True, stop=True), waits=wb1)
            te1 = s.add("scalar", lambda q, np_=np_, bsel=bsel: q.copy(
                btbc[np_][:, 0:TST], bsel[:, 0:TST]),
                waits=[tb1] + list(guard.get(("bt", np_), ())))
            tb2 = s.add("tensor", lambda q, n=n, bsel=bsel: q.matmul(
                bsel[:, 0:TST], sel16[:, (NSS + n) * 128:(NSS + n + 1) * 128],
                bcrow[:, 0:TST], start=True, stop=True), waits=[te1])
            te2 = s.add("scalar", lambda q, np_=np_, bsel=bsel: q.copy(
                ctbc[np_][:, 0:TST], bsel[:, 0:TST]), waits=[tb2])
            guard[("pb", np_)] = (te2,)
            nreaders = []
            for c in range(NCH):
                col = n * NCH + c
                i2 = (n * NCH + c) % 2
                tda = s.add("scalar", lambda q, n=n, c=c, i2=i2: q.activation(
                    dA_s[i2][:], dt_bf[:, c, :], AF.Exp, scale=A_sb[:, c, n:n + 1]),
                    waits=dt_toks + [ta2] + list(guard.get(("da", i2), ())))
                tu = s.add("gpsimd", lambda q, n=n, c=c, i2=i2, np_=np_: q.tensor_tensor(
                    u_s[i2][:], w2_bf[:, c, :], btbc[np_][:, 0:TST], ALU.mult),
                    waits=[te1] + w2_toks + list(guard.get(("u", i2), ())))
                tsc = s.add("vector", lambda q, col=col, i2=i2: q.tensor_tensor_scan(
                    s_s[i2][:], dA_s[i2][:], u_s[i2][:], carry[:, col:col + 1],
                    ALU.mult, ALU.add),
                    waits=[tda, tu, tcz] + list(guard.get(("s", i2), ())))
                guard[("da", i2)] = (tsc,)
                guard[("u", i2)] = (tsc,)
                tcar = s.add("gpsimd", lambda q, col=col, i2=i2: q.tensor_copy(
                    carry[:, col:col + 1], s_s[i2][:, TST - 1:TST]), waits=[tsc])
                twm = s.add("vector", lambda q, i2=i2, np_=np_: q.tensor_tensor(
                    w_s[i2][:], s_s[i2][:], ctbc[np_][:, 0:TST], ALU.mult),
                    waits=[tsc, te2] + list(guard.get(("w", i2), ())))
                guard[("s", i2)] = (twm, tcar)
                tya = s.add("vector", lambda q, c=c, i2=i2: q.tensor_tensor(
                    y_acc[:, c, :], y_acc[:, c, :], w_s[i2][:], ALU.add),
                    waits=[twm, tyz])
                guard[("w", i2)] = (tya,)
                nreaders = [tu, twm]
            guard[("bt", np_)] = tuple(nreaders)
            t_scan_done = tya
        dprev[("bank", 2)] = guard[("pb", 0)][0]
        dprev[("bank", 3)] = guard[("pb", 1)][0]
        yg_toks = []
        for c in range(NCH):
            t1 = s.add("vector", lambda q, c=c: q.scalar_tensor_tensor(
                y_acc[:, c, :], xc_bf[:, c, :], dssm[:, c:c + 1], y_acc[:, c, :],
                ALU.mult, ALU.add), waits=[t_scan_done])
            t2 = s.add("scalar", lambda q, c=c: q.activation(
                gscr[:, 0:TST], xn_bf[:, c, :], AF.Silu),
                waits=[t1] + ([yg_toks[-1]] if yg_toks else []))
            t3 = s.add("vector", lambda q, c=c: q.tensor_tensor(
                xn_bf[:, c, :], y_acc[:, c, :], gscr[:, 0:TST], ALU.mult),
                waits=[t2])
            yg_toks.append(t3)
        add_toks = []
        for oc in range(NCH):
            bk = oc % 2
            ww = list(yg_toks) + [tw]
            if ("bank", bk) in dprev:
                ww.append(dprev[("bank", bk)])

            def mmo(q, oc=oc, bk=bk):
                r = None
                for c in range(NCH):
                    r = q.matmul(pb[bk][:, 0:TST], woutT[:, c, oc * 128:(oc + 1) * 128],
                                 xn_bf[:, c, :], start=(c == 0), stop=(c == NCH - 1))
                return r
            tm = s.add("tensor", mmo, waits=ww)
            ta0 = s.add("vector", lambda q, oc=oc: q.tensor_tensor(
                x_st[:, oc, :], x_st[:, oc, :], xn0_f[:, oc, :], ALU.add),
                waits=[t_scan_done])
            ta = s.add("vector", lambda q, oc=oc, bk=bk: q.scalar_tensor_tensor(
                x_st[:, oc, :], pb[bk][:, 0:TST], bout[:, oc:oc + 1],
                x_st[:, oc, :], ALU.add, ALU.add), waits=[tm, ta0])
            dprev[("bank", bk)] = ta
            add_toks.append(ta)
        tprev_store = store_fm(b, out_hbm[:, t0g:t0g + TST], x_st, waits=add_toks)
    s.add("sync", lambda q: q.nop(), waits=[tprev_store])
    if sa:
        return b.finish()


def build_l0_attn(b=None, xin=None, xout=None):
    sa = b is None
    if sa:
        b = Builder("l0attn")
    NS0 = 32
    TST = 512
    NSTT = T // TST
    x_hbm = xin if xin is not None else b.dram_in("xres", [D, T])
    inwT_hbm = b.dram_in("in_wT", [D, 2 * D])
    inb_hbm = b.dram_in("in_b", [128, 16])
    convw_hbm = b.dram_in("conv_w", [128, NCH * 4])
    convb_hbm = b.dram_in("conv_b", [128, NCH])
    alog_hbm = b.dram_in("A_log", [D, NS0])
    bwT_hbm = b.dram_in("B_wT", [D, NS0])
    bb_hbm = b.dram_in("B_b", [NS0, 1])
    cwT_hbm = b.dram_in("C_wT", [D, NS0])
    cb_hbm = b.dram_in("C_b", [NS0, 1])
    outwT_hbm = b.dram_in("out_wT", [D, D])
    outb_hbm = b.dram_in("out_b", [128, NCH])
    lnw_hbm = b.dram_in("ln_w", [128, NCH])
    lnb_hbm = b.dram_in("ln_b", [128, NCH])
    ident_hbm = b.dram_in("identf", [128, 128])
    sel32_hbm = b.dram_in("sel32", [NS0, NS0 * 128], BF16)
    out_hbm = xout if xout is not None else b.dram_out("xout", [D, T])

    x_sl = [b.sbuf("x", [128, NCH, TST], F32) for _ in range(2)]
    xnh_sl = [b.sbuf("xnh", [128, NCH, TST], BF16) for _ in range(2)]
    xnl_sl = [b.sbuf("xnl", [128, NCH, TST], BF16) for _ in range(2)]
    xa_sl = [b.sbuf("xa", [128, NCH, TST + 3], BF16) for _ in range(2)]
    z_sl = [b.sbuf("z", [128, NCH, TST], BF16) for _ in range(2)]
    xc_bf = b.sbuf("xc", [128, NCH, TST], BF16)
    y_acc = b.sbuf("y", [128, NCH, TST], F32)
    h_st = b.sbuf("h", [128, NCH, TST], F32)
    u_out = b.sbuf("uout", [128, NCH, TST], F32)
    carry = b.sbuf("carry", [128, NS0 * NCH], F32)
    bm_row = b.sbuf("bmrow", [NS0, TST], BF16)
    cm_row = b.sbuf("cmrow", [NS0, TST], BF16)
    btbc = [b.sbuf("btbc", [128, TST], BF16) for _ in range(2)]
    ctbc = [b.sbuf("ctbc", [128, TST], BF16) for _ in range(2)]
    u_s = [b.sbuf("us", [128, TST], BF16) for _ in range(2)]
    s_s = [b.sbuf("ss", [128, TST], BF16) for _ in range(2)]
    w_s = [b.sbuf("ws", [128, TST], BF16) for _ in range(2)]
    gscr = b.sbuf("gscr", [128, TST], F32)
    loscr = b.sbuf("loscr", [128, 512], F32)
    A_sb = b.sbuf("A", [128, NCH, NS0], F32)
    inb = b.sbuf("inb", [128, 16], F32)
    convw = b.sbuf("convw", [128, NCH * 4], F32)
    convb = b.sbuf("convb", [128, NCH], F32)
    bb = b.sbuf("bb", [NS0, 1], F32)
    cb = b.sbuf("cb", [NS0, 1], F32)
    outb = b.sbuf("outb", [128, NCH], F32)
    lnw = b.sbuf("lnw", [128, NCH], F32)
    lnb2 = b.sbuf("lnb", [128, NCH], F32)
    sel32 = b.sbuf("sel32", [NS0, NS0 * 128], BF16)
    identf = b.sbuf("identf", [128, 128], F32)
    ones_bf = b.sbuf("ones", [128, 128], BF16)
    rowscr = b.sbuf("rows", [1, 4 * TST], F32)
    sqscr = b.sbuf("sqscr", [128, 1024], BF16)
    xtmp = b.sbuf("xtmp", [128, 512], F32)
    wbuf = [b.sbuf("wb", [128, NCH, 128], BF16) for _ in range(2)]
    wobuf = [b.sbuf("wo", [128, NCH, 128], BF16) for _ in range(2)]
    bcw = b.sbuf("bcw", [128, NCH, 2 * NS0], BF16)

    pb = b.alloc_psum_banks(4, F32, 512)
    pbA = b.alloc_psum_banks(1, F32, 512)[0]
    pbB = b.alloc_psum_banks(1, F32, 512)[0]
    pbC = b.alloc_psum_banks(1, F32, 512)[0]
    pbD = b.alloc_psum_banks(1, F32, 512)[0]

    s = b.s

    def ld_misc(q, sem):
        for dst, src in ((inb, inb_hbm), (convw, convw_hbm), (convb, convb_hbm),
                         (bb, bb_hbm), (cb, cb_hbm), (outb, outb_hbm),
                         (lnw, lnw_hbm), (lnb2, lnb_hbm), (identf, ident_hbm)):
            q.dma_start(out=dst[:], in_=src).then_inc(sem, 16)
        q.dma_start(out=sel32[:], in_=sel32_hbm).then_inc(sem, 16)
    tmisc = s.add("sync", ld_misc, dma=True, n_dma=10)

    def ld_w(q, sem):
        q.dma_start(out=A_sb[:], in_=alog_hbm.rearrange("(c p) n -> p c n", p=128)).then_inc(sem, 16)
        q.dma_start(out=bcw[:, :, 0:NS0],
                    in_=bwT_hbm.rearrange("(c p) n -> p c n", p=128)).then_inc(sem, 16)
        q.dma_start(out=bcw[:, :, NS0:2 * NS0],
                    in_=cwT_hbm.rearrange("(c p) n -> p c n", p=128)).then_inc(sem, 16)
    tw = s.add("gpsimd", ld_w, dma=True, n_dma=3)

    tones = s.add("vector", lambda q: q.memset(ones_bf[:], 1.0))
    tcz = s.add("vector", lambda q: q.memset(carry[:], 0.0))
    tez = [s.add("vector", lambda q, sl=sl: q.memset(xa_sl[sl][:, :, 0:3], 0.0))
           for sl in range(2)]
    tez2 = [s.add("vector", lambda q, sl=sl: q.memset(
        xa_sl[sl][:, :, TST + 2:TST + 3], 0.0)) for sl in range(2)]
    ta1 = s.add("vector", lambda q: q.tensor_scalar(
        A_sb[:], A_sb[:], -5.0, 2.0, ALU.max, ALU.min), waits=[tw])
    ta2 = s.add("scalar", lambda q: q.activation(A_sb[:], A_sb[:], AF.Exp),
                waits=[ta1])
    ta3 = s.add("vector", lambda q: q.tensor_scalar_mul(A_sb[:], A_sb[:], -1.0),
                waits=[ta2])

    dprev = {}
    t_inproj = {}
    t_scan_done = tcz
    tprev_store = None
    t_front_prev = None

    def front(st_i):
        nonlocal t_front_prev
        sl = st_i % 2
        t0g = st_i * TST
        wv = []
        if st_i >= 2:
            wv.append(t_inproj[st_i - 2][1])
        tld = load_fm(b, x_sl[sl], x_hbm[:, t0g:t0g + TST],
                      waits=wv + ([t_front_prev] if t_front_prev else [tmisc]))
        t_ln, _ = emit_ln_fm(b, x_sl[sl], xnh_sl[sl], ones_bf, rowscr, sqscr,
                             xtmp, pbA, pbB, pbC, pbD,
                             [tld, tones] + tez + tez2, Tn=TST,
                             out_lo=xnl_sl[sl], loscr=loscr)
        toks = []
        for oc in range(16):
            buf = oc % 2
            wld = [t_ln]
            if ("wi", buf) in dprev:
                wld.append(dprev[("wi", buf)])

            def ldw(q, sem, oc=oc, buf=buf):
                q.dma_start(out=wbuf[buf][:],
                            in_=inwT_hbm.rearrange("(c p) o -> p c o", p=128)[
                                :, :, oc * 128:(oc + 1) * 128]).then_inc(sem, 16)
            twl = s.add("gpsimd", ldw, waits=wld, dma=True)
            bk = oc % 2
            ww = [twl, t_ln]
            if ("bank", bk) in dprev:
                ww.append(dprev[("bank", bk)])

            def mmi(q, oc=oc, bk=bk, buf=buf, sl=sl):
                r = None
                for c in range(NCH):
                    r = q.matmul(pb[bk][:, 0:TST], wbuf[buf][:, c, :],
                                 xnh_sl[sl][:, c, :], start=(c == 0), stop=(c == NCH - 1))
                return r
            tm = s.add("tensor", mmi, waits=ww)
            if oc < NCH:
                dst = xa_sl[sl][:, oc, 2:TST + 2]
            else:
                dst = z_sl[sl][:, oc - NCH, :]
            te = s.add("scalar", lambda q, oc=oc, bk=bk, dst=dst: q.activation(
                dst, pb[bk][:, 0:TST], AF.Identity, bias=inb[:, oc:oc + 1]),
                waits=[tm])
            dprev[("bank", bk)] = te
            dprev[("wi", buf)] = te
            toks.append(te)
        tE = toks[-1]
        if st_i >= 1:
            psl = (st_i - 1) % 2
            tE1 = s.add("vector", lambda q, sl=sl, psl=psl: q.tensor_copy(
                xa_sl[psl][:, :, TST + 2:TST + 3], xa_sl[sl][:, :, 2:3]),
                waits=toks + [t_inproj[st_i - 1][0]])
            tE2 = s.add("vector", lambda q, sl=sl, psl=psl: q.tensor_copy(
                xa_sl[sl][:, :, 0:2], xa_sl[psl][:, :, TST:TST + 2]),
                waits=toks + [t_inproj[st_i - 1][0]])
            tE = tE2
            t_inproj[st_i - 1] = (tE1, t_inproj[st_i - 1][1])
        if st_i == NSTT - 1:
            tE = s.add("vector", lambda q, sl=sl: q.memset(
                xa_sl[sl][:, :, TST + 2:TST + 3], 0.0), waits=toks + [tE])
        t_inproj[st_i] = (tE, None)
        t_front_prev = tE
        return tE

    def back(st_i):
        nonlocal t_scan_done, tprev_store
        sl = st_i % 2
        t0g = st_i * TST
        t_edge_ready = t_inproj[st_i][0]
        xc_toks = []
        for c in range(NCH):
            wv = [t_edge_ready] + ([xc_toks[-1]] if xc_toks else [])
            t1 = s.add("vector", lambda q, c=c, sl=sl: q.tensor_scalar_mul(
                gscr[:, 0:TST], xa_sl[sl][:, c, 0:TST], convw[:, 4 * c:4 * c + 1]),
                waits=wv)
            for k in range(1, 4):
                t1 = s.add("vector", lambda q, c=c, k=k, sl=sl: q.scalar_tensor_tensor(
                    gscr[:, 0:TST], xa_sl[sl][:, c, k:TST + k],
                    convw[:, 4 * c + k:4 * c + k + 1], gscr[:, 0:TST],
                    ALU.mult, ALU.add), waits=[t1])
            t2 = s.add("scalar", lambda q, c=c: q.activation(
                xc_bf[:, c, :], gscr[:, 0:TST], AF.Silu, bias=convb[:, c:c + 1]),
                waits=[t1])
            xc_toks.append(t2)
        ww = [tw] + xc_toks
        if ("bank", 2) in dprev:
            ww.append(dprev[("bank", 2)])

        def mmbc(q):
            r = None
            for c in range(NCH):
                r = q.matmul(pb[2][0:NS0, 0:TST], bcw[:, c, 0:NS0],
                             xc_bf[:, c, :], start=(c == 0), stop=(c == NCH - 1))
            return r
        tb1 = s.add("tensor", mmbc, waits=ww)
        te1 = s.add("scalar", lambda q: q.activation(
            bm_row[:, 0:TST], pb[2][0:NS0, 0:TST], AF.Identity, bias=bb[0:NS0, 0:1]),
            waits=[tb1])

        def mmcc(q):
            r = None
            for c in range(NCH):
                r = q.matmul(pb[2][0:NS0, 0:TST], bcw[:, c, NS0:2 * NS0],
                             xc_bf[:, c, :], start=(c == 0), stop=(c == NCH - 1))
            return r
        tb2 = s.add("tensor", mmcc, waits=[te1])
        te2 = s.add("scalar", lambda q: q.activation(
            cm_row[:, 0:TST], pb[2][0:NS0, 0:TST], AF.Identity, bias=cb[0:NS0, 0:1]),
            waits=[tb2])
        dprev[("bank", 2)] = te2
        tyz = s.add("vector", lambda q: q.memset(y_acc[:], 0.0),
                    waits=[t_scan_done])
        guard = dprev.setdefault("l0sg", {})
        for n in range(NS0):
            np_ = n % 2
            bsel = pb[2 + np_]
            wb1 = [te2] + list(guard.get(("pb", np_), ()))
            tbb = s.add("tensor", lambda q, n=n, bsel=bsel: q.matmul(
                bsel[:, 0:TST], sel32[:, n * 128:(n + 1) * 128], bm_row[:, 0:TST],
                start=True, stop=True), waits=wb1)
            teb = s.add("scalar", lambda q, np_=np_, bsel=bsel: q.copy(
                btbc[np_][:, 0:TST], bsel[:, 0:TST]),
                waits=[tbb] + list(guard.get(("bt", np_), ())))
            tbc_ = s.add("tensor", lambda q, n=n, bsel=bsel: q.matmul(
                bsel[:, 0:TST], sel32[:, n * 128:(n + 1) * 128], cm_row[:, 0:TST],
                start=True, stop=True), waits=[teb, te2])
            tec = s.add("scalar", lambda q, np_=np_, bsel=bsel: q.copy(
                ctbc[np_][:, 0:TST], bsel[:, 0:TST]), waits=[tbc_])
            guard[("pb", np_)] = (tec,)
            nreaders = ()
            for c in range(NCH):
                col = n * NCH + c
                i2 = (n * NCH + c) % 2
                tu = s.add("gpsimd", lambda q, c=c, i2=i2, np_=np_: q.tensor_tensor(
                    u_s[i2][:], xc_bf[:, c, :], btbc[np_][:, 0:TST], ALU.mult),
                    waits=[teb] + xc_toks + list(guard.get(("u", i2), ())))
                tsc = s.add("vector", lambda q, col=col, c=c, n=n, i2=i2: q.tensor_tensor_scan(
                    s_s[i2][:], A_sb[:, c, n:n + 1].broadcast_to([128, TST]),
                    u_s[i2][:], carry[:, col:col + 1], ALU.mult, ALU.add),
                    waits=[tu, tcz, ta3] + list(guard.get(("s", i2), ())))
                guard[("u", i2)] = (tsc,)
                tcl = s.add("vector", lambda q, i2=i2: q.tensor_scalar(
                    s_s[i2][:], s_s[i2][:], -10.0, 10.0, ALU.max, ALU.min),
                    waits=[tsc])
                tcar = s.add("gpsimd", lambda q, col=col, i2=i2: q.tensor_copy(
                    carry[:, col:col + 1], s_s[i2][:, TST - 1:TST]), waits=[tcl])
                twm = s.add("vector", lambda q, i2=i2, np_=np_: q.tensor_tensor(
                    w_s[i2][:], s_s[i2][:], ctbc[np_][:, 0:TST], ALU.mult),
                    waits=[tcl, tec] + list(guard.get(("w", i2), ())))
                guard[("s", i2)] = (twm, tcar)
                tya = s.add("vector", lambda q, c=c, i2=i2: q.tensor_tensor(
                    y_acc[:, c, :], y_acc[:, c, :], w_s[i2][:], ALU.add),
                    waits=[twm, tyz])
                guard[("w", i2)] = (tya,)
                nreaders = (tu, twm)
            guard[("bt", np_)] = nreaders
            t_scan_done = tya
        dprev[("bank", 2)] = guard[("pb", 0)][0]
        dprev[("bank", 3)] = guard[("pb", 1)][0]
        yg_toks = []
        for c in range(NCH):
            t1 = s.add("scalar", lambda q, c=c, sl=sl: q.activation(
                gscr[:, 0:TST], z_sl[sl][:, c, :], AF.Silu),
                waits=[t_scan_done] + ([yg_toks[-1]] if yg_toks else []))
            t2 = s.add("vector", lambda q, c=c: q.tensor_tensor(
                xc_bf[:, c, :], y_acc[:, c, :], gscr[:, 0:TST], ALU.mult),
                waits=[t1])
            yg_toks.append(t2)
        h_toks = []
        for c in range(NCH):
            th1 = s.add("vector", lambda q, c=c, sl=sl: q.tensor_tensor(
                h_st[:, c, :], xnh_sl[sl][:, c, :], xnl_sl[sl][:, c, :], ALU.add),
                waits=[t_scan_done] + ([h_toks[-1]] if h_toks else []))
            h_toks.append(th1)
        for oc in range(NCH):
            buf = oc % 2
            wld = list(yg_toks)
            if ("wo", buf) in dprev:
                wld.append(dprev[("wo", buf)])

            def ldw(q, sem, oc=oc, buf=buf):
                q.dma_start(out=wobuf[buf][:],
                            in_=outwT_hbm.rearrange("(c p) o -> p c o", p=128)[
                                :, :, oc * 128:(oc + 1) * 128]).then_inc(sem, 16)
            twl = s.add("gpsimd", ldw, waits=wld, dma=True)
            bk = oc % 2
            ww = [twl]
            if ("bank", bk) in dprev:
                ww.append(dprev[("bank", bk)])

            def mmo(q, oc=oc, bk=bk, buf=buf):
                r = None
                for c in range(NCH):
                    r = q.matmul(pb[bk][:, 0:TST], wobuf[buf][:, c, :],
                                 xc_bf[:, c, :], start=(c == 0), stop=(c == NCH - 1))
                return r
            tm = s.add("tensor", mmo, waits=ww)
            ta = s.add("vector", lambda q, oc=oc, bk=bk: q.scalar_tensor_tensor(
                h_st[:, oc, :], pb[bk][:, 0:TST], outb[:, oc:oc + 1],
                h_st[:, oc, :], ALU.add, ALU.add), waits=[tm, h_toks[oc]])
            dprev[("bank", bk)] = ta
            dprev[("wo", buf)] = ta
        t_h = ta
        t_ln2, _ = emit_ln_fm(b, h_st, u_out, ones_bf, rowscr, sqscr, xtmp,
                              pbA, pbB, pbC, pbD, [t_h], Tn=TST,
                              gain=lnw, bias2=lnb2, cast_buf=xc_bf)
        add_toks = []
        for c in range(NCH):
            ta2_ = s.add("vector", lambda q, c=c, sl=sl: q.tensor_tensor(
                x_sl[sl][:, c, :], x_sl[sl][:, c, :], u_out[:, c, :], ALU.add),
                waits=[t_ln2])
            add_toks.append(ta2_)
        tprev_store = store_fm(b, out_hbm[:, t0g:t0g + TST], x_sl[sl],
                               waits=add_toks)
        t_inproj[st_i] = (t_inproj[st_i][0], tprev_store)

    front(0)
    for st_i in range(NSTT):
        if st_i + 1 < NSTT:
            front(st_i + 1)
        back(st_i)
    s.add("sync", lambda q: q.nop(), waits=[tprev_store])
    if sa:
        return b.finish()


def build_l1_attn(b=None, xin=None, xout=None):
    sa = b is None
    if sa:
        b = Builder("l1attn")
    NH = 16
    DH = 64
    TST = 512
    NSTT = T // TST
    x_hbm = xin if xin is not None else b.dram_in("xres", [D, T])
    ln1g_hbm = b.dram_in("ln1_g", [128, NCH])
    ln1b_hbm = b.dram_in("ln1_b", [128, NCH])
    inwT_hbm = b.dram_in("in_wT", [D, 3 * D])
    outwT_hbm = b.dram_in("out_wT", [D, D])
    ln2g_hbm = b.dram_in("ln2_g", [128, NCH])
    ln2b_hbm = b.dram_in("ln2_b", [128, NCH])
    slopes_hbm = b.dram_in("slopes", [1, NH])
    smear_hbm = b.dram_in("smear", [1, NH])
    jramp_hbm = b.dram_in("jramp", [128, NH])
    subend_hbm = b.dram_in("subend", [128, 64])
    maskq_hbm = b.dram_in("maskq4", [128, 4 * 512])
    maskt_hbm = b.dram_in("maskt4", [128, 4 * 512])
    ident_hbm = b.dram_in("identf", [128, 128])
    identb_hbm = b.dram_in("identb", [128, 128], BF16)
    out_hbm = xout if xout is not None else b.dram_out("xout", [D, T])

    qkv_dram = b.dram_tmp("qkv", [3 * D, T], BF16)
    ao_dram = b.dram_tmp("ao", [D, T], BF16)

    x_st = b.sbuf("x", [128, NCH, TST], F32)
    hin_st = b.sbuf("hin", [128, NCH, TST], BF16)
    stg = b.sbuf("stg", [128, TST], BF16)
    qh = [b.sbuf("qh", [64, T], BF16) for _ in range(2)]
    kh = [b.sbuf("kh", [64, T], BF16) for _ in range(2)]
    ksm = [b.sbuf("ksm", [64, T], BF16) for _ in range(2)]
    vstg = [b.sbuf("vstg", [64, T], BF16) for _ in range(2)]
    vtm = [b.sbuf("vtm", [128, NTT, DH], BF16) for _ in range(2)]
    PT = [b.sbuf("PT", [128, NTT, TST], BF16) for _ in range(2)]
    shf = [b.sbuf("shf", [128, TST], F32) for _ in range(2)]
    m8bc = [b.sbuf("m8bc", [128, TST], F32) for _ in range(2)]
    submax = [b.sbuf("submax", [128, 64], F32) for _ in range(2)]
    cand = [b.sbuf("cand", [128, 64], F32) for _ in range(2)]
    m8col = [b.sbuf("m8col", [128, 1], F32) for _ in range(2)]
    m8row = [b.sbuf("m8row", [1, TST], F32) for _ in range(2)]
    rrow = [b.sbuf("rrow", [1, TST], F32) for _ in range(2)]
    rrowh = [b.sbuf("rrowh", [1, TST], BF16) for _ in range(2)]
    rrowl = [b.sbuf("rrowl", [1, TST], BF16) for _ in range(2)]
    osb = [b.sbuf("osb", [64, TST], F32) for _ in range(2)]
    ao_sb = [b.sbuf("aosb", [64, T], BF16) for _ in range(2)]
    y2_st = b.sbuf("y2", [128, NCH, TST], F32)
    u_out = b.sbuf("uout", [128, NCH, TST], F32)
    cast_b = b.sbuf("castb", [128, NCH, TST], BF16)
    srow = b.sbuf("srowx", [1, 2 * NH], F32)
    sg_bc = b.sbuf("sgbc", [64, NH], F32)
    om_bc = b.sbuf("ombc", [64, NH], F32)
    slope_bc = b.sbuf("slopebc", [128, NH], F32)
    jramp = b.sbuf("jramp", [128, NH], F32)
    subend = b.sbuf("subend", [128, 64], F32)
    aramp = [b.sbuf("aramp", [128, NH], F32) for _ in range(2)]
    aramp32 = [b.sbuf("aramp32", [128, 64], F32) for _ in range(2)]
    maskq = b.sbuf("maskq", [128, 4 * 512], F32)
    maskt = b.sbuf("maskt", [128, 4 * 512], F32)
    ln1g = b.sbuf("ln1g", [128, NCH], F32)
    ln1b = b.sbuf("ln1b", [128, NCH], F32)
    ln2g = b.sbuf("ln2g", [128, NCH], F32)
    ln2b = b.sbuf("ln2b", [128, NCH], F32)
    identf = b.sbuf("identf", [128, 128], F32)
    identb_l1 = b.sbuf("identb", [128, 128], BF16)
    ones_bf = b.sbuf("ones", [128, 128], BF16)
    onesf = b.sbuf("onesf", [128, 128], F32)
    rowscr = b.sbuf("rows", [1, 4 * TST], F32)
    sqscr = b.sbuf("sqscr", [128, 1024], BF16)
    xtmp = b.sbuf("xtmp", [128, 512], F32)
    wbuf = [b.sbuf("wb", [128, NCH, 128], BF16) for _ in range(2)]

    pb = b.alloc_psum_banks(3, F32, 512)
    pbT = b.alloc_psum_banks(1, BF16, 1024)[0]
    pbA = b.alloc_psum_banks(1, F32, 512)[0]
    pbB = b.alloc_psum_banks(1, F32, 512)[0]
    pbC = b.alloc_psum_banks(1, F32, 512)[0]
    pbD = b.alloc_psum_banks(1, F32, 512)[0]

    s = b.s

    def ld_misc(q, sem):
        for dst, src in ((ln1g, ln1g_hbm), (ln1b, ln1b_hbm), (ln2g, ln2g_hbm),
                         (ln2b, ln2b_hbm), (jramp, jramp_hbm),
                         (subend, subend_hbm), (maskq, maskq_hbm),
                         (maskt, maskt_hbm), (identf, ident_hbm),
                         (identb_l1, identb_hbm)):
            q.dma_start(out=dst[:], in_=src).then_inc(sem, 16)
        q.dma_start(out=srow[0:1, 0:NH], in_=smear_hbm).then_inc(sem, 16)
        q.dma_start(out=srow[0:1, NH:2 * NH], in_=slopes_hbm).then_inc(sem, 16)
    tmisc = s.add("sync", ld_misc, dma=True, n_dma=12)
    tones = s.add("vector", lambda q: q.memset(ones_bf[:], 1.0))
    tonesf = s.add("vector", lambda q: q.memset(onesf[:], 1.0))
    tsg = s.add("scalar", lambda q: q.activation(
        srow[0:1, 0:NH], srow[0:1, 0:NH], AF.Sigmoid), waits=[tmisc])
    tb1 = s.add("tensor", lambda q: q.matmul(
        pbA[0:64, 0:NH], onesf[0:1, 0:64],
        srow[0:1, 0:NH], start=True, stop=True), waits=[tsg, tonesf])
    te1 = s.add("scalar", lambda q: q.copy(sg_bc[:], pbA[0:64, 0:NH]), waits=[tb1])
    tom = s.add("vector", lambda q: q.tensor_scalar(
        om_bc[:], sg_bc[:], -1.0, 1.0, ALU.mult, ALU.add), waits=[te1])
    tb2 = s.add("tensor", lambda q: q.matmul(
        pbA[0:128, 0:NH], onesf[0:1, 0:128],
        srow[0:1, NH:2 * NH], start=True, stop=True), waits=[te1, tmisc])
    te2 = s.add("scalar", lambda q: q.copy(slope_bc[:], pbA[0:128, 0:NH]),
                waits=[tb2])
    dprev = {}

    tq_done = []
    t_lnp = None
    for st_i in range(NSTT):
        t0g = st_i * TST
        tld = load_fm(b, x_st, x_hbm[:, t0g:t0g + TST],
                      waits=[t_lnp] if t_lnp else [tmisc])
        t_ln, _ = emit_ln_fm(b, x_st, hin_st, ones_bf, rowscr, sqscr, xtmp,
                             pbA, pbB, pbC, pbD, [tld, tones, te2], Tn=TST,
                             gain=ln1g, bias2=ln1b, double_ln=True)
        t_lnp = t_ln
        for oc in range(3 * NCH):
            buf = oc % 2
            wld = [t_ln]
            if ("wi", buf) in dprev:
                wld.append(dprev[("wi", buf)])

            def ldw(q, sem, oc=oc, buf=buf):
                q.dma_start(out=wbuf[buf][:],
                            in_=inwT_hbm.rearrange("(c p) o -> p c o", p=128)[
                                :, :, oc * 128:(oc + 1) * 128]).then_inc(sem, 16)
            twl = s.add("gpsimd", ldw, waits=wld, dma=True)
            bk = oc % 2
            ww = [twl]
            if ("bank", bk) in dprev:
                ww.append(dprev[("bank", bk)])

            def mmi(q, oc=oc, bk=bk, buf=buf):
                r = None
                for c in range(NCH):
                    r = q.matmul(pb[bk][:, 0:TST], wbuf[buf][:, c, :],
                                 hin_st[:, c, :], start=(c == 0), stop=(c == NCH - 1))
                return r
            tm = s.add("tensor", mmi, waits=ww)
            tev = s.add("scalar", lambda q, bk=bk: q.copy(stg[:], pb[bk][:, 0:TST]),
                        waits=[tm] + ([dprev["stg"]] if "stg" in dprev else []))
            dprev[("bank", bk)] = tev

            def st_dma(q, sem, oc=oc, t0g=t0g):
                q.dma_start(out=qkv_dram[oc * 128:(oc + 1) * 128, t0g:t0g + TST],
                            in_=stg[:]).then_inc(sem, 16)
            tsd = s.add("sync", st_dma, waits=[tev], dma=True)
            dprev["stg"] = tsd
            dprev[("wi", buf)] = tev
            tq_done.append(tsd)

    ao_done = []
    hguard = {}

    def head_body(h):
        hp = h % 2
        qh_ = qh[hp]; kh_ = kh[hp]; ksm_ = ksm[hp]; vstg_ = vstg[hp]
        vtm_ = vtm[hp]; PT_ = PT[hp]; shf_ = shf[hp]; m8bc_ = m8bc[hp]
        submax_ = submax[hp]; cand_ = cand[hp]; m8col_ = m8col[hp]
        m8row_ = m8row[hp]; rrow_ = rrow[hp]; rrowh_ = rrowh[hp]
        rrowl_ = rrowl[hp]; osb_ = osb[hp]; ao_sb_ = ao_sb[hp]
        aramp_ = aramp[hp]; aramp32_ = aramp32[hp]
        g = [hguard[hp]] if hp in hguard else []
        wv = [tq_done[-1]] + g

        def ldqkv(q, sem, h=h, qh_=qh_, kh_=kh_, vstg_=vstg_):
            q.dma_start(out=qh_[:], in_=qkv_dram[h * 64:(h + 1) * 64, :]).then_inc(sem, 16)
            q.dma_start(out=kh_[:], in_=qkv_dram[D + h * 64:D + (h + 1) * 64, :]).then_inc(sem, 16)
            q.dma_start(out=vstg_[:], in_=qkv_dram[2 * D + h * 64:2 * D + (h + 1) * 64, :]).then_inc(sem, 16)
        tqkv = s.add("sync", ldqkv, waits=wv, dma=True, n_dma=3)
        tar = s.add("vector", lambda q, h=h, aramp_=aramp_: q.tensor_scalar_mul(
            aramp_[:], jramp[:], slope_bc[:, h:h + 1]), waits=[te2, tmisc] + g)
        tar32 = s.add("vector", lambda q, h=h, aramp32_=aramp32_: q.tensor_scalar_mul(
            aramp32_[:], subend[:], slope_bc[:, h:h + 1]), waits=[tar])
        yield
        tk1 = s.add("vector", lambda q, h=h, ksm_=ksm_, kh_=kh_: q.tensor_scalar_mul(
            ksm_[:], kh_[:], om_bc[:, h:h + 1]), waits=[tqkv, tom, tar32])
        tk2 = s.add("vector", lambda q, h=h, ksm_=ksm_, kh_=kh_: q.scalar_tensor_tensor(
            ksm_[:, 1:T], kh_[:, 0:T - 1], sg_bc[:, h:h + 1], ksm_[:, 1:T],
            ALU.mult, ALU.add), waits=[tk1])
        yield
        vt_toks = []
        for jt in range(NTT):
            ww = [tqkv]
            if ("pbT",) in dprev:
                ww.append(dprev[("pbT",)])
            tt = s.add("tensor", lambda q, jt=jt, vstg_=vstg_: q.transpose(
                pbT[:, 0:64], vstg_[:, jt * 128:(jt + 1) * 128],
                identb_l1[0:64, 0:64]), waits=ww)
            tev = s.add("scalar", lambda q, jt=jt, vtm_=vtm_: q.copy(
                vtm_[:, jt, :], pbT[:, 0:64]), waits=[tt])
            dprev[("pbT",)] = tev
            vt_toks.append(tev)
            if jt % 4 == 3:
                yield
        for qg in range(4):
            q0 = qg * TST
            for qs in range(4):
                qt = qg * 4 + qs
                nsub = (qt + 1) * 4
                njg = ceil_div(qt + 1, 4)
                treds = []
                for jg in range(njg):
                    jw = min(512, (qt + 1) * 128 - jg * 512)
                    bk = jg % 2
                    ww = [tk2]
                    if ("bank", bk) in dprev:
                        ww.append(dprev[("bank", bk)])

                    def mmS(q, h=h, qt=qt, jg=jg, jw=jw, bk=bk, qh_=qh_, ksm_=ksm_):
                        return q.matmul(pb[bk][:, 0:jw],
                                        qh_[:, qt * 128:(qt + 1) * 128],
                                        ksm_[:, jg * 512:jg * 512 + jw],
                                        start=True, stop=True)
                    tmS = s.add("tensor", mmS, waits=ww)
                    if jg == njg - 1:
                        r = qt - jg * 4
                        tmS = s.add("vector", lambda q, r=r, jw=jw, bk=bk: q.tensor_tensor(
                            pb[bk][:, 0:jw], pb[bk][:, 0:jw],
                            maskq[:, r * 512:r * 512 + jw], ALU.add), waits=[tmS])
                    tred = s.add("vector", lambda q, jg=jg, jw=jw, bk=bk, submax_=submax_: q.reduce_max(
                        submax_[:, jg * 16:jg * 16 + jw // 32],
                        pb[bk][:, 0:jw].rearrange("p (a k) -> p a k", k=32),
                        axis=AX.X), waits=[tmS])
                    dprev[("bank", bk)] = tred
                    treds.append(tred)
                tc1 = s.add("vector", lambda q, nsub=nsub, cand_=cand_, submax_=submax_: q.tensor_scalar_mul(
                    cand_[:, 0:nsub], submax_[:, 0:nsub], 0.125), waits=treds)
                tc2 = s.add("vector", lambda q, nsub=nsub, cand_=cand_, aramp32_=aramp32_: q.tensor_tensor(
                    cand_[:, 0:nsub], cand_[:, 0:nsub], aramp32_[:, 0:nsub], ALU.add),
                    waits=[tc1, tar32])
                tc3 = s.add("vector", lambda q, nsub=nsub, cand_=cand_, m8col_=m8col_: q.reduce_max(
                    m8col_[:], cand_[:, 0:nsub], axis=AX.X), waits=[tc2])
                tc4 = s.add("vector", lambda q, m8col_=m8col_: q.tensor_scalar_mul(
                    m8col_[:], m8col_[:], 8.0), waits=[tc3])
                ww = [tc4]
                if ("bank", 2) in dprev:
                    ww.append(dprev[("bank", 2)])
                tt = s.add("tensor", lambda q, m8col_=m8col_: q.transpose(
                    pb[2][0:1, 0:128], m8col_[:], identf[:]), waits=ww)
                tev = s.add("scalar", lambda q, qs=qs, m8row_=m8row_: q.copy(
                    m8row_[0:1, qs * 128:(qs + 1) * 128], pb[2][0:1, 0:128]),
                    waits=[tt])
                dprev[("bank", 2)] = tev
                yield
            t_m8 = tev
            ww = [t_m8, tonesf]
            if ("bank", 2) in dprev:
                ww.append(dprev[("bank", 2)])
            tbm = s.add("tensor", lambda q, m8row_=m8row_: q.matmul(
                pb[2][:, 0:TST], onesf[0:1, 0:128],
                m8row_[0:1, 0:TST], start=True, stop=True), waits=ww)
            tem = s.add("scalar", lambda q, m8bc_=m8bc_: q.copy(m8bc_[:], pb[2][:, 0:TST]),
                        waits=[tbm])
            dprev[("bank", 2)] = tem
            yield
            njt = 4 * qg + 4
            for jt in range(njt):
                bk = jt % 2
                ww = [tem, tk2]
                if ("bank", bk) in dprev:
                    ww.append(dprev[("bank", bk)])

                def mmT(q, h=h, jt=jt, q0=q0, bk=bk, ksm_=ksm_, qh_=qh_):
                    return q.matmul(pb[bk][:, 0:TST],
                                    ksm_[:, jt * 128:(jt + 1) * 128],
                                    qh_[:, q0:q0 + TST], start=True, stop=True)
                tmT = s.add("tensor", mmT, waits=ww)
                wsh = [tmT]
                if ("shf", hp) in dprev:
                    wsh.append(dprev[("shf", hp)])
                tsh = s.add("vector", lambda q, bk=bk, shf_=shf_, m8bc_=m8bc_: q.scalar_tensor_tensor(
                    shf_[:], m8bc_[:], -1.0, pb[bk][:, 0:TST], ALU.mult, ALU.add),
                    waits=wsh)
                if jt >= 4 * qg:
                    r = jt - 4 * qg
                    tsh = s.add("vector", lambda q, r=r, shf_=shf_: q.tensor_tensor(
                        shf_[:], shf_[:], maskt[:, r * 512:(r + 1) * 512], ALU.add),
                        waits=[tsh])
                dprev[("bank", bk)] = tsh
                tex = s.add("scalar", lambda q, jt=jt, PT_=PT_, shf_=shf_, aramp_=aramp_: q.activation(
                    PT_[:, jt, :], shf_[:], AF.Exp, scale=0.125,
                    bias=aramp_[:, jt:jt + 1]), waits=[tsh, tar])
                dprev[("shf", hp)] = tex
                yield
            def mmRall(q, njt=njt, PT_=PT_):
                r = None
                for jt in range(njt):
                    r = q.matmul(pbA[0:1, 0:TST], ones_bf[:, 0:1],
                                 PT_[:, jt, :], start=(jt == 0),
                                 stop=(jt == njt - 1), skip_group_check=True)
                return r
            wR = [dprev[("shf", hp)]]
            if ("bankA",) in dprev:
                wR.append(dprev[("bankA",)])
            tmR = s.add("tensor", mmRall, waits=wR)
            ter = s.add("scalar", lambda q, rrow_=rrow_: q.copy(rrow_[0:1, :], pbA[0:1, 0:TST]),
                        waits=[tmR])
            dprev[("bankA",)] = ter
            tr1 = s.add("vector", lambda q, rrow_=rrow_: q.reciprocal(rrow_[0:1, :], rrow_[0:1, :]),
                        waits=[ter])
            tr2 = s.add("vector", lambda q, rrow_=rrow_, rrowh_=rrowh_: q.tensor_copy(
                rrowh_[0:1, :], rrow_[0:1, :]), waits=[tr1])
            tr3 = s.add("vector", lambda q, rrow_=rrow_, rrowh_=rrowh_, rrowl_=rrowl_: q.tensor_tensor(
                rrowl_[0:1, :], rrow_[0:1, :], rrowh_[0:1, :], ALU.subtract),
                waits=[tr2])
            tr4 = tr3
            yield
            def mmVall(q, njt=njt, vtm_=vtm_, PT_=PT_):
                r = None
                for jt in range(njt):
                    r = q.matmul(pbB[0:64, 0:TST], vtm_[:, jt, :],
                                 PT_[:, jt, :], start=(jt == 0),
                                 stop=(jt == njt - 1))
                return r
            wV = [dprev[("shf", hp)]] + vt_toks
            if ("bankB",) in dprev:
                wV.append(dprev[("bankB",)])
            tmV = s.add("tensor", mmVall, waits=wV)
            teo = s.add("scalar", lambda q, osb_=osb_: q.copy(osb_[:], pbB[0:64, 0:TST]),
                        waits=[tmV])
            dprev[("bankB",)] = teo
            yield
            ww = [tr4, teo]
            if ("bankC",) in dprev:
                ww.append(dprev[("bankC",)])

            def mmrb(q, rrowh_=rrowh_, rrowl_=rrowl_):
                q.matmul(pbC[0:64, 0:TST], ones_bf[0:1, 0:64], rrowh_[0:1, :],
                         start=True, stop=False)
                return q.matmul(pbC[0:64, 0:TST], ones_bf[0:1, 0:64], rrowl_[0:1, :],
                                start=False, stop=True)
            tmrb = s.add("tensor", mmrb, waits=ww)
            tno = s.add("vector", lambda q, q0=q0, ao_sb_=ao_sb_, osb_=osb_: q.tensor_tensor(
                ao_sb_[:, q0:q0 + TST], osb_[:], pbC[0:64, 0:TST], ALU.mult),
                waits=[tmrb, teo])
            dprev[("bankC",)] = tno
            yield
        def st_ao(q, sem, h=h, ao_sb_=ao_sb_):
            q.dma_start(out=ao_dram[h * 64:(h + 1) * 64, :], in_=ao_sb_[:]).then_inc(sem, 16)
        t_store = s.add("sync", st_ao, waits=[tno], dma=True)
        hguard[hp] = t_store
        ao_done.append(t_store)

    gens = [head_body(h) for h in range(NH)]
    act = []
    nxt = 0
    while act or nxt < NH:
        while len(act) < 2 and nxt < NH:
            act.append(gens[nxt])
            nxt += 1
        for gnr in list(act):
            try:
                next(gnr)
            except StopIteration:
                act.remove(gnr)

    tprev_store = None
    for st_i in range(NSTT):
        t0g = st_i * TST
        tld = load_fm(b, x_st, x_hbm[:, t0g:t0g + TST],
                      waits=[tprev_store] if tprev_store else [ao_done[-1]])
        tla = s.add("gpsimd", lambda q, sem, t0g=t0g: q.dma_start(
            out=cast_b[:], in_=ao_dram.rearrange("(c p) t -> p c t", p=128)[
                :, :, t0g:t0g + TST]).then_inc(sem, 16),
            waits=[ao_done[-1]] + ([tprev_store] if tprev_store else []), dma=True)
        mm_toks = []
        for oc in range(NCH):
            buf = oc % 2
            wld = [tla]
            if ("wo", buf) in dprev:
                wld.append(dprev[("wo", buf)])

            def ldw(q, sem, oc=oc, buf=buf):
                q.dma_start(out=wbuf[buf][:],
                            in_=outwT_hbm.rearrange("(c p) o -> p c o", p=128)[
                                :, :, oc * 128:(oc + 1) * 128]).then_inc(sem, 16)
            twl = s.add("gpsimd", ldw, waits=wld, dma=True)
            bk = oc % 2
            ww = [twl, tla]
            if ("bank", bk) in dprev:
                ww.append(dprev[("bank", bk)])

            def mmo(q, oc=oc, bk=bk, buf=buf):
                r = None
                for c in range(NCH):
                    r = q.matmul(pb[bk][:, 0:TST], wbuf[buf][:, c, :],
                                 cast_b[:, c, :], start=(c == 0), stop=(c == NCH - 1))
                return r
            tm = s.add("tensor", mmo, waits=ww)
            tev = s.add("vector", lambda q, oc=oc, bk=bk: q.tensor_copy(
                y2_st[:, oc, :], pb[bk][:, 0:TST]), waits=[tm])
            dprev[("bank", bk)] = tev
            dprev[("wo", buf)] = tev
            mm_toks.append(tev)
        t_ln2, _ = emit_ln_fm(b, y2_st, u_out, ones_bf, rowscr, sqscr, xtmp,
                              pbA, pbB, pbC, pbD, mm_toks, Tn=TST,
                              gain=ln2g, bias2=ln2b, cast_buf=hin_st)
        add_toks = []
        for c in range(NCH):
            ta = s.add("vector", lambda q, c=c: q.tensor_tensor(
                x_st[:, c, :], x_st[:, c, :], u_out[:, c, :], ALU.add),
                waits=[t_ln2, tld])
            add_toks.append(ta)
        tprev_store = store_fm(b, out_hbm[:, t0g:t0g + TST], x_st, waits=add_toks)
    s.add("sync", lambda q: q.nop(), waits=[tprev_store])
    if sa:
        return b.finish()


def build_l3_attn(b=None, xin=None, xout=None):
    sa = b is None
    if sa:
        b = Builder("l3attn")
    NH = 16
    TST = 512
    NSTT = T // TST
    NG = TST // 8
    x_hbm = xin if xin is not None else b.dram_in("xres", [D, T])
    inwT_hbm = b.dram_in("in_wT", [D, 3 * D])
    inb_hbm = b.dram_in("in_b", [128, 3 * NCH])
    outwT_hbm = b.dram_in("out_wT", [D, D])
    outb_hbm = b.dram_in("out_b", [128, NCH])
    hm_hbm = b.dram_in("hm", [128, NCH * NH], BF16)
    hmT_hbm = b.dram_in("hmT", [NH, NCH * 128], BF16)
    out_hbm = xout if xout is not None else b.dram_out("xout", [D, T])

    x_st = b.sbuf("x", [128, NCH, TST], F32)
    hin = b.sbuf("hin", [128, NCH, TST], BF16)
    qkv = b.sbuf("qkv", [128, 3 * NCH, TST], BF16)
    qkscr = b.sbuf("qkscr", [128, TST], BF16)
    ltile = b.sbuf("ltile", [NH, 8 * TST], F32)
    lmax = b.sbuf("lmax", [NH, TST], F32)
    abf = b.sbuf("abf", [NH, 8 * TST], BF16)
    vascr = b.sbuf("vascr", [128, TST], F32)
    o_acc = b.sbuf("oacc", [128, NCH, TST], F32)
    inb = b.sbuf("inb", [128, 3 * NCH], F32)
    outb = b.sbuf("outb", [128, NCH], F32)
    hm = b.sbuf("hm", [128, NCH * NH], BF16)
    hmT = b.sbuf("hmT", [NH, NCH * 128], BF16)
    ones_bf = b.sbuf("ones", [128, 128], BF16)
    rowscr = b.sbuf("rows", [1, 4 * TST], F32)
    sqscr = b.sbuf("sqscr", [128, 1024], BF16)
    xtmp = b.sbuf("xtmp", [128, 512], F32)
    wbuf = [b.sbuf("wb", [128, NCH, 128], BF16) for _ in range(2)]

    pb = b.alloc_psum_banks(4, F32, 512)
    pbA = b.alloc_psum_banks(1, F32, 512)[0]
    pbB = b.alloc_psum_banks(1, F32, 512)[0]
    pbC = b.alloc_psum_banks(1, F32, 512)[0]
    pbD = b.alloc_psum_banks(1, F32, 512)[0]

    s = b.s

    def ld_misc(q, sem):
        for dst, src in ((inb, inb_hbm), (outb, outb_hbm), (hm, hm_hbm),
                         (hmT, hmT_hbm)):
            q.dma_start(out=dst[:], in_=src).then_inc(sem, 16)
    tmisc = s.add("sync", ld_misc, dma=True, n_dma=4)
    tones = s.add("vector", lambda q: q.memset(ones_bf[:], 1.0))

    dprev = {}
    tprev_store = None
    t_lnp = None
    for st_i in range(NSTT):
        t0g = st_i * TST
        tld = load_fm(b, x_st, x_hbm[:, t0g:t0g + TST],
                      waits=[t for t in (tprev_store, t_lnp) if t] or [tmisc])
        t_ln, _ = emit_ln_fm(b, x_st, hin, ones_bf, rowscr, sqscr, xtmp,
                             pbA, pbB, pbC, pbD, [tld, tones], Tn=TST)
        t_lnp = t_ln
        qk_toks = []
        for oc in range(3 * NCH):
            buf = oc % 2
            wld = [t_ln]
            if ("wi", buf) in dprev:
                wld.append(dprev[("wi", buf)])

            def ldw(q, sem, oc=oc, buf=buf):
                q.dma_start(out=wbuf[buf][:],
                            in_=inwT_hbm.rearrange("(c p) o -> p c o", p=128)[
                                :, :, oc * 128:(oc + 1) * 128]).then_inc(sem, 16)
            twl = s.add("gpsimd", ldw, waits=wld, dma=True)
            bk = oc % 2
            ww = [twl]
            if ("bank", bk) in dprev:
                ww.append(dprev[("bank", bk)])

            def mmi(q, oc=oc, bk=bk, buf=buf):
                r = None
                for c in range(NCH):
                    r = q.matmul(pb[bk][:, 0:TST], wbuf[buf][:, c, :],
                                 hin[:, c, :], start=(c == 0), stop=(c == NCH - 1))
                return r
            tm = s.add("tensor", mmi, waits=ww)
            te = s.add("scalar", lambda q, oc=oc, bk=bk: q.activation(
                qkv[:, oc, :], pb[bk][:, 0:TST], AF.Identity,
                bias=inb[:, oc:oc + 1]), waits=[tm])
            dprev[("bank", bk)] = te
            dprev[("wi", buf)] = te
            qk_toks.append(te)
        for dd in range(8):
            mm_toks = []
            for c in range(NCH):
                wq = list(qk_toks)
                if ("qk",) in dprev:
                    wq.append(dprev[("qk",)])
                kpart = qkv[:, NCH + c, :].rearrange("p (g e) -> p g e", e=8)[
                    :, :, dd:dd + 1].squeeze(2).broadcast_to([128, NG, 8])
                tqk = s.add("vector", lambda q, c=c, kpart=kpart: q.tensor_tensor(
                    qkscr[:].rearrange("p (g e) -> p g e", e=8),
                    qkv[:, c, :].rearrange("p (g e) -> p g e", e=8),
                    kpart, ALU.mult), waits=wq)

                def mml(q, c=c, dd=dd):
                    return q.matmul(pbA[0:NH, 0:TST], hm[:, c * NH:(c + 1) * NH],
                                    qkscr[:], start=(c == 0), stop=(c == NCH - 1),
                                    skip_group_check=True)
                tl = s.add("tensor", mml, waits=[tqk] + (
                    [dprev[("lA",)]] if c == 0 and ("lA",) in dprev else []))
                dprev[("qk",)] = tl
                mm_toks.append(tl)
            tev = s.add("scalar", lambda q, dd=dd: q.copy(
                ltile[:, dd * TST:(dd + 1) * TST], pbA[0:NH, 0:TST]),
                waits=[mm_toks[-1]])
            dprev[("lA",)] = tev
            dprev[("lt", dd)] = tev
        l3d = ltile[:].rearrange("p (e t) -> p t e", e=8)
        lt_toks = [dprev[("lt", dd)] for dd in range(8)]
        t_m = s.add("vector", lambda q: q.reduce_max(
            lmax[:], l3d, axis=AX.X), waits=lt_toks)
        t_sub = s.add("vector", lambda q: q.tensor_tensor(
            l3d, l3d, lmax[:].broadcast_to([NH, TST, 8]), ALU.subtract),
            waits=[t_m])
        t_exp = s.add("scalar", lambda q: q.activation(
            ltile[:], ltile[:], AF.Exp, scale=0.125), waits=[t_sub])
        t_sm = s.add("vector", lambda q: q.reduce_sum(
            lmax[:], l3d, axis=AX.X), waits=[t_exp])
        t_rc = s.add("vector", lambda q: q.reciprocal(lmax[:], lmax[:]),
                     waits=[t_sm])
        t_nm = s.add("vector", lambda q: q.tensor_tensor(
            abf[:].rearrange("p (e t) -> p t e", e=8), l3d,
            lmax[:].broadcast_to([NH, TST, 8]), ALU.mult), waits=[t_rc])
        tz = s.add("vector", lambda q: q.memset(o_acc[:], 0.0),
                   waits=[t_ln] + ([dprev[("oa",)]] if ("oa",) in dprev else []))
        t_oa = tz
        for dd in range(8):
            for c in range(NCH):
                ww = [t_nm, tmisc]
                if ("bank", 2 + c % 2) in dprev:
                    ww.append(dprev[("bank", 2 + c % 2)])
                bk = 2 + c % 2

                def mmb(q, c=c, dd=dd, bk=bk):
                    return q.matmul(pb[bk][:, 0:TST], hmT[:, c * 128:(c + 1) * 128],
                                    abf[:, dd * TST:(dd + 1) * TST],
                                    start=True, stop=True)
                tb_ = s.add("tensor", mmb, waits=ww)
                vpart = qkv[:, 2 * NCH + c, :].rearrange("p (g e) -> p g e", e=8)[
                    :, :, dd:dd + 1].squeeze(2).broadcast_to([128, NG, 8])
                tva = s.add("vector", lambda q, c=c, bk=bk, vpart=vpart: q.tensor_tensor(
                    vascr[:].rearrange("p (g e) -> p g e", e=8), vpart,
                    pb[bk][:, 0:TST].rearrange("p (g e) -> p g e", e=8), ALU.mult),
                    waits=[tb_, t_oa])
                t_oa = s.add("vector", lambda q, c=c: q.tensor_tensor(
                    o_acc[:, c, :], o_acc[:, c, :], vascr[:], ALU.add),
                    waits=[tva])
                dprev[("bank", bk)] = t_oa
        dprev[("oa",)] = t_oa
        oc_toks = []
        for c in range(NCH):
            tc = s.add("vector", lambda q, c=c: q.tensor_copy(
                hin[:, c, :], o_acc[:, c, :]), waits=[t_oa])
            oc_toks.append(tc)
        add_toks = []
        for oc in range(NCH):
            buf = oc % 2
            wld = list(oc_toks)
            if ("wo", buf) in dprev:
                wld.append(dprev[("wo", buf)])

            def ldwo(q, sem, oc=oc, buf=buf):
                q.dma_start(out=wbuf[buf][:],
                            in_=outwT_hbm.rearrange("(c p) o -> p c o", p=128)[
                                :, :, oc * 128:(oc + 1) * 128]).then_inc(sem, 16)
            twl = s.add("gpsimd", ldwo, waits=wld, dma=True)
            bk = oc % 2
            ww = [twl]
            if ("bank", bk) in dprev:
                ww.append(dprev[("bank", bk)])

            def mmo(q, oc=oc, bk=bk, buf=buf):
                r = None
                for c in range(NCH):
                    r = q.matmul(pb[bk][:, 0:TST], wbuf[buf][:, c, :],
                                 hin[:, c, :], start=(c == 0), stop=(c == NCH - 1))
                return r
            tm = s.add("tensor", mmo, waits=ww)
            ta = s.add("vector", lambda q, oc=oc, bk=bk: q.scalar_tensor_tensor(
                x_st[:, oc, :], pb[bk][:, 0:TST], outb[:, oc:oc + 1],
                x_st[:, oc, :], ALU.add, ALU.add), waits=[tm, tld])
            dprev[("bank", bk)] = ta
            dprev[("wo", buf)] = ta
            add_toks.append(ta)
        tprev_store = store_fm(b, out_hbm[:, t0g:t0g + TST], x_st, waits=add_toks)
    s.add("sync", lambda q: q.nop(), waits=[tprev_store])
    if sa:
        return b.finish()


def np_ident(dtype):
    return np.eye(128, dtype=dtype)


def pack_pc(v):
    return np.ascontiguousarray(v.reshape(NCH, 128).T)


def l2_ffn_inmaps(x_cores, weight, bias):
    maps = []
    identf = np_ident(np.float32)
    identb = np_ident(ml_dtypes.bfloat16)
    for i in range(N_CORES):
        maps.append({
            "xres": np.ascontiguousarray(x_cores[i]),
            "w": np.ascontiguousarray(weight),
            "bias": pack_pc(bias),
            "identf": identf,
            "identb": identb,
        })
    return maps


def build_fused_a():
    b = Builder("fuseda", fused=True)
    xin = b.dram_in("xres", [D, T])
    xbuf = [b.dram_tmp("xping", [D, T]) for _ in range(2)]
    xout = b.dram_out("xout", [D, T])
    seq = [(build_l0_attn, "l0a_"), (build_l0_ffn, "l0f_"),
           (build_l1_attn, "l1a_"), (build_l1_ffn, "l1f_"),
           (build_l2_attn, "l2a_"), (build_l2_ffn, "l2f_")]
    cur = xin
    for i, (fn, pref) in enumerate(seq):
        b.pref = pref
        nxt = xout if i == len(seq) - 1 else xbuf[i % 2]
        fn(b, xin=cur, xout=nxt)
        b.s.full_barrier()
        b.reset_arena()
        cur = nxt
    return b.finish()


def build_fused_b():
    b = Builder("fusedb", fused=True)
    xin = b.dram_in("xres", [D, T])
    xmid = b.dram_tmp("xmid", [D, T])
    xout = b.dram_out("xout", [D, T])
    b.pref = "l3a_"
    build_l3_attn(b, xin=xin, xout=xmid)
    b.s.full_barrier()
    b.reset_arena()
    b.pref = "l3f_"
    build_l3_ffn(b, xin=xmid, xout=xout)
    return b.finish()


_BUILD_CACHE = {}


def _built(name, fn):
    if name not in _BUILD_CACHE:
        _BUILD_CACHE[name] = fn()
    return _BUILD_CACHE[name]


def _mk_consts():
    c = {}
    c["identf"] = np_ident(np.float32)
    c["identb"] = np_ident(ml_dtypes.bfloat16)
    for nm, n in (("sel8", 8), ("sel16", 32), ("sel32", 32)):
        m = np.zeros((n, n * 128), dtype=ml_dtypes.bfloat16)
        for e in range(n):
            m[e, e * 128:(e + 1) * 128] = 1
        c[nm] = m
    c["jramp"] = (np.arange(16)[None, :] * 128
                  + np.arange(128)[:, None]).astype(np.float32)
    c["subend"] = ((np.arange(64)[None, :] * 32 + 31)
                   * np.ones((128, 1))).astype(np.float32)
    maskq = np.zeros((128, 4, 4, 128), np.float32)
    maskt = np.zeros((128, 4, 4, 128), np.float32)
    for r in range(4):
        for a_ in range(4):
            for pp in range(128):
                if a_ > r:
                    maskq[pp, r, a_, :] = -1e9
                elif a_ == r:
                    maskq[pp, r, a_, pp + 1:] = -1e9
                if a_ < r:
                    maskt[pp, r, a_, :] = -1e9
                elif a_ == r:
                    maskt[pp, r, a_, :pp] = -1e9
    c["maskq4"] = maskq.reshape(128, 2048)
    c["maskt4"] = maskt.reshape(128, 2048)
    hm = np.zeros((128, 8 * 16), dtype=ml_dtypes.bfloat16)
    hmT = np.zeros((16, 8 * 128), dtype=ml_dtypes.bfloat16)
    for cc in range(8):
        for p in range(128):
            h = (2 * cc) + (p // 64)
            hm[p, cc * 16 + h] = 1
            hmT[h, cc * 128 + p] = 1
    c["hm"] = hm
    c["hmT"] = hmT
    return c


def _pack_conv(conv_w):
    return np.ascontiguousarray(
        conv_w.reshape(8, 128, 4).transpose(1, 0, 2).reshape(128, 32))


def _run(nc, shared, per_core):
    maps = []
    for i in range(N_CORES):
        m = dict(shared)
        m.update(per_core[i])
        maps.append(m)
    res = run_bass_kernel_spmd(nc, maps, core_ids=list(range(N_CORES)))
    return [np.asarray(res.results[i]["xout"]) for i in range(N_CORES)]


def _shared_dicts(a, C):
    sh = {}
    sh["l0a"] = {
        "in_wT": np.ascontiguousarray(a["L0_attn_in_w"].T),
        "in_b": np.ascontiguousarray(a["L0_attn_in_b"].reshape(16, 128).T),
        "conv_w": _pack_conv(a["L0_attn_conv_w"]),
        "conv_b": pack_pc(a["L0_attn_conv_b"]),
        "A_log": a["L0_attn_A_log"],
        "B_wT": np.ascontiguousarray(a["L0_attn_B_w"].T),
        "B_b": a["L0_attn_B_b"].reshape(32, 1),
        "C_wT": np.ascontiguousarray(a["L0_attn_C_w"].T),
        "C_b": a["L0_attn_C_b"].reshape(32, 1),
        "out_wT": np.ascontiguousarray(a["L0_attn_out_w"].T),
        "out_b": pack_pc(a["L0_attn_out_b"]),
        "ln_w": pack_pc(a["L0_attn_ln_w"]),
        "ln_b": pack_pc(a["L0_attn_ln_b"]),
        "identf": C["identf"], "sel32": C["sel32"],
    }
    sh["l0f"] = {
        "wgate": a["L0_ffn_w_gate"], "wup": a["L0_ffn_w_up"],
        "wdown": a["L0_ffn_w_down"],
        "identf": C["identf"], "identb": C["identb"],
    }
    sh["l1a"] = {
        "ln1_g": pack_pc(a["L1_attn_ln1_g"]), "ln1_b": pack_pc(a["L1_attn_ln1_b"]),
        "in_wT": np.ascontiguousarray(a["L1_attn_in_w"].T),
        "out_wT": np.ascontiguousarray(a["L1_attn_out_w"].T),
        "ln2_g": pack_pc(a["L1_attn_ln2_g"]), "ln2_b": pack_pc(a["L1_attn_ln2_b"]),
        "slopes": a["L1_attn_slopes"].reshape(1, 16),
        "smear": a["L1_attn_smear"].reshape(1, 16),
        "jramp": C["jramp"], "subend": C["subend"],
        "maskq4": C["maskq4"], "maskt4": C["maskt4"],
        "identf": C["identf"], "identb": C["identb"],
    }
    sh["l1f"] = {
        "gate_w": a["L1_ffn_gate_w"], "base_wgu": a["L1_ffn_base_wgu"],
        "base_wd": a["L1_ffn_base_wd"], "exp_wgu": a["L1_ffn_exp_wgu"],
        "exp_wd": a["L1_ffn_exp_wd"],
        "identf": C["identf"], "sel8": C["sel8"],
    }
    sh["l2a"] = {
        "ln_g": pack_pc(a["L2_attn_ln_g"]), "ln_b": pack_pc(a["L2_attn_ln_b"]),
        "w_in": a["L2_attn_w_in"], "b_in": pack_pc(a["L2_attn_b_in"]),
        "conv_w": _pack_conv(a["L2_attn_conv_w"]),
        "conv_b": pack_pc(a["L2_attn_conv_b"]),
        "A_log": a["L2_attn_A_log"], "D_ssm": pack_pc(a["L2_attn_D_ssm"]),
        "w_dt": a["L2_attn_w_dt"], "b_dt": pack_pc(a["L2_attn_b_dt"]),
        "w_bc": a["L2_attn_w_bc"],
        "w_out": a["L2_attn_w_out"], "b_out": pack_pc(a["L2_attn_b_out"]),
        "sel16": C["sel16"],
    }
    sh["l2f"] = {
        "w": a["L2_ffn_weight"], "bias": pack_pc(a["L2_ffn_bias"]),
        "identf": C["identf"], "identb": C["identb"],
    }
    sh["l3a"] = {
        "in_wT": np.ascontiguousarray(a["L3_attn_in_proj_weight"].T),
        "in_b": np.ascontiguousarray(a["L3_attn_in_proj_bias"].reshape(24, 128).T),
        "out_wT": np.ascontiguousarray(a["L3_attn_out_proj_weight"].T),
        "out_b": pack_pc(a["L3_attn_out_proj_bias"]),
        "hm": C["hm"], "hmT": C["hmT"],
    }
    sh["l3f"] = {
        "router_wT": np.ascontiguousarray(a["L3_ffn_router_w"].T),
        "rbias_rep": np.ascontiguousarray(np.tile(a["L3_ffn_router_bias"], (128, 16))),
        "sw1": a["L3_ffn_sw1"], "sw3": a["L3_ffn_sw3"], "sw2": a["L3_ffn_sw2"],
        "ew1": a["L3_ffn_ew1"], "ew2": a["L3_ffn_ew2"],
        "identf": C["identf"], "sel8": C["sel8"],
    }
    return sh


def kernel(**inp):
    C = _mk_consts()
    a = {k: np.asarray(v, np.float32) for k, v in inp.items()}
    sh = _shared_dicts(a, C)
    x = a["L0_attn_x"]
    xs = [np.ascontiguousarray(x[i].T) for i in range(N_CORES)]

    nc = _built("fa", build_fused_a)
    shared_a = {}
    for blk in ("l0a", "l0f", "l1a", "l1f", "l2a", "l2f"):
        for k, v in sh[blk].items():
            shared_a[blk + "_" + k] = v
    xs = _run(nc, shared_a, [{"xres": v} for v in xs])

    X = np.stack(xs)
    xs = [np.ascontiguousarray(
        X[:, :, m * 256:(m + 1) * 256].transpose(1, 2, 0).reshape(D, T))
        for m in range(N_CORES)]

    nc = _built("fb", build_fused_b)
    shared_b = {}
    for blk in ("l3a", "l3f"):
        for k, v in sh[blk].items():
            shared_b[blk + "_" + k] = v
    xs = _run(nc, shared_b, [{"xres": v} for v in xs])

    Y = np.stack(xs).reshape(N_CORES, D, 256, 8)
    out = Y.transpose(3, 0, 2, 1).reshape(N_CORES, T, D)
    return np.ascontiguousarray(out)
